# version 05
# baseline (speedup 1.0000x reference)
import functools
import numpy as np
from contextlib import ExitStack

import concourse.bass as bass
import concourse.mybir as mybir
from concourse.bass import IndirectOffsetOnAxis
from concourse.bass_utils import run_bass_kernel_spmd
from concourse.aot_env import allow_ici_all_to_all

F32 = mybir.dt.float32
I32 = mybir.dt.int32
BF16 = mybir.dt.bfloat16
AF = mybir.ActivationFunctionType
ALU = mybir.AluOpType
AX = mybir.AxisListType

B = 2
S = 8192
D = 2048
NH = 16
DH = 128
NSLOT = 4
TCORE = 2048
NE = 16
I_FF = 1408
NI = 11
SH = 2816
NSI = 22
EPS = 1e-5
SLOPES = [np.float32(1.0 / 2.0 ** (i / 2.0)) for i in range(NH)]

ENGINES = ("pe", "dve", "act", "pool", "sp")
ENG_ATTR = {"pe": "tensor", "dve": "vector", "act": "scalar", "pool": "gpsimd", "sp": "sync"}


class _It:
    __slots__ = ("eng", "fn", "fam", "ford", "deps", "sig", "need", "gi", "hz")

    def __init__(self, eng, fn, fam, ford, gi, hz=False):
        self.eng, self.fn, self.fam, self.ford, self.gi = eng, fn, fam, ford, gi
        self.deps = []
        self.sig = None
        self.need = False
        self.hz = hz


class Prog:

    def __init__(self):
        self.items = []
        self.per_eng = {e: [] for e in ENGINES}
        self.res = {}
        self.fam_count = {}
        self.split_gi = None

    def mark_split(self):
        self.split_gi = len(self.items)

    def emit(self, eng, fn, reads=(), writes=(), fam=None, deps=(), hz=False):
        ford = None
        if fam is not None:
            self.fam_count[fam] = self.fam_count.get(fam, 0) + 1
            ford = self.fam_count[fam]
        it = _It(eng, fn, fam, ford, len(self.items), hz)
        lane = (eng, fam)
        dd = {}
        for k in reads:
            e = self.res.get(k)
            if e:
                for d in e["w"].values():
                    dd[d.gi] = d
        for k in writes:
            e = self.res.get(k)
            if e:
                for d in e["r"].values():
                    dd[d.gi] = d
                for d in e["w"].values():
                    dd[d.gi] = d
        for d in deps:
            dd[d.gi] = d
        dd.pop(it.gi, None)
        it.deps = list(dd.values())
        for k in reads:
            self.res.setdefault(k, {"w": {}, "r": {}})["r"][lane] = it
        for k in writes:
            e = self.res.setdefault(k, {"w": {}, "r": {}})
            e["w"] = {lane: it}
            e["r"] = {}
        self.items.append(it)
        self.per_eng[eng].append(it)
        return it

    def barrier(self):
        lasts = [self.per_eng[e][-1] for e in ENGINES if self.per_eng[e]]
        for e in ENGINES:
            self.emit(e, lambda eng: eng.nop(), deps=lasts)

    def finalize_and_emit(self, nc, ctx):
        for it in self.items:
            for d in it.deps:
                if d.eng == it.eng and d.fam is None and not d.hz:
                    continue
                d.need = True
        sems = {}
        for e in ENGINES:
            cnt = 0
            for it in self.per_eng[e]:
                if it.fam is None and it.need:
                    cnt += 1
                    it.sig = cnt
            if cnt:
                sems[("e", e)] = ctx.enter_context(nc.semaphore("sm_" + e))
        for f in self.fam_count:
            sems[("f", f)] = ctx.enter_context(nc.semaphore("sf_" + str(f)))

        def wait_list(it):
            w = {}
            for d in it.deps:
                if d.eng == it.eng and d.fam is None and not d.hz:
                    continue
                if d.fam is not None:
                    sk, val = ("f", d.fam), 16 * d.ford
                else:
                    sk, val = ("e", d.eng), d.sig
                if w.get(sk, 0) < val:
                    w[sk] = val
            return w

        waters = {e: {} for e in ENGINES}

        def emit_block(block, lo, hi):
            for e in ENGINES:
                items = [it for it in self.per_eng[e] if lo <= it.gi < hi]
                if not items:
                    continue

                def body(eng, items=items, ename=e):
                    water = waters[ename]
                    for it in items:
                        for sk, val in sorted(wait_list(it).items(), key=lambda kv: str(kv[0])):
                            if water.get(sk, 0) < val:
                                eng.wait_ge(sems[sk], val)
                                water[sk] = val
                        inst = it.fn(eng)
                        if it.fam is not None:
                            inst.then_inc(sems[("f", it.fam)], 16)
                        elif it.need:
                            inst.then_inc(sems[("e", ename)], 1)

                getattr(block, ENG_ATTR[e])(body)

        if self.split_gi is None:
            block = ctx.enter_context(nc.Block())
            emit_block(block, 0, len(self.items))
        else:
            with nc.Block() as b1:
                emit_block(b1, 0, self.split_gi)
            nc.all_core_barrier()
            with nc.Block() as b2:
                emit_block(b2, self.split_gi, len(self.items))


def build_nc(part="a", debug=False):
    nc = bass.Bass("TRN2", target_bir_lowering=False)
    p = Prog()

    def din(name, shape, dtype=F32):
        return nc.dram_tensor(name, list(shape), dtype, kind="ExternalInput").ap()

    identw = din("identw", [128, 128])
    onesrow = din("onesrow", [1, 128])
    dbg = {}
    if part in ("a", "p0"):
        xb = din("xb", [S, D])
        qkvw = din("qkvw", [128, 16, 12 * 128])
        qkvb = din("qkvb", [128, 12])
        vbrow = din("vbrow", [1, 512])
        jrow = din("jrow", [1, 512])
        p_idx = din("p_idx", [128, 1])
        slopes = din("slopes", [128, 4])
        s128 = din("s128", [1, 4])
        s512 = din("s512", [1, 4])
        s511 = din("s511", [1, 4])
        diagm = din("diagm", [128, 4, 512])
        diagmA = din("diagmA", [128, 4, 512])
        ctx_dram = nc.dram_tensor("ctxout", [S, NSLOT * DH], BF16, kind="ExternalOutput").ap()
        q_dram = nc.dram_tensor("q_dram", [NSLOT * DH, S], BF16, kind="Internal").ap()
        k_dram = nc.dram_tensor("k_dram", [NSLOT * DH, S], BF16, kind="Internal").ap()
        v_dram = nc.dram_tensor("v_dram", [S, NSLOT * DH], BF16, kind="Internal").ap()
        if debug:
            for nm, t in (("d_q", q_dram), ("d_k", k_dram), ("d_v", v_dram)):
                dbg[nm] = (nc.dram_tensor(nm, list(t.shape), BF16, kind="ExternalOutput").ap(), t)
    else:
        x_own = din("x_own", [TCORE, D])
        ctx_recv = din("ctx_recv", [S, NSLOT * DH], BF16)
        outw = din("outw", [128, 16, D])
        outbrow = din("outbrow", [1, D])
        gatew = din("gatew", [128, 16, 16])
        w1 = din("w1", [NE, 128, 16, I_FF])
        w3 = din("w3", [NE, 128, 16, I_FF])
        w2 = din("w2", [NE, NI, 128, D])
        ws1 = din("ws1", [128, 16, SH])
        ws3 = din("ws3", [128, 16, SH])
        ws2 = din("ws2", [NSI, 128, D])
        out = nc.dram_tensor("out", [TCORE, D], F32, kind="ExternalOutput").ap()
        lnx2_dram = nc.dram_tensor("lnx2_dram", [128, 16, TCORE], BF16, kind="Internal").ap()
        comb_dram = nc.dram_tensor("comb_dram", [16, TCORE], F32, kind="Internal").ap()
        if debug:
            dbg["d_ln2"] = (nc.dram_tensor("d_ln2", [128, 16, TCORE], BF16,
                                           kind="ExternalOutput").ap(), lnx2_dram)
            g_dump = nc.dram_tensor("d_g", [128, 22528], BF16, kind="ExternalOutput").ap()
            comb_dump = nc.dram_tensor("d_comb", [16, TCORE], F32, kind="ExternalOutput").ap()

    with ExitStack() as ctx:
        def alloc(name, shape, dtype):
            return nc.alloc_sbuf_tensor(name, list(shape), dtype).ap()

        POOLN = 90112
        POOL = nc.alloc_sbuf_tensor("POOL", [128, POOLN], mybir.dt.uint8).ap()

        def sub(off, shape, dtype):
            es = 2 if dtype == BF16 else 4
            n = int(np.prod(shape[1:])) * es
            assert off % 4 == 0 and off + n <= POOLN, (off, n)
            ap = POOL[0:shape[0], off:off + n].bitcast(dtype)
            if len(shape) == 3:
                ap = ap.rearrange("p (a b) -> p a b", a=shape[1])
            return ap

        xt = [sub(0, [128, D], F32), sub(8192, [128, D], F32)]
        sq = sub(16384, [128, D], F32)
        lnt = [sub(24576, [128, D], BF16), sub(28672, [128, D], BF16)]
        qkt = [sub(32768, [128, 512], BF16), sub(33792, [128, 512], BF16)]
        vt = [sub(34816, [128, 512], BF16), sub(35840, [128, 512], BF16)]
        vbB = sub(36864, [128, 512], F32)
        qkvw_sb = sub(38912, [128, 16, 1536], BF16)
        vbrow_s = sub(88064, [1, 512], F32)
        outbrow_s = sub(88064, [1, 512], F32)
        pt = [sub(38912, [128, 512], BF16), sub(39936, [128, 512], BF16)]
        ctt = [sub(40960, [128, 16, 128], BF16), sub(45056, [128, 16, 128], BF16)]
        lnf = [sub(49152, [128, 16, 128], BF16), sub(53248, [128, 16, 128], BF16)]
        x1t = [sub(57344, [128, D], F32), sub(65536, [128, D], F32)]
        outbB = sub(73728, [128, D], F32)
        diag_s = sub(81920, [128, 4, 512], BF16)
        diagA_s = sub(86016, [128, 4, 512], BF16)
        ctxbf = [sub(57344 + 256 * i, [128, 128], BF16) for i in range(4)]
        negirow = sub(58368, [1, 512], BF16)
        ctf = ctt
        LNX2 = sub(0, [128, 16, 1024], BF16)
        s1b = [sub(32768, [128, 1024], BF16), sub(34816, [128, 1024], BF16)]
        t3f = [sub(36864, [128, 1024], F32), sub(40960, [128, 1024], F32)]
        ytile = [sub(45056, [128, 1024], F32), sub(49152, [128, 1024], F32)]
        combf = sub(53248, [16, 1024], F32)

        ident_bf = alloc("ident_bf", [128, 128], BF16)
        ident_f = alloc("ident_f", [128, 128], F32)
        ones_f = alloc("ones_f", [1, 128], F32)
        ones_bf = alloc("ones_bf", [1, 128], BF16)
        jrow_s = alloc("jrow_s", [1, 512], F32)
        pidx_s = alloc("pidx_s", [128, 1], F32)
        slopes_s = alloc("slopes_s", [128, 4], F32)
        s128_s = alloc("s128_s", [1, 4], F32)
        s512_s = alloc("s512_s", [1, 4], F32)
        s511_s = alloc("s511_s", [1, 4], F32)
        qkvb_s = alloc("qkvb_s", [128, 12], F32)
        gate_s = alloc("gate_s", [128, 16, 16], BF16)
        biasb = [alloc(f"biasb{i}", [128, 1], F32) for i in range(4)]
        mhat2 = alloc("mhat2", [128, 16], F32)
        zr4 = alloc("zr4", [128, 4], F32)
        stat = [alloc(f"stat{i}", [128, 8], F32) for i in range(2)]
        top8b = alloc("top8b", [128, 8], F32)
        mhat = alloc("mhat", [128, 64], F32)
        mcol = alloc("mcol", [64, 1], F32)
        drow = alloc("drow", [1, 16], F32)
        alibi_loc = alloc("alibi_loc", [1, 512], BF16)
        crow = alloc("crow", [1, 64], F32)
        adjrow = alloc("adjrow", [1, 16], F32)
        adjB = alloc("adjBb", [128, 16], F32)
        sp_col = alloc("sp_col", [128, 1], F32)
        alibi_p = alloc("alibi_p", [128, 64], F32)
        negD = alloc("negD", [128, 16], F32)
        sco = [alloc(f"sco{i}", [128, 16], F32) for i in range(2)]
        rscr = alloc("rscr", [128, 40], F32)
        combrow = alloc("combrow", [1, 1024], F32)
        sidxst = sub(32768, [128, 256], I32)
        sidxld = sub(33792, [128, 256], I32)
        sidxq = sub(34816, [128, 8], I32)

        R1 = alloc("R1", [128, 22528], BF16)
        R2 = alloc("R2", [128, 32768], BF16)

        PS = nc.alloc_psum_tensor("PS", [128, 4096], F32).ap()
        PSb = PS.bitcast(BF16)

        def bank_f(i, n=512):
            return PS[:, i * 512: i * 512 + n]

        def bank_bf(i, n=128):
            return PSb[:, i * 1024: i * 1024 + n]

        lnx_fm = R2.rearrange("p (dc t) -> p dc t", dc=16)
        K_sb = R2[:, 0:8192]
        Q_sb = R2[:, 8192:16384]
        V_sb = R2[:, 16384:16384 + 64 * 129].rearrange("p (c e) -> p c e", c=64)
        outw_sb = R2.rearrange("p (hd t) -> p hd t", hd=16)
        g_sb = R2[:, 0:22528].rearrange("p (ic t) -> p ic t", ic=22)
        w1b = [R2[:, 22528 + i * 2048: 22528 + (i + 1) * 2048].rearrange("p (dc j) -> p dc j", dc=16)
               for i in range(2)]
        w3b = [R2[:, 26624 + i * 2048: 26624 + (i + 1) * 2048].rearrange("p (dc j) -> p dc j", dc=16)
               for i in range(2)]
        combB = R2[:, 30720:32768].bitcast(F32)
        w2b_r = [R1[:, i * 11264:(i + 1) * 11264].rearrange("p (ic j) -> p ic j", ic=NI) for i in range(2)]
        w2b_s = [R1[:, i * 11264:(i + 1) * 11264].rearrange("p (ic j) -> p ic j", ic=NSI) for i in range(2)]

        _fam_ctr = [0]

        def dma(q, dst, src, fam, reads=(), writes=(), accum=False):
            if fam is None:
                _fam_ctr[0] += 1
                fam = f"u{_fam_ctr[0]}"
            if accum:
                fn = lambda eng, dst=dst, src=src: eng.dma_start(out=dst, in_=src, accum_op=ALU.add)
            else:
                fn = lambda eng, dst=dst, src=src: eng.dma_start(out=dst, in_=src)
            return p.emit(q, fn, reads=reads, writes=writes, fam=fam)

        dma("pool", ident_bf[:], identw, None, writes=["ident_bf"])
        dma("sp", ident_f[:], identw, None, writes=["ident_f"])
        dma("sp", ones_f[:], onesrow, None, writes=["ones_f"])
        dma("pool", ones_bf[:], onesrow, None, writes=["ones_bf"])
        if part in ("a", "p0"):
            dma("sp", jrow_s[:], jrow, None, writes=["jrow_s"])
            dma("sp", pidx_s[:], p_idx, None, writes=["pidx_s"])
            dma("sp", slopes_s[:], slopes, None, writes=["slopes_s"])
            dma("sp", s128_s[:], s128, None, writes=["s128_s"])
            dma("sp", s512_s[:], s512, None, writes=["s512_s"])
            dma("sp", s511_s[:], s511, None, writes=["s511_s"])
            dma("sp", qkvb_s[:], qkvb, None, writes=["qkvb_s"])
            dma("sp", vbrow_s[:], vbrow, None, writes=["vbrow_s"])
            for i in range(4):
                dma("pool", qkvw_sb[:, 4 * i:4 * (i + 1), :], qkvw[:, 4 * i:4 * (i + 1), :],
                    "wqkv", writes=[f"R1_{i}"])
            p.emit("pe", lambda eng: eng.matmul(bank_f(6), ones_f[:], vbrow_s[:],
                                                start=True, stop=True),
                   reads=["ones_f", "vbrow_s"], writes=["pb6"])
            p.emit("dve", lambda eng: eng.tensor_copy(vbB[:], bank_f(6)),
                   reads=["pb6"], writes=["vbB"], hz=True)
        else:
            dma("pool", gate_s[:], gatew, None, writes=["gate_s"])
            for i in range(4):
                dma("pool", outw_sb[:, 4 * i:4 * (i + 1), :], outw[:, 4 * i:4 * (i + 1), :],
                    "wout", writes=[f"OW{i}"])

        def ln_tile(xin, xkey, outbf, outkey, sl):
            st = stat[sl]
            s1, nm, v1, sd, rstd, mr = (st[:, i:i + 1] for i in range(6))
            kst = f"stat{sl}"
            p.emit("dve", lambda eng: eng.reduce_sum(s1, xin, axis=AX.X),
                   reads=[xkey], writes=[kst], hz=True)
            p.emit("dve", lambda eng: eng.tensor_scalar(nm, s1, -1.0 / D, None, op0=ALU.mult),
                   reads=[kst], writes=[kst], hz=True)
            p.emit("act", lambda eng: eng.activation(sq[:], xin, AF.Square, bias=nm, scale=1.0),
                   reads=[xkey, kst], writes=["sq"])
            p.emit("dve", lambda eng: eng.reduce_sum(v1, sq[:], axis=AX.X),
                   reads=["sq"], writes=[kst], hz=True)
            p.emit("dve", lambda eng: eng.tensor_scalar(v1, v1, 1.0 / D, EPS,
                                                        op0=ALU.mult, op1=ALU.add),
                   reads=[kst], writes=[kst], hz=True)
            p.emit("act", lambda eng: eng.activation(sd, v1, AF.Sqrt),
                   reads=[kst], writes=[kst], hz=True)
            p.emit("dve", lambda eng: eng.reciprocal(rstd, sd), reads=[kst], writes=[kst], hz=True)
            p.emit("dve", lambda eng: eng.tensor_tensor(mr, nm, rstd, op=ALU.mult),
                   reads=[kst], writes=[kst], hz=True)
            p.emit("dve", lambda eng: eng.tensor_scalar(outbf, xin, nm, rstd,
                                                        op0=ALU.add, op1=ALU.mult),
                   reads=[xkey, kst], writes=[outkey])

        def transpose16(src_bf, srckey, dst3, dstkey, tslice):
            for dc in range(16):
                bk = 4 + (dc % 4)
                p.emit("pe", lambda eng, dc=dc, bk=bk: eng.transpose(
                    bank_bf(bk), src_bf[:, dc * 128:(dc + 1) * 128], ident_bf[:]),
                    reads=[srckey, "ident_bf"], writes=[f"pb{bk}"])
                p.emit("dve", lambda eng, dc=dc, bk=bk: eng.tensor_copy(
                    dst3[:, dc, tslice], bank_bf(bk)),
                    reads=[f"pb{bk}"], writes=[dstkey])

        for slab in (range(4) if part in ("a", "p0") else []):
            base = slab * 2048
            for t in range(16):
                sl = t % 2
                dma("pool", xt[sl][:], xb[base + t * 128: base + (t + 1) * 128, :], "xb",
                    writes=[f"xt{sl}"])
                ln_tile(xt[sl][:], f"xt{sl}", lnt[sl][:], f"lnt{sl}", sl)
                transpose16(lnt[sl][:], f"lnt{sl}", lnx_fm, "lnxfm",
                            slice(t * 128, (t + 1) * 128))
            for f in range(8):
                for ts in range(4):
                    bk = (f * 4 + ts) % 2
                    sl = (f * 4 + ts) % 2

                    def mm_qk(eng, f=f, ts=ts, bk=bk):
                        for dc in range(16):
                            inst = eng.matmul(bank_f(bk),
                                              qkvw_sb[:, dc, f * 128:(f + 1) * 128],
                                              lnx_fm[:, dc, ts * 512:(ts + 1) * 512],
                                              start=(dc == 0), stop=(dc == 15))
                        return inst
                    p.emit("pe", mm_qk, reads=["lnxfm", "R1_0", "R1_1", "R1_2", "R1_3"],
                           writes=[f"pb{bk}"])
                    p.emit("act", lambda eng, f=f, bk=bk, sl=sl: eng.activation(
                        qkt[sl][:], bank_f(bk), AF.Identity, bias=qkvb_s[:, f:f + 1], scale=1.0),
                        reads=[f"pb{bk}", "qkvb_s"], writes=[f"qkt{sl}"])
                    dst = q_dram if f < 4 else k_dram
                    dkey = "qdram" if f < 4 else "kdram"
                    frow = (f % 4) * 128
                    dma("sp", dst[frow:frow + 128, base + ts * 512: base + (ts + 1) * 512],
                        qkt[sl][:], "qkst", reads=[f"qkt{sl}"], writes=[dkey])
            for t in range(16):
                bk = 2 + t % 2
                sl = t % 2

                def mm_v(eng, t=t, bk=bk):
                    for dc in range(16):
                        inst = eng.matmul(bank_f(bk),
                                          lnx_fm[:, dc, t * 128:(t + 1) * 128],
                                          qkvw_sb[:, dc, 1024:1536],
                                          start=(dc == 0), stop=(dc == 15))
                    return inst
                p.emit("pe", mm_v, reads=["lnxfm", "R1_0", "R1_1", "R1_2", "R1_3"],
                       writes=[f"pb{bk}"])
                p.emit("dve", lambda eng, bk=bk, sl=sl: eng.tensor_tensor(
                    vt[sl][:], bank_f(bk), vbB[:], op=ALU.add),
                    reads=[f"pb{bk}", "vbB"], writes=[f"vt{sl}"])
                dma("sp", v_dram[base + t * 128: base + (t + 1) * 128, :], vt[sl][:],
                    "vst", reads=[f"vt{sl}"], writes=["vdram"])

        if part in ("a", "p0"):
            p.barrier()

        if part == "a":
            dma("pool", diag_s[:], diagm, None, writes=["diag_s"])
            dma("pool", diagA_s[:], diagmA, None, writes=["diagA_s"])
            p.emit("dve", lambda eng: eng.memset(V_sb[:, :, 128:129], 1.0), writes=["Vsb"])

        for s in (range(NSLOT) if part == "a" else []):
            dma("sp", K_sb, k_dram[s * 128:(s + 1) * 128, :], "kqv",
                reads=["kdram"], writes=["Ksb"])
            dma("sp", Q_sb, q_dram[s * 128:(s + 1) * 128, :], "kqv",
                reads=["qdram"], writes=["Qsb"])
            dma("sp", V_sb[:, :, 0:128],
                v_dram.rearrange("(c p) d -> p c d", p=128)[:, :, s * 128:(s + 1) * 128],
                "kqv", reads=["vdram"], writes=["Vsb"])

            p.emit("act", lambda eng, s=s: eng.activation(
                alibi_loc[:], jrow_s[:], AF.Identity, bias=0.0, scale=slopes_s[0:1, s:s + 1]),
                reads=["jrow_s", "slopes_s"], writes=["aloc"])
            if s < 2:
                p.emit("dve", lambda eng, s=s: eng.tensor_scalar(
                    negirow[:], jrow_s[:], slopes_s[0:1, s:s + 1], -1.0,
                    op0=ALU.mult, op1=ALU.mult),
                    reads=["jrow_s", "slopes_s"], writes=["negi"], hz=True)
            p.emit("dve", lambda eng, s=s: eng.tensor_scalar(
                crow[:], jrow_s[0:1, 0:64], s128_s[0:1, s:s + 1], None, op0=ALU.mult),
                reads=["jrow_s", "s128_s"], writes=["crow"], hz=True)
            p.emit("dve", lambda eng, s=s: eng.tensor_scalar(
                sp_col[:], pidx_s[:], slopes_s[:, s:s + 1], None, op0=ALU.mult),
                reads=["pidx_s", "slopes_s"], writes=["sp_col"], hz=True)
            p.emit("pe", lambda eng: eng.matmul(bank_f(2, 64), ones_f[:], crow[:],
                                                start=True, stop=True),
                   reads=["ones_f", "crow"], writes=["pb2"])
            p.emit("dve", lambda eng: eng.tensor_scalar(
                alibi_p[:], bank_f(2, 64), sp_col[:], None, op0=ALU.add),
                reads=["pb2", "sp_col"], writes=["alibi_p"], hz=True)
            if s < 2:
                p.emit("dve", lambda eng, s=s: eng.tensor_scalar(
                    adjrow[:], jrow_s[0:1, 0:16], s512_s[0:1, s:s + 1], None, op0=ALU.mult),
                    reads=["jrow_s", "s512_s"], writes=["adjrow"], hz=True)
            else:
                p.emit("dve", lambda eng, s=s: eng.tensor_scalar(
                    adjrow[:], jrow_s[0:1, 0:16], s512_s[0:1, s:s + 1],
                    s511_s[0:1, s:s + 1], op0=ALU.mult, op1=ALU.add),
                    reads=["jrow_s", "s512_s", "s511_s"], writes=["adjrow"], hz=True)
            p.emit("pe", lambda eng: eng.matmul(bank_f(2, 16), ones_f[:], adjrow[:],
                                                start=True, stop=True),
                   reads=["ones_f", "adjrow"], writes=["pb2"])
            p.emit("dve", lambda eng: eng.tensor_copy(adjB[:], bank_f(2, 16)),
                   reads=["pb2"], writes=["adjB"], hz=True)

            for u in range(64):
                nch = u // 4 + 1
                for c in range(nch):
                    bk = (u + c) % 2

                    def mm_s(eng, u=u, c=c, bk=bk, s=s):
                        if s < 2:
                            eng.matmul(bank_f(bk), Q_sb[:, u * 128:(u + 1) * 128],
                                       K_sb[:, c * 512:(c + 1) * 512], start=True, stop=False)
                            return eng.matmul(bank_f(bk), ones_bf[:], alibi_loc[:],
                                              start=False, stop=True)
                        return eng.matmul(bank_f(bk), Q_sb[:, u * 128:(u + 1) * 128],
                                          K_sb[:, c * 512:(c + 1) * 512], start=True, stop=True)
                    p.emit("pe", mm_s, reads=["Qsb", "Ksb", "aloc", "ones_bf"],
                           writes=[f"pb{bk}"])
                    if s < 2 and c == nch - 1:
                        p.emit("dve", lambda eng, u=u, bk=bk: eng.tensor_tensor(
                            bank_f(bk), bank_f(bk), diagA_s[:, u % 4, :], op=ALU.add),
                            reads=["diagA_s"], writes=[f"pb{bk}"])
                    p.emit("dve", lambda eng, c=c, bk=bk: eng.reduce_max(
                        mhat2[:, c:c + 1], bank_f(bk), axis=AX.X),
                        reads=[f"pb{bk}"], writes=[f"mh{c}"], hz=True)
                p.emit("dve", lambda eng, nch=nch: eng.tensor_tensor(
                    mhat2[:, 0:nch], mhat2[:, 0:nch], adjB[:, 0:nch], op=ALU.add),
                    reads=["adjB"] + [f"mh{c}" for c in range(nch)],
                    writes=[f"mh{c}" for c in range(nch)], hz=True)
                p.emit("dve", lambda eng, u=u, nch=nch: eng.reduce_max(
                    mhat[:, u:u + 1], mhat2[:, 0:nch], axis=AX.X),
                    reads=[f"mh{c}" for c in range(nch)], writes=["mhat"], hz=True)
                if s < 2:
                    p.emit("dve", lambda eng, u=u: eng.tensor_tensor(
                        mhat[:, u:u + 1], mhat[:, u:u + 1], alibi_p[:, u:u + 1],
                        op=ALU.subtract),
                        reads=["alibi_p", "mhat"], writes=["mhat"], hz=True)

            p.emit("pe", lambda eng: eng.transpose(PS[0:64, 1024:1152], mhat[:, 0:64], ident_f[:]),
                   reads=["mhat", "ident_f"], writes=["pb2"])
            p.emit("dve", lambda eng: eng.reduce_max(mcol[:], PS[0:64, 1024:1152], axis=AX.X),
                   reads=["pb2"], writes=["mcol"], hz=True)
            p.emit("pe", lambda eng: eng.transpose(PS[0:1, 1536:1600], mcol[0:64, 0:1],
                                                   ident_f[0:64, 0:64]),
                   reads=["mcol", "ident_f"], writes=["pb3"])
            p.emit("dve", lambda eng: eng.tensor_reduce(
                drow[:], PS[0:1, 1536:1600].rearrange("p (a b) -> p a b", a=16),
                axis=AX.X, op=ALU.max),
                reads=["pb3"], writes=["drow"], hz=True)
            if s < 2:
                p.emit("dve", lambda eng: eng.tensor_tensor(
                    drow[:], drow[:], adjrow[0:1, 0:16], op=ALU.add),
                    reads=["drow", "adjrow"], writes=["drow"], hz=True)
            p.emit("dve", lambda eng: eng.tensor_scalar(drow[:], drow[:], -1.0, None, op0=ALU.mult),
                   reads=["drow"], writes=["drow"], hz=True)
            p.emit("pe", lambda eng: eng.matmul(bank_f(2, 16), ones_f[:], drow[:],
                                                start=True, stop=True),
                   reads=["ones_f", "drow"], writes=["pb2"])
            p.emit("dve", lambda eng: eng.tensor_copy(negD[:], bank_f(2, 16)),
                   reads=["pb2"], writes=["negD"], hz=True)

            for t in range(16):
                nkc = 4 * t + 4
                for kc in range(nkc):
                    bk = kc % 2

                    def mm_st(eng, t=t, kc=kc, bk=bk, s=s):
                        if s < 2:
                            eng.matmul(bank_f(bk), K_sb[:, kc * 128:(kc + 1) * 128],
                                       Q_sb[:, t * 512:(t + 1) * 512], start=True, stop=False)
                            return eng.matmul(bank_f(bk), ones_bf[:], negirow[:],
                                              start=False, stop=True)
                        return eng.matmul(bank_f(bk), K_sb[:, kc * 128:(kc + 1) * 128],
                                          Q_sb[:, t * 512:(t + 1) * 512], start=True, stop=True)
                    p.emit("pe", mm_st, reads=["Ksb", "Qsb", "negi", "ones_bf"],
                           writes=[f"pb{bk}"])
                    if kc >= 4 * t:
                        p.emit("dve", lambda eng, t=t, kc=kc, bk=bk: eng.tensor_tensor(
                            bank_f(bk), bank_f(bk), diag_s[:, kc - 4 * t, :], op=ALU.add),
                            reads=["diag_s"], writes=[f"pb{bk}"])
                    bsl = kc % 4
                    p.emit("dve", lambda eng, t=t, kc=kc, bsl=bsl: eng.tensor_tensor(
                        biasb[bsl][:], alibi_p[:, kc:kc + 1], negD[:, t:t + 1], op=ALU.add),
                        reads=["alibi_p", "negD"], writes=[f"bias{bsl}"])
                    psl = kc % 2
                    p.emit("act", lambda eng, bk=bk, bsl=bsl, psl=psl: eng.activation(
                        pt[psl][:], bank_f(bk), AF.Exp, bias=biasb[bsl][:], scale=1.0),
                        reads=[f"pb{bk}", f"bias{bsl}"], writes=[f"pt{psl}"])

                    if debug and s == 0 and t == 0 and kc == 0:
                        dma("sp", d_pt, pt[psl][:], None, reads=[f"pt{psl}"])
                        dma("sp", d_bias, biasb[bsl][:], None, reads=[f"bias{bsl}"])

                    def pv(eng, kc=kc, psl=psl, nkc=nkc):
                        for qq in range(4):
                            inst = eng.matmul(PS[:, (4 + qq) * 512:(4 + qq) * 512 + 129],
                                              pt[psl][:, qq * 128:(qq + 1) * 128],
                                              V_sb[:, kc, :],
                                              start=(kc == 0), stop=(kc == nkc - 1))
                        return inst
                    p.emit("pe", pv, reads=[f"pt{psl}", "Vsb"],
                           writes=["pb4", "pb5", "pb6", "pb7"])
                zcols = PS[:, 2048:4096].rearrange("p (q x) -> p q x", q=4)[:, :, 128:129]
                p.emit("dve", lambda eng, zcols=zcols: eng.tensor_scalar(
                    zr4[:, :, None], zcols, 1e-37, None, op0=ALU.max),
                    reads=["pb4", "pb5", "pb6", "pb7"], writes=["zr4"], hz=True)
                p.emit("dve", lambda eng: eng.reciprocal(zr4[:], zr4[:]),
                       reads=["zr4"], writes=["zr4"], hz=True)
                for qq in range(4):
                    p.emit("dve", lambda eng, qq=qq: eng.tensor_scalar(
                        ctxbf[qq][:], PS[:, (4 + qq) * 512:(4 + qq) * 512 + 128],
                        zr4[:, qq:qq + 1], None, op0=ALU.mult),
                        reads=[f"pb{4 + qq}", "zr4"], writes=[f"ctxbf{qq}"])
                    dma("sp", ctx_dram[t * 512 + qq * 128: t * 512 + (qq + 1) * 128,
                                       s * 128:(s + 1) * 128],
                        ctxbf[qq][:], "ctxst", reads=[f"ctxbf{qq}"], writes=["ctxdram"])

        for i in (range(4) if part == "b" else []):
            dma("sp", outbrow_s[:], outbrow[0:1, i * 512:(i + 1) * 512], None,
                reads=["outbrow_done"], writes=["outbrow_s"])
            p.emit("pe", lambda eng: eng.matmul(bank_f(2), ones_f[:], outbrow_s[:],
                                                start=True, stop=True),
                   reads=["ones_f", "outbrow_s"], writes=["pb2"])
            p.emit("dve", lambda eng, i=i: eng.tensor_copy(outbB[:, i * 512:(i + 1) * 512],
                                                           bank_f(2)),
                   reads=["pb2", "outbrow_s"], writes=["outbB", "outbrow_done"], hz=True)

        for t in (range(16) if part == "b" else []):
            sl = t % 2
            for h in range(16):
                dma("sp", ctt[sl][:, h, :],
                    ctx_recv.rearrange("(r tt) d -> r tt d", r=4)[
                        h % 4, t * 128:(t + 1) * 128, (h // 4) * 128:(h // 4 + 1) * 128],
                    "ctld", reads=["ctxrecv"], writes=[f"ctt{sl}"])
            for hd in range(16):
                bk = hd % 2
                p.emit("pe", lambda eng, hd=hd, bk=bk, sl=sl: eng.transpose(
                    bank_bf(bk), ctt[sl][:, hd, :], ident_bf[:]),
                    reads=[f"ctt{sl}", "ident_bf"], writes=[f"pb{bk}"])
                p.emit("dve", lambda eng, hd=hd, bk=bk, sl=sl: eng.tensor_copy(
                    ctf[sl][:, hd, :], bank_bf(bk)),
                    reads=[f"pb{bk}"], writes=[f"ctt{sl}"])

            def mm_out(eng, sl=sl):
                for hd in range(16):
                    for n in range(4):
                        inst = eng.matmul(PS[:, (4 + n) * 512:(5 + n) * 512],
                                          ctf[sl][:, hd, :],
                                          outw_sb[:, hd, n * 512:(n + 1) * 512],
                                          start=(hd == 0), stop=(hd == 15))
                return inst
            p.emit("pe", mm_out, reads=[f"ctt{sl}", "OW0", "OW1", "OW2", "OW3"],
                   writes=["pb4", "pb5", "pb6", "pb7"])
            dma("sp", xt[sl][:], x_own[t * 128:(t + 1) * 128, :], "xo", writes=[f"xt{sl}"])
            p.emit("dve", lambda eng, sl=sl: eng.tensor_tensor(
                x1t[sl][:], PS[:, 2048:4096], outbB[:], op=ALU.add),
                reads=["pb4", "pb5", "pb6", "pb7", "outbB"], writes=[f"x1t{sl}"])
            p.emit("dve", lambda eng, sl=sl: eng.tensor_tensor(
                x1t[sl][:], x1t[sl][:], xt[sl][:], op=ALU.add),
                reads=[f"xt{sl}", f"x1t{sl}"], writes=[f"x1t{sl}"])
            dma("sp", out[t * 128:(t + 1) * 128, :], x1t[sl][:], "x1st",
                reads=[f"x1t{sl}"],
                writes=[f"y{t // 8}_{t % 8}_0", f"y{t // 8}_{t % 8}_1"])
            ln_tile(x1t[sl][:], f"x1t{sl}", lnt[sl][:], f"lnt{sl}", sl)
            transpose16(lnt[sl][:], f"lnt{sl}", lnf[sl], f"lnf{sl}", slice(0, 128))
            dma("sp", lnx2_dram[:, :, t * 128:(t + 1) * 128], lnf[sl][:, :, :], "ln2st",
                reads=[f"lnf{sl}"], writes=["ln2dram"])

        if part == "b":
            p.barrier()

        NSUB = 8
        for b2 in (range(2) if part == "b" else []):
            tb = b2 * 1024
            dma("pool", LNX2[:, :, :], lnx2_dram[:, :, tb:tb + 1024], "ln2ld",
                reads=["ln2dram"], writes=["lnx2b"])
            for tt in range(8):
                sl = tt % 2

                def mm_gate(eng, tt=tt):
                    for dc in range(16):
                        inst = eng.matmul(bank_f(6, 16), LNX2[:, dc, tt * 128:(tt + 1) * 128],
                                          gate_s[:, dc, :], start=(dc == 0), stop=(dc == 15))
                    return inst
                p.emit("pe", mm_gate, reads=["lnx2b", "gate_s"], writes=["pb6"])
                p.emit("act", lambda eng, sl=sl: eng.activation(
                    sco[sl][:], bank_f(6, 16), AF.Sigmoid, bias=0.0, scale=1.0),
                    reads=["pb6"], writes=[f"sco{sl}"])

                sc = sco[sl]
                gmax = rscr[:, 0:4]
                m1 = rscr[:, 4:5]
                mk1 = rscr[:, 5:9]
                g2 = rscr[:, 9:13]
                m2 = rscr[:, 13:14]
                gmask = rscr[:, 14:18]
                masked = rscr[:, 18:34]
                ssum = rscr[:, 35:36]
                RT = ["rt"]
                p.emit("dve", lambda eng, sc=sc: eng.tensor_reduce(
                    gmax, sc.rearrange("p (g m) -> p g m", g=4), axis=AX.X, op=ALU.max),
                    reads=[f"sco{sl}"], writes=RT, hz=True)
                p.emit("dve", lambda eng: eng.reduce_max(m1, gmax, axis=AX.X),
                       reads=RT, writes=RT, hz=True)
                p.emit("dve", lambda eng: eng.tensor_scalar(mk1, gmax, m1, None, op0=ALU.is_ge),
                       reads=RT, writes=RT, hz=True)
                p.emit("dve", lambda eng: eng.scalar_tensor_tensor(
                    g2, mk1, -1e30, gmax, op0=ALU.mult, op1=ALU.add),
                    reads=RT, writes=RT, hz=True)
                p.emit("dve", lambda eng: eng.reduce_max(m2, g2, axis=AX.X),
                       reads=RT, writes=RT, hz=True)
                p.emit("dve", lambda eng: eng.tensor_scalar(gmask, gmax, m2, None, op0=ALU.is_ge),
                       reads=RT, writes=RT, hz=True)

                def maskmul(eng, sc=sc):
                    for g4 in range(4):
                        inst = eng.tensor_scalar(masked[:, g4 * 4:(g4 + 1) * 4],
                                                 sc[:, g4 * 4:(g4 + 1) * 4],
                                                 gmask[:, g4:g4 + 1], None, op0=ALU.mult)
                    return inst
                p.emit("dve", maskmul, reads=RT + [f"sco{sl}"], writes=RT, hz=True)
                p.emit("dve", lambda eng: eng.max(out=top8b[:], in_=masked),
                       reads=RT, writes=RT + ["top8b"], hz=True)
                p.emit("dve", lambda eng: eng.tensor_scalar(
                    masked, masked, top8b[:, 3:4], None, op0=ALU.is_ge),
                    reads=RT + ["top8b"], writes=RT, hz=True)
                p.emit("dve", lambda eng, sc=sc: eng.tensor_tensor(masked, sc, masked, op=ALU.mult),
                       reads=RT + [f"sco{sl}"], writes=RT, hz=True)
                p.emit("dve", lambda eng: eng.reduce_sum(ssum, masked, axis=AX.X),
                       reads=RT, writes=RT, hz=True)
                p.emit("dve", lambda eng: eng.reciprocal(ssum, ssum),
                       reads=RT, writes=RT, hz=True)
                p.emit("dve", lambda eng: eng.tensor_scalar(masked, masked, ssum, None, op0=ALU.mult),
                       reads=RT, writes=RT, hz=True)
                p.emit("pe", lambda eng: eng.transpose(PS[0:16, 3584:3712], rscr[:, 18:34],
                                                       ident_f[:]),
                       reads=["rt", "ident_f"], writes=["pb7"])
                p.emit("dve", lambda eng, tt=tt: eng.tensor_copy(
                    combf[0:16, tt * 128:(tt + 1) * 128], PS[0:16, 3584:3712]),
                    reads=["pb7"], writes=["combf"], hz=True)
            dma("sp", comb_dram[:, tb:tb + 1024], combf[:], "combst",
                reads=["combf"], writes=["combdram"])

            for e in range(NE + 1):
                shared = (e == NE)
                nIc = NSI if shared else NI
                w1src = ws1 if shared else w1[e]
                w3src = ws3 if shared else w3[e]
                if not shared:
                    dma("pool", combrow[:], comb_dram[e:e + 1, tb:tb + 1024], "combld",
                        reads=["combdram"], writes=["combrow"])
                    for i in range(2):
                        p.emit("pe", lambda eng, i=i: eng.matmul(
                            bank_f(6), ones_f[:],
                            combrow[0:1, i * 512:(i + 1) * 512], start=True, stop=True),
                            reads=["combrow", "ones_f"], writes=["pb6"])
                        p.emit("dve", lambda eng, i=i: eng.tensor_copy(
                            combB[:, i * 512:(i + 1) * 512], bank_f(6)),
                            reads=["pb6"], writes=["combB"], hz=True)
                for ic in range(nIc):
                    wsl = ic % 2
                    dma("pool", w1b[wsl][:, :, :], w1src[:, :, ic * 128:(ic + 1) * 128],
                        "w1", writes=[f"w1b{wsl}"])
                    dma("pool", w3b[wsl][:, :, :], w3src[:, :, ic * 128:(ic + 1) * 128],
                        "w3", writes=[f"w3b{wsl}"])

                    def mm_h(eng, wb, bko, wkey):
                        for ns in range(2):
                            for dc in range(16):
                                inst = eng.matmul(bank_f(bko + ns),
                                                  wb[:, dc, :],
                                                  LNX2[:, dc, ns * 512:(ns + 1) * 512],
                                                  start=(dc == 0), stop=(dc == 15))
                        return inst
                    p.emit("pe", functools.partial(mm_h, wb=w1b[wsl], bko=0, wkey=None),
                           reads=[f"w1b{wsl}", "lnx2b"], writes=["pb0", "pb1"])
                    p.emit("pe", functools.partial(mm_h, wb=w3b[wsl], bko=2, wkey=None),
                           reads=[f"w3b{wsl}", "lnx2b"], writes=["pb2", "pb3"])
                    ssl = ic % 2
                    p.emit("act", lambda eng, ssl=ssl: eng.activation(
                        s1b[ssl][:], PS[:, 0:1024], AF.Silu, bias=0.0, scale=1.0),
                        reads=["pb0", "pb1"], writes=[f"s1b{ssl}"])
                    if shared:
                        p.emit("dve", lambda eng, ic=ic, ssl=ssl: eng.tensor_tensor(
                            g_sb[:, ic, :], PS[:, 1024:2048], s1b[ssl][:], op=ALU.mult),
                            reads=["pb2", "pb3", f"s1b{ssl}"], writes=["g_sb"])
                    else:
                        p.emit("dve", lambda eng, ssl=ssl: eng.tensor_tensor(
                            t3f[ssl][:], PS[:, 1024:2048], combB[:], op=ALU.mult),
                            reads=["pb2", "pb3", "combB"], writes=[f"t3f{ssl}"])
                        p.emit("dve", lambda eng, ic=ic, ssl=ssl: eng.tensor_tensor(
                            g_sb[:, ic, :], t3f[ssl][:], s1b[ssl][:], op=ALU.mult),
                            reads=[f"t3f{ssl}", f"s1b{ssl}"], writes=["g_sb"])
                if debug and b2 == 0 and e == 0:
                    dma("sp", g_dump, R2[:, 0:22528], None, reads=["g_sb"])
                nH = 4 if shared else 2
                wdt = 512 if shared else 1024
                w2srcv = ws2 if shared else w2[e]
                w2bv = w2b_s if shared else w2b_r
                for hf in range(nH):
                    wsl = hf % 2
                    dma("pool", w2bv[wsl][:, :, :],
                        w2srcv.rearrange("a p j -> p a j")[:, :, hf * wdt:(hf + 1) * wdt],
                        "w2", writes=[f"w2b{wsl}"])
                    for sub in range(NSUB):
                        nbk = wdt // 512
                        ybk = 4 + (sub % 2) * nbk

                        def mm_y(eng, hf=hf, sub=sub, wsl=wsl, ybk=ybk, nIc=nIc, nbk=nbk,
                                 w2bv=w2bv):
                            for ic in range(nIc):
                                for n2 in range(nbk):
                                    inst = eng.matmul(
                                        bank_f(ybk + n2),
                                        g_sb[:, ic, sub * 128:(sub + 1) * 128],
                                        w2bv[wsl][:, ic, n2 * 512:(n2 + 1) * 512],
                                        start=(ic == 0), stop=(ic == nIc - 1))
                            return inst
                        ykeys = [f"pb{ybk + n2}" for n2 in range(nbk)]
                        p.emit("pe", mm_y, reads=["g_sb", f"w2b{wsl}"], writes=ykeys)
                        ysl = sub % 2
                        p.emit("dve", lambda eng, ybk=ybk, ysl=ysl, wdt=wdt: eng.tensor_copy(
                            ytile[ysl][:, 0:wdt], PS[:, ybk * 512: ybk * 512 + wdt]),
                            reads=ykeys, writes=[f"ytile{ysl}"])
                        colo = hf * wdt
                        dma("pool",
                            out[tb + sub * 128: tb + (sub + 1) * 128, colo:colo + wdt],
                            ytile[ysl][:, 0:wdt], "yacc",
                            reads=[f"ytile{ysl}"],
                            writes=[f"y{b2}_{sub}_{colo // 1024}"], accum=True)

        if debug:
            for nm in dbg:
                dst, src = dbg[nm]
                dma("pool", dst, src, None,
                    reads=["ctxdram", "ln2dram", "qdram", "kdram", "vdram"])
            if part == "b":
                dma("pool", comb_dump, comb_dram, None, reads=["combdram"])

        p.barrier()
        p.finalize_and_emit(nc, ctx)
    return nc



def build_fused(debug=False):
    nc = bass.Bass("TRN2", target_bir_lowering=False, num_devices=8)
    p = Prog()
    NS = 8
    NTt = 8
    NU = 32

    def din(name, shape, dtype=F32):
        return nc.dram_tensor(name, list(shape), dtype, kind="ExternalInput").ap()

    identw = din("identw", [128, 128])
    onesrow = din("onesrow", [1, 128])
    xb = din("xb", [S, D])
    x_own = din("x_own", [TCORE, D])
    qkvw = din("qkvw", [128, 16, 24 * 128])
    qkvb = din("qkvb", [128, 16])
    vbrow = din("vbrow", [1, 1024])
    outw = din("outw", [128, 16, D])
    outbrow = din("outbrow", [1, D])
    gatew = din("gatew", [128, 16, 16])
    w1 = din("w1", [NE, 128, 16, I_FF])
    w3 = din("w3", [NE, 128, 16, I_FF])
    w2 = din("w2", [NE, NI, 128, D])
    ws1 = din("ws1", [128, 16, SH])
    ws3 = din("ws3", [128, 16, SH])
    ws2 = din("ws2", [NSI, 128, D])
    jrow = din("jrow", [1, 512])
    p_idx = din("p_idx", [128, 1])
    slopes = din("slopes", [128, 8])
    s128 = din("s128", [1, 8])
    s512 = din("s512", [1, 16])
    s511 = din("s511", [1, 16])
    tilepos = din("tilepos", [1, 8])
    mypos128 = din("mypos128", [1, 32])
    diagA2 = din("diagA2", [128, 8, 512])
    diagB2 = din("diagB2", [128, 8, 512])
    idxq = din("idxq", [128, 8], I32)
    idxst = din("idxst", [128, 256], I32)
    idxld = din("idxld", [128, 256], I32)

    out = nc.dram_tensor("out", [TCORE, D], F32, kind="ExternalOutput").ap()

    q_dram = nc.dram_tensor("q_dram", [2 * 8 * DH, 4096], BF16, kind="Internal").ap()
    k_dram = nc.dram_tensor("k_dram", [NS * DH, S], BF16, kind="Internal").ap()
    v_dram = nc.dram_tensor("v_dram", [S, NS * DH], BF16, kind="Internal").ap()
    ctxsh = nc.dram_tensor("ctxsh", [2 * NS * 4096, DH], BF16, kind="Internal",
                           addr_space="Shared").ap()
    lnx2_dram = nc.dram_tensor("lnx2_dram", [128, 16, TCORE], BF16, kind="Internal").ap()
    comb_dram = nc.dram_tensor("comb_dram", [16, TCORE], F32, kind="Internal").ap()

    dbg = {}
    if debug:
        for nm, t in (("d_q", q_dram), ("d_k", k_dram), ("d_v", v_dram),
                      ("d_ctx", ctxsh), ("d_ln2", lnx2_dram)):
            dbg[nm] = (nc.dram_tensor(nm, list(t.shape), BF16, kind="ExternalOutput").ap(), t)
        d_comb = nc.dram_tensor("d_comb", [16, TCORE], F32, kind="ExternalOutput").ap()
        d_diagB = nc.dram_tensor("d_diagB", [128, 8 * 512], BF16, kind="ExternalOutput").ap()
        d_mhat = nc.dram_tensor("d_mhat", [128, 32], F32, kind="ExternalOutput").ap()
        d_negD = nc.dram_tensor("d_negD", [128, 8], F32, kind="ExternalOutput").ap()
        d_alip = nc.dram_tensor("d_alip", [128, 64], F32, kind="ExternalOutput").ap()
        d_pt = nc.dram_tensor("d_pt", [128, 512], BF16, kind="ExternalOutput").ap()
        d_bias = nc.dram_tensor("d_bias", [128, 1], F32, kind="ExternalOutput").ap()
        d_qsb = nc.dram_tensor("d_qsb", [128, 4096], BF16, kind="ExternalOutput").ap()
        d_ksb = nc.dram_tensor("d_ksb", [128, 8192], BF16, kind="ExternalOutput").ap()

    with ExitStack() as ctx:
        def alloc(name, shape, dtype):
            return nc.alloc_sbuf_tensor(name, list(shape), dtype).ap()

        POOLN = 90112
        POOL = nc.alloc_sbuf_tensor("POOL", [128, POOLN], mybir.dt.uint8).ap()

        def sub(off, shape, dtype):
            es = 2 if dtype == BF16 else 4
            n = int(np.prod(shape[1:])) * es
            assert off % 4 == 0 and off + n <= POOLN, (off, n)
            ap = POOL[0:shape[0], off:off + n].bitcast(dtype)
            if len(shape) == 3:
                ap = ap.rearrange("p (a b) -> p a b", a=shape[1])
            return ap

        xt = [sub(0, [128, D], F32), sub(8192, [128, D], F32)]
        sq = sub(16384, [128, D], F32)
        lnt = [sub(24576, [128, D], BF16), sub(28672, [128, D], BF16)]
        qkt = [sub(32768, [128, 512], BF16), sub(33792, [128, 512], BF16)]
        vt = [sub(34816, [128, 512], BF16), sub(35840, [128, 512], BF16)]
        wqk = [sub(38912, [128, 16, 128], BF16), sub(43008, [128, 16, 128], BF16)]
        vw = sub(47104, [128, 16, 512], BF16)
        vbB = sub(63488, [128, 1024], F32)
        vbrow_s = sub(67584, [1, 1024], F32)
        outbrow_s = sub(67584, [1, 512], F32)
        pt = [sub(38912, [128, 512], BF16), sub(39936, [128, 512], BF16)]
        ctt = [sub(40960, [128, 16, 128], BF16), sub(45056, [128, 16, 128], BF16)]
        lnf = [sub(49152, [128, 16, 128], BF16), sub(53248, [128, 16, 128], BF16)]
        x1t = [sub(57344, [128, D], F32), sub(65536, [128, D], F32)]
        outbB = sub(73728, [128, D], F32)
        diagA_s = sub(81920, [128, 8, 512], BF16)
        diagB_s = sub(40960, [128, 8, 512], BF16)
        ctxbf = [sub(57344 + 256 * i, [128, 128], BF16) for i in range(4)]
        negirow = sub(58368, [1, 512], BF16)
        ctf = ctt
        LNX2 = sub(0, [128, 16, 1024], BF16)
        s1b = [sub(32768, [128, 1024], BF16), sub(34816, [128, 1024], BF16)]
        t3f = [sub(36864, [128, 1024], F32), sub(40960, [128, 1024], F32)]
        ytile = [sub(45056, [128, 1024], F32), sub(49152, [128, 1024], F32)]
        combf = sub(53248, [16, 1024], F32)

        ident_bf = alloc("ident_bf", [128, 128], BF16)
        ident_f = alloc("ident_f", [128, 128], F32)
        ones_f = alloc("ones_f", [1, 128], F32)
        ones_bf = alloc("ones_bf", [1, 128], BF16)
        jrow_s = alloc("jrow_s", [1, 512], F32)
        pidx_s = alloc("pidx_s", [128, 1], F32)
        slopes_s = alloc("slopes_s", [128, 8], F32)
        s128_s = alloc("s128_s", [1, 8], F32)
        s512_s = alloc("s512_s", [1, 16], F32)
        s511_s = alloc("s511_s", [1, 16], F32)
        tilepos_s = alloc("tilepos_s", [1, 8], F32)
        mypos_s = alloc("mypos_s", [1, 32], F32)
        qkvb_s = alloc("qkvb_s", [128, 16], F32)
        gate_s = alloc("gate_s", [128, 16, 16], BF16)
        mhat2 = alloc("mhat2", [128, 16], F32)
        zr4 = alloc("zr4", [128, 4], F32)
        stat = [alloc(f"stat{i}", [128, 8], F32) for i in range(2)]
        top8b = alloc("top8b", [128, 8], F32)
        mhat = alloc("mhat", [128, 32], F32)
        mcol = alloc("mcol", [32, 1], F32)
        drow = alloc("drow", [1, 8], F32)
        alibi_loc = alloc("alibi_loc", [1, 512], BF16)
        crow = alloc("crow", [1, 64], F32)
        crowq = alloc("crowq", [1, 32], F32)
        adjrow = alloc("adjrow", [1, 16], F32)
        adjrow2 = alloc("adjrow2", [1, 8], F32)
        adjB = alloc("adjBb", [128, 16], F32)
        sp_col = alloc("sp_col", [128, 1], F32)
        alibi_p = alloc("alibi_p", [128, 64], F32)
        alibi_pq = alloc("alibi_pq", [128, 32], F32)
        negD = alloc("negD", [128, 8], F32)
        biasb = [alloc(f"biasb{i}", [128, 1], F32) for i in range(4)]
        sco = [alloc(f"sco{i}", [128, 16], F32) for i in range(2)]
        rscr = alloc("rscr", [128, 40], F32)
        combrow = alloc("combrow", [1, 1024], F32)
        sidxst = sub(32768, [128, 256], I32)
        sidxld = sub(33792, [128, 256], I32)
        sidxq = sub(34816, [128, 8], I32)

        R1 = alloc("R1", [128, 22528], BF16)
        R2 = alloc("R2", [128, 32768], BF16)

        PS = nc.alloc_psum_tensor("PS", [128, 4096], F32).ap()
        PSb = PS.bitcast(BF16)

        def bank_f(i, n=512):
            return PS[:, i * 512: i * 512 + n]

        def bank_bf(i, n=128):
            return PSb[:, i * 1024: i * 1024 + n]

        lnx_fm = R2.rearrange("p (dc t) -> p dc t", dc=16)
        K_sb = R2[:, 0:8192]
        Q_sb = R2[:, 8192:12288]
        V_sb = R2[:, 16384:16384 + 64 * 129].rearrange("p (c e) -> p c e", c=64)
        outw_sb = R2.rearrange("p (hd t) -> p hd t", hd=16)
        g_sb = R2[:, 0:22528].rearrange("p (ic t) -> p ic t", ic=22)
        w1b = [R2[:, 22528 + i * 2048: 22528 + (i + 1) * 2048].rearrange("p (dc j) -> p dc j", dc=16)
               for i in range(2)]
        w3b = [R2[:, 26624 + i * 2048: 26624 + (i + 1) * 2048].rearrange("p (dc j) -> p dc j", dc=16)
               for i in range(2)]
        combB = R2[:, 30720:32768].bitcast(F32)
        w2b_r = [R1[:, i * 11264:(i + 1) * 11264].rearrange("p (ic j) -> p ic j", ic=NI) for i in range(2)]
        w2b_s = [R1[:, i * 11264:(i + 1) * 11264].rearrange("p (ic j) -> p ic j", ic=NSI) for i in range(2)]

        _fam_ctr = [0]

        def dma(q, dst, src, fam, reads=(), writes=(), accum=False):
            if fam is None:
                _fam_ctr[0] += 1
                fam = f"u{_fam_ctr[0]}"
            if accum:
                fn = lambda eng, dst=dst, src=src: eng.dma_start(out=dst, in_=src, accum_op=ALU.add)
            else:
                fn = lambda eng, dst=dst, src=src: eng.dma_start(out=dst, in_=src)
            return p.emit(q, fn, reads=reads, writes=writes, fam=fam)

        dma("pool", ident_bf[:], identw, None, writes=["ident_bf"])
        dma("sp", ident_f[:], identw, None, writes=["ident_f"])
        dma("sp", ones_f[:], onesrow, None, writes=["ones_f"])
        dma("pool", ones_bf[:], onesrow, None, writes=["ones_bf"])
        dma("sp", jrow_s[:], jrow, None, writes=["jrow_s"])
        dma("sp", pidx_s[:], p_idx, None, writes=["pidx_s"])
        dma("sp", slopes_s[:], slopes, None, writes=["slopes_s"])
        dma("sp", s128_s[:], s128, None, writes=["s128_s"])
        dma("sp", s512_s[:], s512, None, writes=["s512_s"])
        dma("sp", s511_s[:], s511, None, writes=["s511_s"])
        dma("sp", tilepos_s[:], tilepos, None, writes=["tilepos_s"])
        dma("sp", mypos_s[:], mypos128, None, writes=["mypos_s"])
        dma("sp", qkvb_s[:], qkvb, None, writes=["qkvb_s"])
        dma("pool", gate_s[:], gatew, None, writes=["gate_s"])
        dma("sp", vbrow_s[:], vbrow, None, writes=["vbrow_s"])
        p.emit("pe", lambda eng: eng.matmul(bank_f(6), ones_f[:], vbrow_s[0:1, 0:512],
                                            start=True, stop=True),
               reads=["ones_f", "vbrow_s"], writes=["pb6"])
        p.emit("dve", lambda eng: eng.tensor_copy(vbB[:, 0:512], bank_f(6)),
               reads=["pb6"], writes=["vbB"], hz=True)
        p.emit("pe", lambda eng: eng.matmul(bank_f(6), ones_f[:], vbrow_s[0:1, 512:1024],
                                            start=True, stop=True),
               reads=["ones_f", "vbrow_s", "vbB"], writes=["pb6"])
        p.emit("dve", lambda eng: eng.tensor_copy(vbB[:, 512:1024], bank_f(6)),
               reads=["pb6"], writes=["vbB"], hz=True)

        def ln_tile(xin, xkey, outbf, outkey, sl):
            st = stat[sl]
            s1, nm, v1, sd, rstd, mr = (st[:, i:i + 1] for i in range(6))
            kst = f"stat{sl}"
            p.emit("dve", lambda eng: eng.reduce_sum(s1, xin, axis=AX.X),
                   reads=[xkey], writes=[kst], hz=True)
            p.emit("dve", lambda eng: eng.tensor_scalar(nm, s1, -1.0 / D, None, op0=ALU.mult),
                   reads=[kst], writes=[kst], hz=True)
            p.emit("act", lambda eng: eng.activation(sq[:], xin, AF.Square, bias=nm, scale=1.0),
                   reads=[xkey, kst], writes=["sq"])
            p.emit("dve", lambda eng: eng.reduce_sum(v1, sq[:], axis=AX.X),
                   reads=["sq"], writes=[kst], hz=True)
            p.emit("dve", lambda eng: eng.tensor_scalar(v1, v1, 1.0 / D, EPS,
                                                        op0=ALU.mult, op1=ALU.add),
                   reads=[kst], writes=[kst], hz=True)
            p.emit("act", lambda eng: eng.activation(sd, v1, AF.Sqrt),
                   reads=[kst], writes=[kst], hz=True)
            p.emit("dve", lambda eng: eng.reciprocal(rstd, sd), reads=[kst], writes=[kst], hz=True)
            p.emit("dve", lambda eng: eng.tensor_tensor(mr, nm, rstd, op=ALU.mult),
                   reads=[kst], writes=[kst], hz=True)
            p.emit("dve", lambda eng: eng.tensor_scalar(outbf, xin, nm, rstd,
                                                        op0=ALU.add, op1=ALU.mult),
                   reads=[xkey, kst], writes=[outkey])

        def transpose16(src_bf, srckey, dst3, dstkey, tslice):
            for dc in range(16):
                bk = 4 + (dc % 4)
                p.emit("pe", lambda eng, dc=dc, bk=bk: eng.transpose(
                    bank_bf(bk), src_bf[:, dc * 128:(dc + 1) * 128], ident_bf[:]),
                    reads=[srckey, "ident_bf"], writes=[f"pb{bk}"])
                p.emit("dve", lambda eng, dc=dc, bk=bk: eng.tensor_copy(
                    dst3[:, dc, tslice], bank_bf(bk)),
                    reads=[f"pb{bk}"], writes=[dstkey])

        for slab in range(4):
            base = slab * 2048
            for t in range(16):
                sl = t % 2
                dma("pool", xt[sl][:], xb[base + t * 128: base + (t + 1) * 128, :], "xb",
                    writes=[f"xt{sl}"])
                ln_tile(xt[sl][:], f"xt{sl}", lnt[sl][:], f"lnt{sl}", sl)
                transpose16(lnt[sl][:], f"lnt{sl}", lnx_fm, "lnxfm",
                            slice(t * 128, (t + 1) * 128))
            for f in range(16):
                wsl = f % 2
                dma("pool", wqk[wsl][:, :, :], qkvw[:, :, f * 128:(f + 1) * 128], "wqk",
                    writes=[f"wqk{wsl}"])
                for ts in range(4):
                    bk = ts % 2
                    sl = ts % 2

                    def mm_qk(eng, f=f, ts=ts, bk=bk, wsl=wsl):
                        for dc in range(16):
                            inst = eng.matmul(bank_f(bk),
                                              wqk[wsl][:, dc, :],
                                              lnx_fm[:, dc, ts * 512:(ts + 1) * 512],
                                              start=(dc == 0), stop=(dc == 15))
                        return inst
                    p.emit("pe", mm_qk, reads=["lnxfm", f"wqk{wsl}"], writes=[f"pb{bk}"])
                    p.emit("act", lambda eng, f=f, bk=bk, sl=sl: eng.activation(
                        qkt[sl][:], bank_f(bk), AF.Identity, bias=qkvb_s[:, f:f + 1], scale=1.0),
                        reads=[f"pb{bk}", "qkvb_s"], writes=[f"qkt{sl}"])
                    g_t = slab * 4 + ts
                    if f < 8:
                        dma("sp", q_dram[(g_t % 2) * 1024 + f * 128:(g_t % 2) * 1024 + (f + 1) * 128,
                                         (g_t // 2) * 512:(g_t // 2 + 1) * 512],
                            qkt[sl][:], "qkst", reads=[f"qkt{sl}"], writes=["qdram"])
                    else:
                        dma("sp", k_dram[(f - 8) * 128:(f - 7) * 128,
                                         base + ts * 512: base + (ts + 1) * 512],
                            qkt[sl][:], "qkst", reads=[f"qkt{sl}"], writes=["kdram"])
            for vh in range(2):
                dma("pool", vw[:, :, :], qkvw[:, :, 2048 + vh * 512: 2048 + (vh + 1) * 512],
                    "vw", writes=["vw"])
                for t in range(16):
                    bk = 2 + t % 2
                    sl = t % 2

                    def mm_v2(eng, t=t, bk=bk):
                        for dc in range(16):
                            inst = eng.matmul(bank_f(bk),
                                              lnx_fm[:, dc, t * 128:(t + 1) * 128],
                                              vw[:, dc, :],
                                              start=(dc == 0), stop=(dc == 15))
                        return inst
                    p.emit("pe", mm_v2, reads=["lnxfm", "vw"], writes=[f"pb{bk}"])
                    p.emit("dve", lambda eng, sl=sl, bk=bk, vh=vh: eng.tensor_tensor(
                        vt[sl][:], bank_f(bk), vbB[:, vh * 512:(vh + 1) * 512], op=ALU.add),
                        reads=[f"pb{bk}", "vbB"], writes=[f"vt{sl}"])
                    dma("sp", v_dram[base + t * 128: base + (t + 1) * 128,
                                     vh * 512:(vh + 1) * 512], vt[sl][:],
                        "vst", reads=[f"vt{sl}"], writes=["vdram"])

        p.barrier()

        dma("pool", diagA_s[:], diagA2, None, writes=["diagA_s"])
        dma("pool", diagB_s[:], diagB2, None, writes=["diagB_s"])
        dma("sp", sidxq[:], idxq, None, writes=["sidxq"])
        dma("sp", sidxst[:], idxst, None, writes=["sidxst"])
        dma("sp", sidxld[:], idxld, None, writes=["sidxld"])
        p.emit("dve", lambda eng: eng.memset(V_sb[:, :, 128:129], 1.0), writes=["Vsb"])

        for s in range(NS):
            steep = (s <= 3)
            dma("sp", K_sb, k_dram[s * 128:(s + 1) * 128, :], "kqv",
                reads=["kdram"], writes=["Ksb"])

            def qload(eng, s=s):
                return eng.indirect_dma_start(
                    out=Q_sb, out_offset=None,
                    in_=q_dram[0:128, :],
                    in_offset=IndirectOffsetOnAxis(sidxq[:, s:s + 1], 0))
            p.emit("pool", qload, reads=["qdram", "sidxq"], writes=["Qsb"], fam="kqv2")
            dma("sp", V_sb[:, :, 0:128],
                v_dram.rearrange("(c p) d -> p c d", p=128)[:, :, s * 128:(s + 1) * 128],
                "kqv", reads=["vdram"], writes=["Vsb"])

            p.emit("act", lambda eng, s=s: eng.activation(
                alibi_loc[:], jrow_s[:], AF.Identity, bias=0.0, scale=slopes_s[0:1, s:s + 1]),
                reads=["jrow_s", "slopes_s"], writes=["aloc"])
            if steep:
                p.emit("dve", lambda eng, s=s: eng.tensor_scalar(
                    negirow[:], jrow_s[:], slopes_s[0:1, s:s + 1], -1.0,
                    op0=ALU.mult, op1=ALU.mult),
                    reads=["jrow_s", "slopes_s"], writes=["negi"], hz=True)
            p.emit("dve", lambda eng, s=s: eng.tensor_scalar(
                crow[:], jrow_s[0:1, 0:64], s128_s[0:1, s:s + 1], None, op0=ALU.mult),
                reads=["jrow_s", "s128_s"], writes=["crow"], hz=True)
            p.emit("dve", lambda eng, s=s: eng.tensor_scalar(
                crowq[:], mypos_s[0:1, :], slopes_s[0:1, s:s + 1], None, op0=ALU.mult),
                reads=["mypos_s", "slopes_s"], writes=["crowq"], hz=True)
            p.emit("dve", lambda eng, s=s: eng.tensor_scalar(
                sp_col[:], pidx_s[:], slopes_s[:, s:s + 1], None, op0=ALU.mult),
                reads=["pidx_s", "slopes_s"], writes=["sp_col"], hz=True)
            p.emit("pe", lambda eng: eng.matmul(bank_f(2, 64), ones_f[:], crow[:],
                                                start=True, stop=True),
                   reads=["ones_f", "crow"], writes=["pb2"])
            p.emit("dve", lambda eng: eng.tensor_scalar(
                alibi_p[:], bank_f(2, 64), sp_col[:], None, op0=ALU.add),
                reads=["pb2", "sp_col"], writes=["alibi_p"], hz=True)
            p.emit("pe", lambda eng: eng.matmul(bank_f(2, 32), ones_f[:], crowq[:],
                                                start=True, stop=True),
                   reads=["ones_f", "crowq", "alibi_p"], writes=["pb2"])
            p.emit("dve", lambda eng: eng.tensor_scalar(
                alibi_pq[:], bank_f(2, 32), sp_col[:], None, op0=ALU.add),
                reads=["pb2", "sp_col"], writes=["alibi_pq"], hz=True)
            if steep:
                p.emit("dve", lambda eng, s=s: eng.tensor_scalar(
                    adjrow[:], jrow_s[0:1, 0:16], s512_s[0:1, s:s + 1], None, op0=ALU.mult),
                    reads=["jrow_s", "s512_s"], writes=["adjrow"], hz=True)
                p.emit("dve", lambda eng, s=s: eng.tensor_scalar(
                    adjrow2[:], tilepos_s[0:1, :], slopes_s[0:1, s:s + 1], None, op0=ALU.mult),
                    reads=["tilepos_s", "slopes_s"], writes=["adjrow2"], hz=True)
            else:
                p.emit("dve", lambda eng, s=s: eng.tensor_scalar(
                    adjrow[:], jrow_s[0:1, 0:16], s512_s[0:1, s:s + 1],
                    s511_s[0:1, s:s + 1], op0=ALU.mult, op1=ALU.add),
                    reads=["jrow_s", "s512_s", "s511_s"], writes=["adjrow"], hz=True)
            p.emit("pe", lambda eng: eng.matmul(bank_f(2, 16), ones_f[:], adjrow[:],
                                                start=True, stop=True),
                   reads=["ones_f", "adjrow", "alibi_pq"], writes=["pb2"])
            p.emit("dve", lambda eng: eng.tensor_copy(adjB[:], bank_f(2, 16)),
                   reads=["pb2"], writes=["adjB"], hz=True)

            for u in range(NU):
                nch = 2 * (u // 4) + 2
                for c in range(nch):
                    bk = (u + c) % 2

                    def mm_s(eng, u=u, c=c, bk=bk, steep=steep):
                        if steep:
                            eng.matmul(bank_f(bk), Q_sb[:, u * 128:(u + 1) * 128],
                                       K_sb[:, c * 512:(c + 1) * 512], start=True, stop=False)
                            return eng.matmul(bank_f(bk), ones_bf[:], alibi_loc[:],
                                              start=False, stop=True)
                        return eng.matmul(bank_f(bk), Q_sb[:, u * 128:(u + 1) * 128],
                                          K_sb[:, c * 512:(c + 1) * 512], start=True, stop=True)
                    p.emit("pe", mm_s, reads=["Qsb", "Ksb", "aloc", "ones_bf"],
                           writes=[f"pb{bk}"])
                    if c >= nch - 2:
                        mi = (u % 4) * 2 + (c - (nch - 2))
                        p.emit("dve", lambda eng, mi=mi, bk=bk: eng.tensor_tensor(
                            bank_f(bk), bank_f(bk), diagA_s[:, mi, :], op=ALU.add),
                            reads=["diagA_s"], writes=[f"pb{bk}"])
                    p.emit("dve", lambda eng, c=c, bk=bk: eng.reduce_max(
                        mhat2[:, c:c + 1], bank_f(bk), axis=AX.X),
                        reads=[f"pb{bk}"], writes=[f"mh{c}"], hz=True)
                p.emit("dve", lambda eng, nch=nch: eng.tensor_tensor(
                    mhat2[:, 0:nch], mhat2[:, 0:nch], adjB[:, 0:nch], op=ALU.add),
                    reads=["adjB"] + [f"mh{c}" for c in range(nch)],
                    writes=[f"mh{c}" for c in range(nch)], hz=True)
                p.emit("dve", lambda eng, u=u, nch=nch: eng.reduce_max(
                    mhat[:, u:u + 1], mhat2[:, 0:nch], axis=AX.X),
                    reads=[f"mh{c}" for c in range(nch)], writes=["mhat"], hz=True)
                if steep:
                    p.emit("dve", lambda eng, u=u: eng.tensor_tensor(
                        mhat[:, u:u + 1], mhat[:, u:u + 1], alibi_pq[:, u:u + 1],
                        op=ALU.subtract),
                        reads=["alibi_pq", "mhat"], writes=["mhat"], hz=True)

            p.emit("pe", lambda eng: eng.transpose(PS[0:32, 1024:1152], mhat[:, 0:32], ident_f[:]),
                   reads=["mhat", "ident_f"], writes=["pb2"])
            p.emit("dve", lambda eng: eng.reduce_max(mcol[:], PS[0:32, 1024:1152], axis=AX.X),
                   reads=["pb2"], writes=["mcol"], hz=True)
            p.emit("pe", lambda eng: eng.transpose(PS[0:1, 1536:1568], mcol[0:32, 0:1],
                                                   ident_f[0:32, 0:32]),
                   reads=["mcol", "ident_f"], writes=["pb3"])
            p.emit("dve", lambda eng: eng.tensor_reduce(
                drow[:], PS[0:1, 1536:1568].rearrange("p (a b) -> p a b", a=8),
                axis=AX.X, op=ALU.max),
                reads=["pb3"], writes=["drow"], hz=True)
            if steep:
                p.emit("dve", lambda eng: eng.tensor_tensor(
                    drow[:], drow[:], adjrow2[0:1, 0:8], op=ALU.add),
                    reads=["drow", "adjrow2"], writes=["drow"], hz=True)
            p.emit("dve", lambda eng: eng.tensor_scalar(drow[:], drow[:], -1.0, None, op0=ALU.mult),
                   reads=["drow"], writes=["drow"], hz=True)
            p.emit("pe", lambda eng: eng.matmul(bank_f(2, 8), ones_f[:], drow[:],
                                                start=True, stop=True),
                   reads=["ones_f", "drow"], writes=["pb2"])
            p.emit("dve", lambda eng: eng.tensor_copy(negD[:], bank_f(2, 8)),
                   reads=["pb2"], writes=["negD"], hz=True)

            if debug and s == 0:
                dma("sp", d_qsb, Q_sb, None, reads=["Qsb"])
                dma("sp", d_ksb, K_sb, None, reads=["Ksb"])
                dma("sp", d_mhat, mhat[:], None, reads=["mhat"])
                dma("sp", d_negD, negD[:], None, reads=["negD"])
                dma("sp", d_alip, alibi_p[:], None, reads=["alibi_p"])
                dma("sp", d_diagB, diag_s[:, :, :] if False else diagB_s.rearrange("p a b -> p (a b)"),
                    None, reads=["diagB_s"])
            for t in range(NTt):
                nkc = 8 * t + 8
                for kc in range(nkc):
                    bk = kc % 2

                    def mm_st(eng, t=t, kc=kc, bk=bk, steep=steep):
                        if steep:
                            eng.matmul(bank_f(bk), K_sb[:, kc * 128:(kc + 1) * 128],
                                       Q_sb[:, t * 512:(t + 1) * 512], start=True, stop=False)
                            return eng.matmul(bank_f(bk), ones_bf[:], negirow[:],
                                              start=False, stop=True)
                        return eng.matmul(bank_f(bk), K_sb[:, kc * 128:(kc + 1) * 128],
                                          Q_sb[:, t * 512:(t + 1) * 512], start=True, stop=True)
                    p.emit("pe", mm_st, reads=["Ksb", "Qsb", "negi", "ones_bf"],
                           writes=[f"pb{bk}"])
                    if kc >= 8 * t:
                        p.emit("dve", lambda eng, t=t, kc=kc, bk=bk: eng.tensor_tensor(
                            bank_f(bk), bank_f(bk), diagB_s[:, kc - 8 * t, :], op=ALU.add),
                            reads=["diagB_s"], writes=[f"pb{bk}"])
                    bsl = kc % 4
                    p.emit("dve", lambda eng, t=t, kc=kc, bsl=bsl: eng.tensor_tensor(
                        biasb[bsl][:], alibi_p[:, kc:kc + 1], negD[:, t:t + 1], op=ALU.add),
                        reads=["alibi_p", "negD"], writes=[f"bias{bsl}"])
                    psl = kc % 2
                    p.emit("act", lambda eng, bk=bk, bsl=bsl, psl=psl: eng.activation(
                        pt[psl][:], bank_f(bk), AF.Exp, bias=biasb[bsl][:], scale=1.0),
                        reads=[f"pb{bk}", f"bias{bsl}"], writes=[f"pt{psl}"])

                    if debug and s == 0 and t == 0 and kc == 0:
                        dma("sp", d_pt, pt[psl][:], None, reads=[f"pt{psl}"])
                        dma("sp", d_bias, biasb[bsl][:], None, reads=[f"bias{bsl}"])

                    def pv(eng, kc=kc, psl=psl, nkc=nkc):
                        for qq in range(4):
                            inst = eng.matmul(PS[:, (4 + qq) * 512:(4 + qq) * 512 + 129],
                                              pt[psl][:, qq * 128:(qq + 1) * 128],
                                              V_sb[:, kc, :],
                                              start=(kc == 0), stop=(kc == nkc - 1))
                        return inst
                    p.emit("pe", pv, reads=[f"pt{psl}", "Vsb"],
                           writes=["pb4", "pb5", "pb6", "pb7"])
                zcols = PS[:, 2048:4096].rearrange("p (q x) -> p q x", q=4)[:, :, 128:129]
                p.emit("dve", lambda eng, zcols=zcols: eng.tensor_scalar(
                    zr4[:, :, None], zcols, 1e-37, None, op0=ALU.max),
                    reads=["pb4", "pb5", "pb6", "pb7"], writes=["zr4"], hz=True)
                p.emit("dve", lambda eng: eng.reciprocal(zr4[:], zr4[:]),
                       reads=["zr4"], writes=["zr4"], hz=True)
                for qq in range(4):
                    p.emit("dve", lambda eng, qq=qq: eng.tensor_scalar(
                        ctxbf[qq][:], PS[:, (4 + qq) * 512:(4 + qq) * 512 + 128],
                        zr4[:, qq:qq + 1], None, op0=ALU.mult),
                        reads=[f"pb{4 + qq}", "zr4"], writes=[f"ctxbf{qq}"])

                    def ctxstore(eng, s=s, t=t, qq=qq):
                        col = s * 32 + t * 4 + qq
                        return eng.indirect_dma_start(
                            out=ctxsh[0:128, :],
                            out_offset=IndirectOffsetOnAxis(sidxst[:, col:col + 1], 0),
                            in_=ctxbf[qq][:], in_offset=None)
                    p.emit("pool", ctxstore, reads=[f"ctxbf{qq}", "sidxst"], writes=["ctxsh"], fam="ctxst")

        for i in range(4):
            dma("pool", outw_sb[:, 4 * i:4 * (i + 1), :], outw[:, 4 * i:4 * (i + 1), :],
                "wout", writes=["Ksb", "Qsb", "Vsb", f"OW{i}"])
        p.barrier()
        p.mark_split()

        for i in range(4):
            dma("sp", outbrow_s[:], outbrow[0:1, i * 512:(i + 1) * 512], None,
                reads=["outbrow_done"], writes=["outbrow_s"])
            p.emit("pe", lambda eng: eng.matmul(bank_f(2), ones_f[:], outbrow_s[:],
                                                start=True, stop=True),
                   reads=["ones_f", "outbrow_s"], writes=["pb2"])
            p.emit("dve", lambda eng, i=i: eng.tensor_copy(outbB[:, i * 512:(i + 1) * 512],
                                                           bank_f(2)),
                   reads=["pb2", "outbrow_s"], writes=["outbB", "outbrow_done"], hz=True)

        for t in range(16):
            sl = t % 2
            for h in range(16):
                def ctld(eng, t=t, h=h, sl=sl):
                    col = t * 16 + h
                    return eng.indirect_dma_start(
                        out=ctt[sl][:, h, :], out_offset=None,
                        in_=ctxsh[0:128, :],
                        in_offset=IndirectOffsetOnAxis(sidxld[:, col:col + 1], 0))
                p.emit("pool", ctld, reads=["ctxsh", "sidxld"], writes=[f"ctt{sl}"], fam="ctld")
            for hd in range(16):
                bk = hd % 2
                p.emit("pe", lambda eng, hd=hd, bk=bk, sl=sl: eng.transpose(
                    bank_bf(bk), ctt[sl][:, hd, :], ident_bf[:]),
                    reads=[f"ctt{sl}", "ident_bf"], writes=[f"pb{bk}"])
                p.emit("dve", lambda eng, hd=hd, bk=bk, sl=sl: eng.tensor_copy(
                    ctf[sl][:, hd, :], bank_bf(bk)),
                    reads=[f"pb{bk}"], writes=[f"ctt{sl}"])

            def mm_out(eng, sl=sl):
                for hd in range(16):
                    for n in range(4):
                        inst = eng.matmul(PS[:, (4 + n) * 512:(5 + n) * 512],
                                          ctf[sl][:, hd, :],
                                          outw_sb[:, hd, n * 512:(n + 1) * 512],
                                          start=(hd == 0), stop=(hd == 15))
                return inst
            p.emit("pe", mm_out, reads=[f"ctt{sl}", "OW0", "OW1", "OW2", "OW3"],
                   writes=["pb4", "pb5", "pb6", "pb7"])
            dma("sp", xt[sl][:], x_own[t * 128:(t + 1) * 128, :], "xo", writes=[f"xt{sl}"])
            p.emit("dve", lambda eng, sl=sl: eng.tensor_tensor(
                x1t[sl][:], PS[:, 2048:4096], outbB[:], op=ALU.add),
                reads=["pb4", "pb5", "pb6", "pb7", "outbB"], writes=[f"x1t{sl}"])
            p.emit("dve", lambda eng, sl=sl: eng.tensor_tensor(
                x1t[sl][:], x1t[sl][:], xt[sl][:], op=ALU.add),
                reads=[f"xt{sl}", f"x1t{sl}"], writes=[f"x1t{sl}"])
            dma("sp", out[t * 128:(t + 1) * 128, :], x1t[sl][:], "x1st",
                reads=[f"x1t{sl}"],
                writes=[f"y{t // 8}_{t % 8}_0", f"y{t // 8}_{t % 8}_1"])
            ln_tile(x1t[sl][:], f"x1t{sl}", lnt[sl][:], f"lnt{sl}", sl)
            transpose16(lnt[sl][:], f"lnt{sl}", lnf[sl], f"lnf{sl}", slice(0, 128))
            dma("sp", lnx2_dram[:, :, t * 128:(t + 1) * 128], lnf[sl][:, :, :], "ln2st",
                reads=[f"lnf{sl}"], writes=["ln2dram"])

        p.barrier()

        NSUB = 8
        for b2 in range(2):
            tb = b2 * 1024
            dma("pool", LNX2[:, :, :], lnx2_dram[:, :, tb:tb + 1024], "ln2ld",
                reads=["ln2dram"], writes=["lnx2b"])
            for tt in range(8):
                sl = tt % 2

                def mm_gate(eng, tt=tt):
                    for dc in range(16):
                        inst = eng.matmul(bank_f(6, 16), LNX2[:, dc, tt * 128:(tt + 1) * 128],
                                          gate_s[:, dc, :], start=(dc == 0), stop=(dc == 15))
                    return inst
                p.emit("pe", mm_gate, reads=["lnx2b", "gate_s"], writes=["pb6"])
                p.emit("act", lambda eng, sl=sl: eng.activation(
                    sco[sl][:], bank_f(6, 16), AF.Sigmoid, bias=0.0, scale=1.0),
                    reads=["pb6"], writes=[f"sco{sl}"])

                sc = sco[sl]
                gmax = rscr[:, 0:4]
                m1 = rscr[:, 4:5]
                mk1 = rscr[:, 5:9]
                g2 = rscr[:, 9:13]
                m2 = rscr[:, 13:14]
                gmask = rscr[:, 14:18]
                masked = rscr[:, 18:34]
                ssum = rscr[:, 35:36]
                RT = ["rt"]
                p.emit("dve", lambda eng, sc=sc: eng.tensor_reduce(
                    gmax, sc.rearrange("p (g m) -> p g m", g=4), axis=AX.X, op=ALU.max),
                    reads=[f"sco{sl}"], writes=RT, hz=True)
                p.emit("dve", lambda eng: eng.reduce_max(m1, gmax, axis=AX.X),
                       reads=RT, writes=RT, hz=True)
                p.emit("dve", lambda eng: eng.tensor_scalar(mk1, gmax, m1, None, op0=ALU.is_ge),
                       reads=RT, writes=RT, hz=True)
                p.emit("dve", lambda eng: eng.scalar_tensor_tensor(
                    g2, mk1, -1e30, gmax, op0=ALU.mult, op1=ALU.add),
                    reads=RT, writes=RT, hz=True)
                p.emit("dve", lambda eng: eng.reduce_max(m2, g2, axis=AX.X),
                       reads=RT, writes=RT, hz=True)
                p.emit("dve", lambda eng: eng.tensor_scalar(gmask, gmax, m2, None, op0=ALU.is_ge),
                       reads=RT, writes=RT, hz=True)

                def maskmul(eng, sc=sc):
                    for g4 in range(4):
                        inst = eng.tensor_scalar(masked[:, g4 * 4:(g4 + 1) * 4],
                                                 sc[:, g4 * 4:(g4 + 1) * 4],
                                                 gmask[:, g4:g4 + 1], None, op0=ALU.mult)
                    return inst
                p.emit("dve", maskmul, reads=RT + [f"sco{sl}"], writes=RT, hz=True)
                p.emit("dve", lambda eng: eng.max(out=top8b[:], in_=masked),
                       reads=RT, writes=RT + ["top8b"], hz=True)
                p.emit("dve", lambda eng: eng.tensor_scalar(
                    masked, masked, top8b[:, 3:4], None, op0=ALU.is_ge),
                    reads=RT + ["top8b"], writes=RT, hz=True)
                p.emit("dve", lambda eng, sc=sc: eng.tensor_tensor(masked, sc, masked, op=ALU.mult),
                       reads=RT + [f"sco{sl}"], writes=RT, hz=True)
                p.emit("dve", lambda eng: eng.reduce_sum(ssum, masked, axis=AX.X),
                       reads=RT, writes=RT, hz=True)
                p.emit("dve", lambda eng: eng.reciprocal(ssum, ssum),
                       reads=RT, writes=RT, hz=True)
                p.emit("dve", lambda eng: eng.tensor_scalar(masked, masked, ssum, None, op0=ALU.mult),
                       reads=RT, writes=RT, hz=True)
                p.emit("pe", lambda eng: eng.transpose(PS[0:16, 3584:3712], rscr[:, 18:34],
                                                       ident_f[:]),
                       reads=["rt", "ident_f"], writes=["pb7"])
                p.emit("dve", lambda eng, tt=tt: eng.tensor_copy(
                    combf[0:16, tt * 128:(tt + 1) * 128], PS[0:16, 3584:3712]),
                    reads=["pb7"], writes=["combf"], hz=True)
            dma("sp", comb_dram[:, tb:tb + 1024], combf[:], "combst",
                reads=["combf"], writes=["combdram"])

            for e in range(NE + 1):
                shared = (e == NE)
                nIc = NSI if shared else NI
                w1src = ws1 if shared else w1[e]
                w3src = ws3 if shared else w3[e]
                if not shared:
                    dma("pool", combrow[:], comb_dram[e:e + 1, tb:tb + 1024], "combld",
                        reads=["combdram"], writes=["combrow"])
                    for i in range(2):
                        p.emit("pe", lambda eng, i=i: eng.matmul(
                            bank_f(6), ones_f[:],
                            combrow[0:1, i * 512:(i + 1) * 512], start=True, stop=True),
                            reads=["combrow", "ones_f"], writes=["pb6"])
                        p.emit("dve", lambda eng, i=i: eng.tensor_copy(
                            combB[:, i * 512:(i + 1) * 512], bank_f(6)),
                            reads=["pb6"], writes=["combB"], hz=True)
                for ic in range(nIc):
                    wsl = ic % 2
                    dma("pool", w1b[wsl][:, :, :], w1src[:, :, ic * 128:(ic + 1) * 128],
                        "w1", writes=[f"w1b{wsl}"])
                    dma("pool", w3b[wsl][:, :, :], w3src[:, :, ic * 128:(ic + 1) * 128],
                        "w3", writes=[f"w3b{wsl}"])

                    def mm_h(eng, wb, bko, wkey):
                        for ns in range(2):
                            for dc in range(16):
                                inst = eng.matmul(bank_f(bko + ns),
                                                  wb[:, dc, :],
                                                  LNX2[:, dc, ns * 512:(ns + 1) * 512],
                                                  start=(dc == 0), stop=(dc == 15))
                        return inst
                    p.emit("pe", functools.partial(mm_h, wb=w1b[wsl], bko=0, wkey=None),
                           reads=[f"w1b{wsl}", "lnx2b"], writes=["pb0", "pb1"])
                    p.emit("pe", functools.partial(mm_h, wb=w3b[wsl], bko=2, wkey=None),
                           reads=[f"w3b{wsl}", "lnx2b"], writes=["pb2", "pb3"])
                    ssl = ic % 2
                    p.emit("act", lambda eng, ssl=ssl: eng.activation(
                        s1b[ssl][:], PS[:, 0:1024], AF.Silu, bias=0.0, scale=1.0),
                        reads=["pb0", "pb1"], writes=[f"s1b{ssl}"])
                    if shared:
                        p.emit("dve", lambda eng, ic=ic, ssl=ssl: eng.tensor_tensor(
                            g_sb[:, ic, :], PS[:, 1024:2048], s1b[ssl][:], op=ALU.mult),
                            reads=["pb2", "pb3", f"s1b{ssl}"], writes=["g_sb"])
                    else:
                        p.emit("dve", lambda eng, ssl=ssl: eng.tensor_tensor(
                            t3f[ssl][:], PS[:, 1024:2048], combB[:], op=ALU.mult),
                            reads=["pb2", "pb3", "combB"], writes=[f"t3f{ssl}"])
                        p.emit("dve", lambda eng, ic=ic, ssl=ssl: eng.tensor_tensor(
                            g_sb[:, ic, :], t3f[ssl][:], s1b[ssl][:], op=ALU.mult),
                            reads=[f"t3f{ssl}", f"s1b{ssl}"], writes=["g_sb"])
                nH = 4 if shared else 2
                wdt = 512 if shared else 1024
                w2srcv = ws2 if shared else w2[e]
                w2bv = w2b_s if shared else w2b_r
                for hf in range(nH):
                    wsl = hf % 2
                    dma("pool", w2bv[wsl][:, :, :],
                        w2srcv.rearrange("a p j -> p a j")[:, :, hf * wdt:(hf + 1) * wdt],
                        "w2", writes=[f"w2b{wsl}"])
                    for sb_ in range(NSUB):
                        nbk = wdt // 512
                        ybk = 4 + (sb_ % 2) * nbk

                        def mm_y(eng, hf=hf, sb_=sb_, wsl=wsl, ybk=ybk, nIc=nIc, nbk=nbk,
                                 w2bv=w2bv):
                            for ic in range(nIc):
                                for n2 in range(nbk):
                                    inst = eng.matmul(
                                        bank_f(ybk + n2),
                                        g_sb[:, ic, sb_ * 128:(sb_ + 1) * 128],
                                        w2bv[wsl][:, ic, n2 * 512:(n2 + 1) * 512],
                                        start=(ic == 0), stop=(ic == nIc - 1))
                            return inst
                        ykeys = [f"pb{ybk + n2}" for n2 in range(nbk)]
                        p.emit("pe", mm_y, reads=["g_sb", f"w2b{wsl}"], writes=ykeys)
                        ysl = sb_ % 2
                        p.emit("dve", lambda eng, ybk=ybk, ysl=ysl, wdt=wdt: eng.tensor_copy(
                            ytile[ysl][:, 0:wdt], PS[:, ybk * 512: ybk * 512 + wdt]),
                            reads=ykeys, writes=[f"ytile{ysl}"])
                        colo = hf * wdt
                        dma("pool",
                            out[tb + sb_ * 128: tb + (sb_ + 1) * 128, colo:colo + wdt],
                            ytile[ysl][:, 0:wdt], "yacc",
                            reads=[f"ytile{ysl}"],
                            writes=[f"y{b2}_{sb_}_{colo // 1024}"], accum=True)

        if debug:
            for nm in dbg:
                dst, src = dbg[nm]
                dma("pool", dst, src, None,
                    reads=["ctxsh", "ln2dram", "qdram", "kdram", "vdram"])
            dma("pool", d_comb, comb_dram, None, reads=["combdram"])

        p.barrier()
        p.finalize_and_emit(nc, ctx)
    return nc


def _images(out_w, gate_w, W1, W2, W3, Ws1, Ws2, Ws3):
    f32 = np.float32
    imgs = {}
    imgs["outw"] = np.ascontiguousarray(out_w.reshape(16, 128, D).transpose(1, 0, 2))
    gt = gate_w.T
    imgs["gatew"] = np.ascontiguousarray(gt.reshape(16, 128, 16).transpose(1, 0, 2))
    for nm, W in (("w1", W1), ("w3", W3)):
        wt = W.transpose(0, 2, 1)
        imgs[nm] = np.ascontiguousarray(wt.reshape(NE, 16, 128, I_FF).transpose(0, 2, 1, 3))
    w2t = W2.transpose(0, 2, 1)
    imgs["w2"] = np.ascontiguousarray(w2t.reshape(NE, NI, 128, D))
    for nm, W in (("ws1", Ws1), ("ws3", Ws3)):
        wt = W.T
        imgs[nm] = np.ascontiguousarray(wt.reshape(16, 128, SH).transpose(1, 0, 2))
    imgs["ws2"] = np.ascontiguousarray(Ws2.T.reshape(NSI, 128, D))
    return {k: np.asarray(v, dtype=f32) for k, v in imgs.items()}


def _consts():
    c = {}
    c["identw"] = np.eye(128, dtype=np.float32)
    c["onesrow"] = np.ones((1, 128), dtype=np.float32)
    c["jrow"] = np.arange(512, dtype=np.float32)[None, :]
    c["p_idx"] = np.arange(128, dtype=np.float32)[:, None]
    dm = np.zeros((128, 4, 512), dtype=np.float32)
    jj = np.arange(128)[:, None]
    ii = np.arange(512)[None, :]
    for kk in range(4):
        dm[:, kk, :] = np.where(128 * kk + jj > ii, -30000.0, 0.0)
    c["diagm"] = dm
    dma_ = np.zeros((128, 4, 512), dtype=np.float32)
    qq_ = np.arange(128)[:, None]
    kl_ = np.arange(512)[None, :]
    for uu in range(4):
        dma_[:, uu, :] = np.where(kl_ > 128 * uu + qq_, -30000.0, 0.0)
    c["diagmA"] = dma_
    return c


def _fused_maps(x, qkv_w, qkv_b, out_b, imgs, consts):
    f32 = np.float32
    dmA = consts["diagmA"]
    dmB = consts["diagm"]
    maps = []
    for c in range(8):
        b, P, m = c // 4, (c % 4) // 2, c % 2
        heads = [2 * s_ + m for s_ in range(8)]
        cols = []
        for part in range(3):
            for h in heads:
                cols.append(qkv_w[:, part * 2048 + h * 128: part * 2048 + (h + 1) * 128])
        qw = np.concatenate(cols, axis=1)
        qkvw_img = np.ascontiguousarray(qw.reshape(16, 128, 3072).transpose(1, 0, 2))
        bcols = []
        for part in range(2):
            for h in heads:
                bcols.append(qkv_b[part * 2048 + h * 128: part * 2048 + (h + 1) * 128])
        qkvb_img = np.ascontiguousarray(np.stack(bcols, axis=1))
        vb = np.concatenate([qkv_b[2 * 2048 + h * 128: 2 * 2048 + (h + 1) * 128]
                             for h in heads])[None, :]
        sl = np.array([SLOPES[h] for h in heads], f32)
        sl16 = np.concatenate([sl, np.zeros(8, f32)])
        tp = (512.0 * (2 * np.arange(8) + P)).astype(f32)[None, :]
        uu = np.arange(32)
        blk = 8 * (uu // 4) + 4 * P + (uu % 4)
        myp = (128.0 * blk).astype(f32)[None, :]
        dA = np.zeros((128, 8, 512), f32)
        dB = np.zeros((128, 8, 512), f32)
        for u4 in range(4):
            if P == 0:
                dA[:, u4 * 2 + 0] = dmA[:, u4]
                dA[:, u4 * 2 + 1] = -30000.0
            else:
                dA[:, u4 * 2 + 0] = 0.0
                dA[:, u4 * 2 + 1] = dmA[:, u4]
        for kcr in range(8):
            if P == 0:
                dB[:, kcr] = dmB[:, kcr] if kcr < 4 else -30000.0
            else:
                dB[:, kcr] = 0.0 if kcr < 4 else dmB[:, kcr - 4]
        xo = np.concatenate([x[b, (2 * lt + P) * 512:(2 * lt + P) * 512 + 512]
                             for lt in range(4 * m, 4 * m + 4)], axis=0)
        pp = np.arange(128, dtype=np.int64)
        iq = np.empty((128, 8), np.int32)
        for s_ in range(8):
            iq[:, s_] = P * 1024 + s_ * 128 + pp
        ist = np.empty((128, 256), np.int32)
        for s_ in range(8):
            for t_ in range(8):
                for qq_ in range(4):
                    ist[:, s_ * 32 + t_ * 4 + qq_] = (m * 32768 + s_ * 4096
                                                      + t_ * 512 + qq_ * 128 + pp)
        ild = np.empty((128, 256), np.int32)
        for t_ in range(16):
            for h_ in range(16):
                ild[:, t_ * 16 + h_] = ((h_ % 2) * 32768 + (h_ // 2) * 4096
                                        + m * 2048 + t_ * 128 + pp)
        mp = dict(imgs)
        mp.update(dict(
            identw=consts["identw"], onesrow=consts["onesrow"],
            jrow=consts["jrow"], p_idx=consts["p_idx"],
            xb=x[b], x_own=np.ascontiguousarray(xo),
            qkvw=qkvw_img, qkvb=qkvb_img, vbrow=np.ascontiguousarray(vb),
            outbrow=out_b[None, :].copy(),
            slopes=np.tile(sl[None, :], (128, 1)),
            s128=(128.0 * sl)[None, :].copy(),
            s512=(512.0 * sl16)[None, :].copy(), s511=(511.0 * sl16)[None, :].copy(),
            tilepos=tp, mypos128=myp, diagA2=dA, diagB2=dB,
            idxq=iq, idxst=ist, idxld=ild,
        ))
        maps.append(mp)
    return maps


def _fused_assemble(results):
    f32 = np.float32
    outp = np.empty((B, S, D), f32)
    for c in range(8):
        b, P, m = c // 4, (c % 4) // 2, c % 2
        sh = np.asarray(results[c]["out"], f32)
        for i, lt in enumerate(range(4 * m, 4 * m + 4)):
            g = 2 * lt + P
            outp[b, g * 512:(g + 1) * 512] = sh[i * 512:(i + 1) * 512]
    return outp


def kernel(L0_attn_x, L0_attn_qkv_w, L0_attn_qkv_b, L0_attn_out_w, L0_attn_out_b,
           L0_ffn_gate_w, L0_ffn_W1, L0_ffn_W2, L0_ffn_W3,
           L0_ffn_Ws1, L0_ffn_Ws2, L0_ffn_Ws3, _debug=False, _ncs=None):
    f32 = np.float32
    x = np.ascontiguousarray(np.asarray(L0_attn_x, f32))
    qkv_w = np.asarray(L0_attn_qkv_w, f32)
    qkv_b = np.asarray(L0_attn_qkv_b, f32)
    out_w = np.asarray(L0_attn_out_w, f32)
    out_b = np.asarray(L0_attn_out_b, f32)
    gate_w = np.asarray(L0_ffn_gate_w, f32)
    W1 = np.asarray(L0_ffn_W1, f32)
    W2 = np.asarray(L0_ffn_W2, f32)
    W3 = np.asarray(L0_ffn_W3, f32)
    Ws1 = np.asarray(L0_ffn_Ws1, f32)
    Ws2 = np.asarray(L0_ffn_Ws2, f32)
    Ws3 = np.asarray(L0_ffn_Ws3, f32)

    imgs = _images(out_w, gate_w, W1, W2, W3, Ws1, Ws2, Ws3)
    consts = _consts()
    nc = _ncs if _ncs is not None else build_fused(debug=_debug)
    maps = _fused_maps(x, qkv_w, qkv_b, out_b, imgs, consts)
    res = run_bass_kernel_spmd(nc, maps, core_ids=list(range(8)))
    outp = _fused_assemble(res.results)
    if _debug:
        return outp, res
    return outp


# version 06
# speedup vs baseline: 1.1027x; 1.1027x over previous
import functools
import numpy as np
from contextlib import ExitStack

import concourse.bass as bass
import concourse.mybir as mybir
from concourse.bass import IndirectOffsetOnAxis
from concourse.bass_utils import run_bass_kernel_spmd
from concourse.aot_env import allow_ici_all_to_all

F32 = mybir.dt.float32
I32 = mybir.dt.int32
BF16 = mybir.dt.bfloat16
AF = mybir.ActivationFunctionType
ALU = mybir.AluOpType
AX = mybir.AxisListType

B = 2
S = 8192
D = 2048
NH = 16
DH = 128
NSLOT = 4
TCORE = 2048
NE = 16
I_FF = 1408
NI = 11
SH = 2816
NSI = 22
EPS = 1e-5
SLOPES = [np.float32(1.0 / 2.0 ** (i / 2.0)) for i in range(NH)]

ENGINES = ("pe", "dve", "act", "pool", "sp")
ENG_ATTR = {"pe": "tensor", "dve": "vector", "act": "scalar", "pool": "gpsimd", "sp": "sync"}


class _It:
    __slots__ = ("eng", "fn", "fam", "ford", "deps", "sig", "need", "gi", "hz")

    def __init__(self, eng, fn, fam, ford, gi, hz=False):
        self.eng, self.fn, self.fam, self.ford, self.gi = eng, fn, fam, ford, gi
        self.deps = []
        self.sig = None
        self.need = False
        self.hz = hz


class Prog:

    def __init__(self):
        self.items = []
        self.per_eng = {e: [] for e in ENGINES}
        self.res = {}
        self.fam_count = {}
        self.split_gi = None

    def mark_split(self):
        self.split_gi = len(self.items)

    def emit(self, eng, fn, reads=(), writes=(), fam=None, deps=(), hz=False):
        ford = None
        if fam is not None:
            self.fam_count[fam] = self.fam_count.get(fam, 0) + 1
            ford = self.fam_count[fam]
        it = _It(eng, fn, fam, ford, len(self.items), hz)
        lane = (eng, fam)
        dd = {}
        for k in reads:
            e = self.res.get(k)
            if e:
                for d in e["w"].values():
                    dd[d.gi] = d
        for k in writes:
            e = self.res.get(k)
            if e:
                for d in e["r"].values():
                    dd[d.gi] = d
                for d in e["w"].values():
                    dd[d.gi] = d
        for d in deps:
            dd[d.gi] = d
        dd.pop(it.gi, None)
        it.deps = list(dd.values())
        for k in reads:
            self.res.setdefault(k, {"w": {}, "r": {}})["r"][lane] = it
        for k in writes:
            e = self.res.setdefault(k, {"w": {}, "r": {}})
            e["w"] = {lane: it}
            e["r"] = {}
        self.items.append(it)
        self.per_eng[eng].append(it)
        return it

    def barrier(self):
        lasts = [self.per_eng[e][-1] for e in ENGINES if self.per_eng[e]]
        for e in ENGINES:
            self.emit(e, lambda eng: eng.nop(), deps=lasts)

    def finalize_and_emit(self, nc, ctx):
        for it in self.items:
            for d in it.deps:
                if d.eng == it.eng and d.fam is None and not d.hz:
                    continue
                d.need = True
        sems = {}
        for e in ENGINES:
            cnt = 0
            for it in self.per_eng[e]:
                if it.fam is None and it.need:
                    cnt += 1
                    it.sig = cnt
            if cnt:
                sems[("e", e)] = ctx.enter_context(nc.semaphore("sm_" + e))
        for f in self.fam_count:
            sems[("f", f)] = ctx.enter_context(nc.semaphore("sf_" + str(f)))

        def wait_list(it):
            w = {}
            for d in it.deps:
                if d.eng == it.eng and d.fam is None and not d.hz:
                    continue
                if d.fam is not None:
                    sk, val = ("f", d.fam), 16 * d.ford
                else:
                    sk, val = ("e", d.eng), d.sig
                if w.get(sk, 0) < val:
                    w[sk] = val
            return w

        waters = {e: {} for e in ENGINES}

        def emit_block(block, lo, hi):
            for e in ENGINES:
                items = [it for it in self.per_eng[e] if lo <= it.gi < hi]
                if not items:
                    continue

                def body(eng, items=items, ename=e):
                    water = waters[ename]
                    for it in items:
                        for sk, val in sorted(wait_list(it).items(), key=lambda kv: str(kv[0])):
                            if water.get(sk, 0) < val:
                                eng.wait_ge(sems[sk], val)
                                water[sk] = val
                        inst = it.fn(eng)
                        if it.fam is not None:
                            inst.then_inc(sems[("f", it.fam)], 16)
                        elif it.need:
                            inst.then_inc(sems[("e", ename)], 1)

                getattr(block, ENG_ATTR[e])(body)

        if self.split_gi is None:
            block = ctx.enter_context(nc.Block())
            emit_block(block, 0, len(self.items))
        else:
            with nc.Block() as b1:
                emit_block(b1, 0, self.split_gi)
            nc.all_core_barrier()
            with nc.Block() as b2:
                emit_block(b2, self.split_gi, len(self.items))


def build_nc(part="a", debug=False):
    nc = bass.Bass("TRN2", target_bir_lowering=False)
    p = Prog()

    def din(name, shape, dtype=F32):
        return nc.dram_tensor(name, list(shape), dtype, kind="ExternalInput").ap()

    identw = din("identw", [128, 128])
    onesrow = din("onesrow", [1, 128])
    dbg = {}
    if part in ("a", "p0"):
        xb = din("xb", [S, D])
        qkvw = din("qkvw", [128, 16, 12 * 128])
        qkvb = din("qkvb", [128, 12])
        vbrow = din("vbrow", [1, 512])
        jrow = din("jrow", [1, 512])
        p_idx = din("p_idx", [128, 1])
        slopes = din("slopes", [128, 4])
        s128 = din("s128", [1, 4])
        s512 = din("s512", [1, 4])
        s511 = din("s511", [1, 4])
        diagm = din("diagm", [128, 4, 512])
        diagmA = din("diagmA", [128, 4, 512])
        ctx_dram = nc.dram_tensor("ctxout", [S, NSLOT * DH], BF16, kind="ExternalOutput").ap()
        q_dram = nc.dram_tensor("q_dram", [NSLOT * DH, S], BF16, kind="Internal").ap()
        k_dram = nc.dram_tensor("k_dram", [NSLOT * DH, S], BF16, kind="Internal").ap()
        v_dram = nc.dram_tensor("v_dram", [S, NSLOT * DH], BF16, kind="Internal").ap()
        if debug:
            for nm, t in (("d_q", q_dram), ("d_k", k_dram), ("d_v", v_dram)):
                dbg[nm] = (nc.dram_tensor(nm, list(t.shape), BF16, kind="ExternalOutput").ap(), t)
    else:
        x_own = din("x_own", [TCORE, D])
        ctx_recv = din("ctx_recv", [S, NSLOT * DH], BF16)
        outw = din("outw", [128, 16, D])
        outbrow = din("outbrow", [1, D])
        gatew = din("gatew", [128, 16, 16])
        w1 = din("w1", [NE, 128, 16, I_FF])
        w3 = din("w3", [NE, 128, 16, I_FF])
        w2 = din("w2", [NE, NI, 128, D])
        ws1 = din("ws1", [128, 16, SH])
        ws3 = din("ws3", [128, 16, SH])
        ws2 = din("ws2", [NSI, 128, D])
        out = nc.dram_tensor("out", [TCORE, D], F32, kind="ExternalOutput").ap()
        lnx2_dram = nc.dram_tensor("lnx2_dram", [128, 16, TCORE], BF16, kind="Internal").ap()
        comb_dram = nc.dram_tensor("comb_dram", [16, TCORE], F32, kind="Internal").ap()
        if debug:
            dbg["d_ln2"] = (nc.dram_tensor("d_ln2", [128, 16, TCORE], BF16,
                                           kind="ExternalOutput").ap(), lnx2_dram)
            g_dump = nc.dram_tensor("d_g", [128, 22528], BF16, kind="ExternalOutput").ap()
            comb_dump = nc.dram_tensor("d_comb", [16, TCORE], F32, kind="ExternalOutput").ap()

    with ExitStack() as ctx:
        def alloc(name, shape, dtype):
            return nc.alloc_sbuf_tensor(name, list(shape), dtype).ap()

        POOLN = 90112
        POOL = nc.alloc_sbuf_tensor("POOL", [128, POOLN], mybir.dt.uint8).ap()

        def sub(off, shape, dtype):
            es = 2 if dtype == BF16 else 4
            n = int(np.prod(shape[1:])) * es
            assert off % 4 == 0 and off + n <= POOLN, (off, n)
            ap = POOL[0:shape[0], off:off + n].bitcast(dtype)
            if len(shape) == 3:
                ap = ap.rearrange("p (a b) -> p a b", a=shape[1])
            return ap

        xt = [sub(0, [128, D], F32), sub(8192, [128, D], F32)]
        sq = sub(16384, [128, D], F32)
        lnt = [sub(24576, [128, D], BF16), sub(28672, [128, D], BF16)]
        qkt = [sub(32768 + 1024 * i, [128, 512], BF16) for i in range(4)]
        vt = [sub(36864, [128, 512], BF16), sub(37888, [128, 512], BF16)]
        vbB = sub(36864, [128, 512], F32)
        qkvw_sb = sub(38912, [128, 16, 1536], BF16)
        vbrow_s = sub(88064, [1, 512], F32)
        outbrow_s = sub(88064, [1, 512], F32)
        pt = [sub(38912, [128, 512], BF16), sub(39936, [128, 512], BF16)]
        ctt = [sub(40960, [128, 16, 128], BF16), sub(45056, [128, 16, 128], BF16)]
        lnf = [sub(49152, [128, 16, 128], BF16), sub(53248, [128, 16, 128], BF16)]
        x1t = [sub(57344, [128, D], F32), sub(65536, [128, D], F32)]
        outbB = sub(73728, [128, D], F32)
        diag_s = sub(81920, [128, 4, 512], BF16)
        diagA_s = sub(86016, [128, 4, 512], BF16)
        ctxbf = [sub(57344 + 256 * i, [128, 128], BF16) for i in range(4)]
        negirow = sub(58368, [1, 512], BF16)
        ctf = ctt
        LNX2 = sub(0, [128, 16, 1024], BF16)
        s1b = [sub(32768, [128, 1024], BF16), sub(34816, [128, 1024], BF16)]
        t3f = [sub(36864, [128, 1024], F32), sub(40960, [128, 1024], F32)]
        ytile = [sub(45056, [128, 1024], F32), sub(49152, [128, 1024], F32)]
        combf = sub(53248, [16, 1024], F32)

        ident_bf = alloc("ident_bf", [128, 128], BF16)
        ident_f = alloc("ident_f", [128, 128], F32)
        ones_f = alloc("ones_f", [1, 128], F32)
        ones_bf = alloc("ones_bf", [1, 128], BF16)
        jrow_s = alloc("jrow_s", [1, 512], F32)
        pidx_s = alloc("pidx_s", [128, 1], F32)
        slopes_s = alloc("slopes_s", [128, 4], F32)
        s128_s = alloc("s128_s", [1, 4], F32)
        s512_s = alloc("s512_s", [1, 4], F32)
        s511_s = alloc("s511_s", [1, 4], F32)
        qkvb_s = alloc("qkvb_s", [128, 12], F32)
        gate_s = alloc("gate_s", [128, 16, 16], BF16)
        biasb = [alloc(f"biasb{i}", [128, 1], F32) for i in range(4)]
        mhat2 = alloc("mhat2", [128, 16], F32)
        zr4 = alloc("zr4", [128, 4], F32)
        stat = [alloc(f"stat{i}", [128, 8], F32) for i in range(2)]
        top8b = alloc("top8b", [128, 8], F32)
        mhat = alloc("mhat", [128, 64], F32)
        mcol = alloc("mcol", [64, 1], F32)
        drow = alloc("drow", [1, 16], F32)
        alibi_loc = alloc("alibi_loc", [1, 512], BF16)
        crow = alloc("crow", [1, 64], F32)
        adjrow = alloc("adjrow", [1, 16], F32)
        adjB = alloc("adjBb", [128, 16], F32)
        sp_col = alloc("sp_col", [128, 1], F32)
        alibi_p = alloc("alibi_p", [128, 64], F32)
        negD = alloc("negD", [128, 16], F32)
        sco = [alloc(f"sco{i}", [128, 16], F32) for i in range(2)]
        rscr = alloc("rscr", [128, 40], F32)
        combrow = alloc("combrow", [1, 1024], F32)
        sidxst = sub(32768, [128, 256], I32)
        sidxld = sub(33792, [128, 256], I32)
        sidxq = sub(34816, [128, 8], I32)

        R1 = alloc("R1", [128, 22528], BF16)
        R2 = alloc("R2", [128, 32768], BF16)

        PS = nc.alloc_psum_tensor("PS", [128, 4096], F32).ap()
        PSb = PS.bitcast(BF16)

        def bank_f(i, n=512):
            return PS[:, i * 512: i * 512 + n]

        def bank_bf(i, n=128):
            return PSb[:, i * 1024: i * 1024 + n]

        lnx_fm = R2.rearrange("p (dc t) -> p dc t", dc=16)
        K_sb = R2[:, 0:8192]
        Q_sb = R2[:, 8192:16384]
        V_sb = R2[:, 16384:16384 + 64 * 129].rearrange("p (c e) -> p c e", c=64)
        outw_sb = R2.rearrange("p (hd t) -> p hd t", hd=16)
        g_sb = R2[:, 0:22528].rearrange("p (ic t) -> p ic t", ic=22)
        w1b = [R2[:, 22528 + i * 2048: 22528 + (i + 1) * 2048].rearrange("p (dc j) -> p dc j", dc=16)
               for i in range(2)]
        w3b = [R2[:, 26624 + i * 2048: 26624 + (i + 1) * 2048].rearrange("p (dc j) -> p dc j", dc=16)
               for i in range(2)]
        combB = R2[:, 30720:32768].bitcast(F32)
        w2b_r = [R1[:, i * 11264:(i + 1) * 11264].rearrange("p (ic j) -> p ic j", ic=NI) for i in range(2)]
        w2b_s = [R1[:, i * 11264:(i + 1) * 11264].rearrange("p (ic j) -> p ic j", ic=NSI) for i in range(2)]

        _fam_ctr = [0]

        def dma(q, dst, src, fam, reads=(), writes=(), accum=False):
            if fam is None:
                _fam_ctr[0] += 1
                fam = f"u{_fam_ctr[0]}"
            if accum:
                fn = lambda eng, dst=dst, src=src: eng.dma_start(out=dst, in_=src, accum_op=ALU.add)
            else:
                fn = lambda eng, dst=dst, src=src: eng.dma_start(out=dst, in_=src)
            return p.emit(q, fn, reads=reads, writes=writes, fam=fam)

        dma("pool", ident_bf[:], identw, None, writes=["ident_bf"])
        dma("sp", ident_f[:], identw, None, writes=["ident_f"])
        dma("sp", ones_f[:], onesrow, None, writes=["ones_f"])
        dma("pool", ones_bf[:], onesrow, None, writes=["ones_bf"])
        if part in ("a", "p0"):
            dma("sp", jrow_s[:], jrow, None, writes=["jrow_s"])
            dma("sp", pidx_s[:], p_idx, None, writes=["pidx_s"])
            dma("sp", slopes_s[:], slopes, None, writes=["slopes_s"])
            dma("sp", s128_s[:], s128, None, writes=["s128_s"])
            dma("sp", s512_s[:], s512, None, writes=["s512_s"])
            dma("sp", s511_s[:], s511, None, writes=["s511_s"])
            dma("sp", qkvb_s[:], qkvb, None, writes=["qkvb_s"])
            dma("sp", vbrow_s[:], vbrow, None, writes=["vbrow_s"])
            for i in range(4):
                dma("pool", qkvw_sb[:, 4 * i:4 * (i + 1), :], qkvw[:, 4 * i:4 * (i + 1), :],
                    "wqkv", writes=[f"R1_{i}"])
            p.emit("pe", lambda eng: eng.matmul(bank_f(6), ones_f[:], vbrow_s[:],
                                                start=True, stop=True),
                   reads=["ones_f", "vbrow_s"], writes=["pb6"])
            p.emit("dve", lambda eng: eng.tensor_copy(vbB[:], bank_f(6)),
                   reads=["pb6"], writes=["vbB"], hz=True)
        else:
            dma("pool", gate_s[:], gatew, None, writes=["gate_s"])
            for i in range(4):
                dma("pool", outw_sb[:, 4 * i:4 * (i + 1), :], outw[:, 4 * i:4 * (i + 1), :],
                    "wout", writes=[f"OW{i}"])

        def ln_tile(xin, xkey, outbf, outkey, sl):
            st = stat[sl]
            s1, nm, v1, sd, rstd, mr = (st[:, i:i + 1] for i in range(6))
            kst = f"stat{sl}"
            p.emit("dve", lambda eng: eng.reduce_sum(s1, xin, axis=AX.X),
                   reads=[xkey], writes=[kst], hz=True)
            p.emit("dve", lambda eng: eng.tensor_scalar(nm, s1, -1.0 / D, None, op0=ALU.mult),
                   reads=[kst], writes=[kst], hz=True)
            p.emit("act", lambda eng: eng.activation(sq[:], xin, AF.Square, bias=nm, scale=1.0),
                   reads=[xkey, kst], writes=["sq"])
            p.emit("dve", lambda eng: eng.reduce_sum(v1, sq[:], axis=AX.X),
                   reads=["sq"], writes=[kst], hz=True)
            p.emit("dve", lambda eng: eng.tensor_scalar(v1, v1, 1.0 / D, EPS,
                                                        op0=ALU.mult, op1=ALU.add),
                   reads=[kst], writes=[kst], hz=True)
            p.emit("act", lambda eng: eng.activation(sd, v1, AF.Sqrt),
                   reads=[kst], writes=[kst], hz=True)
            p.emit("dve", lambda eng: eng.reciprocal(rstd, sd), reads=[kst], writes=[kst], hz=True)
            p.emit("dve", lambda eng: eng.tensor_tensor(mr, nm, rstd, op=ALU.mult),
                   reads=[kst], writes=[kst], hz=True)
            p.emit("dve", lambda eng: eng.tensor_scalar(outbf, xin, nm, rstd,
                                                        op0=ALU.add, op1=ALU.mult),
                   reads=[xkey, kst], writes=[outkey])

        def transpose16(src_bf, srckey, dst3, dstkey, tslice):
            for dc in range(16):
                bk = 4 + (dc % 4)
                p.emit("pe", lambda eng, dc=dc, bk=bk: eng.transpose(
                    bank_bf(bk), src_bf[:, dc * 128:(dc + 1) * 128], ident_bf[:]),
                    reads=[srckey, "ident_bf"], writes=[f"pb{bk}"])
                p.emit("dve", lambda eng, dc=dc, bk=bk: eng.tensor_copy(
                    dst3[:, dc, tslice], bank_bf(bk)),
                    reads=[f"pb{bk}"], writes=[dstkey])

        for slab in (range(4) if part in ("a", "p0") else []):
            base = slab * 2048
            for t in range(16):
                sl = t % 2
                dma("pool", xt[sl][:], xb[base + t * 128: base + (t + 1) * 128, :], "xb",
                    writes=[f"xt{sl}"])
                ln_tile(xt[sl][:], f"xt{sl}", lnt[sl][:], f"lnt{sl}", sl)
                transpose16(lnt[sl][:], f"lnt{sl}", lnx_fm, "lnxfm",
                            slice(t * 128, (t + 1) * 128))
            for f in range(8):
                for ts in range(4):
                    bk = (f * 4 + ts) % 2
                    sl = (f * 4 + ts) % 2

                    def mm_qk(eng, f=f, ts=ts, bk=bk):
                        for dc in range(16):
                            inst = eng.matmul(bank_f(bk),
                                              qkvw_sb[:, dc, f * 128:(f + 1) * 128],
                                              lnx_fm[:, dc, ts * 512:(ts + 1) * 512],
                                              start=(dc == 0), stop=(dc == 15))
                        return inst
                    p.emit("pe", mm_qk, reads=["lnxfm", "R1_0", "R1_1", "R1_2", "R1_3"],
                           writes=[f"pb{bk}"])
                    p.emit("act", lambda eng, f=f, bk=bk, sl=sl: eng.activation(
                        qkt[sl][:], bank_f(bk), AF.Identity, bias=qkvb_s[:, f:f + 1], scale=1.0),
                        reads=[f"pb{bk}", "qkvb_s"], writes=[f"qkt{sl}"])
                    dst = q_dram if f < 4 else k_dram
                    dkey = "qdram" if f < 4 else "kdram"
                    frow = (f % 4) * 128
                    dma("sp", dst[frow:frow + 128, base + ts * 512: base + (ts + 1) * 512],
                        qkt[sl][:], "qkst", reads=[f"qkt{sl}"], writes=[dkey])
            for t in range(16):
                bk = 2 + t % 2
                sl = t % 2

                def mm_v(eng, t=t, bk=bk):
                    for dc in range(16):
                        inst = eng.matmul(bank_f(bk),
                                          lnx_fm[:, dc, t * 128:(t + 1) * 128],
                                          qkvw_sb[:, dc, 1024:1536],
                                          start=(dc == 0), stop=(dc == 15))
                    return inst
                p.emit("pe", mm_v, reads=["lnxfm", "R1_0", "R1_1", "R1_2", "R1_3"],
                       writes=[f"pb{bk}"])
                p.emit("dve", lambda eng, bk=bk, sl=sl: eng.tensor_tensor(
                    vt[sl][:], bank_f(bk), vbB[:], op=ALU.add),
                    reads=[f"pb{bk}", "vbB"], writes=[f"vt{sl}"])
                dma("sp", v_dram[base + t * 128: base + (t + 1) * 128, :], vt[sl][:],
                    "vst", reads=[f"vt{sl}"], writes=["vdram"])

        if part in ("a", "p0"):
            p.barrier()

        if part == "a":
            dma("pool", diag_s[:], diagm, None, writes=["diag_s"])
            dma("pool", diagA_s[:], diagmA, None, writes=["diagA_s"])
            p.emit("dve", lambda eng: eng.memset(V_sb[:, :, 128:129], 1.0), writes=["Vsb"])

        for s in (range(NSLOT) if part == "a" else []):
            dma("sp", K_sb, k_dram[s * 128:(s + 1) * 128, :], "kqv",
                reads=["kdram"], writes=["Ksb"])
            dma("sp", Q_sb, q_dram[s * 128:(s + 1) * 128, :], "kqv",
                reads=["qdram"], writes=["Qsb"])
            dma("sp", V_sb[:, :, 0:128],
                v_dram.rearrange("(c p) d -> p c d", p=128)[:, :, s * 128:(s + 1) * 128],
                "kqv", reads=["vdram"], writes=["Vsb"])

            p.emit("act", lambda eng, s=s: eng.activation(
                alibi_loc[:], jrow_s[:], AF.Identity, bias=0.0, scale=slopes_s[0:1, s:s + 1]),
                reads=["jrow_s", "slopes_s"], writes=["aloc"])
            if s < 2:
                p.emit("dve", lambda eng, s=s: eng.tensor_scalar(
                    negirow[:], jrow_s[:], slopes_s[0:1, s:s + 1], -1.0,
                    op0=ALU.mult, op1=ALU.mult),
                    reads=["jrow_s", "slopes_s"], writes=["negi"], hz=True)
            p.emit("dve", lambda eng, s=s: eng.tensor_scalar(
                crow[:], jrow_s[0:1, 0:64], s128_s[0:1, s:s + 1], None, op0=ALU.mult),
                reads=["jrow_s", "s128_s"], writes=["crow"], hz=True)
            p.emit("dve", lambda eng, s=s: eng.tensor_scalar(
                sp_col[:], pidx_s[:], slopes_s[:, s:s + 1], None, op0=ALU.mult),
                reads=["pidx_s", "slopes_s"], writes=["sp_col"], hz=True)
            p.emit("pe", lambda eng: eng.matmul(bank_f(2, 64), ones_f[:], crow[:],
                                                start=True, stop=True),
                   reads=["ones_f", "crow"], writes=["pb2"])
            p.emit("dve", lambda eng: eng.tensor_scalar(
                alibi_p[:], bank_f(2, 64), sp_col[:], None, op0=ALU.add),
                reads=["pb2", "sp_col"], writes=["alibi_p"], hz=True)
            if s < 2:
                p.emit("dve", lambda eng, s=s: eng.tensor_scalar(
                    adjrow[:], jrow_s[0:1, 0:16], s512_s[0:1, s:s + 1], None, op0=ALU.mult),
                    reads=["jrow_s", "s512_s"], writes=["adjrow"], hz=True)
            else:
                p.emit("dve", lambda eng, s=s: eng.tensor_scalar(
                    adjrow[:], jrow_s[0:1, 0:16], s512_s[0:1, s:s + 1],
                    s511_s[0:1, s:s + 1], op0=ALU.mult, op1=ALU.add),
                    reads=["jrow_s", "s512_s", "s511_s"], writes=["adjrow"], hz=True)
            p.emit("pe", lambda eng: eng.matmul(bank_f(2, 16), ones_f[:], adjrow[:],
                                                start=True, stop=True),
                   reads=["ones_f", "adjrow"], writes=["pb2"])
            p.emit("dve", lambda eng: eng.tensor_copy(adjB[:], bank_f(2, 16)),
                   reads=["pb2"], writes=["adjB"], hz=True)

            for u in range(64):
                nch = u // 4 + 1
                for c in range(nch):
                    bk = (u + c) % 2

                    def mm_s(eng, u=u, c=c, bk=bk, s=s):
                        if s < 2:
                            eng.matmul(bank_f(bk), Q_sb[:, u * 128:(u + 1) * 128],
                                       K_sb[:, c * 512:(c + 1) * 512], start=True, stop=False)
                            return eng.matmul(bank_f(bk), ones_bf[:], alibi_loc[:],
                                              start=False, stop=True)
                        return eng.matmul(bank_f(bk), Q_sb[:, u * 128:(u + 1) * 128],
                                          K_sb[:, c * 512:(c + 1) * 512], start=True, stop=True)
                    p.emit("pe", mm_s, reads=["Qsb", "Ksb", "aloc", "ones_bf"],
                           writes=[f"pb{bk}"])
                    if s < 2 and c == nch - 1:
                        p.emit("dve", lambda eng, u=u, bk=bk: eng.tensor_tensor(
                            bank_f(bk), bank_f(bk), diagA_s[:, u % 4, :], op=ALU.add),
                            reads=["diagA_s"], writes=[f"pb{bk}"])
                    p.emit("dve", lambda eng, c=c, bk=bk: eng.reduce_max(
                        mhat2[:, c:c + 1], bank_f(bk), axis=AX.X),
                        reads=[f"pb{bk}"], writes=[f"mh{c}"], hz=True)
                p.emit("dve", lambda eng, nch=nch, c_lo=c_lo: eng.tensor_tensor(
                    mhat2[:, c_lo:nch], mhat2[:, c_lo:nch], adjB[:, c_lo:nch], op=ALU.add),
                    reads=["adjB"] + [f"mh{c}" for c in range(c_lo, nch)],
                    writes=[f"mh{c}" for c in range(c_lo, nch)], hz=True)
                p.emit("dve", lambda eng, u=u, nch=nch, c_lo=c_lo: eng.reduce_max(
                    mhat[:, u:u + 1], mhat2[:, c_lo:nch], axis=AX.X),
                    reads=[f"mh{c}" for c in range(c_lo, nch)], writes=["mhat"], hz=True)
                if s < 2:
                    p.emit("dve", lambda eng, u=u: eng.tensor_tensor(
                        mhat[:, u:u + 1], mhat[:, u:u + 1], alibi_p[:, u:u + 1],
                        op=ALU.subtract),
                        reads=["alibi_p", "mhat"], writes=["mhat"], hz=True)

            p.emit("pe", lambda eng: eng.transpose(PS[0:64, 1024:1152], mhat[:, 0:64], ident_f[:]),
                   reads=["mhat", "ident_f"], writes=["pb2"])
            p.emit("dve", lambda eng: eng.reduce_max(mcol[:], PS[0:64, 1024:1152], axis=AX.X),
                   reads=["pb2"], writes=["mcol"], hz=True)
            p.emit("pe", lambda eng: eng.transpose(PS[0:1, 1536:1600], mcol[0:64, 0:1],
                                                   ident_f[0:64, 0:64]),
                   reads=["mcol", "ident_f"], writes=["pb3"])
            p.emit("dve", lambda eng: eng.tensor_reduce(
                drow[:], PS[0:1, 1536:1600].rearrange("p (a b) -> p a b", a=16),
                axis=AX.X, op=ALU.max),
                reads=["pb3"], writes=["drow"], hz=True)
            if s < 2:
                p.emit("dve", lambda eng: eng.tensor_tensor(
                    drow[:], drow[:], adjrow[0:1, 0:16], op=ALU.add),
                    reads=["drow", "adjrow"], writes=["drow"], hz=True)
            p.emit("dve", lambda eng: eng.tensor_scalar(drow[:], drow[:], -1.0, None, op0=ALU.mult),
                   reads=["drow"], writes=["drow"], hz=True)
            p.emit("pe", lambda eng: eng.matmul(bank_f(2, 16), ones_f[:], drow[:],
                                                start=True, stop=True),
                   reads=["ones_f", "drow"], writes=["pb2"])
            p.emit("dve", lambda eng: eng.tensor_copy(negD[:], bank_f(2, 16)),
                   reads=["pb2"], writes=["negD"], hz=True)

            for t in range(16):
                nkc = 4 * t + 4
                for kc in range(nkc):
                    bk = kc % 2

                    def mm_st(eng, t=t, kc=kc, bk=bk, s=s):
                        if s < 2:
                            eng.matmul(bank_f(bk), K_sb[:, kc * 128:(kc + 1) * 128],
                                       Q_sb[:, t * 512:(t + 1) * 512], start=True, stop=False)
                            return eng.matmul(bank_f(bk), ones_bf[:], negirow[:],
                                              start=False, stop=True)
                        return eng.matmul(bank_f(bk), K_sb[:, kc * 128:(kc + 1) * 128],
                                          Q_sb[:, t * 512:(t + 1) * 512], start=True, stop=True)
                    p.emit("pe", mm_st, reads=["Ksb", "Qsb", "negi", "ones_bf"],
                           writes=[f"pb{bk}"])
                    if kc >= 4 * t:
                        p.emit("dve", lambda eng, t=t, kc=kc, bk=bk: eng.tensor_tensor(
                            bank_f(bk), bank_f(bk), diag_s[:, kc - 4 * t, :], op=ALU.add),
                            reads=["diag_s"], writes=[f"pb{bk}"])
                    psl = kc % 2
                    p.emit("act", lambda eng, bk=bk, kc=kc, psl=psl: eng.activation(
                        pt[psl][:], bank_f(bk), AF.Exp, bias=biasT[:, kc:kc + 1], scale=1.0),
                        reads=[f"pb{bk}", "biasT"], writes=[f"pt{psl}"])

                    if debug and s == 0 and t == 0 and kc == 0:
                        dma("sp", d_pt, pt[psl][:], None, reads=[f"pt{psl}"])
                        dma("sp", d_bias, biasb[bsl][:], None, reads=[f"bias{bsl}"])

                    def pv(eng, kc=kc, psl=psl, nkc=nkc, kc_lo=kc_lo):
                        for qq in range(4):
                            inst = eng.matmul(PS[:, (4 + qq) * 512:(4 + qq) * 512 + 129],
                                              pt[psl][:, qq * 128:(qq + 1) * 128],
                                              V_sb[:, kc, :],
                                              start=(kc == kc_lo), stop=(kc == nkc - 1))
                        return inst
                    p.emit("pe", pv, reads=[f"pt{psl}", "Vsb"],
                           writes=["pb4", "pb5", "pb6", "pb7"])
                zcols = PS[:, 2048:4096].rearrange("p (q x) -> p q x", q=4)[:, :, 128:129]
                p.emit("dve", lambda eng, zcols=zcols: eng.tensor_scalar(
                    zr4[:, :, None], zcols, 1e-37, None, op0=ALU.max),
                    reads=["pb4", "pb5", "pb6", "pb7"], writes=["zr4"], hz=True)
                p.emit("dve", lambda eng: eng.reciprocal(zr4[:], zr4[:]),
                       reads=["zr4"], writes=["zr4"], hz=True)
                for qq in range(4):
                    p.emit("dve", lambda eng, qq=qq: eng.tensor_scalar(
                        ctxbf[qq][:], PS[:, (4 + qq) * 512:(4 + qq) * 512 + 128],
                        zr4[:, qq:qq + 1], None, op0=ALU.mult),
                        reads=[f"pb{4 + qq}", "zr4"], writes=[f"ctxbf{qq}"])
                    dma("sp", ctx_dram[t * 512 + qq * 128: t * 512 + (qq + 1) * 128,
                                       s * 128:(s + 1) * 128],
                        ctxbf[qq][:], "ctxst", reads=[f"ctxbf{qq}"], writes=["ctxdram"])

        for i in (range(4) if part == "b" else []):
            dma("sp", outbrow_s[:], outbrow[0:1, i * 512:(i + 1) * 512], None,
                reads=["outbrow_done"], writes=["outbrow_s"])
            p.emit("pe", lambda eng: eng.matmul(bank_f(2), ones_f[:], outbrow_s[:],
                                                start=True, stop=True),
                   reads=["ones_f", "outbrow_s"], writes=["pb2"])
            p.emit("dve", lambda eng, i=i: eng.tensor_copy(outbB[:, i * 512:(i + 1) * 512],
                                                           bank_f(2)),
                   reads=["pb2", "outbrow_s"], writes=["outbB", "outbrow_done"], hz=True)

        for t in (range(16) if part == "b" else []):
            sl = t % 2
            for h in range(16):
                dma("sp", ctt[sl][:, h, :],
                    ctx_recv.rearrange("(r tt) d -> r tt d", r=4)[
                        h % 4, t * 128:(t + 1) * 128, (h // 4) * 128:(h // 4 + 1) * 128],
                    "ctld", reads=["ctxrecv"], writes=[f"ctt{sl}"])
            for hd in range(16):
                bk = hd % 2
                p.emit("pe", lambda eng, hd=hd, bk=bk, sl=sl: eng.transpose(
                    bank_bf(bk), ctt[sl][:, hd, :], ident_bf[:]),
                    reads=[f"ctt{sl}", "ident_bf"], writes=[f"pb{bk}"])
                p.emit("dve", lambda eng, hd=hd, bk=bk, sl=sl: eng.tensor_copy(
                    ctf[sl][:, hd, :], bank_bf(bk)),
                    reads=[f"pb{bk}"], writes=[f"ctt{sl}"])

            def mm_out(eng, sl=sl):
                for hd in range(16):
                    for n in range(4):
                        inst = eng.matmul(PS[:, (4 + n) * 512:(5 + n) * 512],
                                          ctf[sl][:, hd, :],
                                          outw_sb[:, hd, n * 512:(n + 1) * 512],
                                          start=(hd == 0), stop=(hd == 15))
                return inst
            p.emit("pe", mm_out, reads=[f"ctt{sl}", "OW0", "OW1", "OW2", "OW3"],
                   writes=["pb4", "pb5", "pb6", "pb7"])
            dma("sp", xt[sl][:], x_own[t * 128:(t + 1) * 128, :], "xo", writes=[f"xt{sl}"])
            p.emit("dve", lambda eng, sl=sl: eng.tensor_tensor(
                x1t[sl][:], PS[:, 2048:4096], outbB[:], op=ALU.add),
                reads=["pb4", "pb5", "pb6", "pb7", "outbB"], writes=[f"x1t{sl}"])
            p.emit("dve", lambda eng, sl=sl: eng.tensor_tensor(
                x1t[sl][:], x1t[sl][:], xt[sl][:], op=ALU.add),
                reads=[f"xt{sl}", f"x1t{sl}"], writes=[f"x1t{sl}"])
            dma("sp", out[t * 128:(t + 1) * 128, :], x1t[sl][:], "x1st",
                reads=[f"x1t{sl}"],
                writes=[f"y{t // 8}_{t % 8}_0", f"y{t // 8}_{t % 8}_1"])
            ln_tile(x1t[sl][:], f"x1t{sl}", lnt[sl][:], f"lnt{sl}", sl)
            transpose16(lnt[sl][:], f"lnt{sl}", lnf[sl], f"lnf{sl}", slice(0, 128))
            dma("sp", lnx2_dram[:, :, t * 128:(t + 1) * 128], lnf[sl][:, :, :], "ln2st",
                reads=[f"lnf{sl}"], writes=["ln2dram"])

        if part == "b":
            p.barrier()

        NSUB = 8
        for b2 in (range(2) if part == "b" else []):
            tb = b2 * 1024
            dma("pool", LNX2[:, :, :], lnx2_dram[:, :, tb:tb + 1024], "ln2ld",
                reads=["ln2dram"], writes=["lnx2b"])
            for tt in range(8):
                sl = tt % 2

                def mm_gate(eng, tt=tt):
                    for dc in range(16):
                        inst = eng.matmul(bank_f(6, 16), LNX2[:, dc, tt * 128:(tt + 1) * 128],
                                          gate_s[:, dc, :], start=(dc == 0), stop=(dc == 15))
                    return inst
                p.emit("pe", mm_gate, reads=["lnx2b", "gate_s"], writes=["pb6"])
                p.emit("act", lambda eng, sl=sl: eng.activation(
                    sco[sl][:], bank_f(6, 16), AF.Sigmoid, bias=0.0, scale=1.0),
                    reads=["pb6"], writes=[f"sco{sl}"])

                sc = sco[sl]
                gmax = rscr[:, 0:4]
                m1 = rscr[:, 4:5]
                mk1 = rscr[:, 5:9]
                g2 = rscr[:, 9:13]
                m2 = rscr[:, 13:14]
                gmask = rscr[:, 14:18]
                masked = rscr[:, 18:34]
                ssum = rscr[:, 35:36]
                RT = ["rt"]
                p.emit("dve", lambda eng, sc=sc: eng.tensor_reduce(
                    gmax, sc.rearrange("p (g m) -> p g m", g=4), axis=AX.X, op=ALU.max),
                    reads=[f"sco{sl}"], writes=RT, hz=True)
                p.emit("dve", lambda eng: eng.reduce_max(m1, gmax, axis=AX.X),
                       reads=RT, writes=RT, hz=True)
                p.emit("dve", lambda eng: eng.tensor_scalar(mk1, gmax, m1, None, op0=ALU.is_ge),
                       reads=RT, writes=RT, hz=True)
                p.emit("dve", lambda eng: eng.scalar_tensor_tensor(
                    g2, mk1, -1e30, gmax, op0=ALU.mult, op1=ALU.add),
                    reads=RT, writes=RT, hz=True)
                p.emit("dve", lambda eng: eng.reduce_max(m2, g2, axis=AX.X),
                       reads=RT, writes=RT, hz=True)
                p.emit("dve", lambda eng: eng.tensor_scalar(gmask, gmax, m2, None, op0=ALU.is_ge),
                       reads=RT, writes=RT, hz=True)

                def maskmul(eng, sc=sc):
                    for g4 in range(4):
                        inst = eng.tensor_scalar(masked[:, g4 * 4:(g4 + 1) * 4],
                                                 sc[:, g4 * 4:(g4 + 1) * 4],
                                                 gmask[:, g4:g4 + 1], None, op0=ALU.mult)
                    return inst
                p.emit("dve", maskmul, reads=RT + [f"sco{sl}"], writes=RT, hz=True)
                p.emit("dve", lambda eng: eng.max(out=top8b[:], in_=masked),
                       reads=RT, writes=RT + ["top8b"], hz=True)
                p.emit("dve", lambda eng: eng.tensor_scalar(
                    masked, masked, top8b[:, 3:4], None, op0=ALU.is_ge),
                    reads=RT + ["top8b"], writes=RT, hz=True)
                p.emit("dve", lambda eng, sc=sc: eng.tensor_tensor(masked, sc, masked, op=ALU.mult),
                       reads=RT + [f"sco{sl}"], writes=RT, hz=True)
                p.emit("dve", lambda eng: eng.reduce_sum(ssum, masked, axis=AX.X),
                       reads=RT, writes=RT, hz=True)
                p.emit("dve", lambda eng: eng.reciprocal(ssum, ssum),
                       reads=RT, writes=RT, hz=True)
                p.emit("dve", lambda eng: eng.tensor_scalar(masked, masked, ssum, None, op0=ALU.mult),
                       reads=RT, writes=RT, hz=True)
                p.emit("pe", lambda eng: eng.transpose(PS[0:16, 3584:3712], rscr[:, 18:34],
                                                       ident_f[:]),
                       reads=["rt", "ident_f"], writes=["pb7"])
                p.emit("dve", lambda eng, tt=tt: eng.tensor_copy(
                    combf[0:16, tt * 128:(tt + 1) * 128], PS[0:16, 3584:3712]),
                    reads=["pb7"], writes=["combf"], hz=True)
            dma("sp", comb_dram[:, tb:tb + 1024], combf[:], "combst",
                reads=["combf"], writes=["combdram"])

            for e in range(NE + 1):
                shared = (e == NE)
                nIc = NSI if shared else NI
                w1src = ws1 if shared else w1[e]
                w3src = ws3 if shared else w3[e]
                if not shared:
                    dma("pool", combrow[:], comb_dram[e:e + 1, tb:tb + 1024], "combld",
                        reads=["combdram"], writes=["combrow"])
                    for i in range(2):
                        p.emit("pe", lambda eng, i=i: eng.matmul(
                            bank_f(6), ones_f[:],
                            combrow[0:1, i * 512:(i + 1) * 512], start=True, stop=True),
                            reads=["combrow", "ones_f"], writes=["pb6"])
                        p.emit("dve", lambda eng, i=i: eng.tensor_copy(
                            combB[:, i * 512:(i + 1) * 512], bank_f(6)),
                            reads=["pb6"], writes=["combB"], hz=True)
                for ic in range(nIc):
                    wsl = ic % 2
                    dma("pool", w1b[wsl][:, :, :], w1src[:, :, ic * 128:(ic + 1) * 128],
                        "w1", writes=[f"w1b{wsl}"])
                    dma("pool", w3b[wsl][:, :, :], w3src[:, :, ic * 128:(ic + 1) * 128],
                        "w3", writes=[f"w3b{wsl}"])

                    def mm_h(eng, wb, bko, wkey):
                        for dc in range(16):
                            for ns in range(2):
                                inst = eng.matmul(bank_f(bko + ns),
                                                  wb[:, dc, :],
                                                  LNX2[:, dc, ns * 512:(ns + 1) * 512],
                                                  start=(dc == 0), stop=(dc == 15))
                        return inst
                    p.emit("pe", functools.partial(mm_h, wb=w1b[wsl], bko=0, wkey=None),
                           reads=[f"w1b{wsl}", "lnx2b"], writes=["pb0", "pb1"])
                    p.emit("pe", functools.partial(mm_h, wb=w3b[wsl], bko=2, wkey=None),
                           reads=[f"w3b{wsl}", "lnx2b"], writes=["pb2", "pb3"])
                    ssl = ic % 2
                    p.emit("act", lambda eng, ssl=ssl: eng.activation(
                        s1b[ssl][:], PS[:, 0:1024], AF.Silu, bias=0.0, scale=1.0),
                        reads=["pb0", "pb1"], writes=[f"s1b{ssl}"])
                    if shared:
                        p.emit("dve", lambda eng, ic=ic, ssl=ssl: eng.tensor_tensor(
                            g_sb[:, ic, :], PS[:, 1024:2048], s1b[ssl][:], op=ALU.mult),
                            reads=["pb2", "pb3", f"s1b{ssl}"], writes=["g_sb"])
                    else:
                        p.emit("dve", lambda eng, ssl=ssl: eng.tensor_tensor(
                            t3f[ssl][:], PS[:, 1024:2048], combB[:], op=ALU.mult),
                            reads=["pb2", "pb3", "combB"], writes=[f"t3f{ssl}"])
                        p.emit("dve", lambda eng, ic=ic, ssl=ssl: eng.tensor_tensor(
                            g_sb[:, ic, :], t3f[ssl][:], s1b[ssl][:], op=ALU.mult),
                            reads=[f"t3f{ssl}", f"s1b{ssl}"], writes=["g_sb"])
                if debug and b2 == 0 and e == 0:
                    dma("sp", g_dump, R2[:, 0:22528], None, reads=["g_sb"])
                nH = 4 if shared else 2
                wdt = 512 if shared else 1024
                w2srcv = ws2 if shared else w2[e]
                w2bv = w2b_s if shared else w2b_r
                for hf in range(nH):
                    wsl = hf % 2
                    dma("pool", w2bv[wsl][:, :, :],
                        w2srcv.rearrange("a p j -> p a j")[:, :, hf * wdt:(hf + 1) * wdt],
                        "w2", writes=[f"w2b{wsl}"])
                    for sub in range(NSUB):
                        nbk = wdt // 512
                        ybk = 4 + (sub % 2) * nbk

                        def mm_y(eng, hf=hf, sub=sub, wsl=wsl, ybk=ybk, nIc=nIc, nbk=nbk,
                                 w2bv=w2bv):
                            for ic in range(nIc):
                                for n2 in range(nbk):
                                    inst = eng.matmul(
                                        bank_f(ybk + n2),
                                        g_sb[:, ic, sub * 128:(sub + 1) * 128],
                                        w2bv[wsl][:, ic, n2 * 512:(n2 + 1) * 512],
                                        start=(ic == 0), stop=(ic == nIc - 1))
                            return inst
                        ykeys = [f"pb{ybk + n2}" for n2 in range(nbk)]
                        p.emit("pe", mm_y, reads=["g_sb", f"w2b{wsl}"], writes=ykeys)
                        ysl = sub % 2
                        p.emit("dve", lambda eng, ybk=ybk, ysl=ysl, wdt=wdt: eng.tensor_copy(
                            ytile[ysl][:, 0:wdt], PS[:, ybk * 512: ybk * 512 + wdt]),
                            reads=ykeys, writes=[f"ytile{ysl}"])
                        colo = hf * wdt
                        dma("pool",
                            out[tb + sub * 128: tb + (sub + 1) * 128, colo:colo + wdt],
                            ytile[ysl][:, 0:wdt], "yacc",
                            reads=[f"ytile{ysl}"],
                            writes=[f"y{b2}_{sub}_{colo // 1024}"], accum=True)

        if debug:
            for nm in dbg:
                dst, src = dbg[nm]
                dma("pool", dst, src, None,
                    reads=["ctxdram", "ln2dram", "qdram", "kdram", "vdram"])
            if part == "b":
                dma("pool", comb_dump, comb_dram, None, reads=["combdram"])

        p.barrier()
        p.finalize_and_emit(nc, ctx)
    return nc



def build_fused(debug=False):
    nc = bass.Bass("TRN2", target_bir_lowering=False, num_devices=8)
    p = Prog()
    NS = 8
    NTt = 8
    NU = 32

    def din(name, shape, dtype=F32):
        return nc.dram_tensor(name, list(shape), dtype, kind="ExternalInput").ap()

    identw = din("identw", [128, 128])
    onesrow = din("onesrow", [1, 128])
    xb = din("xb", [S, D])
    x_own = din("x_own", [TCORE, D])
    qkvw = din("qkvw", [128, 16, 24 * 128])
    qkvb = din("qkvb", [128, 16])
    vbrow = din("vbrow", [1, 1024])
    outw = din("outw", [128, 16, D])
    outbrow = din("outbrow", [1, D])
    gatew = din("gatew", [128, 16, 16])
    w1 = din("w1", [NE, 128, 16, I_FF])
    w3 = din("w3", [NE, 128, 16, I_FF])
    w2 = din("w2", [NE, NI, 128, D])
    ws1 = din("ws1", [128, 16, SH])
    ws3 = din("ws3", [128, 16, SH])
    ws2 = din("ws2", [NSI, 128, D])
    jrow = din("jrow", [1, 512])
    p_idx = din("p_idx", [128, 1])
    slopes = din("slopes", [128, 8])
    s128 = din("s128", [1, 8])
    s512 = din("s512", [1, 16])
    s511 = din("s511", [1, 16])
    tilepos = din("tilepos", [1, 8])
    mypos128 = din("mypos128", [1, 32])
    diagA2 = din("diagA2", [128, 8, 512])
    diagB2 = din("diagB2", [128, 8, 512])
    idxq = din("idxq", [128, 8], I32)
    idxst = din("idxst", [128, 256], I32)
    idxld = din("idxld", [128, 256], I32)

    out = nc.dram_tensor("out", [TCORE, D], F32, kind="ExternalOutput").ap()

    q_dram = nc.dram_tensor("q_dram", [2 * 8 * DH, 4096], BF16, kind="Internal").ap()
    k_dram = nc.dram_tensor("k_dram", [NS * DH, S], BF16, kind="Internal").ap()
    v_dram = nc.dram_tensor("v_dram", [S, NS * DH], BF16, kind="Internal").ap()
    ctxsh = nc.dram_tensor("ctxsh", [2 * NS * 4096, DH], BF16, kind="Internal",
                           addr_space="Shared").ap()
    lnx2_dram = nc.dram_tensor("lnx2_dram", [128, 16, TCORE], BF16, kind="Internal").ap()
    comb_dram = nc.dram_tensor("comb_dram", [16, TCORE], F32, kind="Internal").ap()

    dbg = {}
    if debug:
        for nm, t in (("d_q", q_dram), ("d_k", k_dram), ("d_v", v_dram),
                      ("d_ctx", ctxsh), ("d_ln2", lnx2_dram)):
            dbg[nm] = (nc.dram_tensor(nm, list(t.shape), BF16, kind="ExternalOutput").ap(), t)
        d_comb = nc.dram_tensor("d_comb", [16, TCORE], F32, kind="ExternalOutput").ap()
        d_diagB = nc.dram_tensor("d_diagB", [128, 8 * 512], BF16, kind="ExternalOutput").ap()
        d_mhat = nc.dram_tensor("d_mhat", [128, 32], F32, kind="ExternalOutput").ap()
        d_negD = nc.dram_tensor("d_negD", [128, 8], F32, kind="ExternalOutput").ap()
        d_alip = nc.dram_tensor("d_alip", [128, 64], F32, kind="ExternalOutput").ap()
        d_pt = nc.dram_tensor("d_pt", [128, 512], BF16, kind="ExternalOutput").ap()
        d_bias = nc.dram_tensor("d_bias", [128, 1], F32, kind="ExternalOutput").ap()
        d_qsb = nc.dram_tensor("d_qsb", [128, 4096], BF16, kind="ExternalOutput").ap()
        d_ksb = nc.dram_tensor("d_ksb", [128, 8192], BF16, kind="ExternalOutput").ap()

    with ExitStack() as ctx:
        def alloc(name, shape, dtype):
            return nc.alloc_sbuf_tensor(name, list(shape), dtype).ap()

        POOLN = 90112
        POOL = nc.alloc_sbuf_tensor("POOL", [128, POOLN], mybir.dt.uint8).ap()

        def sub(off, shape, dtype):
            es = 2 if dtype == BF16 else 4
            n = int(np.prod(shape[1:])) * es
            assert off % 4 == 0 and off + n <= POOLN, (off, n)
            ap = POOL[0:shape[0], off:off + n].bitcast(dtype)
            if len(shape) == 3:
                ap = ap.rearrange("p (a b) -> p a b", a=shape[1])
            return ap

        xt = [sub(0, [128, D], F32), sub(8192, [128, D], F32)]
        sq = sub(16384, [128, D], F32)
        lnt = [sub(24576, [128, D], BF16), sub(28672, [128, D], BF16)]
        qkt = [sub(32768 + 1024 * i, [128, 512], BF16) for i in range(4)]
        vt = [sub(36864, [128, 512], BF16), sub(37888, [128, 512], BF16)]
        wqk = [sub(38912, [128, 16, 128], BF16), sub(43008, [128, 16, 128], BF16)]
        vw = sub(47104, [128, 16, 512], BF16)
        vbB = sub(63488, [128, 1024], F32)
        vbrow_s = sub(67584, [1, 1024], F32)
        outbrow_s = sub(67584, [1, 512], F32)
        pt = [sub(38912, [128, 512], BF16), sub(39936, [128, 512], BF16)]
        ctt = [sub(40960, [128, 16, 128], BF16), sub(45056, [128, 16, 128], BF16)]
        lnf = [sub(49152, [128, 16, 128], BF16), sub(53248, [128, 16, 128], BF16)]
        x1t = [sub(57344, [128, D], F32), sub(65536, [128, D], F32)]
        outbB = sub(73728, [128, D], F32)
        diagA_s = sub(81920, [128, 8, 512], BF16)
        diagB_s = sub(40960, [128, 8, 512], BF16)
        ctxbf = [sub(57344 + 256 * i, [128, 128], BF16) for i in range(4)]
        negirow = sub(58368, [1, 512], BF16)
        ctf = ctt
        LNX2 = sub(0, [128, 16, 1024], BF16)
        s1b = [sub(32768, [128, 1024], BF16), sub(34816, [128, 1024], BF16)]
        t3f = [sub(36864, [128, 1024], F32), sub(40960, [128, 1024], F32)]
        ytile = [sub(45056, [128, 1024], F32), sub(49152, [128, 1024], F32)]
        combf = sub(53248, [16, 1024], F32)

        ident_bf = alloc("ident_bf", [128, 128], BF16)
        ident_f = alloc("ident_f", [128, 128], F32)
        ones_f = alloc("ones_f", [1, 128], F32)
        ones_bf = alloc("ones_bf", [1, 128], BF16)
        jrow_s = alloc("jrow_s", [1, 512], F32)
        pidx_s = alloc("pidx_s", [128, 1], F32)
        slopes_s = alloc("slopes_s", [128, 8], F32)
        s128_s = alloc("s128_s", [1, 8], F32)
        s512_s = alloc("s512_s", [1, 16], F32)
        s511_s = alloc("s511_s", [1, 16], F32)
        tilepos_s = alloc("tilepos_s", [1, 8], F32)
        mypos_s = alloc("mypos_s", [1, 32], F32)
        qkvb_s = alloc("qkvb_s", [128, 16], F32)
        gate_s = alloc("gate_s", [128, 16, 16], BF16)
        mhat2 = alloc("mhat2", [128, 16], F32)
        zr4 = alloc("zr4", [128, 4], F32)
        stat = [alloc(f"stat{i}", [128, 8], F32) for i in range(2)]
        top8b = alloc("top8b", [128, 8], F32)
        mhat = alloc("mhat", [128, 32], F32)
        mcol = alloc("mcol", [32, 1], F32)
        drow = alloc("drow", [1, 8], F32)
        alibi_loc = alloc("alibi_loc", [1, 512], BF16)
        crow = alloc("crow", [1, 64], F32)
        crowq = alloc("crowq", [1, 32], F32)
        adjrow = alloc("adjrow", [1, 16], F32)
        adjrow2 = alloc("adjrow2", [1, 8], F32)
        adjB = alloc("adjBb", [128, 16], F32)
        sp_col = alloc("sp_col", [128, 1], F32)
        alibi_p = alloc("alibi_p", [128, 64], F32)
        alibi_pq = alloc("alibi_pq", [128, 32], F32)
        negD = alloc("negD", [128, 8], F32)
        biasb = [alloc(f"biasb{i}", [128, 1], F32) for i in range(4)]
        biasT = sub(63488, [128, 64], F32)
        sco = [alloc(f"sco{i}", [128, 16], F32) for i in range(2)]
        rscr = alloc("rscr", [128, 40], F32)
        combrow = alloc("combrow", [1, 1024], F32)
        sidxst = sub(32768, [128, 256], I32)
        sidxld = sub(33792, [128, 256], I32)
        sidxq = sub(34816, [128, 8], I32)

        R1 = alloc("R1", [128, 22528], BF16)
        R2 = alloc("R2", [128, 32768], BF16)

        PS = nc.alloc_psum_tensor("PS", [128, 4096], F32).ap()
        PSb = PS.bitcast(BF16)

        def bank_f(i, n=512):
            return PS[:, i * 512: i * 512 + n]

        def bank_bf(i, n=128):
            return PSb[:, i * 1024: i * 1024 + n]

        lnx_fm = R2.rearrange("p (dc t) -> p dc t", dc=16)
        K_sb = R2[:, 0:8192]
        Q_sb = R2[:, 8192:12288]
        V_sb = R2[:, 16384:16384 + 64 * 129].rearrange("p (c e) -> p c e", c=64)
        outw_sb = R2.rearrange("p (hd t) -> p hd t", hd=16)
        g_sb = R2[:, 0:22528].rearrange("p (ic t) -> p ic t", ic=22)
        w1b = [R2[:, 22528 + i * 2048: 22528 + (i + 1) * 2048].rearrange("p (dc j) -> p dc j", dc=16)
               for i in range(2)]
        w3b = [R2[:, 26624 + i * 2048: 26624 + (i + 1) * 2048].rearrange("p (dc j) -> p dc j", dc=16)
               for i in range(2)]
        combB = R2[:, 30720:32768].bitcast(F32)
        w2b_r = [R1[:, i * 11264:(i + 1) * 11264].rearrange("p (ic j) -> p ic j", ic=NI) for i in range(2)]
        w2b_s = [R1[:, i * 11264:(i + 1) * 11264].rearrange("p (ic j) -> p ic j", ic=NSI) for i in range(2)]

        _fam_ctr = [0]

        def dma(q, dst, src, fam, reads=(), writes=(), accum=False):
            if fam is None:
                _fam_ctr[0] += 1
                fam = f"u{_fam_ctr[0]}"
            if accum:
                fn = lambda eng, dst=dst, src=src: eng.dma_start(out=dst, in_=src, accum_op=ALU.add)
            else:
                fn = lambda eng, dst=dst, src=src: eng.dma_start(out=dst, in_=src)
            return p.emit(q, fn, reads=reads, writes=writes, fam=fam)

        dma("pool", ident_bf[:], identw, None, writes=["ident_bf"])
        dma("sp", ident_f[:], identw, None, writes=["ident_f"])
        dma("sp", ones_f[:], onesrow, None, writes=["ones_f"])
        dma("pool", ones_bf[:], onesrow, None, writes=["ones_bf"])
        dma("sp", jrow_s[:], jrow, None, writes=["jrow_s"])
        dma("sp", pidx_s[:], p_idx, None, writes=["pidx_s"])
        dma("sp", slopes_s[:], slopes, None, writes=["slopes_s"])
        dma("sp", s128_s[:], s128, None, writes=["s128_s"])
        dma("sp", s512_s[:], s512, None, writes=["s512_s"])
        dma("sp", s511_s[:], s511, None, writes=["s511_s"])
        dma("sp", tilepos_s[:], tilepos, None, writes=["tilepos_s"])
        dma("sp", mypos_s[:], mypos128, None, writes=["mypos_s"])
        dma("sp", qkvb_s[:], qkvb, None, writes=["qkvb_s"])
        dma("pool", gate_s[:], gatew, None, writes=["gate_s"])
        dma("sp", vbrow_s[:], vbrow, None, writes=["vbrow_s"])
        p.emit("pe", lambda eng: eng.matmul(bank_f(6), ones_f[:], vbrow_s[0:1, 0:512],
                                            start=True, stop=True),
               reads=["ones_f", "vbrow_s"], writes=["pb6"])
        p.emit("dve", lambda eng: eng.tensor_copy(vbB[:, 0:512], bank_f(6)),
               reads=["pb6"], writes=["vbB"], hz=True)
        p.emit("pe", lambda eng: eng.matmul(bank_f(6), ones_f[:], vbrow_s[0:1, 512:1024],
                                            start=True, stop=True),
               reads=["ones_f", "vbrow_s", "vbB"], writes=["pb6"])
        p.emit("dve", lambda eng: eng.tensor_copy(vbB[:, 512:1024], bank_f(6)),
               reads=["pb6"], writes=["vbB"], hz=True)

        def ln_tile(xin, xkey, outbf, outkey, sl):
            st = stat[sl]
            s1, nm, v1, sd, rstd, mr = (st[:, i:i + 1] for i in range(6))
            kst = f"stat{sl}"
            p.emit("dve", lambda eng: eng.reduce_sum(s1, xin, axis=AX.X),
                   reads=[xkey], writes=[kst], hz=True)
            p.emit("dve", lambda eng: eng.tensor_scalar(nm, s1, -1.0 / D, None, op0=ALU.mult),
                   reads=[kst], writes=[kst], hz=True)
            p.emit("act", lambda eng: eng.activation(sq[:], xin, AF.Square, bias=nm, scale=1.0),
                   reads=[xkey, kst], writes=["sq"])
            p.emit("dve", lambda eng: eng.reduce_sum(v1, sq[:], axis=AX.X),
                   reads=["sq"], writes=[kst], hz=True)
            p.emit("dve", lambda eng: eng.tensor_scalar(v1, v1, 1.0 / D, EPS,
                                                        op0=ALU.mult, op1=ALU.add),
                   reads=[kst], writes=[kst], hz=True)
            p.emit("act", lambda eng: eng.activation(sd, v1, AF.Sqrt),
                   reads=[kst], writes=[kst], hz=True)
            p.emit("dve", lambda eng: eng.reciprocal(rstd, sd), reads=[kst], writes=[kst], hz=True)
            p.emit("dve", lambda eng: eng.tensor_tensor(mr, nm, rstd, op=ALU.mult),
                   reads=[kst], writes=[kst], hz=True)
            p.emit("dve", lambda eng: eng.tensor_scalar(outbf, xin, nm, rstd,
                                                        op0=ALU.add, op1=ALU.mult),
                   reads=[xkey, kst], writes=[outkey])

        def transpose16(src_bf, srckey, dst3, dstkey, tslice):
            for dc in range(16):
                bk = 4 + (dc % 4)
                p.emit("pe", lambda eng, dc=dc, bk=bk: eng.transpose(
                    bank_bf(bk), src_bf[:, dc * 128:(dc + 1) * 128], ident_bf[:]),
                    reads=[srckey, "ident_bf"], writes=[f"pb{bk}"])
                p.emit("dve", lambda eng, dc=dc, bk=bk: eng.tensor_copy(
                    dst3[:, dc, tslice], bank_bf(bk)),
                    reads=[f"pb{bk}"], writes=[dstkey])

        for slab in range(4):
            base = slab * 2048
            for t in range(16):
                sl = t % 2
                dma("pool", xt[sl][:], xb[base + t * 128: base + (t + 1) * 128, :], "xb",
                    writes=[f"xt{sl}"])
                ln_tile(xt[sl][:], f"xt{sl}", lnt[sl][:], f"lnt{sl}", sl)
                transpose16(lnt[sl][:], f"lnt{sl}", lnx_fm, "lnxfm",
                            slice(t * 128, (t + 1) * 128))
            for f in range(16):
                wsl = f % 2
                dma("pool", wqk[wsl][:, :, :], qkvw[:, :, f * 128:(f + 1) * 128], "wqk",
                    writes=[f"wqk{wsl}"])

                for half in range(2):
                    def mm_qk(eng, f=f, wsl=wsl, half=half):
                        for dc in range(16):
                            for ts in (2 * half, 2 * half + 1):
                                inst = eng.matmul(bank_f(ts),
                                                  wqk[wsl][:, dc, :],
                                                  lnx_fm[:, dc, ts * 512:(ts + 1) * 512],
                                                  start=(dc == 0), stop=(dc == 15))
                        return inst
                    p.emit("pe", mm_qk, reads=["lnxfm", f"wqk{wsl}"],
                           writes=[f"pb{2 * half}", f"pb{2 * half + 1}"])
                for ts in range(4):
                    p.emit("act", lambda eng, f=f, ts=ts: eng.activation(
                        qkt[ts][:], bank_f(ts), AF.Identity, bias=qkvb_s[:, f:f + 1], scale=1.0),
                        reads=[f"pb{ts}", "qkvb_s"], writes=[f"qkt{ts}"])
                    g_t = slab * 4 + ts
                    if f < 8:
                        dma("sp", q_dram[(g_t % 2) * 1024 + f * 128:(g_t % 2) * 1024 + (f + 1) * 128,
                                         (g_t // 2) * 512:(g_t // 2 + 1) * 512],
                            qkt[ts][:], "qkst", reads=[f"qkt{ts}"], writes=["qdram"])
                    else:
                        dma("sp", k_dram[(f - 8) * 128:(f - 7) * 128,
                                         base + ts * 512: base + (ts + 1) * 512],
                            qkt[ts][:], "qkst", reads=[f"qkt{ts}"], writes=["kdram"])
            for vh in range(2):
                dma("pool", vw[:, :, :], qkvw[:, :, 2048 + vh * 512: 2048 + (vh + 1) * 512],
                    "vw", writes=["vw"])
                for t in range(16):
                    bk = 4 + t % 2
                    sl = t % 2

                    def mm_v2(eng, t=t, bk=bk):
                        for dc in range(16):
                            inst = eng.matmul(bank_f(bk),
                                              lnx_fm[:, dc, t * 128:(t + 1) * 128],
                                              vw[:, dc, :],
                                              start=(dc == 0), stop=(dc == 15))
                        return inst
                    p.emit("pe", mm_v2, reads=["lnxfm", "vw"], writes=[f"pb{bk}"])
                    p.emit("dve", lambda eng, sl=sl, bk=bk, vh=vh: eng.tensor_tensor(
                        vt[sl][:], bank_f(bk), vbB[:, vh * 512:(vh + 1) * 512], op=ALU.add),
                        reads=[f"pb{bk}", "vbB"], writes=[f"vt{sl}"])
                    dma("sp", v_dram[base + t * 128: base + (t + 1) * 128,
                                     vh * 512:(vh + 1) * 512], vt[sl][:],
                        "vst", reads=[f"vt{sl}"], writes=["vdram"])

        p.barrier()

        dma("pool", diagA_s[:], diagA2, None, writes=["diagA_s"])
        dma("pool", diagB_s[:], diagB2, None, writes=["diagB_s"])
        dma("sp", sidxq[:], idxq, None, writes=["sidxq"])
        dma("sp", sidxst[:], idxst, None, writes=["sidxst"])
        dma("sp", sidxld[:], idxld, None, writes=["sidxld"])
        p.emit("dve", lambda eng: eng.memset(V_sb[:, :, 128:129], 1.0), writes=["Vsb"])

        WMARG = 320.0
        for s in range(NS):
            steep = (s <= 3)
            wdist = int(WMARG / float(SLOPES[2 * s + 1]))
            dma("sp", K_sb, k_dram[s * 128:(s + 1) * 128, :], "kqv",
                reads=["kdram"], writes=["Ksb"])

            def qload(eng, s=s):
                return eng.indirect_dma_start(
                    out=Q_sb, out_offset=None,
                    in_=q_dram[0:128, :],
                    in_offset=IndirectOffsetOnAxis(sidxq[:, s:s + 1], 0))
            p.emit("pool", qload, reads=["qdram", "sidxq"], writes=["Qsb"], fam="kqv2")
            dma("sp", V_sb[:, :, 0:128],
                v_dram.rearrange("(c p) d -> p c d", p=128)[:, :, s * 128:(s + 1) * 128],
                "kqv", reads=["vdram"], writes=["Vsb"])

            p.emit("act", lambda eng, s=s: eng.activation(
                alibi_loc[:], jrow_s[:], AF.Identity, bias=0.0, scale=slopes_s[0:1, s:s + 1]),
                reads=["jrow_s", "slopes_s"], writes=["aloc"])
            if steep:
                p.emit("dve", lambda eng, s=s: eng.tensor_scalar(
                    negirow[:], jrow_s[:], slopes_s[0:1, s:s + 1], -1.0,
                    op0=ALU.mult, op1=ALU.mult),
                    reads=["jrow_s", "slopes_s"], writes=["negi"], hz=True)
            p.emit("dve", lambda eng, s=s: eng.tensor_scalar(
                crow[:], jrow_s[0:1, 0:64], s128_s[0:1, s:s + 1], None, op0=ALU.mult),
                reads=["jrow_s", "s128_s"], writes=["crow"], hz=True)
            p.emit("dve", lambda eng, s=s: eng.tensor_scalar(
                crowq[:], mypos_s[0:1, :], slopes_s[0:1, s:s + 1], None, op0=ALU.mult),
                reads=["mypos_s", "slopes_s"], writes=["crowq"], hz=True)
            p.emit("dve", lambda eng, s=s: eng.tensor_scalar(
                sp_col[:], pidx_s[:], slopes_s[:, s:s + 1], None, op0=ALU.mult),
                reads=["pidx_s", "slopes_s"], writes=["sp_col"], hz=True)
            p.emit("pe", lambda eng: eng.matmul(bank_f(2, 64), ones_f[:], crow[:],
                                                start=True, stop=True),
                   reads=["ones_f", "crow"], writes=["pb2"])
            p.emit("dve", lambda eng: eng.tensor_scalar(
                alibi_p[:], bank_f(2, 64), sp_col[:], None, op0=ALU.add),
                reads=["pb2", "sp_col"], writes=["alibi_p"], hz=True)
            p.emit("pe", lambda eng: eng.matmul(bank_f(2, 32), ones_f[:], crowq[:],
                                                start=True, stop=True),
                   reads=["ones_f", "crowq", "alibi_p"], writes=["pb2"])
            p.emit("dve", lambda eng: eng.tensor_scalar(
                alibi_pq[:], bank_f(2, 32), sp_col[:], None, op0=ALU.add),
                reads=["pb2", "sp_col"], writes=["alibi_pq"], hz=True)
            if steep:
                p.emit("dve", lambda eng, s=s: eng.tensor_scalar(
                    adjrow[:], jrow_s[0:1, 0:16], s512_s[0:1, s:s + 1], None, op0=ALU.mult),
                    reads=["jrow_s", "s512_s"], writes=["adjrow"], hz=True)
                p.emit("dve", lambda eng, s=s: eng.tensor_scalar(
                    adjrow2[:], tilepos_s[0:1, :], slopes_s[0:1, s:s + 1], None, op0=ALU.mult),
                    reads=["tilepos_s", "slopes_s"], writes=["adjrow2"], hz=True)
            else:
                p.emit("dve", lambda eng, s=s: eng.tensor_scalar(
                    adjrow[:], jrow_s[0:1, 0:16], s512_s[0:1, s:s + 1],
                    s511_s[0:1, s:s + 1], op0=ALU.mult, op1=ALU.add),
                    reads=["jrow_s", "s512_s", "s511_s"], writes=["adjrow"], hz=True)
            p.emit("pe", lambda eng: eng.matmul(bank_f(2, 16), ones_f[:], adjrow[:],
                                                start=True, stop=True),
                   reads=["ones_f", "adjrow", "alibi_pq"], writes=["pb2"])
            p.emit("dve", lambda eng: eng.tensor_copy(adjB[:], bank_f(2, 16)),
                   reads=["pb2"], writes=["adjB"], hz=True)

            for u in range(NU):
                nch = 2 * (u // 4) + 2
                pos_min = 512 * (2 * (u // 4)) + 128 * (u % 4)
                c_lo = max(0, (pos_min - wdist) // 512)
                for c in range(c_lo, nch):
                    bk = (u + c) % 4

                    def mm_s(eng, u=u, c=c, bk=bk, steep=steep):
                        if steep:
                            eng.matmul(bank_f(bk), Q_sb[:, u * 128:(u + 1) * 128],
                                       K_sb[:, c * 512:(c + 1) * 512], start=True, stop=False)
                            return eng.matmul(bank_f(bk), ones_bf[:], alibi_loc[:],
                                              start=False, stop=True)
                        return eng.matmul(bank_f(bk), Q_sb[:, u * 128:(u + 1) * 128],
                                          K_sb[:, c * 512:(c + 1) * 512], start=True, stop=True)
                    p.emit("pe", mm_s, reads=["Qsb", "Ksb", "aloc", "ones_bf"],
                           writes=[f"pb{bk}"])
                    if c >= nch - 2:
                        mi = (u % 4) * 2 + (c - (nch - 2))
                        p.emit("dve", lambda eng, mi=mi, bk=bk: eng.tensor_tensor(
                            bank_f(bk), bank_f(bk), diagA_s[:, mi, :], op=ALU.add),
                            reads=["diagA_s"], writes=[f"pb{bk}"])
                    p.emit("dve", lambda eng, c=c, bk=bk: eng.reduce_max(
                        mhat2[:, c:c + 1], bank_f(bk), axis=AX.X),
                        reads=[f"pb{bk}"], writes=[f"mh{c}"], hz=True)
                p.emit("dve", lambda eng, nch=nch, c_lo=c_lo: eng.tensor_tensor(
                    mhat2[:, c_lo:nch], mhat2[:, c_lo:nch], adjB[:, c_lo:nch], op=ALU.add),
                    reads=["adjB"] + [f"mh{c}" for c in range(c_lo, nch)],
                    writes=[f"mh{c}" for c in range(c_lo, nch)], hz=True)
                p.emit("dve", lambda eng, u=u, nch=nch, c_lo=c_lo: eng.reduce_max(
                    mhat[:, u:u + 1], mhat2[:, c_lo:nch], axis=AX.X),
                    reads=[f"mh{c}" for c in range(c_lo, nch)], writes=["mhat"], hz=True)
                if steep:
                    p.emit("dve", lambda eng, u=u: eng.tensor_tensor(
                        mhat[:, u:u + 1], mhat[:, u:u + 1], alibi_pq[:, u:u + 1],
                        op=ALU.subtract),
                        reads=["alibi_pq", "mhat"], writes=["mhat"], hz=True)

            p.emit("pe", lambda eng: eng.transpose(PS[0:32, 1024:1152], mhat[:, 0:32], ident_f[:]),
                   reads=["mhat", "ident_f"], writes=["pb2"])
            p.emit("dve", lambda eng: eng.reduce_max(mcol[:], PS[0:32, 1024:1152], axis=AX.X),
                   reads=["pb2"], writes=["mcol"], hz=True)
            p.emit("pe", lambda eng: eng.transpose(PS[0:1, 1536:1568], mcol[0:32, 0:1],
                                                   ident_f[0:32, 0:32]),
                   reads=["mcol", "ident_f"], writes=["pb3"])
            p.emit("dve", lambda eng: eng.tensor_reduce(
                drow[:], PS[0:1, 1536:1568].rearrange("p (a b) -> p a b", a=8),
                axis=AX.X, op=ALU.max),
                reads=["pb3"], writes=["drow"], hz=True)
            if steep:
                p.emit("dve", lambda eng: eng.tensor_tensor(
                    drow[:], drow[:], adjrow2[0:1, 0:8], op=ALU.add),
                    reads=["drow", "adjrow2"], writes=["drow"], hz=True)
            p.emit("dve", lambda eng: eng.tensor_scalar(drow[:], drow[:], -1.0, None, op0=ALU.mult),
                   reads=["drow"], writes=["drow"], hz=True)
            p.emit("pe", lambda eng: eng.matmul(bank_f(2, 8), ones_f[:], drow[:],
                                                start=True, stop=True),
                   reads=["ones_f", "drow"], writes=["pb2"])
            p.emit("dve", lambda eng: eng.tensor_copy(negD[:], bank_f(2, 8)),
                   reads=["pb2"], writes=["negD"], hz=True)

            if debug and s == 0:
                dma("sp", d_qsb, Q_sb, None, reads=["Qsb"])
                dma("sp", d_ksb, K_sb, None, reads=["Ksb"])
                dma("sp", d_mhat, mhat[:], None, reads=["mhat"])
                dma("sp", d_negD, negD[:], None, reads=["negD"])
                dma("sp", d_alip, alibi_p[:], None, reads=["alibi_p"])
                dma("sp", d_diagB, diag_s[:, :, :] if False else diagB_s.rearrange("p a b -> p (a b)"),
                    None, reads=["diagB_s"])
            for t in range(NTt):
                nkc = 8 * t + 8
                kc_lo = max(0, (1024 * t - wdist) // 128)
                p.emit("dve", lambda eng, t=t: eng.tensor_scalar(
                    biasT[:], alibi_p[:], negD[:, t:t + 1], None, op0=ALU.add),
                    reads=["alibi_p", "negD"], writes=["biasT"], hz=True)
                for kc in range(kc_lo, nkc):
                    bk = kc % 4

                    def mm_st(eng, t=t, kc=kc, bk=bk, steep=steep):
                        if steep:
                            eng.matmul(bank_f(bk), K_sb[:, kc * 128:(kc + 1) * 128],
                                       Q_sb[:, t * 512:(t + 1) * 512], start=True, stop=False)
                            return eng.matmul(bank_f(bk), ones_bf[:], negirow[:],
                                              start=False, stop=True)
                        return eng.matmul(bank_f(bk), K_sb[:, kc * 128:(kc + 1) * 128],
                                          Q_sb[:, t * 512:(t + 1) * 512], start=True, stop=True)
                    p.emit("pe", mm_st, reads=["Ksb", "Qsb", "negi", "ones_bf"],
                           writes=[f"pb{bk}"])
                    if kc >= 8 * t:
                        p.emit("dve", lambda eng, t=t, kc=kc, bk=bk: eng.tensor_tensor(
                            bank_f(bk), bank_f(bk), diagB_s[:, kc - 8 * t, :], op=ALU.add),
                            reads=["diagB_s"], writes=[f"pb{bk}"])
                    psl = kc % 2
                    p.emit("act", lambda eng, bk=bk, kc=kc, psl=psl: eng.activation(
                        pt[psl][:], bank_f(bk), AF.Exp, bias=biasT[:, kc:kc + 1], scale=1.0),
                        reads=[f"pb{bk}", "biasT"], writes=[f"pt{psl}"])

                    if debug and s == 0 and t == 0 and kc == 0:
                        dma("sp", d_pt, pt[psl][:], None, reads=[f"pt{psl}"])
                        dma("sp", d_bias, biasb[bsl][:], None, reads=[f"bias{bsl}"])

                    def pv(eng, kc=kc, psl=psl, nkc=nkc, kc_lo=kc_lo):
                        for qq in range(4):
                            inst = eng.matmul(PS[:, (4 + qq) * 512:(4 + qq) * 512 + 129],
                                              pt[psl][:, qq * 128:(qq + 1) * 128],
                                              V_sb[:, kc, :],
                                              start=(kc == kc_lo), stop=(kc == nkc - 1))
                        return inst
                    p.emit("pe", pv, reads=[f"pt{psl}", "Vsb"],
                           writes=["pb4", "pb5", "pb6", "pb7"])
                zcols = PS[:, 2048:4096].rearrange("p (q x) -> p q x", q=4)[:, :, 128:129]
                p.emit("dve", lambda eng, zcols=zcols: eng.tensor_scalar(
                    zr4[:, :, None], zcols, 1e-37, None, op0=ALU.max),
                    reads=["pb4", "pb5", "pb6", "pb7"], writes=["zr4"], hz=True)
                p.emit("dve", lambda eng: eng.reciprocal(zr4[:], zr4[:]),
                       reads=["zr4"], writes=["zr4"], hz=True)
                for qq in range(4):
                    p.emit("dve", lambda eng, qq=qq: eng.tensor_scalar(
                        ctxbf[qq][:], PS[:, (4 + qq) * 512:(4 + qq) * 512 + 128],
                        zr4[:, qq:qq + 1], None, op0=ALU.mult),
                        reads=[f"pb{4 + qq}", "zr4"], writes=[f"ctxbf{qq}"])

                    def ctxstore(eng, s=s, t=t, qq=qq):
                        col = s * 32 + t * 4 + qq
                        return eng.indirect_dma_start(
                            out=ctxsh[0:128, :],
                            out_offset=IndirectOffsetOnAxis(sidxst[:, col:col + 1], 0),
                            in_=ctxbf[qq][:], in_offset=None)
                    p.emit("pool", ctxstore, reads=[f"ctxbf{qq}", "sidxst"], writes=["ctxsh"], fam="ctxst")

        for i in range(4):
            dma("pool", outw_sb[:, 4 * i:4 * (i + 1), :], outw[:, 4 * i:4 * (i + 1), :],
                "wout", writes=["Ksb", "Qsb", "Vsb", f"OW{i}"])
        p.barrier()
        p.mark_split()

        for i in range(4):
            dma("sp", outbrow_s[:], outbrow[0:1, i * 512:(i + 1) * 512], None,
                reads=["outbrow_done"], writes=["outbrow_s"])
            p.emit("pe", lambda eng: eng.matmul(bank_f(2), ones_f[:], outbrow_s[:],
                                                start=True, stop=True),
                   reads=["ones_f", "outbrow_s"], writes=["pb2"])
            p.emit("dve", lambda eng, i=i: eng.tensor_copy(outbB[:, i * 512:(i + 1) * 512],
                                                           bank_f(2)),
                   reads=["pb2", "outbrow_s"], writes=["outbB", "outbrow_done"], hz=True)

        for t in range(16):
            sl = t % 2
            for h in range(16):
                def ctld(eng, t=t, h=h, sl=sl):
                    col = t * 16 + h
                    return eng.indirect_dma_start(
                        out=ctt[sl][:, h, :], out_offset=None,
                        in_=ctxsh[0:128, :],
                        in_offset=IndirectOffsetOnAxis(sidxld[:, col:col + 1], 0))
                p.emit("pool", ctld, reads=["ctxsh", "sidxld"], writes=[f"ctt{sl}"], fam="ctld")
            for hd in range(16):
                bk = hd % 2
                p.emit("pe", lambda eng, hd=hd, bk=bk, sl=sl: eng.transpose(
                    bank_bf(bk), ctt[sl][:, hd, :], ident_bf[:]),
                    reads=[f"ctt{sl}", "ident_bf"], writes=[f"pb{bk}"])
                p.emit("dve", lambda eng, hd=hd, bk=bk, sl=sl: eng.tensor_copy(
                    ctf[sl][:, hd, :], bank_bf(bk)),
                    reads=[f"pb{bk}"], writes=[f"ctt{sl}"])

            def mm_out(eng, sl=sl):
                for hd in range(16):
                    for n in range(4):
                        inst = eng.matmul(PS[:, (4 + n) * 512:(5 + n) * 512],
                                          ctf[sl][:, hd, :],
                                          outw_sb[:, hd, n * 512:(n + 1) * 512],
                                          start=(hd == 0), stop=(hd == 15))
                return inst
            p.emit("pe", mm_out, reads=[f"ctt{sl}", "OW0", "OW1", "OW2", "OW3"],
                   writes=["pb4", "pb5", "pb6", "pb7"])
            dma("sp", xt[sl][:], x_own[t * 128:(t + 1) * 128, :], "xo", writes=[f"xt{sl}"])
            p.emit("dve", lambda eng, sl=sl: eng.tensor_tensor(
                x1t[sl][:], PS[:, 2048:4096], outbB[:], op=ALU.add),
                reads=["pb4", "pb5", "pb6", "pb7", "outbB"], writes=[f"x1t{sl}"])
            p.emit("dve", lambda eng, sl=sl: eng.tensor_tensor(
                x1t[sl][:], x1t[sl][:], xt[sl][:], op=ALU.add),
                reads=[f"xt{sl}", f"x1t{sl}"], writes=[f"x1t{sl}"])
            dma("sp", out[t * 128:(t + 1) * 128, :], x1t[sl][:], "x1st",
                reads=[f"x1t{sl}"],
                writes=[f"y{t // 8}_{t % 8}_0", f"y{t // 8}_{t % 8}_1"])
            ln_tile(x1t[sl][:], f"x1t{sl}", lnt[sl][:], f"lnt{sl}", sl)
            transpose16(lnt[sl][:], f"lnt{sl}", lnf[sl], f"lnf{sl}", slice(0, 128))
            dma("sp", lnx2_dram[:, :, t * 128:(t + 1) * 128], lnf[sl][:, :, :], "ln2st",
                reads=[f"lnf{sl}"], writes=["ln2dram"])

        p.barrier()

        NSUB = 8
        for b2 in range(2):
            tb = b2 * 1024
            dma("pool", LNX2[:, :, :], lnx2_dram[:, :, tb:tb + 1024], "ln2ld",
                reads=["ln2dram"], writes=["lnx2b"])
            for tt in range(8):
                sl = tt % 2

                def mm_gate(eng, tt=tt):
                    for dc in range(16):
                        inst = eng.matmul(bank_f(6, 16), LNX2[:, dc, tt * 128:(tt + 1) * 128],
                                          gate_s[:, dc, :], start=(dc == 0), stop=(dc == 15))
                    return inst
                p.emit("pe", mm_gate, reads=["lnx2b", "gate_s"], writes=["pb6"])
                p.emit("act", lambda eng, sl=sl: eng.activation(
                    sco[sl][:], bank_f(6, 16), AF.Sigmoid, bias=0.0, scale=1.0),
                    reads=["pb6"], writes=[f"sco{sl}"])

                sc = sco[sl]
                gmax = rscr[:, 0:4]
                m1 = rscr[:, 4:5]
                mk1 = rscr[:, 5:9]
                g2 = rscr[:, 9:13]
                m2 = rscr[:, 13:14]
                gmask = rscr[:, 14:18]
                masked = rscr[:, 18:34]
                ssum = rscr[:, 35:36]
                RT = ["rt"]
                p.emit("dve", lambda eng, sc=sc: eng.tensor_reduce(
                    gmax, sc.rearrange("p (g m) -> p g m", g=4), axis=AX.X, op=ALU.max),
                    reads=[f"sco{sl}"], writes=RT, hz=True)
                p.emit("dve", lambda eng: eng.reduce_max(m1, gmax, axis=AX.X),
                       reads=RT, writes=RT, hz=True)
                p.emit("dve", lambda eng: eng.tensor_scalar(mk1, gmax, m1, None, op0=ALU.is_ge),
                       reads=RT, writes=RT, hz=True)
                p.emit("dve", lambda eng: eng.scalar_tensor_tensor(
                    g2, mk1, -1e30, gmax, op0=ALU.mult, op1=ALU.add),
                    reads=RT, writes=RT, hz=True)
                p.emit("dve", lambda eng: eng.reduce_max(m2, g2, axis=AX.X),
                       reads=RT, writes=RT, hz=True)
                p.emit("dve", lambda eng: eng.tensor_scalar(gmask, gmax, m2, None, op0=ALU.is_ge),
                       reads=RT, writes=RT, hz=True)

                def maskmul(eng, sc=sc):
                    for g4 in range(4):
                        inst = eng.tensor_scalar(masked[:, g4 * 4:(g4 + 1) * 4],
                                                 sc[:, g4 * 4:(g4 + 1) * 4],
                                                 gmask[:, g4:g4 + 1], None, op0=ALU.mult)
                    return inst
                p.emit("dve", maskmul, reads=RT + [f"sco{sl}"], writes=RT, hz=True)
                p.emit("dve", lambda eng: eng.max(out=top8b[:], in_=masked),
                       reads=RT, writes=RT + ["top8b"], hz=True)
                p.emit("dve", lambda eng: eng.tensor_scalar(
                    masked, masked, top8b[:, 3:4], None, op0=ALU.is_ge),
                    reads=RT + ["top8b"], writes=RT, hz=True)
                p.emit("dve", lambda eng, sc=sc: eng.tensor_tensor(masked, sc, masked, op=ALU.mult),
                       reads=RT + [f"sco{sl}"], writes=RT, hz=True)
                p.emit("dve", lambda eng: eng.reduce_sum(ssum, masked, axis=AX.X),
                       reads=RT, writes=RT, hz=True)
                p.emit("dve", lambda eng: eng.reciprocal(ssum, ssum),
                       reads=RT, writes=RT, hz=True)
                p.emit("dve", lambda eng: eng.tensor_scalar(masked, masked, ssum, None, op0=ALU.mult),
                       reads=RT, writes=RT, hz=True)
                p.emit("pe", lambda eng: eng.transpose(PS[0:16, 3584:3712], rscr[:, 18:34],
                                                       ident_f[:]),
                       reads=["rt", "ident_f"], writes=["pb7"])
                p.emit("dve", lambda eng, tt=tt: eng.tensor_copy(
                    combf[0:16, tt * 128:(tt + 1) * 128], PS[0:16, 3584:3712]),
                    reads=["pb7"], writes=["combf"], hz=True)
            dma("sp", comb_dram[:, tb:tb + 1024], combf[:], "combst",
                reads=["combf"], writes=["combdram"])

            for e in range(NE + 1):
                shared = (e == NE)
                nIc = NSI if shared else NI
                w1src = ws1 if shared else w1[e]
                w3src = ws3 if shared else w3[e]
                if not shared:
                    dma("pool", combrow[:], comb_dram[e:e + 1, tb:tb + 1024], "combld",
                        reads=["combdram"], writes=["combrow"])
                    for i in range(2):
                        p.emit("pe", lambda eng, i=i: eng.matmul(
                            bank_f(6), ones_f[:],
                            combrow[0:1, i * 512:(i + 1) * 512], start=True, stop=True),
                            reads=["combrow", "ones_f"], writes=["pb6"])
                        p.emit("dve", lambda eng, i=i: eng.tensor_copy(
                            combB[:, i * 512:(i + 1) * 512], bank_f(6)),
                            reads=["pb6"], writes=["combB"], hz=True)
                for ic in range(nIc):
                    wsl = ic % 2
                    dma("pool", w1b[wsl][:, :, :], w1src[:, :, ic * 128:(ic + 1) * 128],
                        "w1", writes=[f"w1b{wsl}"])
                    dma("pool", w3b[wsl][:, :, :], w3src[:, :, ic * 128:(ic + 1) * 128],
                        "w3", writes=[f"w3b{wsl}"])

                    def mm_h(eng, wb, bko, wkey):
                        for dc in range(16):
                            for ns in range(2):
                                inst = eng.matmul(bank_f(bko + ns),
                                                  wb[:, dc, :],
                                                  LNX2[:, dc, ns * 512:(ns + 1) * 512],
                                                  start=(dc == 0), stop=(dc == 15))
                        return inst
                    p.emit("pe", functools.partial(mm_h, wb=w1b[wsl], bko=0, wkey=None),
                           reads=[f"w1b{wsl}", "lnx2b"], writes=["pb0", "pb1"])
                    p.emit("pe", functools.partial(mm_h, wb=w3b[wsl], bko=2, wkey=None),
                           reads=[f"w3b{wsl}", "lnx2b"], writes=["pb2", "pb3"])
                    ssl = ic % 2
                    p.emit("act", lambda eng, ssl=ssl: eng.activation(
                        s1b[ssl][:], PS[:, 0:1024], AF.Silu, bias=0.0, scale=1.0),
                        reads=["pb0", "pb1"], writes=[f"s1b{ssl}"])
                    if shared:
                        p.emit("dve", lambda eng, ic=ic, ssl=ssl: eng.tensor_tensor(
                            g_sb[:, ic, :], PS[:, 1024:2048], s1b[ssl][:], op=ALU.mult),
                            reads=["pb2", "pb3", f"s1b{ssl}"], writes=["g_sb"])
                    else:
                        p.emit("dve", lambda eng, ssl=ssl: eng.tensor_tensor(
                            t3f[ssl][:], PS[:, 1024:2048], combB[:], op=ALU.mult),
                            reads=["pb2", "pb3", "combB"], writes=[f"t3f{ssl}"])
                        p.emit("dve", lambda eng, ic=ic, ssl=ssl: eng.tensor_tensor(
                            g_sb[:, ic, :], t3f[ssl][:], s1b[ssl][:], op=ALU.mult),
                            reads=[f"t3f{ssl}", f"s1b{ssl}"], writes=["g_sb"])
                nH = 4 if shared else 2
                wdt = 512 if shared else 1024
                w2srcv = ws2 if shared else w2[e]
                w2bv = w2b_s if shared else w2b_r
                for hf in range(nH):
                    wsl = hf % 2
                    dma("pool", w2bv[wsl][:, :, :],
                        w2srcv.rearrange("a p j -> p a j")[:, :, hf * wdt:(hf + 1) * wdt],
                        "w2", writes=[f"w2b{wsl}"])
                    for sb_ in range(NSUB):
                        nbk = wdt // 512
                        ybk = 4 + (sb_ % 2) * nbk

                        def mm_y(eng, hf=hf, sb_=sb_, wsl=wsl, ybk=ybk, nIc=nIc, nbk=nbk,
                                 w2bv=w2bv):
                            for ic in range(nIc):
                                for n2 in range(nbk):
                                    inst = eng.matmul(
                                        bank_f(ybk + n2),
                                        g_sb[:, ic, sb_ * 128:(sb_ + 1) * 128],
                                        w2bv[wsl][:, ic, n2 * 512:(n2 + 1) * 512],
                                        start=(ic == 0), stop=(ic == nIc - 1))
                            return inst
                        ykeys = [f"pb{ybk + n2}" for n2 in range(nbk)]
                        p.emit("pe", mm_y, reads=["g_sb", f"w2b{wsl}"], writes=ykeys)
                        ysl = sb_ % 2
                        p.emit("dve", lambda eng, ybk=ybk, ysl=ysl, wdt=wdt: eng.tensor_copy(
                            ytile[ysl][:, 0:wdt], PS[:, ybk * 512: ybk * 512 + wdt]),
                            reads=ykeys, writes=[f"ytile{ysl}"])
                        colo = hf * wdt
                        dma("pool",
                            out[tb + sb_ * 128: tb + (sb_ + 1) * 128, colo:colo + wdt],
                            ytile[ysl][:, 0:wdt], "yacc",
                            reads=[f"ytile{ysl}"],
                            writes=[f"y{b2}_{sb_}_{colo // 1024}"], accum=True)

        if debug:
            for nm in dbg:
                dst, src = dbg[nm]
                dma("pool", dst, src, None,
                    reads=["ctxsh", "ln2dram", "qdram", "kdram", "vdram"])
            dma("pool", d_comb, comb_dram, None, reads=["combdram"])

        p.barrier()
        p.finalize_and_emit(nc, ctx)
    return nc


def _images(out_w, gate_w, W1, W2, W3, Ws1, Ws2, Ws3):
    f32 = np.float32
    imgs = {}
    imgs["outw"] = np.ascontiguousarray(out_w.reshape(16, 128, D).transpose(1, 0, 2))
    gt = gate_w.T
    imgs["gatew"] = np.ascontiguousarray(gt.reshape(16, 128, 16).transpose(1, 0, 2))
    for nm, W in (("w1", W1), ("w3", W3)):
        wt = W.transpose(0, 2, 1)
        imgs[nm] = np.ascontiguousarray(wt.reshape(NE, 16, 128, I_FF).transpose(0, 2, 1, 3))
    w2t = W2.transpose(0, 2, 1)
    imgs["w2"] = np.ascontiguousarray(w2t.reshape(NE, NI, 128, D))
    for nm, W in (("ws1", Ws1), ("ws3", Ws3)):
        wt = W.T
        imgs[nm] = np.ascontiguousarray(wt.reshape(16, 128, SH).transpose(1, 0, 2))
    imgs["ws2"] = np.ascontiguousarray(Ws2.T.reshape(NSI, 128, D))
    return {k: np.asarray(v, dtype=f32) for k, v in imgs.items()}


def _consts():
    c = {}
    c["identw"] = np.eye(128, dtype=np.float32)
    c["onesrow"] = np.ones((1, 128), dtype=np.float32)
    c["jrow"] = np.arange(512, dtype=np.float32)[None, :]
    c["p_idx"] = np.arange(128, dtype=np.float32)[:, None]
    dm = np.zeros((128, 4, 512), dtype=np.float32)
    jj = np.arange(128)[:, None]
    ii = np.arange(512)[None, :]
    for kk in range(4):
        dm[:, kk, :] = np.where(128 * kk + jj > ii, -30000.0, 0.0)
    c["diagm"] = dm
    dma_ = np.zeros((128, 4, 512), dtype=np.float32)
    qq_ = np.arange(128)[:, None]
    kl_ = np.arange(512)[None, :]
    for uu in range(4):
        dma_[:, uu, :] = np.where(kl_ > 128 * uu + qq_, -30000.0, 0.0)
    c["diagmA"] = dma_
    return c


def _fused_maps(x, qkv_w, qkv_b, out_b, imgs, consts):
    f32 = np.float32
    dmA = consts["diagmA"]
    dmB = consts["diagm"]
    maps = []
    for c in range(8):
        b, P, m = c // 4, (c % 4) // 2, c % 2
        heads = [2 * s_ + m for s_ in range(8)]
        cols = []
        for part in range(3):
            for h in heads:
                cols.append(qkv_w[:, part * 2048 + h * 128: part * 2048 + (h + 1) * 128])
        qw = np.concatenate(cols, axis=1)
        qkvw_img = np.ascontiguousarray(qw.reshape(16, 128, 3072).transpose(1, 0, 2))
        bcols = []
        for part in range(2):
            for h in heads:
                bcols.append(qkv_b[part * 2048 + h * 128: part * 2048 + (h + 1) * 128])
        qkvb_img = np.ascontiguousarray(np.stack(bcols, axis=1))
        vb = np.concatenate([qkv_b[2 * 2048 + h * 128: 2 * 2048 + (h + 1) * 128]
                             for h in heads])[None, :]
        sl = np.array([SLOPES[h] for h in heads], f32)
        sl16 = np.concatenate([sl, np.zeros(8, f32)])
        tp = (512.0 * (2 * np.arange(8) + P)).astype(f32)[None, :]
        uu = np.arange(32)
        blk = 8 * (uu // 4) + 4 * P + (uu % 4)
        myp = (128.0 * blk).astype(f32)[None, :]
        dA = np.zeros((128, 8, 512), f32)
        dB = np.zeros((128, 8, 512), f32)
        for u4 in range(4):
            if P == 0:
                dA[:, u4 * 2 + 0] = dmA[:, u4]
                dA[:, u4 * 2 + 1] = -30000.0
            else:
                dA[:, u4 * 2 + 0] = 0.0
                dA[:, u4 * 2 + 1] = dmA[:, u4]
        for kcr in range(8):
            if P == 0:
                dB[:, kcr] = dmB[:, kcr] if kcr < 4 else -30000.0
            else:
                dB[:, kcr] = 0.0 if kcr < 4 else dmB[:, kcr - 4]
        xo = np.concatenate([x[b, (2 * lt + P) * 512:(2 * lt + P) * 512 + 512]
                             for lt in range(4 * m, 4 * m + 4)], axis=0)
        pp = np.arange(128, dtype=np.int64)
        iq = np.empty((128, 8), np.int32)
        for s_ in range(8):
            iq[:, s_] = P * 1024 + s_ * 128 + pp
        ist = np.empty((128, 256), np.int32)
        for s_ in range(8):
            for t_ in range(8):
                for qq_ in range(4):
                    ist[:, s_ * 32 + t_ * 4 + qq_] = (m * 32768 + s_ * 4096
                                                      + t_ * 512 + qq_ * 128 + pp)
        ild = np.empty((128, 256), np.int32)
        for t_ in range(16):
            for h_ in range(16):
                ild[:, t_ * 16 + h_] = ((h_ % 2) * 32768 + (h_ // 2) * 4096
                                        + m * 2048 + t_ * 128 + pp)
        mp = dict(imgs)
        mp.update(dict(
            identw=consts["identw"], onesrow=consts["onesrow"],
            jrow=consts["jrow"], p_idx=consts["p_idx"],
            xb=x[b], x_own=np.ascontiguousarray(xo),
            qkvw=qkvw_img, qkvb=qkvb_img, vbrow=np.ascontiguousarray(vb),
            outbrow=out_b[None, :].copy(),
            slopes=np.tile(sl[None, :], (128, 1)),
            s128=(128.0 * sl)[None, :].copy(),
            s512=(512.0 * sl16)[None, :].copy(), s511=(511.0 * sl16)[None, :].copy(),
            tilepos=tp, mypos128=myp, diagA2=dA, diagB2=dB,
            idxq=iq, idxst=ist, idxld=ild,
        ))
        maps.append(mp)
    return maps


def _fused_assemble(results):
    f32 = np.float32
    outp = np.empty((B, S, D), f32)
    for c in range(8):
        b, P, m = c // 4, (c % 4) // 2, c % 2
        sh = np.asarray(results[c]["out"], f32)
        for i, lt in enumerate(range(4 * m, 4 * m + 4)):
            g = 2 * lt + P
            outp[b, g * 512:(g + 1) * 512] = sh[i * 512:(i + 1) * 512]
    return outp


def kernel(L0_attn_x, L0_attn_qkv_w, L0_attn_qkv_b, L0_attn_out_w, L0_attn_out_b,
           L0_ffn_gate_w, L0_ffn_W1, L0_ffn_W2, L0_ffn_W3,
           L0_ffn_Ws1, L0_ffn_Ws2, L0_ffn_Ws3, _debug=False, _ncs=None):
    f32 = np.float32
    x = np.ascontiguousarray(np.asarray(L0_attn_x, f32))
    qkv_w = np.asarray(L0_attn_qkv_w, f32)
    qkv_b = np.asarray(L0_attn_qkv_b, f32)
    out_w = np.asarray(L0_attn_out_w, f32)
    out_b = np.asarray(L0_attn_out_b, f32)
    gate_w = np.asarray(L0_ffn_gate_w, f32)
    W1 = np.asarray(L0_ffn_W1, f32)
    W2 = np.asarray(L0_ffn_W2, f32)
    W3 = np.asarray(L0_ffn_W3, f32)
    Ws1 = np.asarray(L0_ffn_Ws1, f32)
    Ws2 = np.asarray(L0_ffn_Ws2, f32)
    Ws3 = np.asarray(L0_ffn_Ws3, f32)

    imgs = _images(out_w, gate_w, W1, W2, W3, Ws1, Ws2, Ws3)
    consts = _consts()
    nc = _ncs if _ncs is not None else build_fused(debug=_debug)
    maps = _fused_maps(x, qkv_w, qkv_b, out_b, imgs, consts)
    res = run_bass_kernel_spmd(nc, maps, core_ids=list(range(8)))
    outp = _fused_assemble(res.results)
    if _debug:
        return outp, res
    return outp


# version 09 (final)
# speedup vs baseline: 1.1144x; 1.0107x over previous
import functools
import numpy as np
from contextlib import ExitStack

import concourse.bass as bass
import concourse.mybir as mybir
from concourse.bass import IndirectOffsetOnAxis
from concourse.bass_utils import run_bass_kernel_spmd
from concourse.aot_env import allow_ici_all_to_all

F32 = mybir.dt.float32
I32 = mybir.dt.int32
BF16 = mybir.dt.bfloat16
AF = mybir.ActivationFunctionType
ALU = mybir.AluOpType
AX = mybir.AxisListType

B = 2
S = 8192
D = 2048
NH = 16
DH = 128
NSLOT = 4
TCORE = 2048
NE = 16
I_FF = 1408
NI = 11
SH = 2816
NSI = 22
EPS = 1e-5
SLOPES = [np.float32(1.0 / 2.0 ** (i / 2.0)) for i in range(NH)]

ENGINES = ("pe", "dve", "act", "pool", "sp")
ENG_ATTR = {"pe": "tensor", "dve": "vector", "act": "scalar", "pool": "gpsimd", "sp": "sync"}


class _It:
    __slots__ = ("eng", "fn", "fam", "ford", "deps", "sig", "need", "gi", "hz")

    def __init__(self, eng, fn, fam, ford, gi, hz=False):
        self.eng, self.fn, self.fam, self.ford, self.gi = eng, fn, fam, ford, gi
        self.deps = []
        self.sig = None
        self.need = False
        self.hz = hz


class Prog:

    def __init__(self):
        self.items = []
        self.per_eng = {e: [] for e in ENGINES}
        self.res = {}
        self.fam_count = {}
        self.split_gi = None

    def mark_split(self):
        self.split_gi = len(self.items)

    def emit(self, eng, fn, reads=(), writes=(), fam=None, deps=(), hz=False):
        ford = None
        if fam is not None:
            self.fam_count[fam] = self.fam_count.get(fam, 0) + 1
            ford = self.fam_count[fam]
        it = _It(eng, fn, fam, ford, len(self.items), hz)
        lane = (eng, fam)
        dd = {}
        for k in reads:
            e = self.res.get(k)
            if e:
                for d in e["w"].values():
                    dd[d.gi] = d
        for k in writes:
            e = self.res.get(k)
            if e:
                for d in e["r"].values():
                    dd[d.gi] = d
                for d in e["w"].values():
                    dd[d.gi] = d
        for d in deps:
            dd[d.gi] = d
        dd.pop(it.gi, None)
        it.deps = list(dd.values())
        for k in reads:
            self.res.setdefault(k, {"w": {}, "r": {}})["r"][lane] = it
        for k in writes:
            e = self.res.setdefault(k, {"w": {}, "r": {}})
            e["w"] = {lane: it}
            e["r"] = {}
        self.items.append(it)
        self.per_eng[eng].append(it)
        return it

    def barrier(self):
        lasts = [self.per_eng[e][-1] for e in ENGINES if self.per_eng[e]]
        for e in ENGINES:
            self.emit(e, lambda eng: eng.nop(), deps=lasts)

    def finalize_and_emit(self, nc, ctx):
        for it in self.items:
            for d in it.deps:
                if d.eng == it.eng and d.fam is None and not d.hz:
                    continue
                d.need = True
        sems = {}
        for e in ENGINES:
            cnt = 0
            for it in self.per_eng[e]:
                if it.fam is None and it.need:
                    cnt += 1
                    it.sig = cnt
            if cnt:
                sems[("e", e)] = ctx.enter_context(nc.semaphore("sm_" + e))
        for f in self.fam_count:
            sems[("f", f)] = ctx.enter_context(nc.semaphore("sf_" + str(f)))

        def wait_list(it):
            w = {}
            for d in it.deps:
                if d.eng == it.eng and d.fam is None and not d.hz:
                    continue
                if d.fam is not None:
                    sk, val = ("f", d.fam), 16 * d.ford
                else:
                    sk, val = ("e", d.eng), d.sig
                if w.get(sk, 0) < val:
                    w[sk] = val
            return w

        waters = {e: {} for e in ENGINES}

        def emit_block(block, lo, hi):
            for e in ENGINES:
                items = [it for it in self.per_eng[e] if lo <= it.gi < hi]
                if not items:
                    continue

                def body(eng, items=items, ename=e):
                    water = waters[ename]
                    for it in items:
                        for sk, val in sorted(wait_list(it).items(), key=lambda kv: str(kv[0])):
                            if water.get(sk, 0) < val:
                                eng.wait_ge(sems[sk], val)
                                water[sk] = val
                        inst = it.fn(eng)
                        if it.fam is not None:
                            inst.then_inc(sems[("f", it.fam)], 16)
                        elif it.need:
                            inst.then_inc(sems[("e", ename)], 1)

                getattr(block, ENG_ATTR[e])(body)

        if self.split_gi is None:
            block = ctx.enter_context(nc.Block())
            emit_block(block, 0, len(self.items))
        else:
            with nc.Block() as b1:
                emit_block(b1, 0, self.split_gi)
            nc.all_core_barrier()
            with nc.Block() as b2:
                emit_block(b2, self.split_gi, len(self.items))


def build_nc(part="a", debug=False):
    nc = bass.Bass("TRN2", target_bir_lowering=False)
    p = Prog()

    def din(name, shape, dtype=F32):
        return nc.dram_tensor(name, list(shape), dtype, kind="ExternalInput").ap()

    identw = din("identw", [128, 128])
    onesrow = din("onesrow", [1, 128])
    dbg = {}
    if part in ("a", "p0"):
        xb = din("xb", [S, D])
        qkvw = din("qkvw", [128, 16, 12 * 128])
        qkvb = din("qkvb", [128, 12])
        vbrow = din("vbrow", [1, 512])
        jrow = din("jrow", [1, 512])
        p_idx = din("p_idx", [128, 1])
        slopes = din("slopes", [128, 4])
        s128 = din("s128", [1, 4])
        s512 = din("s512", [1, 4])
        s511 = din("s511", [1, 4])
        diagm = din("diagm", [128, 4, 512])
        diagmA = din("diagmA", [128, 4, 512])
        ctx_dram = nc.dram_tensor("ctxout", [S, NSLOT * DH], BF16, kind="ExternalOutput").ap()
        q_dram = nc.dram_tensor("q_dram", [NSLOT * DH, S], BF16, kind="Internal").ap()
        k_dram = nc.dram_tensor("k_dram", [NSLOT * DH, S], BF16, kind="Internal").ap()
        v_dram = nc.dram_tensor("v_dram", [S, NSLOT * DH], BF16, kind="Internal").ap()
        if debug:
            for nm, t in (("d_q", q_dram), ("d_k", k_dram), ("d_v", v_dram)):
                dbg[nm] = (nc.dram_tensor(nm, list(t.shape), BF16, kind="ExternalOutput").ap(), t)
    else:
        x_own = din("x_own", [TCORE, D])
        ctx_recv = din("ctx_recv", [S, NSLOT * DH], BF16)
        outw = din("outw", [128, 16, D])
        outbrow = din("outbrow", [1, D])
        gatew = din("gatew", [128, 16, 16])
        w1 = din("w1", [NE, 128, 16, I_FF])
        w3 = din("w3", [NE, 128, 16, I_FF])
        w2 = din("w2", [NE, NI, 128, D])
        ws1 = din("ws1", [128, 16, SH])
        ws3 = din("ws3", [128, 16, SH])
        ws2 = din("ws2", [NSI, 128, D])
        out = nc.dram_tensor("out", [TCORE, D], F32, kind="ExternalOutput").ap()
        lnx2_dram = nc.dram_tensor("lnx2_dram", [128, 16, TCORE], BF16, kind="Internal").ap()
        comb_dram = nc.dram_tensor("comb_dram", [16, TCORE], F32, kind="Internal").ap()
        if debug:
            dbg["d_ln2"] = (nc.dram_tensor("d_ln2", [128, 16, TCORE], BF16,
                                           kind="ExternalOutput").ap(), lnx2_dram)
            g_dump = nc.dram_tensor("d_g", [128, 22528], BF16, kind="ExternalOutput").ap()
            comb_dump = nc.dram_tensor("d_comb", [16, TCORE], F32, kind="ExternalOutput").ap()

    with ExitStack() as ctx:
        def alloc(name, shape, dtype):
            return nc.alloc_sbuf_tensor(name, list(shape), dtype).ap()

        POOLN = 90112
        POOL = nc.alloc_sbuf_tensor("POOL", [128, POOLN], mybir.dt.uint8).ap()

        def sub(off, shape, dtype):
            es = 2 if dtype == BF16 else 4
            n = int(np.prod(shape[1:])) * es
            assert off % 4 == 0 and off + n <= POOLN, (off, n)
            ap = POOL[0:shape[0], off:off + n].bitcast(dtype)
            if len(shape) == 3:
                ap = ap.rearrange("p (a b) -> p a b", a=shape[1])
            return ap

        xt = [sub(0, [128, D], F32), sub(8192, [128, D], F32)]
        sq = sub(16384, [128, D], F32)
        lnt = [sub(24576, [128, D], BF16), sub(28672, [128, D], BF16)]
        qkt = [sub(32768 + 1024 * i, [128, 512], BF16) for i in range(4)]
        vt = [sub(36864, [128, 512], BF16), sub(37888, [128, 512], BF16)]
        vbB = sub(36864, [128, 512], F32)
        qkvw_sb = sub(38912, [128, 16, 1536], BF16)
        vbrow_s = sub(88064, [1, 512], F32)
        outbrow_s = sub(88064, [1, 512], F32)
        pt = [sub(38912, [128, 512], BF16), sub(39936, [128, 512], BF16),
              sub(59392, [128, 512], BF16), sub(60416, [128, 512], BF16)]
        ctt = [sub(40960, [128, 16, 128], BF16), sub(45056, [128, 16, 128], BF16)]
        lnf = [sub(49152, [128, 16, 128], BF16), sub(53248, [128, 16, 128], BF16)]
        x1t = [sub(57344, [128, D], F32), sub(65536, [128, D], F32)]
        outbB = sub(73728, [128, D], F32)
        diag_s = sub(81920, [128, 4, 512], BF16)
        diagA_s = sub(86016, [128, 4, 512], BF16)
        ctxbf = [sub(57344 + 256 * i, [128, 128], BF16) for i in range(4)]
        negirow = sub(58368, [1, 512], BF16)
        ctf = ctt
        LNX2 = sub(0, [128, 16, 1024], BF16)
        s1b = [sub(32768, [128, 1024], BF16), sub(34816, [128, 1024], BF16)]
        t3f = [sub(36864, [128, 1024], F32), sub(40960, [128, 1024], F32)]
        ytile = [sub(45056, [128, 1024], F32), sub(49152, [128, 1024], F32)]
        combf = sub(53248, [16, 1024], F32)

        ident_bf = alloc("ident_bf", [128, 128], BF16)
        ident_f = alloc("ident_f", [128, 128], F32)
        ones_f = alloc("ones_f", [1, 128], F32)
        ones_bf = alloc("ones_bf", [1, 128], BF16)
        jrow_s = alloc("jrow_s", [1, 512], F32)
        pidx_s = alloc("pidx_s", [128, 1], F32)
        slopes_s = alloc("slopes_s", [128, 4], F32)
        s128_s = alloc("s128_s", [1, 4], F32)
        s512_s = alloc("s512_s", [1, 4], F32)
        s511_s = alloc("s511_s", [1, 4], F32)
        qkvb_s = alloc("qkvb_s", [128, 12], F32)
        gate_s = alloc("gate_s", [128, 16, 16], BF16)
        biasb = [alloc(f"biasb{i}", [128, 1], F32) for i in range(4)]
        mhat2 = alloc("mhat2", [128, 16], F32)
        zr4 = alloc("zr4", [128, 4], F32)
        stat = [alloc(f"stat{i}", [128, 8], F32) for i in range(2)]
        top8b = alloc("top8b", [128, 8], F32)
        mhat = alloc("mhat", [128, 64], F32)
        mcol = alloc("mcol", [64, 1], F32)
        drow = alloc("drow", [1, 16], F32)
        alibi_loc = alloc("alibi_loc", [1, 512], BF16)
        crow = alloc("crow", [1, 64], F32)
        adjrow = alloc("adjrow", [1, 16], F32)
        adjB = alloc("adjBb", [128, 16], F32)
        sp_col = alloc("sp_col", [128, 1], F32)
        alibi_p = alloc("alibi_p", [128, 64], F32)
        negD = alloc("negD", [128, 16], F32)
        sco = [alloc(f"sco{i}", [128, 16], F32) for i in range(2)]
        rscr = alloc("rscr", [128, 40], F32)
        combrow = alloc("combrow", [1, 1024], F32)
        sidxst = sub(32768, [128, 256], I32)
        sidxld = sub(33792, [128, 256], I32)
        sidxq = sub(34816, [128, 8], I32)

        R1 = alloc("R1", [128, 22528], BF16)
        R2 = alloc("R2", [128, 32768], BF16)

        PS = nc.alloc_psum_tensor("PS", [128, 4096], F32).ap()
        PSb = PS.bitcast(BF16)

        def bank_f(i, n=512):
            return PS[:, i * 512: i * 512 + n]

        def bank_bf(i, n=128):
            return PSb[:, i * 1024: i * 1024 + n]

        lnx_fm = R2.rearrange("p (dc t) -> p dc t", dc=16)
        K_sb = R2[:, 0:8192]
        Q_sb = R2[:, 8192:16384]
        V_sb = R2[:, 16384:16384 + 64 * 129].rearrange("p (c e) -> p c e", c=64)
        outw_sb = R2.rearrange("p (hd t) -> p hd t", hd=16)
        g_sb = R2[:, 0:22528].rearrange("p (ic t) -> p ic t", ic=22)
        w1b = [R2[:, 22528 + i * 2048: 22528 + (i + 1) * 2048].rearrange("p (dc j) -> p dc j", dc=16)
               for i in range(2)]
        w3b = [R2[:, 26624 + i * 2048: 26624 + (i + 1) * 2048].rearrange("p (dc j) -> p dc j", dc=16)
               for i in range(2)]
        combB = R2[:, 30720:32768].bitcast(F32)
        w2b_r = [R1[:, i * 11264:(i + 1) * 11264].rearrange("p (ic j) -> p ic j", ic=NI) for i in range(2)]
        w2b_s = [R1[:, i * 11264:(i + 1) * 11264].rearrange("p (ic j) -> p ic j", ic=NSI) for i in range(2)]

        _fam_ctr = [0]

        def dma(q, dst, src, fam, reads=(), writes=(), accum=False):
            if fam is None:
                _fam_ctr[0] += 1
                fam = f"u{_fam_ctr[0]}"
            if accum:
                fn = lambda eng, dst=dst, src=src: eng.dma_start(out=dst, in_=src, accum_op=ALU.add)
            else:
                fn = lambda eng, dst=dst, src=src: eng.dma_start(out=dst, in_=src)
            return p.emit(q, fn, reads=reads, writes=writes, fam=fam)

        dma("pool", ident_bf[:], identw, None, writes=["ident_bf"])
        dma("sp", ident_f[:], identw, None, writes=["ident_f"])
        dma("sp", ones_f[:], onesrow, None, writes=["ones_f"])
        dma("pool", ones_bf[:], onesrow, None, writes=["ones_bf"])
        if part in ("a", "p0"):
            dma("sp", jrow_s[:], jrow, None, writes=["jrow_s"])
            dma("sp", pidx_s[:], p_idx, None, writes=["pidx_s"])
            dma("sp", slopes_s[:], slopes, None, writes=["slopes_s"])
            dma("sp", s128_s[:], s128, None, writes=["s128_s"])
            dma("sp", s512_s[:], s512, None, writes=["s512_s"])
            dma("sp", s511_s[:], s511, None, writes=["s511_s"])
            dma("sp", qkvb_s[:], qkvb, None, writes=["qkvb_s"])
            dma("sp", vbrow_s[:], vbrow, None, writes=["vbrow_s"])
            for i in range(4):
                dma("pool", qkvw_sb[:, 4 * i:4 * (i + 1), :], qkvw[:, 4 * i:4 * (i + 1), :],
                    "wqkv", writes=[f"R1_{i}"])
            p.emit("pe", lambda eng: eng.matmul(bank_f(6), ones_f[:], vbrow_s[:],
                                                start=True, stop=True),
                   reads=["ones_f", "vbrow_s"], writes=["pb6"])
            p.emit("dve", lambda eng: eng.tensor_copy(vbB[:], bank_f(6)),
                   reads=["pb6"], writes=["vbB"], hz=True)
        else:
            dma("pool", gate_s[:], gatew, None, writes=["gate_s"])
            for i in range(4):
                dma("pool", outw_sb[:, 4 * i:4 * (i + 1), :], outw[:, 4 * i:4 * (i + 1), :],
                    "wout", writes=[f"OW{i}"])

        def ln_tile(xin, xkey, outbf, outkey, sl):
            st = stat[sl]
            s1, nm, v1, sd, rstd, mr = (st[:, i:i + 1] for i in range(6))
            kst = f"stat{sl}"
            p.emit("dve", lambda eng: eng.reduce_sum(s1, xin, axis=AX.X),
                   reads=[xkey], writes=[kst], hz=True)
            p.emit("dve", lambda eng: eng.tensor_scalar(nm, s1, -1.0 / D, None, op0=ALU.mult),
                   reads=[kst], writes=[kst], hz=True)
            p.emit("act", lambda eng: eng.activation(sq[:], xin, AF.Square, bias=nm, scale=1.0),
                   reads=[xkey, kst], writes=["sq"])
            p.emit("dve", lambda eng: eng.reduce_sum(v1, sq[:], axis=AX.X),
                   reads=["sq"], writes=[kst], hz=True)
            p.emit("dve", lambda eng: eng.tensor_scalar(v1, v1, 1.0 / D, EPS,
                                                        op0=ALU.mult, op1=ALU.add),
                   reads=[kst], writes=[kst], hz=True)
            p.emit("act", lambda eng: eng.activation(sd, v1, AF.Sqrt),
                   reads=[kst], writes=[kst], hz=True)
            p.emit("dve", lambda eng: eng.reciprocal(rstd, sd), reads=[kst], writes=[kst], hz=True)
            p.emit("dve", lambda eng: eng.tensor_tensor(mr, nm, rstd, op=ALU.mult),
                   reads=[kst], writes=[kst], hz=True)
            p.emit("dve", lambda eng: eng.tensor_scalar(outbf, xin, nm, rstd,
                                                        op0=ALU.add, op1=ALU.mult),
                   reads=[xkey, kst], writes=[outkey])

        def transpose16(src_bf, srckey, dst3, dstkey, tslice):
            for dc in range(16):
                bk = 4 + (dc % 4)
                p.emit("pe", lambda eng, dc=dc, bk=bk: eng.transpose(
                    bank_bf(bk), src_bf[:, dc * 128:(dc + 1) * 128], ident_bf[:]),
                    reads=[srckey, "ident_bf"], writes=[f"pb{bk}"])
                p.emit("dve", lambda eng, dc=dc, bk=bk: eng.tensor_copy(
                    dst3[:, dc, tslice], bank_bf(bk)),
                    reads=[f"pb{bk}"], writes=[dstkey])

        for slab in (range(4) if part in ("a", "p0") else []):
            base = slab * 2048
            for t in range(16):
                sl = t % 2
                dma("pool", xt[sl][:], xb[base + t * 128: base + (t + 1) * 128, :], "xb",
                    writes=[f"xt{sl}"])
                ln_tile(xt[sl][:], f"xt{sl}", lnt[sl][:], f"lnt{sl}", sl)
                transpose16(lnt[sl][:], f"lnt{sl}", lnx_fm, "lnxfm",
                            slice(t * 128, (t + 1) * 128))
            for f in range(8):
                for ts in range(4):
                    bk = (f * 4 + ts) % 2
                    sl = (f * 4 + ts) % 2

                    def mm_qk(eng, f=f, ts=ts, bk=bk):
                        for dc in range(16):
                            inst = eng.matmul(bank_f(bk),
                                              qkvw_sb[:, dc, f * 128:(f + 1) * 128],
                                              lnx_fm[:, dc, ts * 512:(ts + 1) * 512],
                                              start=(dc == 0), stop=(dc == 15))
                        return inst
                    p.emit("pe", mm_qk, reads=["lnxfm", "R1_0", "R1_1", "R1_2", "R1_3"],
                           writes=[f"pb{bk}"])
                    p.emit("act", lambda eng, f=f, bk=bk, sl=sl: eng.activation(
                        qkt[sl][:], bank_f(bk), AF.Identity, bias=qkvb_s[:, f:f + 1], scale=1.0),
                        reads=[f"pb{bk}", "qkvb_s"], writes=[f"qkt{sl}"])
                    dst = q_dram if f < 4 else k_dram
                    dkey = "qdram" if f < 4 else "kdram"
                    frow = (f % 4) * 128
                    dma("sp", dst[frow:frow + 128, base + ts * 512: base + (ts + 1) * 512],
                        qkt[sl][:], "qkst", reads=[f"qkt{sl}"], writes=[dkey])
            for t in range(16):
                bk = 2 + t % 2
                sl = t % 2

                def mm_v(eng, t=t, bk=bk):
                    for dc in range(16):
                        inst = eng.matmul(bank_f(bk),
                                          lnx_fm[:, dc, t * 128:(t + 1) * 128],
                                          qkvw_sb[:, dc, 1024:1536],
                                          start=(dc == 0), stop=(dc == 15))
                    return inst
                p.emit("pe", mm_v, reads=["lnxfm", "R1_0", "R1_1", "R1_2", "R1_3"],
                       writes=[f"pb{bk}"])
                p.emit("dve", lambda eng, bk=bk, sl=sl: eng.tensor_tensor(
                    vt[sl][:], bank_f(bk), vbB[:], op=ALU.add),
                    reads=[f"pb{bk}", "vbB"], writes=[f"vt{sl}"])
                dma("sp", v_dram[base + t * 128: base + (t + 1) * 128, :], vt[sl][:],
                    "vst", reads=[f"vt{sl}"], writes=["vdram"])

        if part in ("a", "p0"):
            p.barrier()

        if part == "a":
            dma("pool", diag_s[:], diagm, None, writes=["diag_s"])
            dma("pool", diagA_s[:], diagmA, None, writes=["diagA_s"])
            p.emit("dve", lambda eng: eng.memset(V_sb[:, :, 128:129], 1.0), writes=["Vsb"])

        for s in (range(NSLOT) if part == "a" else []):
            dma("sp", K_sb, k_dram[s * 128:(s + 1) * 128, :], "kqv",
                reads=["kdram"], writes=["Ksb"])
            dma("sp", Q_sb, q_dram[s * 128:(s + 1) * 128, :], "kqv",
                reads=["qdram"], writes=["Qsb"])
            dma("sp", V_sb[:, :, 0:128],
                v_dram.rearrange("(c p) d -> p c d", p=128)[:, :, s * 128:(s + 1) * 128],
                "kqv", reads=["vdram"], writes=["Vsb"])

            p.emit("act", lambda eng, s=s: eng.activation(
                alibi_loc[:], jrow_s[:], AF.Identity, bias=0.0, scale=slopes_s[0:1, s:s + 1]),
                reads=["jrow_s", "slopes_s"], writes=["aloc"])
            if s < 2:
                p.emit("dve", lambda eng, s=s: eng.tensor_scalar(
                    negirow[:], jrow_s[:], slopes_s[0:1, s:s + 1], -1.0,
                    op0=ALU.mult, op1=ALU.mult),
                    reads=["jrow_s", "slopes_s"], writes=["negi"], hz=True)
            p.emit("dve", lambda eng, s=s: eng.tensor_scalar(
                crow[:], jrow_s[0:1, 0:64], s128_s[0:1, s:s + 1], None, op0=ALU.mult),
                reads=["jrow_s", "s128_s"], writes=["crow"], hz=True)
            p.emit("dve", lambda eng, s=s: eng.tensor_scalar(
                sp_col[:], pidx_s[:], slopes_s[:, s:s + 1], None, op0=ALU.mult),
                reads=["pidx_s", "slopes_s"], writes=["sp_col"], hz=True)
            p.emit("pe", lambda eng: eng.matmul(bank_f(2, 64), ones_f[:], crow[:],
                                                start=True, stop=True),
                   reads=["ones_f", "crow"], writes=["pb2"])
            p.emit("dve", lambda eng: eng.tensor_scalar(
                alibi_p[:], bank_f(2, 64), sp_col[:], None, op0=ALU.add),
                reads=["pb2", "sp_col"], writes=["alibi_p"], hz=True)
            if s < 2:
                p.emit("dve", lambda eng, s=s: eng.tensor_scalar(
                    adjrow[:], jrow_s[0:1, 0:16], s512_s[0:1, s:s + 1], None, op0=ALU.mult),
                    reads=["jrow_s", "s512_s"], writes=["adjrow"], hz=True)
            else:
                p.emit("dve", lambda eng, s=s: eng.tensor_scalar(
                    adjrow[:], jrow_s[0:1, 0:16], s512_s[0:1, s:s + 1],
                    s511_s[0:1, s:s + 1], op0=ALU.mult, op1=ALU.add),
                    reads=["jrow_s", "s512_s", "s511_s"], writes=["adjrow"], hz=True)
            p.emit("pe", lambda eng: eng.matmul(bank_f(2, 16), ones_f[:], adjrow[:],
                                                start=True, stop=True),
                   reads=["ones_f", "adjrow"], writes=["pb2"])
            p.emit("dve", lambda eng: eng.tensor_copy(adjB[:], bank_f(2, 16)),
                   reads=["pb2"], writes=["adjB"], hz=True)

            for u in range(64):
                nch = u // 4 + 1
                for c in range(nch):
                    bk = (u + c) % 2

                    def mm_s(eng, u=u, c=c, bk=bk, s=s):
                        if s < 2:
                            eng.matmul(bank_f(bk), Q_sb[:, u * 128:(u + 1) * 128],
                                       K_sb[:, c * 512:(c + 1) * 512], start=True, stop=False)
                            return eng.matmul(bank_f(bk), ones_bf[:], alibi_loc[:],
                                              start=False, stop=True)
                        return eng.matmul(bank_f(bk), Q_sb[:, u * 128:(u + 1) * 128],
                                          K_sb[:, c * 512:(c + 1) * 512], start=True, stop=True)
                    p.emit("pe", mm_s, reads=["Qsb", "Ksb", "aloc", "ones_bf"],
                           writes=[f"pb{bk}"])
                    if s < 2 and c == nch - 1:
                        p.emit("dve", lambda eng, u=u, bk=bk: eng.tensor_tensor(
                            bank_f(bk), bank_f(bk), diagA_s[:, u % 4, :], op=ALU.add),
                            reads=["diagA_s"], writes=[f"pb{bk}"])
                    p.emit("dve", lambda eng, c=c, bk=bk: eng.reduce_max(
                        mhat2[:, c:c + 1], bank_f(bk), axis=AX.X),
                        reads=[f"pb{bk}"], writes=[f"mh{c}"], hz=True)
                p.emit("dve", lambda eng, nch=nch, c_lo=c_lo: eng.tensor_tensor(
                    mhat2[:, c_lo:nch], mhat2[:, c_lo:nch], adjB[:, c_lo:nch], op=ALU.add),
                    reads=["adjB"] + [f"mh{c}" for c in range(c_lo, nch)],
                    writes=[f"mh{c}" for c in range(c_lo, nch)], hz=True)
                p.emit("dve", lambda eng, u=u, nch=nch, c_lo=c_lo: eng.reduce_max(
                    mhat[:, u:u + 1], mhat2[:, c_lo:nch], axis=AX.X),
                    reads=[f"mh{c}" for c in range(c_lo, nch)], writes=["mhat"], hz=True)
                if s < 2:
                    p.emit("dve", lambda eng, u=u: eng.tensor_tensor(
                        mhat[:, u:u + 1], mhat[:, u:u + 1], alibi_p[:, u:u + 1],
                        op=ALU.subtract),
                        reads=["alibi_p", "mhat"], writes=["mhat"], hz=True)

            p.emit("pe", lambda eng: eng.transpose(PS[0:64, 1024:1152], mhat[:, 0:64], ident_f[:]),
                   reads=["mhat", "ident_f"], writes=["pb2"])
            p.emit("dve", lambda eng: eng.reduce_max(mcol[:], PS[0:64, 1024:1152], axis=AX.X),
                   reads=["pb2"], writes=["mcol"], hz=True)
            p.emit("pe", lambda eng: eng.transpose(PS[0:1, 1536:1600], mcol[0:64, 0:1],
                                                   ident_f[0:64, 0:64]),
                   reads=["mcol", "ident_f"], writes=["pb3"])
            p.emit("dve", lambda eng: eng.tensor_reduce(
                drow[:], PS[0:1, 1536:1600].rearrange("p (a b) -> p a b", a=16),
                axis=AX.X, op=ALU.max),
                reads=["pb3"], writes=["drow"], hz=True)
            if s < 2:
                p.emit("dve", lambda eng: eng.tensor_tensor(
                    drow[:], drow[:], adjrow[0:1, 0:16], op=ALU.add),
                    reads=["drow", "adjrow"], writes=["drow"], hz=True)
            p.emit("dve", lambda eng: eng.tensor_scalar(drow[:], drow[:], -1.0, None, op0=ALU.mult),
                   reads=["drow"], writes=["drow"], hz=True)
            p.emit("pe", lambda eng: eng.matmul(bank_f(2, 16), ones_f[:], drow[:],
                                                start=True, stop=True),
                   reads=["ones_f", "drow"], writes=["pb2"])
            p.emit("dve", lambda eng: eng.tensor_copy(negD[:], bank_f(2, 16)),
                   reads=["pb2"], writes=["negD"], hz=True)

            for t in range(16):
                nkc = 4 * t + 4
                for kc in range(nkc):
                    bk = kc % 2

                    def mm_st(eng, t=t, kc=kc, bk=bk, s=s):
                        if s < 2:
                            eng.matmul(bank_f(bk), K_sb[:, kc * 128:(kc + 1) * 128],
                                       Q_sb[:, t * 512:(t + 1) * 512], start=True, stop=False)
                            return eng.matmul(bank_f(bk), ones_bf[:], negirow[:],
                                              start=False, stop=True)
                        return eng.matmul(bank_f(bk), K_sb[:, kc * 128:(kc + 1) * 128],
                                          Q_sb[:, t * 512:(t + 1) * 512], start=True, stop=True)
                    p.emit("pe", mm_st, reads=["Ksb", "Qsb", "negi", "ones_bf"],
                           writes=[f"pb{bk}"])
                    if kc >= 4 * t:
                        p.emit("dve", lambda eng, t=t, kc=kc, bk=bk: eng.tensor_tensor(
                            bank_f(bk), bank_f(bk), diag_s[:, kc - 4 * t, :], op=ALU.add),
                            reads=["diag_s"], writes=[f"pb{bk}"])
                    psl = kc % 4
                    p.emit("act", lambda eng, bk=bk, kc=kc, psl=psl, bT=bT: eng.activation(
                        pt[psl][:], bank_f(bk), AF.Exp, bias=bT[:, kc:kc + 1], scale=1.0),
                        reads=[f"pb{bk}", f"biasT{t % 2}"], writes=[f"pt{psl}"])

                    if debug and s == 0 and t == 0 and kc == 0:
                        dma("sp", d_pt, pt[psl][:], None, reads=[f"pt{psl}"])
                        dma("sp", d_bias, bT[:, 0:1], None, reads=[f"biasT{t % 2}"])

                    def pv(eng, kc=kc, psl=psl, nkc=nkc, kc_lo=kc_lo):
                        for qq in range(4):
                            inst = eng.matmul(PS[:, (4 + qq) * 512:(4 + qq) * 512 + 129],
                                              pt[psl][:, qq * 128:(qq + 1) * 128],
                                              V_sb[:, kc, :],
                                              start=(kc == kc_lo), stop=(kc == nkc - 1))
                        return inst
                    p.emit("pe", pv, reads=[f"pt{psl}", "Vsb"],
                           writes=["pb4", "pb5", "pb6", "pb7"])
                zcols = PS[:, 2048:4096].rearrange("p (q x) -> p q x", q=4)[:, :, 128:129]
                p.emit("dve", lambda eng, zcols=zcols: eng.tensor_scalar(
                    zr4[:, :, None], zcols, 1e-37, None, op0=ALU.max),
                    reads=["pb4", "pb5", "pb6", "pb7"], writes=["zr4"], hz=True)
                p.emit("dve", lambda eng: eng.reciprocal(zr4[:], zr4[:]),
                       reads=["zr4"], writes=["zr4"], hz=True)
                for qq in range(4):
                    p.emit("dve", lambda eng, qq=qq: eng.tensor_scalar(
                        ctxbf[qq][:], PS[:, (4 + qq) * 512:(4 + qq) * 512 + 128],
                        zr4[:, qq:qq + 1], None, op0=ALU.mult),
                        reads=[f"pb{4 + qq}", "zr4"], writes=[f"ctxbf{qq}"])
                    dma("sp", ctx_dram[t * 512 + qq * 128: t * 512 + (qq + 1) * 128,
                                       s * 128:(s + 1) * 128],
                        ctxbf[qq][:], "ctxst", reads=[f"ctxbf{qq}"], writes=["ctxdram"])

        for i in (range(4) if part == "b" else []):
            dma("sp", outbrow_s[:], outbrow[0:1, i * 512:(i + 1) * 512], None,
                reads=["outbrow_done"], writes=["outbrow_s"])
            p.emit("pe", lambda eng: eng.matmul(bank_f(2), ones_f[:], outbrow_s[:],
                                                start=True, stop=True),
                   reads=["ones_f", "outbrow_s"], writes=["pb2"])
            p.emit("dve", lambda eng, i=i: eng.tensor_copy(outbB[:, i * 512:(i + 1) * 512],
                                                           bank_f(2)),
                   reads=["pb2", "outbrow_s"], writes=["outbB", "outbrow_done"], hz=True)

        for t in (range(16) if part == "b" else []):
            sl = t % 2
            for h in range(16):
                dma("sp", ctt[sl][:, h, :],
                    ctx_recv.rearrange("(r tt) d -> r tt d", r=4)[
                        h % 4, t * 128:(t + 1) * 128, (h // 4) * 128:(h // 4 + 1) * 128],
                    "ctld", reads=["ctxrecv"], writes=[f"ctt{sl}"])
            for hd in range(16):
                bk = hd % 2
                p.emit("pe", lambda eng, hd=hd, bk=bk, sl=sl: eng.transpose(
                    bank_bf(bk), ctt[sl][:, hd, :], ident_bf[:]),
                    reads=[f"ctt{sl}", "ident_bf"], writes=[f"pb{bk}"])
                p.emit("dve", lambda eng, hd=hd, bk=bk, sl=sl: eng.tensor_copy(
                    ctf[sl][:, hd, :], bank_bf(bk)),
                    reads=[f"pb{bk}"], writes=[f"ctt{sl}"])

            def mm_out(eng, sl=sl):
                for hd in range(16):
                    for n in range(4):
                        inst = eng.matmul(PS[:, (4 + n) * 512:(5 + n) * 512],
                                          ctf[sl][:, hd, :],
                                          outw_sb[:, hd, n * 512:(n + 1) * 512],
                                          start=(hd == 0), stop=(hd == 15))
                return inst
            p.emit("pe", mm_out, reads=[f"ctt{sl}", "OW0", "OW1", "OW2", "OW3"],
                   writes=["pb4", "pb5", "pb6", "pb7"])
            dma("sp", xt[sl][:], x_own[t * 128:(t + 1) * 128, :], "xo", writes=[f"xt{sl}"])
            p.emit("dve", lambda eng, sl=sl: eng.tensor_tensor(
                x1t[sl][:], PS[:, 2048:4096], outbB[:], op=ALU.add),
                reads=["pb4", "pb5", "pb6", "pb7", "outbB"], writes=[f"x1t{sl}"])
            p.emit("dve", lambda eng, sl=sl: eng.tensor_tensor(
                x1t[sl][:], x1t[sl][:], xt[sl][:], op=ALU.add),
                reads=[f"xt{sl}", f"x1t{sl}"], writes=[f"x1t{sl}"])
            dma("sp", out[t * 128:(t + 1) * 128, :], x1t[sl][:], "x1st",
                reads=[f"x1t{sl}"],
                writes=[f"y{t // 8}_{t % 8}_0", f"y{t // 8}_{t % 8}_1"])
            ln_tile(x1t[sl][:], f"x1t{sl}", lnt[sl][:], f"lnt{sl}", sl)
            transpose16(lnt[sl][:], f"lnt{sl}", lnf[sl], f"lnf{sl}", slice(0, 128))
            dma("sp", lnx2_dram[:, :, t * 128:(t + 1) * 128], lnf[sl][:, :, :], "ln2st",
                reads=[f"lnf{sl}"], writes=["ln2dram"])

        if part == "b":
            p.barrier()

        NSUB = 8
        for b2 in (range(2) if part == "b" else []):
            tb = b2 * 1024
            dma("pool", LNX2[:, :, :], lnx2_dram[:, :, tb:tb + 1024], "ln2ld",
                reads=["ln2dram"], writes=["lnx2b"])
            for tt in range(8):
                sl = tt % 2

                def mm_gate(eng, tt=tt):
                    for dc in range(16):
                        inst = eng.matmul(bank_f(6, 16), LNX2[:, dc, tt * 128:(tt + 1) * 128],
                                          gate_s[:, dc, :], start=(dc == 0), stop=(dc == 15))
                    return inst
                p.emit("pe", mm_gate, reads=["lnx2b", "gate_s"], writes=["pb6"])
                p.emit("act", lambda eng, sl=sl: eng.activation(
                    sco[sl][:], bank_f(6, 16), AF.Sigmoid, bias=0.0, scale=1.0),
                    reads=["pb6"], writes=[f"sco{sl}"])

                sc = sco[sl]
                gmax = rscr[:, 0:4]
                m1 = rscr[:, 4:5]
                mk1 = rscr[:, 5:9]
                g2 = rscr[:, 9:13]
                m2 = rscr[:, 13:14]
                gmask = rscr[:, 14:18]
                masked = rscr[:, 18:34]
                ssum = rscr[:, 35:36]
                RT = ["rt"]
                p.emit("dve", lambda eng, sc=sc: eng.tensor_reduce(
                    gmax, sc.rearrange("p (g m) -> p g m", g=4), axis=AX.X, op=ALU.max),
                    reads=[f"sco{sl}"], writes=RT, hz=True)
                p.emit("dve", lambda eng: eng.reduce_max(m1, gmax, axis=AX.X),
                       reads=RT, writes=RT, hz=True)
                p.emit("dve", lambda eng: eng.tensor_scalar(mk1, gmax, m1, None, op0=ALU.is_ge),
                       reads=RT, writes=RT, hz=True)
                p.emit("dve", lambda eng: eng.scalar_tensor_tensor(
                    g2, mk1, -1e30, gmax, op0=ALU.mult, op1=ALU.add),
                    reads=RT, writes=RT, hz=True)
                p.emit("dve", lambda eng: eng.reduce_max(m2, g2, axis=AX.X),
                       reads=RT, writes=RT, hz=True)
                p.emit("dve", lambda eng: eng.tensor_scalar(gmask, gmax, m2, None, op0=ALU.is_ge),
                       reads=RT, writes=RT, hz=True)

                def maskmul(eng, sc=sc):
                    for g4 in range(4):
                        inst = eng.tensor_scalar(masked[:, g4 * 4:(g4 + 1) * 4],
                                                 sc[:, g4 * 4:(g4 + 1) * 4],
                                                 gmask[:, g4:g4 + 1], None, op0=ALU.mult)
                    return inst
                p.emit("dve", maskmul, reads=RT + [f"sco{sl}"], writes=RT, hz=True)
                p.emit("dve", lambda eng: eng.max(out=top8b[:], in_=masked),
                       reads=RT, writes=RT + ["top8b"], hz=True)
                p.emit("dve", lambda eng: eng.tensor_scalar(
                    masked, masked, top8b[:, 3:4], None, op0=ALU.is_ge),
                    reads=RT + ["top8b"], writes=RT, hz=True)
                p.emit("dve", lambda eng, sc=sc: eng.tensor_tensor(masked, sc, masked, op=ALU.mult),
                       reads=RT + [f"sco{sl}"], writes=RT, hz=True)
                p.emit("dve", lambda eng: eng.reduce_sum(ssum, masked, axis=AX.X),
                       reads=RT, writes=RT, hz=True)
                p.emit("dve", lambda eng: eng.reciprocal(ssum, ssum),
                       reads=RT, writes=RT, hz=True)
                p.emit("dve", lambda eng: eng.tensor_scalar(masked, masked, ssum, None, op0=ALU.mult),
                       reads=RT, writes=RT, hz=True)
                p.emit("pe", lambda eng: eng.transpose(PS[0:16, 3584:3712], rscr[:, 18:34],
                                                       ident_f[:]),
                       reads=["rt", "ident_f"], writes=["pb7"])
                p.emit("dve", lambda eng, tt=tt: eng.tensor_copy(
                    combf[0:16, tt * 128:(tt + 1) * 128], PS[0:16, 3584:3712]),
                    reads=["pb7"], writes=["combf"], hz=True)
            dma("sp", comb_dram[:, tb:tb + 1024], combf[:], "combst",
                reads=["combf"], writes=["combdram"])

            for e in range(NE + 1):
                shared = (e == NE)
                nIc = NSI if shared else NI
                w1src = ws1 if shared else w1[e]
                w3src = ws3 if shared else w3[e]
                if not shared:
                    dma("pool", combrow[:], comb_dram[e:e + 1, tb:tb + 1024], "combld",
                        reads=["combdram"], writes=["combrow"])
                    for i in range(2):
                        p.emit("pe", lambda eng, i=i: eng.matmul(
                            bank_f(6), ones_f[:],
                            combrow[0:1, i * 512:(i + 1) * 512], start=True, stop=True),
                            reads=["combrow", "ones_f"], writes=["pb6"])
                        p.emit("dve", lambda eng, i=i: eng.tensor_copy(
                            combB[:, i * 512:(i + 1) * 512], bank_f(6)),
                            reads=["pb6"], writes=["combB"], hz=True)
                for ic in range(nIc):
                    wsl = ic % 2
                    dma("pool", w1b[wsl][:, :, :], w1src[:, :, ic * 128:(ic + 1) * 128],
                        "w1", writes=[f"w1b{wsl}"])
                    dma("pool", w3b[wsl][:, :, :], w3src[:, :, ic * 128:(ic + 1) * 128],
                        "w3", writes=[f"w3b{wsl}"])

                    def mm_h(eng, wb, bko, wkey):
                        for dc in range(16):
                            for ns in range(2):
                                inst = eng.matmul(bank_f(bko + ns),
                                                  wb[:, dc, :],
                                                  LNX2[:, dc, ns * 512:(ns + 1) * 512],
                                                  start=(dc == 0), stop=(dc == 15))
                        return inst
                    p.emit("pe", functools.partial(mm_h, wb=w1b[wsl], bko=0, wkey=None),
                           reads=[f"w1b{wsl}", "lnx2b"], writes=["pb0", "pb1"])
                    p.emit("pe", functools.partial(mm_h, wb=w3b[wsl], bko=2, wkey=None),
                           reads=[f"w3b{wsl}", "lnx2b"], writes=["pb2", "pb3"])
                    ssl = ic % 2
                    p.emit("act", lambda eng, ssl=ssl: eng.activation(
                        s1b[ssl][:], PS[:, 0:1024], AF.Silu, bias=0.0, scale=1.0),
                        reads=["pb0", "pb1"], writes=[f"s1b{ssl}"])
                    if shared:
                        p.emit("dve", lambda eng, ic=ic, ssl=ssl: eng.tensor_tensor(
                            g_sb[:, ic, :], PS[:, 1024:2048], s1b[ssl][:], op=ALU.mult),
                            reads=["pb2", "pb3", f"s1b{ssl}"], writes=["g_sb"])
                    else:
                        p.emit("dve", lambda eng, ssl=ssl: eng.tensor_tensor(
                            t3f[ssl][:], PS[:, 1024:2048], combB[:], op=ALU.mult),
                            reads=["pb2", "pb3", "combB"], writes=[f"t3f{ssl}"])
                        p.emit("dve", lambda eng, ic=ic, ssl=ssl: eng.tensor_tensor(
                            g_sb[:, ic, :], t3f[ssl][:], s1b[ssl][:], op=ALU.mult),
                            reads=[f"t3f{ssl}", f"s1b{ssl}"], writes=["g_sb"])
                if debug and b2 == 0 and e == 0:
                    dma("sp", g_dump, R2[:, 0:22528], None, reads=["g_sb"])
                nH = 4 if shared else 2
                wdt = 512 if shared else 1024
                w2srcv = ws2 if shared else w2[e]
                w2bv = w2b_s if shared else w2b_r
                for hf in range(nH):
                    wsl = hf % 2
                    dma("pool", w2bv[wsl][:, :, :],
                        w2srcv.rearrange("a p j -> p a j")[:, :, hf * wdt:(hf + 1) * wdt],
                        "w2", writes=[f"w2b{wsl}"])
                    for sub in range(NSUB):
                        nbk = wdt // 512
                        ybk = 4 + (sub % 2) * nbk

                        def mm_y(eng, hf=hf, sub=sub, wsl=wsl, ybk=ybk, nIc=nIc, nbk=nbk,
                                 w2bv=w2bv):
                            for ic in range(nIc):
                                for n2 in range(nbk):
                                    inst = eng.matmul(
                                        bank_f(ybk + n2),
                                        g_sb[:, ic, sub * 128:(sub + 1) * 128],
                                        w2bv[wsl][:, ic, n2 * 512:(n2 + 1) * 512],
                                        start=(ic == 0), stop=(ic == nIc - 1))
                            return inst
                        ykeys = [f"pb{ybk + n2}" for n2 in range(nbk)]
                        p.emit("pe", mm_y, reads=["g_sb", f"w2b{wsl}"], writes=ykeys)
                        ysl = sub % 2
                        p.emit("dve", lambda eng, ybk=ybk, ysl=ysl, wdt=wdt: eng.tensor_copy(
                            ytile[ysl][:, 0:wdt], PS[:, ybk * 512: ybk * 512 + wdt]),
                            reads=ykeys, writes=[f"ytile{ysl}"])
                        colo = hf * wdt
                        dma("pool",
                            out[tb + sub * 128: tb + (sub + 1) * 128, colo:colo + wdt],
                            ytile[ysl][:, 0:wdt], "yacc",
                            reads=[f"ytile{ysl}"],
                            writes=[f"y{b2}_{sub}_{colo // 1024}"], accum=True)

        if debug:
            for nm in dbg:
                dst, src = dbg[nm]
                dma("pool", dst, src, None,
                    reads=["ctxdram", "ln2dram", "qdram", "kdram", "vdram"])
            if part == "b":
                dma("pool", comb_dump, comb_dram, None, reads=["combdram"])

        p.barrier()
        p.finalize_and_emit(nc, ctx)
    return nc



def build_fused(debug=False):
    nc = bass.Bass("TRN2", target_bir_lowering=False, num_devices=8)
    p = Prog()
    NS = 8
    NTt = 8
    NU = 32

    def din(name, shape, dtype=F32):
        return nc.dram_tensor(name, list(shape), dtype, kind="ExternalInput").ap()

    identw = din("identw", [128, 128])
    onesrow = din("onesrow", [1, 128])
    xb = din("xb", [S, D])
    x_own = din("x_own", [TCORE, D])
    qkvw = din("qkvw", [128, 16, 24 * 128])
    qkvb = din("qkvb", [128, 16])
    vbrow = din("vbrow", [1, 1024])
    outw = din("outw", [128, 16, D])
    outbrow = din("outbrow", [1, D])
    gatew = din("gatew", [128, 16, 16])
    w1 = din("w1", [NE, 128, 16, I_FF])
    w3 = din("w3", [NE, 128, 16, I_FF])
    w2 = din("w2", [NE, NI, 128, D])
    ws1 = din("ws1", [128, 16, SH])
    ws3 = din("ws3", [128, 16, SH])
    ws2 = din("ws2", [NSI, 128, D])
    jrow = din("jrow", [1, 512])
    p_idx = din("p_idx", [128, 1])
    slopes = din("slopes", [128, 8])
    s128 = din("s128", [1, 8])
    s512 = din("s512", [1, 16])
    s511 = din("s511", [1, 16])
    tilepos = din("tilepos", [1, 8])
    mypos128 = din("mypos128", [1, 32])
    diagA2 = din("diagA2", [128, 8, 512])
    diagB2 = din("diagB2", [128, 8, 512])
    idxq = din("idxq", [128, 8], I32)
    idxst = din("idxst", [128, 256], I32)
    idxld = din("idxld", [128, 256], I32)

    out = nc.dram_tensor("out", [TCORE, D], F32, kind="ExternalOutput").ap()

    q_dram = nc.dram_tensor("q_dram", [2 * 8 * DH, 4096], BF16, kind="Internal").ap()
    k_dram = nc.dram_tensor("k_dram", [NS * DH, S], BF16, kind="Internal").ap()
    v_dram = nc.dram_tensor("v_dram", [S, NS * DH], BF16, kind="Internal").ap()
    ctxsh = nc.dram_tensor("ctxsh", [2 * NS * 4096, DH], BF16, kind="Internal",
                           addr_space="Shared").ap()
    lnx2_dram = nc.dram_tensor("lnx2_dram", [128, 16, TCORE], BF16, kind="Internal").ap()
    comb_dram = nc.dram_tensor("comb_dram", [16, TCORE], F32, kind="Internal").ap()

    dbg = {}
    if debug:
        for nm, t in (("d_q", q_dram), ("d_k", k_dram), ("d_v", v_dram),
                      ("d_ctx", ctxsh), ("d_ln2", lnx2_dram)):
            dbg[nm] = (nc.dram_tensor(nm, list(t.shape), BF16, kind="ExternalOutput").ap(), t)
        d_comb = nc.dram_tensor("d_comb", [16, TCORE], F32, kind="ExternalOutput").ap()
        d_diagB = nc.dram_tensor("d_diagB", [128, 8 * 512], BF16, kind="ExternalOutput").ap()
        d_mhat = nc.dram_tensor("d_mhat", [128, 32], F32, kind="ExternalOutput").ap()
        d_negD = nc.dram_tensor("d_negD", [128, 8], F32, kind="ExternalOutput").ap()
        d_alip = nc.dram_tensor("d_alip", [128, 64], F32, kind="ExternalOutput").ap()
        d_pt = nc.dram_tensor("d_pt", [128, 512], BF16, kind="ExternalOutput").ap()
        d_bias = nc.dram_tensor("d_bias", [128, 1], F32, kind="ExternalOutput").ap()
        d_qsb = nc.dram_tensor("d_qsb", [128, 4096], BF16, kind="ExternalOutput").ap()
        d_ksb = nc.dram_tensor("d_ksb", [128, 8192], BF16, kind="ExternalOutput").ap()

    with ExitStack() as ctx:
        def alloc(name, shape, dtype):
            return nc.alloc_sbuf_tensor(name, list(shape), dtype).ap()

        POOLN = 90112
        POOL = nc.alloc_sbuf_tensor("POOL", [128, POOLN], mybir.dt.uint8).ap()

        def sub(off, shape, dtype):
            es = 2 if dtype == BF16 else 4
            n = int(np.prod(shape[1:])) * es
            assert off % 4 == 0 and off + n <= POOLN, (off, n)
            ap = POOL[0:shape[0], off:off + n].bitcast(dtype)
            if len(shape) == 3:
                ap = ap.rearrange("p (a b) -> p a b", a=shape[1])
            return ap

        xt = [sub(0, [128, D], F32), sub(8192, [128, D], F32)]
        sq = sub(16384, [128, D], F32)
        lnt = [sub(24576, [128, D], BF16), sub(28672, [128, D], BF16)]
        qkt = [sub(32768 + 1024 * i, [128, 512], BF16) for i in range(4)]
        vt = [sub(36864, [128, 512], BF16), sub(37888, [128, 512], BF16)]
        wqk = [sub(38912, [128, 16, 128], BF16), sub(43008, [128, 16, 128], BF16)]
        vw = sub(47104, [128, 16, 512], BF16)
        vbB = sub(63488, [128, 1024], F32)
        vbrow_s = sub(67584, [1, 1024], F32)
        outbrow_s = sub(67584, [1, 512], F32)
        pt = [sub(38912, [128, 512], BF16), sub(39936, [128, 512], BF16),
              sub(59392, [128, 512], BF16), sub(60416, [128, 512], BF16)]
        ctt = [sub(40960, [128, 16, 128], BF16), sub(45056, [128, 16, 128], BF16)]
        lnf = [sub(49152, [128, 16, 128], BF16), sub(53248, [128, 16, 128], BF16)]
        x1t = [sub(57344, [128, D], F32), sub(65536, [128, D], F32)]
        outbB = sub(73728, [128, D], F32)
        diagA_s = sub(81920, [128, 8, 512], BF16)
        diagB_s = sub(40960, [128, 8, 512], BF16)
        ctxbf = [sub(57344 + 256 * i, [128, 128], BF16) for i in range(4)]
        negirow = sub(58368, [1, 512], BF16)
        ctf = ctt
        LNX2 = sub(0, [128, 16, 1024], BF16)
        s1b = [sub(32768, [128, 1024], BF16), sub(34816, [128, 1024], BF16)]
        t3f = [sub(36864, [128, 1024], F32), sub(40960, [128, 1024], F32)]
        ytile = [sub(45056, [128, 1024], F32), sub(49152, [128, 1024], F32)]
        combf = sub(53248, [16, 1024], F32)

        ident_bf = alloc("ident_bf", [128, 128], BF16)
        ident_f = alloc("ident_f", [128, 128], F32)
        ones_f = alloc("ones_f", [1, 128], F32)
        ones_bf = alloc("ones_bf", [1, 128], BF16)
        jrow_s = alloc("jrow_s", [1, 512], F32)
        pidx_s = alloc("pidx_s", [128, 1], F32)
        slopes_s = alloc("slopes_s", [128, 8], F32)
        s128_s = alloc("s128_s", [1, 8], F32)
        s512_s = alloc("s512_s", [1, 16], F32)
        s511_s = alloc("s511_s", [1, 16], F32)
        tilepos_s = alloc("tilepos_s", [1, 8], F32)
        mypos_s = alloc("mypos_s", [1, 32], F32)
        qkvb_s = alloc("qkvb_s", [128, 16], F32)
        gate_s = alloc("gate_s", [128, 16, 16], BF16)
        mhat2 = alloc("mhat2", [128, 16], F32)
        zr4 = alloc("zr4", [128, 4], F32)
        stat = [alloc(f"stat{i}", [128, 8], F32) for i in range(2)]
        top8b = alloc("top8b", [128, 8], F32)
        mhat = alloc("mhat", [128, 32], F32)
        mcol = alloc("mcol", [32, 1], F32)
        drow = alloc("drow", [1, 8], F32)
        alibi_loc = alloc("alibi_loc", [1, 512], BF16)
        crow = alloc("crow", [1, 64], F32)
        crowq = alloc("crowq", [1, 32], F32)
        adjrow = alloc("adjrow", [1, 16], F32)
        adjrow2 = alloc("adjrow2", [1, 8], F32)
        adjB = alloc("adjBb", [128, 16], F32)
        sp_col = alloc("sp_col", [128, 1], F32)
        alibi_p = alloc("alibi_p", [128, 64], F32)
        alibi_pq = alloc("alibi_pq", [128, 32], F32)
        negD = alloc("negD", [128, 8], F32)
        biasb = [alloc(f"biasb{i}", [128, 1], F32) for i in range(4)]
        biasT = [sub(63488, [128, 64], F32), sub(63744, [128, 64], F32)]
        sco = [alloc(f"sco{i}", [128, 16], F32) for i in range(2)]
        rscr = alloc("rscr", [128, 40], F32)
        combrow = alloc("combrow", [1, 1024], F32)
        sidxst = sub(32768, [128, 256], I32)
        sidxld = sub(33792, [128, 256], I32)
        sidxq = sub(34816, [128, 8], I32)

        R1 = alloc("R1", [128, 22528], BF16)
        R2 = alloc("R2", [128, 32768], BF16)

        PS = nc.alloc_psum_tensor("PS", [128, 4096], F32).ap()
        PSb = PS.bitcast(BF16)

        def bank_f(i, n=512):
            return PS[:, i * 512: i * 512 + n]

        def bank_bf(i, n=128):
            return PSb[:, i * 1024: i * 1024 + n]

        lnx_fm = R2.rearrange("p (dc t) -> p dc t", dc=16)
        K_sb = R2[:, 0:8192]
        Q_sb = R2[:, 8192:12288]
        V_sb = R2[:, 16384:16384 + 64 * 129].rearrange("p (c e) -> p c e", c=64)
        outw_sb = R2.rearrange("p (hd t) -> p hd t", hd=16)
        g_sb = R2[:, 0:22528].rearrange("p (ic t) -> p ic t", ic=22)
        w1b = [R2[:, 22528 + i * 2048: 22528 + (i + 1) * 2048].rearrange("p (dc j) -> p dc j", dc=16)
               for i in range(2)]
        w3b = [R2[:, 26624 + i * 2048: 26624 + (i + 1) * 2048].rearrange("p (dc j) -> p dc j", dc=16)
               for i in range(2)]
        combB = R2[:, 30720:32768].bitcast(F32)
        w2b_r = [R1[:, i * 11264:(i + 1) * 11264].rearrange("p (ic j) -> p ic j", ic=NI) for i in range(2)]
        w2b_s = [R1[:, i * 11264:(i + 1) * 11264].rearrange("p (ic j) -> p ic j", ic=NSI) for i in range(2)]

        _fam_ctr = [0]

        def dma(q, dst, src, fam, reads=(), writes=(), accum=False):
            if fam is None:
                _fam_ctr[0] += 1
                fam = f"u{_fam_ctr[0]}"
            if accum:
                fn = lambda eng, dst=dst, src=src: eng.dma_start(out=dst, in_=src, accum_op=ALU.add)
            else:
                fn = lambda eng, dst=dst, src=src: eng.dma_start(out=dst, in_=src)
            return p.emit(q, fn, reads=reads, writes=writes, fam=fam)

        dma("pool", ident_bf[:], identw, None, writes=["ident_bf"])
        dma("sp", ident_f[:], identw, None, writes=["ident_f"])
        dma("sp", ones_f[:], onesrow, None, writes=["ones_f"])
        dma("pool", ones_bf[:], onesrow, None, writes=["ones_bf"])
        dma("sp", jrow_s[:], jrow, None, writes=["jrow_s"])
        dma("sp", pidx_s[:], p_idx, None, writes=["pidx_s"])
        dma("sp", slopes_s[:], slopes, None, writes=["slopes_s"])
        dma("sp", s128_s[:], s128, None, writes=["s128_s"])
        dma("sp", s512_s[:], s512, None, writes=["s512_s"])
        dma("sp", s511_s[:], s511, None, writes=["s511_s"])
        dma("sp", tilepos_s[:], tilepos, None, writes=["tilepos_s"])
        dma("sp", mypos_s[:], mypos128, None, writes=["mypos_s"])
        dma("sp", qkvb_s[:], qkvb, None, writes=["qkvb_s"])
        dma("pool", gate_s[:], gatew, None, writes=["gate_s"])
        dma("sp", vbrow_s[:], vbrow, None, writes=["vbrow_s"])
        p.emit("pe", lambda eng: eng.matmul(bank_f(6), ones_f[:], vbrow_s[0:1, 0:512],
                                            start=True, stop=True),
               reads=["ones_f", "vbrow_s"], writes=["pb6"])
        p.emit("dve", lambda eng: eng.tensor_copy(vbB[:, 0:512], bank_f(6)),
               reads=["pb6"], writes=["vbB"], hz=True)
        p.emit("pe", lambda eng: eng.matmul(bank_f(6), ones_f[:], vbrow_s[0:1, 512:1024],
                                            start=True, stop=True),
               reads=["ones_f", "vbrow_s", "vbB"], writes=["pb6"])
        p.emit("dve", lambda eng: eng.tensor_copy(vbB[:, 512:1024], bank_f(6)),
               reads=["pb6"], writes=["vbB"], hz=True)

        def ln_tile(xin, xkey, outbf, outkey, sl):
            st = stat[sl]
            s1, nm, v1, sd, rstd, mr = (st[:, i:i + 1] for i in range(6))
            kst = f"stat{sl}"
            p.emit("dve", lambda eng: eng.reduce_sum(s1, xin, axis=AX.X),
                   reads=[xkey], writes=[kst], hz=True)
            p.emit("dve", lambda eng: eng.tensor_scalar(nm, s1, -1.0 / D, None, op0=ALU.mult),
                   reads=[kst], writes=[kst], hz=True)
            p.emit("act", lambda eng: eng.activation(sq[:], xin, AF.Square, bias=nm, scale=1.0),
                   reads=[xkey, kst], writes=["sq"])
            p.emit("dve", lambda eng: eng.reduce_sum(v1, sq[:], axis=AX.X),
                   reads=["sq"], writes=[kst], hz=True)
            p.emit("dve", lambda eng: eng.tensor_scalar(v1, v1, 1.0 / D, EPS,
                                                        op0=ALU.mult, op1=ALU.add),
                   reads=[kst], writes=[kst], hz=True)
            p.emit("act", lambda eng: eng.activation(sd, v1, AF.Sqrt),
                   reads=[kst], writes=[kst], hz=True)
            p.emit("dve", lambda eng: eng.reciprocal(rstd, sd), reads=[kst], writes=[kst], hz=True)
            p.emit("dve", lambda eng: eng.tensor_tensor(mr, nm, rstd, op=ALU.mult),
                   reads=[kst], writes=[kst], hz=True)
            p.emit("dve", lambda eng: eng.tensor_scalar(outbf, xin, nm, rstd,
                                                        op0=ALU.add, op1=ALU.mult),
                   reads=[xkey, kst], writes=[outkey])

        def transpose16(src_bf, srckey, dst3, dstkey, tslice):
            for dc in range(16):
                bk = 4 + (dc % 4)
                p.emit("pe", lambda eng, dc=dc, bk=bk: eng.transpose(
                    bank_bf(bk), src_bf[:, dc * 128:(dc + 1) * 128], ident_bf[:]),
                    reads=[srckey, "ident_bf"], writes=[f"pb{bk}"])
                p.emit("dve", lambda eng, dc=dc, bk=bk: eng.tensor_copy(
                    dst3[:, dc, tslice], bank_bf(bk)),
                    reads=[f"pb{bk}"], writes=[dstkey])

        for slab in range(4):
            base = slab * 2048
            for t in range(16):
                sl = t % 2
                dma("pool", xt[sl][:], xb[base + t * 128: base + (t + 1) * 128, :], "xb",
                    writes=[f"xt{sl}"])
                ln_tile(xt[sl][:], f"xt{sl}", lnt[sl][:], f"lnt{sl}", sl)
                transpose16(lnt[sl][:], f"lnt{sl}", lnx_fm, "lnxfm",
                            slice(t * 128, (t + 1) * 128))
            for f in range(16):
                wsl = f % 2
                dma("pool", wqk[wsl][:, :, :], qkvw[:, :, f * 128:(f + 1) * 128], "wqk",
                    writes=[f"wqk{wsl}"])

                for half in range(2):
                    def mm_qk(eng, f=f, wsl=wsl, half=half):
                        for dc in range(16):
                            for ts in (2 * half, 2 * half + 1):
                                inst = eng.matmul(bank_f(ts),
                                                  wqk[wsl][:, dc, :],
                                                  lnx_fm[:, dc, ts * 512:(ts + 1) * 512],
                                                  start=(dc == 0), stop=(dc == 15))
                        return inst
                    p.emit("pe", mm_qk, reads=["lnxfm", f"wqk{wsl}"],
                           writes=[f"pb{2 * half}", f"pb{2 * half + 1}"])
                for ts in range(4):
                    p.emit("act", lambda eng, f=f, ts=ts: eng.activation(
                        qkt[ts][:], bank_f(ts), AF.Identity, bias=qkvb_s[:, f:f + 1], scale=1.0),
                        reads=[f"pb{ts}", "qkvb_s"], writes=[f"qkt{ts}"])
                    g_t = slab * 4 + ts
                    if f < 8:
                        dma("sp", q_dram[(g_t % 2) * 1024 + f * 128:(g_t % 2) * 1024 + (f + 1) * 128,
                                         (g_t // 2) * 512:(g_t // 2 + 1) * 512],
                            qkt[ts][:], "qkst", reads=[f"qkt{ts}"], writes=["qdram"])
                    else:
                        dma("sp", k_dram[(f - 8) * 128:(f - 7) * 128,
                                         base + ts * 512: base + (ts + 1) * 512],
                            qkt[ts][:], "qkst", reads=[f"qkt{ts}"], writes=["kdram"])
            for vh in range(2):
                dma("pool", vw[:, :, :], qkvw[:, :, 2048 + vh * 512: 2048 + (vh + 1) * 512],
                    "vw", writes=["vw"])
                for t in range(16):
                    bk = 4 + t % 2
                    sl = t % 2

                    def mm_v2(eng, t=t, bk=bk):
                        for dc in range(16):
                            inst = eng.matmul(bank_f(bk),
                                              lnx_fm[:, dc, t * 128:(t + 1) * 128],
                                              vw[:, dc, :],
                                              start=(dc == 0), stop=(dc == 15))
                        return inst
                    p.emit("pe", mm_v2, reads=["lnxfm", "vw"], writes=[f"pb{bk}"])
                    p.emit("dve", lambda eng, sl=sl, bk=bk, vh=vh: eng.tensor_tensor(
                        vt[sl][:], bank_f(bk), vbB[:, vh * 512:(vh + 1) * 512], op=ALU.add),
                        reads=[f"pb{bk}", "vbB"], writes=[f"vt{sl}"])
                    dma("sp", v_dram[base + t * 128: base + (t + 1) * 128,
                                     vh * 512:(vh + 1) * 512], vt[sl][:],
                        "vst", reads=[f"vt{sl}"], writes=["vdram"])

        p.barrier()

        dma("pool", diagA_s[:], diagA2, None, writes=["diagA_s"])
        dma("pool", diagB_s[:], diagB2, None, writes=["diagB_s"])
        dma("sp", sidxq[:], idxq, None, writes=["sidxq"])
        dma("sp", sidxst[:], idxst, None, writes=["sidxst"])
        dma("sp", sidxld[:], idxld, None, writes=["sidxld"])
        p.emit("dve", lambda eng: eng.memset(V_sb[:, :, 128:129], 1.0), writes=["Vsb"])

        WMARG = 224.0
        for s in range(NS):
            steep = (s <= 3)
            wdist = int(WMARG / float(SLOPES[2 * s + 1]))
            dma("sp", K_sb, k_dram[s * 128:(s + 1) * 128, :], "kqv",
                reads=["kdram"], writes=["Ksb"])

            def qload(eng, s=s):
                return eng.indirect_dma_start(
                    out=Q_sb, out_offset=None,
                    in_=q_dram[0:128, :],
                    in_offset=IndirectOffsetOnAxis(sidxq[:, s:s + 1], 0))
            p.emit("pool", qload, reads=["qdram", "sidxq"], writes=["Qsb"], fam="kqv2")
            dma("sp", V_sb[:, :, 0:128],
                v_dram.rearrange("(c p) d -> p c d", p=128)[:, :, s * 128:(s + 1) * 128],
                "kqv", reads=["vdram"], writes=["Vsb"])

            p.emit("act", lambda eng, s=s: eng.activation(
                alibi_loc[:], jrow_s[:], AF.Identity, bias=0.0, scale=slopes_s[0:1, s:s + 1]),
                reads=["jrow_s", "slopes_s"], writes=["aloc"])
            if steep:
                p.emit("dve", lambda eng, s=s: eng.tensor_scalar(
                    negirow[:], jrow_s[:], slopes_s[0:1, s:s + 1], -1.0,
                    op0=ALU.mult, op1=ALU.mult),
                    reads=["jrow_s", "slopes_s"], writes=["negi"], hz=True)
            p.emit("dve", lambda eng, s=s: eng.tensor_scalar(
                crow[:], jrow_s[0:1, 0:64], s128_s[0:1, s:s + 1], None, op0=ALU.mult),
                reads=["jrow_s", "s128_s"], writes=["crow"], hz=True)
            p.emit("dve", lambda eng, s=s: eng.tensor_scalar(
                crowq[:], mypos_s[0:1, :], slopes_s[0:1, s:s + 1], None, op0=ALU.mult),
                reads=["mypos_s", "slopes_s"], writes=["crowq"], hz=True)
            p.emit("dve", lambda eng, s=s: eng.tensor_scalar(
                sp_col[:], pidx_s[:], slopes_s[:, s:s + 1], None, op0=ALU.mult),
                reads=["pidx_s", "slopes_s"], writes=["sp_col"], hz=True)
            p.emit("pe", lambda eng: eng.matmul(bank_f(2, 64), ones_f[:], crow[:],
                                                start=True, stop=True),
                   reads=["ones_f", "crow"], writes=["pb2"])
            p.emit("dve", lambda eng: eng.tensor_scalar(
                alibi_p[:], bank_f(2, 64), sp_col[:], None, op0=ALU.add),
                reads=["pb2", "sp_col"], writes=["alibi_p"], hz=True)
            p.emit("pe", lambda eng: eng.matmul(bank_f(2, 32), ones_f[:], crowq[:],
                                                start=True, stop=True),
                   reads=["ones_f", "crowq", "alibi_p"], writes=["pb2"])
            p.emit("dve", lambda eng: eng.tensor_scalar(
                alibi_pq[:], bank_f(2, 32), sp_col[:], None, op0=ALU.add),
                reads=["pb2", "sp_col"], writes=["alibi_pq"], hz=True)
            if steep:
                p.emit("dve", lambda eng, s=s: eng.tensor_scalar(
                    adjrow[:], jrow_s[0:1, 0:16], s512_s[0:1, s:s + 1], None, op0=ALU.mult),
                    reads=["jrow_s", "s512_s"], writes=["adjrow"], hz=True)
                p.emit("dve", lambda eng, s=s: eng.tensor_scalar(
                    adjrow2[:], tilepos_s[0:1, :], slopes_s[0:1, s:s + 1], None, op0=ALU.mult),
                    reads=["tilepos_s", "slopes_s"], writes=["adjrow2"], hz=True)
            else:
                p.emit("dve", lambda eng, s=s: eng.tensor_scalar(
                    adjrow[:], jrow_s[0:1, 0:16], s512_s[0:1, s:s + 1],
                    s511_s[0:1, s:s + 1], op0=ALU.mult, op1=ALU.add),
                    reads=["jrow_s", "s512_s", "s511_s"], writes=["adjrow"], hz=True)
            p.emit("pe", lambda eng: eng.matmul(bank_f(2, 16), ones_f[:], adjrow[:],
                                                start=True, stop=True),
                   reads=["ones_f", "adjrow", "alibi_pq"], writes=["pb2"])
            p.emit("dve", lambda eng: eng.tensor_copy(adjB[:], bank_f(2, 16)),
                   reads=["pb2"], writes=["adjB"], hz=True)

            for u in range(NU):
                nch = 2 * (u // 4) + 2
                pos_min = 512 * (2 * (u // 4)) + 128 * (u % 4)
                c_lo = max(0, (pos_min - wdist) // 512)
                p.emit("dve", lambda eng, c_lo=c_lo, nch=nch: eng.tensor_scalar(
                    mhat2[:, c_lo:nch], adjB[:, c_lo:nch], 0.0, -30000.0,
                    op0=ALU.mult, op1=ALU.add),
                    reads=["adjB"], writes=[f"mh{c}" for c in range(c_lo, nch)], hz=True)
                cs = list(range(c_lo, nch))
                groups = []
                i = 0
                while i < len(cs):
                    c0 = cs[i]
                    if not steep and (u + c0) % 2 == 0 and i + 1 < len(cs):
                        groups.append((c0, 2)); i += 2
                    else:
                        groups.append((c0, 1)); i += 1
                for (c0, w) in groups:
                    for c in range(c0, c0 + w):
                        bk = (u + c) % 4

                        def mm_s(eng, u=u, c=c, bk=bk, steep=steep):
                            if steep:
                                eng.matmul(bank_f(bk), Q_sb[:, u * 128:(u + 1) * 128],
                                           K_sb[:, c * 512:(c + 1) * 512], start=True, stop=False)
                                return eng.matmul(bank_f(bk), ones_bf[:], alibi_loc[:],
                                                  start=False, stop=True)
                            return eng.matmul(bank_f(bk), Q_sb[:, u * 128:(u + 1) * 128],
                                              K_sb[:, c * 512:(c + 1) * 512], start=True, stop=True)
                        p.emit("pe", mm_s, reads=["Qsb", "Ksb", "aloc", "ones_bf"],
                               writes=[f"pb{bk}"])
                        if c >= nch - 2:
                            mi = (u % 4) * 2 + (c - (nch - 2))
                            p.emit("dve", lambda eng, mi=mi, bk=bk: eng.tensor_tensor(
                                bank_f(bk), bank_f(bk), diagA_s[:, mi, :], op=ALU.add),
                                reads=["diagA_s"], writes=[f"pb{bk}"])
                    bkA = (u + c0) % 4
                    if w == 2:
                        p.emit("dve", lambda eng, c0=c0, bkA=bkA: eng.reduce_max(
                            mhat2[:, c0 + 1:c0 + 2], bank_f(bkA, 1024), axis=AX.X),
                            reads=[f"pb{bkA}", f"pb{bkA + 1}"],
                            writes=[f"mh{c0}", f"mh{c0 + 1}"], hz=True)
                    else:
                        p.emit("dve", lambda eng, c0=c0, bkA=bkA: eng.reduce_max(
                            mhat2[:, c0:c0 + 1], bank_f(bkA), axis=AX.X),
                            reads=[f"pb{bkA}"], writes=[f"mh{c0}"], hz=True)
                p.emit("dve", lambda eng, nch=nch, c_lo=c_lo: eng.tensor_tensor(
                    mhat2[:, c_lo:nch], mhat2[:, c_lo:nch], adjB[:, c_lo:nch], op=ALU.add),
                    reads=["adjB"] + [f"mh{c}" for c in range(c_lo, nch)],
                    writes=[f"mh{c}" for c in range(c_lo, nch)], hz=True)
                p.emit("dve", lambda eng, u=u, nch=nch, c_lo=c_lo: eng.reduce_max(
                    mhat[:, u:u + 1], mhat2[:, c_lo:nch], axis=AX.X),
                    reads=[f"mh{c}" for c in range(c_lo, nch)], writes=["mhat"], hz=True)
                if steep:
                    p.emit("dve", lambda eng, u=u: eng.tensor_tensor(
                        mhat[:, u:u + 1], mhat[:, u:u + 1], alibi_pq[:, u:u + 1],
                        op=ALU.subtract),
                        reads=["alibi_pq", "mhat"], writes=["mhat"], hz=True)

            p.emit("pe", lambda eng: eng.transpose(PS[0:32, 1024:1152], mhat[:, 0:32], ident_f[:]),
                   reads=["mhat", "ident_f"], writes=["pb2"])
            p.emit("dve", lambda eng: eng.reduce_max(mcol[:], PS[0:32, 1024:1152], axis=AX.X),
                   reads=["pb2"], writes=["mcol"], hz=True)
            p.emit("pe", lambda eng: eng.transpose(PS[0:1, 1536:1568], mcol[0:32, 0:1],
                                                   ident_f[0:32, 0:32]),
                   reads=["mcol", "ident_f"], writes=["pb3"])
            p.emit("dve", lambda eng: eng.tensor_reduce(
                drow[:], PS[0:1, 1536:1568].rearrange("p (a b) -> p a b", a=8),
                axis=AX.X, op=ALU.max),
                reads=["pb3"], writes=["drow"], hz=True)
            if steep:
                p.emit("dve", lambda eng: eng.tensor_tensor(
                    drow[:], drow[:], adjrow2[0:1, 0:8], op=ALU.add),
                    reads=["drow", "adjrow2"], writes=["drow"], hz=True)
            p.emit("dve", lambda eng: eng.tensor_scalar(drow[:], drow[:], -1.0, None, op0=ALU.mult),
                   reads=["drow"], writes=["drow"], hz=True)
            p.emit("pe", lambda eng: eng.matmul(bank_f(2, 8), ones_f[:], drow[:],
                                                start=True, stop=True),
                   reads=["ones_f", "drow"], writes=["pb2"])
            p.emit("dve", lambda eng: eng.tensor_copy(negD[:], bank_f(2, 8)),
                   reads=["pb2"], writes=["negD"], hz=True)

            if debug and s == 0:
                dma("sp", d_qsb, Q_sb, None, reads=["Qsb"])
                dma("sp", d_ksb, K_sb, None, reads=["Ksb"])
                dma("sp", d_mhat, mhat[:], None, reads=["mhat"])
                dma("sp", d_negD, negD[:], None, reads=["negD"])
                dma("sp", d_alip, alibi_p[:], None, reads=["alibi_p"])
                dma("sp", d_diagB, diag_s[:, :, :] if False else diagB_s.rearrange("p a b -> p (a b)"),
                    None, reads=["diagB_s"])
            for t in range(NTt):
                nkc = 8 * t + 8
                kc_lo = max(0, (1024 * t - wdist) // 128)
                bT = biasT[t % 2]
                p.emit("dve", lambda eng, t=t, bT=bT: eng.tensor_scalar(
                    bT[:], alibi_p[:], negD[:, t:t + 1], None, op0=ALU.add),
                    reads=["alibi_p", "negD"], writes=[f"biasT{t % 2}"], hz=True)
                for kc in range(kc_lo, nkc):
                    bk = kc % 4

                    def mm_st(eng, t=t, kc=kc, bk=bk, steep=steep):
                        if steep:
                            eng.matmul(bank_f(bk), K_sb[:, kc * 128:(kc + 1) * 128],
                                       Q_sb[:, t * 512:(t + 1) * 512], start=True, stop=False)
                            return eng.matmul(bank_f(bk), ones_bf[:], negirow[:],
                                              start=False, stop=True)
                        return eng.matmul(bank_f(bk), K_sb[:, kc * 128:(kc + 1) * 128],
                                          Q_sb[:, t * 512:(t + 1) * 512], start=True, stop=True)
                    p.emit("pe", mm_st, reads=["Ksb", "Qsb", "negi", "ones_bf"],
                           writes=[f"pb{bk}"])
                    if kc >= 8 * t:
                        p.emit("dve", lambda eng, t=t, kc=kc, bk=bk: eng.tensor_tensor(
                            bank_f(bk), bank_f(bk), diagB_s[:, kc - 8 * t, :], op=ALU.add),
                            reads=["diagB_s"], writes=[f"pb{bk}"])
                    psl = kc % 4
                    p.emit("act", lambda eng, bk=bk, kc=kc, psl=psl, bT=bT: eng.activation(
                        pt[psl][:], bank_f(bk), AF.Exp, bias=bT[:, kc:kc + 1], scale=1.0),
                        reads=[f"pb{bk}", f"biasT{t % 2}"], writes=[f"pt{psl}"])

                    if debug and s == 0 and t == 0 and kc == 0:
                        dma("sp", d_pt, pt[psl][:], None, reads=[f"pt{psl}"])
                        dma("sp", d_bias, bT[:, 0:1], None, reads=[f"biasT{t % 2}"])

                    def pv(eng, kc=kc, psl=psl, nkc=nkc, kc_lo=kc_lo):
                        for qq in range(4):
                            inst = eng.matmul(PS[:, (4 + qq) * 512:(4 + qq) * 512 + 129],
                                              pt[psl][:, qq * 128:(qq + 1) * 128],
                                              V_sb[:, kc, :],
                                              start=(kc == kc_lo), stop=(kc == nkc - 1))
                        return inst
                    p.emit("pe", pv, reads=[f"pt{psl}", "Vsb"],
                           writes=["pb4", "pb5", "pb6", "pb7"])
                zcols = PS[:, 2048:4096].rearrange("p (q x) -> p q x", q=4)[:, :, 128:129]
                p.emit("dve", lambda eng, zcols=zcols: eng.tensor_scalar(
                    zr4[:, :, None], zcols, 1e-37, None, op0=ALU.max),
                    reads=["pb4", "pb5", "pb6", "pb7"], writes=["zr4"], hz=True)
                p.emit("dve", lambda eng: eng.reciprocal(zr4[:], zr4[:]),
                       reads=["zr4"], writes=["zr4"], hz=True)
                for qq in range(4):
                    p.emit("dve", lambda eng, qq=qq: eng.tensor_scalar(
                        ctxbf[qq][:], PS[:, (4 + qq) * 512:(4 + qq) * 512 + 128],
                        zr4[:, qq:qq + 1], None, op0=ALU.mult),
                        reads=[f"pb{4 + qq}", "zr4"], writes=[f"ctxbf{qq}"])

                    def ctxstore(eng, s=s, t=t, qq=qq):
                        col = s * 32 + t * 4 + qq
                        return eng.indirect_dma_start(
                            out=ctxsh[0:128, :],
                            out_offset=IndirectOffsetOnAxis(sidxst[:, col:col + 1], 0),
                            in_=ctxbf[qq][:], in_offset=None)
                    p.emit("pool", ctxstore, reads=[f"ctxbf{qq}", "sidxst"], writes=["ctxsh"], fam="ctxst")

        for i in range(4):
            dma("pool", outw_sb[:, 4 * i:4 * (i + 1), :], outw[:, 4 * i:4 * (i + 1), :],
                "wout", writes=["Ksb", "Qsb", "Vsb", f"OW{i}"])
        p.barrier()
        p.mark_split()

        for i in range(4):
            dma("sp", outbrow_s[:], outbrow[0:1, i * 512:(i + 1) * 512], None,
                reads=["outbrow_done"], writes=["outbrow_s"])
            p.emit("pe", lambda eng: eng.matmul(bank_f(2), ones_f[:], outbrow_s[:],
                                                start=True, stop=True),
                   reads=["ones_f", "outbrow_s"], writes=["pb2"])
            p.emit("dve", lambda eng, i=i: eng.tensor_copy(outbB[:, i * 512:(i + 1) * 512],
                                                           bank_f(2)),
                   reads=["pb2", "outbrow_s"], writes=["outbB", "outbrow_done"], hz=True)

        for t in range(16):
            sl = t % 2
            for h in range(16):
                def ctld(eng, t=t, h=h, sl=sl):
                    col = t * 16 + h
                    return eng.indirect_dma_start(
                        out=ctt[sl][:, h, :], out_offset=None,
                        in_=ctxsh[0:128, :],
                        in_offset=IndirectOffsetOnAxis(sidxld[:, col:col + 1], 0))
                p.emit("pool", ctld, reads=["ctxsh", "sidxld"], writes=[f"ctt{sl}"], fam="ctld")
            for hd in range(16):
                bk = hd % 2
                p.emit("pe", lambda eng, hd=hd, bk=bk, sl=sl: eng.transpose(
                    bank_bf(bk), ctt[sl][:, hd, :], ident_bf[:]),
                    reads=[f"ctt{sl}", "ident_bf"], writes=[f"pb{bk}"])
                p.emit("dve", lambda eng, hd=hd, bk=bk, sl=sl: eng.tensor_copy(
                    ctf[sl][:, hd, :], bank_bf(bk)),
                    reads=[f"pb{bk}"], writes=[f"ctt{sl}"])

            def mm_out(eng, sl=sl):
                for hd in range(16):
                    for n in range(4):
                        inst = eng.matmul(PS[:, (4 + n) * 512:(5 + n) * 512],
                                          ctf[sl][:, hd, :],
                                          outw_sb[:, hd, n * 512:(n + 1) * 512],
                                          start=(hd == 0), stop=(hd == 15))
                return inst
            p.emit("pe", mm_out, reads=[f"ctt{sl}", "OW0", "OW1", "OW2", "OW3"],
                   writes=["pb4", "pb5", "pb6", "pb7"])
            dma("sp", xt[sl][:], x_own[t * 128:(t + 1) * 128, :], "xo", writes=[f"xt{sl}"])
            p.emit("dve", lambda eng, sl=sl: eng.tensor_tensor(
                x1t[sl][:], PS[:, 2048:4096], outbB[:], op=ALU.add),
                reads=["pb4", "pb5", "pb6", "pb7", "outbB"], writes=[f"x1t{sl}"])
            p.emit("dve", lambda eng, sl=sl: eng.tensor_tensor(
                x1t[sl][:], x1t[sl][:], xt[sl][:], op=ALU.add),
                reads=[f"xt{sl}", f"x1t{sl}"], writes=[f"x1t{sl}"])
            dma("sp", out[t * 128:(t + 1) * 128, :], x1t[sl][:], "x1st",
                reads=[f"x1t{sl}"],
                writes=[f"y{t // 8}_{t % 8}_0", f"y{t // 8}_{t % 8}_1"])
            ln_tile(x1t[sl][:], f"x1t{sl}", lnt[sl][:], f"lnt{sl}", sl)
            transpose16(lnt[sl][:], f"lnt{sl}", lnf[sl], f"lnf{sl}", slice(0, 128))
            dma("sp", lnx2_dram[:, :, t * 128:(t + 1) * 128], lnf[sl][:, :, :], "ln2st",
                reads=[f"lnf{sl}"], writes=["ln2dram"])

        p.barrier()

        NSUB = 8
        for b2 in range(2):
            tb = b2 * 1024
            dma("pool", LNX2[:, :, :], lnx2_dram[:, :, tb:tb + 1024], "ln2ld",
                reads=["ln2dram"], writes=["lnx2b"])
            for tt in range(8):
                sl = tt % 2

                def mm_gate(eng, tt=tt):
                    for dc in range(16):
                        inst = eng.matmul(bank_f(6, 16), LNX2[:, dc, tt * 128:(tt + 1) * 128],
                                          gate_s[:, dc, :], start=(dc == 0), stop=(dc == 15))
                    return inst
                p.emit("pe", mm_gate, reads=["lnx2b", "gate_s"], writes=["pb6"])
                p.emit("act", lambda eng, sl=sl: eng.activation(
                    sco[sl][:], bank_f(6, 16), AF.Sigmoid, bias=0.0, scale=1.0),
                    reads=["pb6"], writes=[f"sco{sl}"])

                sc = sco[sl]
                gmax = rscr[:, 0:4]
                m1 = rscr[:, 4:5]
                mk1 = rscr[:, 5:9]
                g2 = rscr[:, 9:13]
                m2 = rscr[:, 13:14]
                gmask = rscr[:, 14:18]
                masked = rscr[:, 18:34]
                ssum = rscr[:, 35:36]
                RT = ["rt"]
                p.emit("dve", lambda eng, sc=sc: eng.tensor_reduce(
                    gmax, sc.rearrange("p (g m) -> p g m", g=4), axis=AX.X, op=ALU.max),
                    reads=[f"sco{sl}"], writes=RT, hz=True)
                p.emit("dve", lambda eng: eng.reduce_max(m1, gmax, axis=AX.X),
                       reads=RT, writes=RT, hz=True)
                p.emit("dve", lambda eng: eng.tensor_scalar(mk1, gmax, m1, None, op0=ALU.is_ge),
                       reads=RT, writes=RT, hz=True)
                p.emit("dve", lambda eng: eng.scalar_tensor_tensor(
                    g2, mk1, -1e30, gmax, op0=ALU.mult, op1=ALU.add),
                    reads=RT, writes=RT, hz=True)
                p.emit("dve", lambda eng: eng.reduce_max(m2, g2, axis=AX.X),
                       reads=RT, writes=RT, hz=True)
                p.emit("dve", lambda eng: eng.tensor_scalar(gmask, gmax, m2, None, op0=ALU.is_ge),
                       reads=RT, writes=RT, hz=True)

                def maskmul(eng, sc=sc):
                    for g4 in range(4):
                        inst = eng.tensor_scalar(masked[:, g4 * 4:(g4 + 1) * 4],
                                                 sc[:, g4 * 4:(g4 + 1) * 4],
                                                 gmask[:, g4:g4 + 1], None, op0=ALU.mult)
                    return inst
                p.emit("dve", maskmul, reads=RT + [f"sco{sl}"], writes=RT, hz=True)
                p.emit("dve", lambda eng: eng.max(out=top8b[:], in_=masked),
                       reads=RT, writes=RT + ["top8b"], hz=True)
                p.emit("dve", lambda eng: eng.tensor_scalar(
                    masked, masked, top8b[:, 3:4], None, op0=ALU.is_ge),
                    reads=RT + ["top8b"], writes=RT, hz=True)
                p.emit("dve", lambda eng, sc=sc: eng.tensor_tensor(masked, sc, masked, op=ALU.mult),
                       reads=RT + [f"sco{sl}"], writes=RT, hz=True)
                p.emit("dve", lambda eng: eng.reduce_sum(ssum, masked, axis=AX.X),
                       reads=RT, writes=RT, hz=True)
                p.emit("dve", lambda eng: eng.reciprocal(ssum, ssum),
                       reads=RT, writes=RT, hz=True)
                p.emit("dve", lambda eng: eng.tensor_scalar(masked, masked, ssum, None, op0=ALU.mult),
                       reads=RT, writes=RT, hz=True)
                p.emit("pe", lambda eng: eng.transpose(PS[0:16, 3584:3712], rscr[:, 18:34],
                                                       ident_f[:]),
                       reads=["rt", "ident_f"], writes=["pb7"])
                p.emit("dve", lambda eng, tt=tt: eng.tensor_copy(
                    combf[0:16, tt * 128:(tt + 1) * 128], PS[0:16, 3584:3712]),
                    reads=["pb7"], writes=["combf"], hz=True)
            dma("sp", comb_dram[:, tb:tb + 1024], combf[:], "combst",
                reads=["combf"], writes=["combdram"])

            for e in range(NE + 1):
                shared = (e == NE)
                nIc = NSI if shared else NI
                w1src = ws1 if shared else w1[e]
                w3src = ws3 if shared else w3[e]
                if not shared:
                    dma("pool", combrow[:], comb_dram[e:e + 1, tb:tb + 1024], "combld",
                        reads=["combdram"], writes=["combrow"])
                    for i in range(2):
                        p.emit("pe", lambda eng, i=i: eng.matmul(
                            bank_f(6), ones_f[:],
                            combrow[0:1, i * 512:(i + 1) * 512], start=True, stop=True),
                            reads=["combrow", "ones_f"], writes=["pb6"])
                        p.emit("dve", lambda eng, i=i: eng.tensor_copy(
                            combB[:, i * 512:(i + 1) * 512], bank_f(6)),
                            reads=["pb6"], writes=["combB"], hz=True)
                for ic in range(nIc):
                    wsl = ic % 2
                    dma("pool", w1b[wsl][:, :, :], w1src[:, :, ic * 128:(ic + 1) * 128],
                        "w1", writes=[f"w1b{wsl}"])
                    dma("pool", w3b[wsl][:, :, :], w3src[:, :, ic * 128:(ic + 1) * 128],
                        "w3", writes=[f"w3b{wsl}"])

                    def mm_h(eng, wb, bko, wkey):
                        for dc in range(16):
                            for ns in range(2):
                                inst = eng.matmul(bank_f(bko + ns),
                                                  wb[:, dc, :],
                                                  LNX2[:, dc, ns * 512:(ns + 1) * 512],
                                                  start=(dc == 0), stop=(dc == 15))
                        return inst
                    p.emit("pe", functools.partial(mm_h, wb=w1b[wsl], bko=0, wkey=None),
                           reads=[f"w1b{wsl}", "lnx2b"], writes=["pb0", "pb1"])
                    p.emit("pe", functools.partial(mm_h, wb=w3b[wsl], bko=2, wkey=None),
                           reads=[f"w3b{wsl}", "lnx2b"], writes=["pb2", "pb3"])
                    ssl = ic % 2
                    p.emit("act", lambda eng, ssl=ssl: eng.activation(
                        s1b[ssl][:], PS[:, 0:1024], AF.Silu, bias=0.0, scale=1.0),
                        reads=["pb0", "pb1"], writes=[f"s1b{ssl}"])
                    if shared:
                        p.emit("dve", lambda eng, ic=ic, ssl=ssl: eng.tensor_tensor(
                            g_sb[:, ic, :], PS[:, 1024:2048], s1b[ssl][:], op=ALU.mult),
                            reads=["pb2", "pb3", f"s1b{ssl}"], writes=["g_sb"])
                    else:
                        p.emit("dve", lambda eng, ssl=ssl: eng.tensor_tensor(
                            t3f[ssl][:], PS[:, 1024:2048], combB[:], op=ALU.mult),
                            reads=["pb2", "pb3", "combB"], writes=[f"t3f{ssl}"])
                        p.emit("dve", lambda eng, ic=ic, ssl=ssl: eng.tensor_tensor(
                            g_sb[:, ic, :], t3f[ssl][:], s1b[ssl][:], op=ALU.mult),
                            reads=[f"t3f{ssl}", f"s1b{ssl}"], writes=["g_sb"])
                nH = 4 if shared else 2
                wdt = 512 if shared else 1024
                w2srcv = ws2 if shared else w2[e]
                w2bv = w2b_s if shared else w2b_r
                for hf in range(nH):
                    wsl = hf % 2
                    dma("pool", w2bv[wsl][:, :, :],
                        w2srcv.rearrange("a p j -> p a j")[:, :, hf * wdt:(hf + 1) * wdt],
                        "w2", writes=[f"w2b{wsl}"])
                    for sb_ in range(NSUB):
                        nbk = wdt // 512
                        ybk = 4 + (sb_ % 2) * nbk

                        def mm_y(eng, hf=hf, sb_=sb_, wsl=wsl, ybk=ybk, nIc=nIc, nbk=nbk,
                                 w2bv=w2bv):
                            for ic in range(nIc):
                                for n2 in range(nbk):
                                    inst = eng.matmul(
                                        bank_f(ybk + n2),
                                        g_sb[:, ic, sb_ * 128:(sb_ + 1) * 128],
                                        w2bv[wsl][:, ic, n2 * 512:(n2 + 1) * 512],
                                        start=(ic == 0), stop=(ic == nIc - 1))
                            return inst
                        ykeys = [f"pb{ybk + n2}" for n2 in range(nbk)]
                        p.emit("pe", mm_y, reads=["g_sb", f"w2b{wsl}"], writes=ykeys)
                        ysl = sb_ % 2
                        p.emit("dve", lambda eng, ybk=ybk, ysl=ysl, wdt=wdt: eng.tensor_copy(
                            ytile[ysl][:, 0:wdt], PS[:, ybk * 512: ybk * 512 + wdt]),
                            reads=ykeys, writes=[f"ytile{ysl}"])
                        colo = hf * wdt
                        dma("pool",
                            out[tb + sb_ * 128: tb + (sb_ + 1) * 128, colo:colo + wdt],
                            ytile[ysl][:, 0:wdt], "yacc",
                            reads=[f"ytile{ysl}"],
                            writes=[f"y{b2}_{sb_}_{colo // 1024}"], accum=True)

        if debug:
            for nm in dbg:
                dst, src = dbg[nm]
                dma("pool", dst, src, None,
                    reads=["ctxsh", "ln2dram", "qdram", "kdram", "vdram"])
            dma("pool", d_comb, comb_dram, None, reads=["combdram"])

        p.barrier()
        p.finalize_and_emit(nc, ctx)
    return nc


def _images(out_w, gate_w, W1, W2, W3, Ws1, Ws2, Ws3):
    f32 = np.float32
    imgs = {}
    imgs["outw"] = np.ascontiguousarray(out_w.reshape(16, 128, D).transpose(1, 0, 2))
    gt = gate_w.T
    imgs["gatew"] = np.ascontiguousarray(gt.reshape(16, 128, 16).transpose(1, 0, 2))
    for nm, W in (("w1", W1), ("w3", W3)):
        wt = W.transpose(0, 2, 1)
        imgs[nm] = np.ascontiguousarray(wt.reshape(NE, 16, 128, I_FF).transpose(0, 2, 1, 3))
    w2t = W2.transpose(0, 2, 1)
    imgs["w2"] = np.ascontiguousarray(w2t.reshape(NE, NI, 128, D))
    for nm, W in (("ws1", Ws1), ("ws3", Ws3)):
        wt = W.T
        imgs[nm] = np.ascontiguousarray(wt.reshape(16, 128, SH).transpose(1, 0, 2))
    imgs["ws2"] = np.ascontiguousarray(Ws2.T.reshape(NSI, 128, D))
    return {k: np.asarray(v, dtype=f32) for k, v in imgs.items()}


def _consts():
    c = {}
    c["identw"] = np.eye(128, dtype=np.float32)
    c["onesrow"] = np.ones((1, 128), dtype=np.float32)
    c["jrow"] = np.arange(512, dtype=np.float32)[None, :]
    c["p_idx"] = np.arange(128, dtype=np.float32)[:, None]
    dm = np.zeros((128, 4, 512), dtype=np.float32)
    jj = np.arange(128)[:, None]
    ii = np.arange(512)[None, :]
    for kk in range(4):
        dm[:, kk, :] = np.where(128 * kk + jj > ii, -30000.0, 0.0)
    c["diagm"] = dm
    dma_ = np.zeros((128, 4, 512), dtype=np.float32)
    qq_ = np.arange(128)[:, None]
    kl_ = np.arange(512)[None, :]
    for uu in range(4):
        dma_[:, uu, :] = np.where(kl_ > 128 * uu + qq_, -30000.0, 0.0)
    c["diagmA"] = dma_
    return c


def _fused_maps(x, qkv_w, qkv_b, out_b, imgs, consts):
    f32 = np.float32
    dmA = consts["diagmA"]
    dmB = consts["diagm"]
    maps = []
    for c in range(8):
        b, P, m = c // 4, (c % 4) // 2, c % 2
        heads = [2 * s_ + m for s_ in range(8)]
        cols = []
        for part in range(3):
            for h in heads:
                cols.append(qkv_w[:, part * 2048 + h * 128: part * 2048 + (h + 1) * 128])
        qw = np.concatenate(cols, axis=1)
        qkvw_img = np.ascontiguousarray(qw.reshape(16, 128, 3072).transpose(1, 0, 2))
        bcols = []
        for part in range(2):
            for h in heads:
                bcols.append(qkv_b[part * 2048 + h * 128: part * 2048 + (h + 1) * 128])
        qkvb_img = np.ascontiguousarray(np.stack(bcols, axis=1))
        vb = np.concatenate([qkv_b[2 * 2048 + h * 128: 2 * 2048 + (h + 1) * 128]
                             for h in heads])[None, :]
        sl = np.array([SLOPES[h] for h in heads], f32)
        sl16 = np.concatenate([sl, np.zeros(8, f32)])
        tp = (512.0 * (2 * np.arange(8) + P)).astype(f32)[None, :]
        uu = np.arange(32)
        blk = 8 * (uu // 4) + 4 * P + (uu % 4)
        myp = (128.0 * blk).astype(f32)[None, :]
        dA = np.zeros((128, 8, 512), f32)
        dB = np.zeros((128, 8, 512), f32)
        for u4 in range(4):
            if P == 0:
                dA[:, u4 * 2 + 0] = dmA[:, u4]
                dA[:, u4 * 2 + 1] = -30000.0
            else:
                dA[:, u4 * 2 + 0] = 0.0
                dA[:, u4 * 2 + 1] = dmA[:, u4]
        for kcr in range(8):
            if P == 0:
                dB[:, kcr] = dmB[:, kcr] if kcr < 4 else -30000.0
            else:
                dB[:, kcr] = 0.0 if kcr < 4 else dmB[:, kcr - 4]
        xo = np.concatenate([x[b, (2 * lt + P) * 512:(2 * lt + P) * 512 + 512]
                             for lt in range(4 * m, 4 * m + 4)], axis=0)
        pp = np.arange(128, dtype=np.int64)
        iq = np.empty((128, 8), np.int32)
        for s_ in range(8):
            iq[:, s_] = P * 1024 + s_ * 128 + pp
        ist = np.empty((128, 256), np.int32)
        for s_ in range(8):
            for t_ in range(8):
                for qq_ in range(4):
                    ist[:, s_ * 32 + t_ * 4 + qq_] = (m * 32768 + s_ * 4096
                                                      + t_ * 512 + qq_ * 128 + pp)
        ild = np.empty((128, 256), np.int32)
        for t_ in range(16):
            for h_ in range(16):
                ild[:, t_ * 16 + h_] = ((h_ % 2) * 32768 + (h_ // 2) * 4096
                                        + m * 2048 + t_ * 128 + pp)
        mp = dict(imgs)
        mp.update(dict(
            identw=consts["identw"], onesrow=consts["onesrow"],
            jrow=consts["jrow"], p_idx=consts["p_idx"],
            xb=x[b], x_own=np.ascontiguousarray(xo),
            qkvw=qkvw_img, qkvb=qkvb_img, vbrow=np.ascontiguousarray(vb),
            outbrow=out_b[None, :].copy(),
            slopes=np.tile(sl[None, :], (128, 1)),
            s128=(128.0 * sl)[None, :].copy(),
            s512=(512.0 * sl16)[None, :].copy(), s511=(511.0 * sl16)[None, :].copy(),
            tilepos=tp, mypos128=myp, diagA2=dA, diagB2=dB,
            idxq=iq, idxst=ist, idxld=ild,
        ))
        maps.append(mp)
    return maps


def _fused_assemble(results):
    f32 = np.float32
    outp = np.empty((B, S, D), f32)
    for c in range(8):
        b, P, m = c // 4, (c % 4) // 2, c % 2
        sh = np.asarray(results[c]["out"], f32)
        for i, lt in enumerate(range(4 * m, 4 * m + 4)):
            g = 2 * lt + P
            outp[b, g * 512:(g + 1) * 512] = sh[i * 512:(i + 1) * 512]
    return outp


def kernel(L0_attn_x, L0_attn_qkv_w, L0_attn_qkv_b, L0_attn_out_w, L0_attn_out_b,
           L0_ffn_gate_w, L0_ffn_W1, L0_ffn_W2, L0_ffn_W3,
           L0_ffn_Ws1, L0_ffn_Ws2, L0_ffn_Ws3, _debug=False, _ncs=None):
    f32 = np.float32
    x = np.ascontiguousarray(np.asarray(L0_attn_x, f32))
    qkv_w = np.asarray(L0_attn_qkv_w, f32)
    qkv_b = np.asarray(L0_attn_qkv_b, f32)
    out_w = np.asarray(L0_attn_out_w, f32)
    out_b = np.asarray(L0_attn_out_b, f32)
    gate_w = np.asarray(L0_ffn_gate_w, f32)
    W1 = np.asarray(L0_ffn_W1, f32)
    W2 = np.asarray(L0_ffn_W2, f32)
    W3 = np.asarray(L0_ffn_W3, f32)
    Ws1 = np.asarray(L0_ffn_Ws1, f32)
    Ws2 = np.asarray(L0_ffn_Ws2, f32)
    Ws3 = np.asarray(L0_ffn_Ws3, f32)

    imgs = _images(out_w, gate_w, W1, W2, W3, Ws1, Ws2, Ws3)
    consts = _consts()
    nc = _ncs if _ncs is not None else build_fused(debug=_debug)
    maps = _fused_maps(x, qkv_w, qkv_b, out_b, imgs, consts)
    res = run_bass_kernel_spmd(nc, maps, core_ids=list(range(8)))
    outp = _fused_assemble(res.results)
    if _debug:
        return outp, res
    return outp


# version 11
# speedup vs baseline: 1.1176x; 1.0028x over previous
import functools
import numpy as np
from contextlib import ExitStack

import concourse.bass as bass
import concourse.mybir as mybir
from concourse.bass import IndirectOffsetOnAxis
from concourse.bass_utils import run_bass_kernel_spmd
from concourse.aot_env import allow_ici_all_to_all

F32 = mybir.dt.float32
I32 = mybir.dt.int32
BF16 = mybir.dt.bfloat16
AF = mybir.ActivationFunctionType
ALU = mybir.AluOpType
AX = mybir.AxisListType

B = 2
S = 8192
D = 2048
NH = 16
DH = 128
NSLOT = 4
TCORE = 2048
NE = 16
I_FF = 1408
NI = 11
SH = 2816
NSI = 22
EPS = 1e-5
SLOPES = [np.float32(1.0 / 2.0 ** (i / 2.0)) for i in range(NH)]

ENGINES = ("pe", "dve", "act", "pool", "sp")
ENG_ATTR = {"pe": "tensor", "dve": "vector", "act": "scalar", "pool": "gpsimd", "sp": "sync"}


class _It:
    __slots__ = ("eng", "fn", "fam", "ford", "deps", "sig", "need", "gi", "hz")

    def __init__(self, eng, fn, fam, ford, gi, hz=False):
        self.eng, self.fn, self.fam, self.ford, self.gi = eng, fn, fam, ford, gi
        self.deps = []
        self.sig = None
        self.need = False
        self.hz = hz


class Prog:

    def __init__(self):
        self.items = []
        self.per_eng = {e: [] for e in ENGINES}
        self.res = {}
        self.fam_count = {}
        self.split_gi = None

    def mark_split(self):
        self.split_gi = len(self.items)

    def emit(self, eng, fn, reads=(), writes=(), fam=None, deps=(), hz=False):
        ford = None
        if fam is not None:
            self.fam_count[fam] = self.fam_count.get(fam, 0) + 1
            ford = self.fam_count[fam]
        it = _It(eng, fn, fam, ford, len(self.items), hz)
        lane = (eng, fam)
        dd = {}
        for k in reads:
            e = self.res.get(k)
            if e:
                for d in e["w"].values():
                    dd[d.gi] = d
        for k in writes:
            e = self.res.get(k)
            if e:
                for d in e["r"].values():
                    dd[d.gi] = d
                for d in e["w"].values():
                    dd[d.gi] = d
        for d in deps:
            dd[d.gi] = d
        dd.pop(it.gi, None)
        it.deps = list(dd.values())
        for k in reads:
            self.res.setdefault(k, {"w": {}, "r": {}})["r"][lane] = it
        for k in writes:
            e = self.res.setdefault(k, {"w": {}, "r": {}})
            e["w"] = {lane: it}
            e["r"] = {}
        self.items.append(it)
        self.per_eng[eng].append(it)
        return it

    def barrier(self):
        lasts = [self.per_eng[e][-1] for e in ENGINES if self.per_eng[e]]
        for e in ENGINES:
            self.emit(e, lambda eng: eng.nop(), deps=lasts)

    def finalize_and_emit(self, nc, ctx):
        for it in self.items:
            for d in it.deps:
                if d.eng == it.eng and d.fam is None and not d.hz:
                    continue
                d.need = True
        sems = {}
        for e in ENGINES:
            cnt = 0
            for it in self.per_eng[e]:
                if it.fam is None and it.need:
                    cnt += 1
                    it.sig = cnt
            if cnt:
                sems[("e", e)] = ctx.enter_context(nc.semaphore("sm_" + e))
        for f in self.fam_count:
            sems[("f", f)] = ctx.enter_context(nc.semaphore("sf_" + str(f)))

        def wait_list(it):
            w = {}
            for d in it.deps:
                if d.eng == it.eng and d.fam is None and not d.hz:
                    continue
                if d.fam is not None:
                    sk, val = ("f", d.fam), 16 * d.ford
                else:
                    sk, val = ("e", d.eng), d.sig
                if w.get(sk, 0) < val:
                    w[sk] = val
            return w

        waters = {e: {} for e in ENGINES}

        def emit_block(block, lo, hi):
            for e in ENGINES:
                items = [it for it in self.per_eng[e] if lo <= it.gi < hi]
                if not items:
                    continue

                def body(eng, items=items, ename=e):
                    water = waters[ename]
                    for it in items:
                        for sk, val in sorted(wait_list(it).items(), key=lambda kv: str(kv[0])):
                            if water.get(sk, 0) < val:
                                eng.wait_ge(sems[sk], val)
                                water[sk] = val
                        inst = it.fn(eng)
                        if it.fam is not None:
                            inst.then_inc(sems[("f", it.fam)], 16)
                        elif it.need:
                            inst.then_inc(sems[("e", ename)], 1)

                getattr(block, ENG_ATTR[e])(body)

        if self.split_gi is None:
            block = ctx.enter_context(nc.Block())
            emit_block(block, 0, len(self.items))
        else:
            with nc.Block() as b1:
                emit_block(b1, 0, self.split_gi)
            nc.all_core_barrier()
            with nc.Block() as b2:
                emit_block(b2, self.split_gi, len(self.items))


def build_nc(part="a", debug=False):
    nc = bass.Bass("TRN2", target_bir_lowering=False)
    p = Prog()

    def din(name, shape, dtype=F32):
        return nc.dram_tensor(name, list(shape), dtype, kind="ExternalInput").ap()

    identw = din("identw", [128, 128])
    onesrow = din("onesrow", [1, 128])
    dbg = {}
    if part in ("a", "p0"):
        xb = din("xb", [S, D])
        qkvw = din("qkvw", [128, 16, 12 * 128])
        qkvb = din("qkvb", [128, 12])
        vbrow = din("vbrow", [1, 512])
        jrow = din("jrow", [1, 512])
        p_idx = din("p_idx", [128, 1])
        slopes = din("slopes", [128, 4])
        s128 = din("s128", [1, 4])
        s512 = din("s512", [1, 4])
        s511 = din("s511", [1, 4])
        diagm = din("diagm", [128, 4, 512])
        diagmA = din("diagmA", [128, 4, 512])
        ctx_dram = nc.dram_tensor("ctxout", [S, NSLOT * DH], BF16, kind="ExternalOutput").ap()
        q_dram = nc.dram_tensor("q_dram", [NSLOT * DH, S], BF16, kind="Internal").ap()
        k_dram = nc.dram_tensor("k_dram", [NSLOT * DH, S], BF16, kind="Internal").ap()
        v_dram = nc.dram_tensor("v_dram", [S, NSLOT * DH], BF16, kind="Internal").ap()
        if debug:
            for nm, t in (("d_q", q_dram), ("d_k", k_dram), ("d_v", v_dram)):
                dbg[nm] = (nc.dram_tensor(nm, list(t.shape), BF16, kind="ExternalOutput").ap(), t)
    else:
        x_own = din("x_own", [TCORE, D])
        ctx_recv = din("ctx_recv", [S, NSLOT * DH], BF16)
        outw = din("outw", [128, 16, D])
        outbrow = din("outbrow", [1, D])
        gatew = din("gatew", [128, 16, 16])
        w1 = din("w1", [NE, 128, 16, I_FF])
        w3 = din("w3", [NE, 128, 16, I_FF])
        w2 = din("w2", [NE, NI, 128, D])
        ws1 = din("ws1", [128, 16, SH])
        ws3 = din("ws3", [128, 16, SH])
        ws2 = din("ws2", [NSI, 128, D])
        out = nc.dram_tensor("out", [TCORE, D], F32, kind="ExternalOutput").ap()
        lnx2_dram = nc.dram_tensor("lnx2_dram", [128, 16, TCORE], BF16, kind="Internal").ap()
        comb_dram = nc.dram_tensor("comb_dram", [16, TCORE], F32, kind="Internal").ap()
        if debug:
            dbg["d_ln2"] = (nc.dram_tensor("d_ln2", [128, 16, TCORE], BF16,
                                           kind="ExternalOutput").ap(), lnx2_dram)
            g_dump = nc.dram_tensor("d_g", [128, 22528], BF16, kind="ExternalOutput").ap()
            comb_dump = nc.dram_tensor("d_comb", [16, TCORE], F32, kind="ExternalOutput").ap()

    with ExitStack() as ctx:
        def alloc(name, shape, dtype):
            return nc.alloc_sbuf_tensor(name, list(shape), dtype).ap()

        POOLN = 90112
        POOL = nc.alloc_sbuf_tensor("POOL", [128, POOLN], mybir.dt.uint8).ap()

        def sub(off, shape, dtype):
            es = 2 if dtype == BF16 else 4
            n = int(np.prod(shape[1:])) * es
            assert off % 4 == 0 and off + n <= POOLN, (off, n)
            ap = POOL[0:shape[0], off:off + n].bitcast(dtype)
            if len(shape) == 3:
                ap = ap.rearrange("p (a b) -> p a b", a=shape[1])
            return ap

        xt = [sub(0, [128, D], F32), sub(8192, [128, D], F32)]
        sq = sub(16384, [128, D], F32)
        lnt = [sub(24576, [128, D], BF16), sub(28672, [128, D], BF16)]
        qkt = [sub(32768 + 1024 * i, [128, 512], BF16) for i in range(4)]
        vt = [sub(36864, [128, 512], BF16), sub(37888, [128, 512], BF16)]
        vbB = sub(36864, [128, 512], F32)
        qkvw_sb = sub(38912, [128, 16, 1536], BF16)
        vbrow_s = sub(88064, [1, 512], F32)
        outbrow_s = sub(88064, [1, 512], F32)
        pt = [sub(38912, [128, 512], BF16), sub(39936, [128, 512], BF16),
              sub(59392, [128, 512], BF16), sub(60416, [128, 512], BF16)]
        ctt = [sub(40960, [128, 16, 128], BF16), sub(45056, [128, 16, 128], BF16)]
        lnf = [sub(49152, [128, 16, 128], BF16), sub(53248, [128, 16, 128], BF16)]
        x1t = [sub(57344, [128, D], F32), sub(65536, [128, D], F32)]
        outbB = sub(73728, [128, D], F32)
        diag_s = sub(81920, [128, 4, 512], BF16)
        diagA_s = sub(86016, [128, 4, 512], BF16)
        ctxbf = [sub(57344 + 256 * i, [128, 128], BF16) for i in range(4)]
        negirow = sub(58368, [1, 512], BF16)
        ctf = ctt
        LNX2 = sub(0, [128, 16, 1024], BF16)
        s1b = [sub(32768, [128, 1024], BF16), sub(34816, [128, 1024], BF16)]
        t3f = [sub(36864, [128, 1024], F32), sub(40960, [128, 1024], F32)]
        ytile = [sub(45056, [128, 1024], F32), sub(49152, [128, 1024], F32)]
        combf = sub(53248, [16, 1024], F32)

        ident_bf = alloc("ident_bf", [128, 128], BF16)
        ident_f = alloc("ident_f", [128, 128], F32)
        ones_f = alloc("ones_f", [1, 128], F32)
        ones_bf = alloc("ones_bf", [1, 128], BF16)
        jrow_s = alloc("jrow_s", [1, 512], F32)
        pidx_s = alloc("pidx_s", [128, 1], F32)
        slopes_s = alloc("slopes_s", [128, 4], F32)
        s128_s = alloc("s128_s", [1, 4], F32)
        s512_s = alloc("s512_s", [1, 4], F32)
        s511_s = alloc("s511_s", [1, 4], F32)
        qkvb_s = alloc("qkvb_s", [128, 12], F32)
        gate_s = alloc("gate_s", [128, 16, 16], BF16)
        biasb = [alloc(f"biasb{i}", [128, 1], F32) for i in range(4)]
        mhat2 = alloc("mhat2", [128, 16], F32)
        zr4 = alloc("zr4", [128, 4], F32)
        stat = [alloc(f"stat{i}", [128, 8], F32) for i in range(2)]
        top8b = alloc("top8b", [128, 8], F32)
        mhat = alloc("mhat", [128, 64], F32)
        mcol = alloc("mcol", [64, 1], F32)
        drow = alloc("drow", [1, 16], F32)
        alibi_loc = alloc("alibi_loc", [1, 512], BF16)
        crow = alloc("crow", [1, 64], F32)
        adjrow = alloc("adjrow", [1, 16], F32)
        adjB = alloc("adjBb", [128, 16], F32)
        sp_col = alloc("sp_col", [128, 1], F32)
        alibi_p = alloc("alibi_p", [128, 64], F32)
        negD = alloc("negD", [128, 16], F32)
        sco = [alloc(f"sco{i}", [128, 16], F32) for i in range(2)]
        rscr = alloc("rscr", [128, 40], F32)
        combrow = alloc("combrow", [1, 1024], F32)
        sidxst = sub(32768, [128, 256], I32)
        sidxld = sub(33792, [128, 256], I32)
        sidxq = sub(34816, [128, 8], I32)

        R1 = alloc("R1", [128, 22528], BF16)
        R2 = alloc("R2", [128, 32768], BF16)

        PS = nc.alloc_psum_tensor("PS", [128, 4096], F32).ap()
        PSb = PS.bitcast(BF16)

        def bank_f(i, n=512):
            return PS[:, i * 512: i * 512 + n]

        def bank_bf(i, n=128):
            return PSb[:, i * 1024: i * 1024 + n]

        lnx_fm = R2.rearrange("p (dc t) -> p dc t", dc=16)
        K_sb = R2[:, 0:8192]
        Q_sb = R2[:, 8192:16384]
        V_sb = R2[:, 16384:16384 + 64 * 129].rearrange("p (c e) -> p c e", c=64)
        outw_sb = R2.rearrange("p (hd t) -> p hd t", hd=16)
        g_sb = R2[:, 0:22528].rearrange("p (ic t) -> p ic t", ic=22)
        w1b = [R2[:, 22528 + i * 2048: 22528 + (i + 1) * 2048].rearrange("p (dc j) -> p dc j", dc=16)
               for i in range(2)]
        w3b = [R2[:, 26624 + i * 2048: 26624 + (i + 1) * 2048].rearrange("p (dc j) -> p dc j", dc=16)
               for i in range(2)]
        combB = R2[:, 30720:32768].bitcast(F32)
        w2b_r = [R1[:, i * 11264:(i + 1) * 11264].rearrange("p (ic j) -> p ic j", ic=NI) for i in range(2)]
        w2b_s = [R1[:, i * 11264:(i + 1) * 11264].rearrange("p (ic j) -> p ic j", ic=NSI) for i in range(2)]

        _fam_ctr = [0]

        def dma(q, dst, src, fam, reads=(), writes=(), accum=False):
            if fam is None:
                _fam_ctr[0] += 1
                fam = f"u{_fam_ctr[0]}"
            if accum:
                fn = lambda eng, dst=dst, src=src: eng.dma_start(out=dst, in_=src, accum_op=ALU.add)
            else:
                fn = lambda eng, dst=dst, src=src: eng.dma_start(out=dst, in_=src)
            return p.emit(q, fn, reads=reads, writes=writes, fam=fam)

        dma("pool", ident_bf[:], identw, None, writes=["ident_bf"])
        dma("sp", ident_f[:], identw, None, writes=["ident_f"])
        dma("sp", ones_f[:], onesrow, None, writes=["ones_f"])
        dma("pool", ones_bf[:], onesrow, None, writes=["ones_bf"])
        if part in ("a", "p0"):
            dma("sp", jrow_s[:], jrow, None, writes=["jrow_s"])
            dma("sp", pidx_s[:], p_idx, None, writes=["pidx_s"])
            dma("sp", slopes_s[:], slopes, None, writes=["slopes_s"])
            dma("sp", s128_s[:], s128, None, writes=["s128_s"])
            dma("sp", s512_s[:], s512, None, writes=["s512_s"])
            dma("sp", s511_s[:], s511, None, writes=["s511_s"])
            dma("sp", qkvb_s[:], qkvb, None, writes=["qkvb_s"])
            dma("sp", vbrow_s[:], vbrow, None, writes=["vbrow_s"])
            for i in range(4):
                dma("pool", qkvw_sb[:, 4 * i:4 * (i + 1), :], qkvw[:, 4 * i:4 * (i + 1), :],
                    "wqkv", writes=[f"R1_{i}"])
            p.emit("pe", lambda eng: eng.matmul(bank_f(6), ones_f[:], vbrow_s[:],
                                                start=True, stop=True),
                   reads=["ones_f", "vbrow_s"], writes=["pb6"])
            p.emit("dve", lambda eng: eng.tensor_copy(vbB[:], bank_f(6)),
                   reads=["pb6"], writes=["vbB"], hz=True)
        else:
            dma("pool", gate_s[:], gatew, None, writes=["gate_s"])
            for i in range(4):
                dma("pool", outw_sb[:, 4 * i:4 * (i + 1), :], outw[:, 4 * i:4 * (i + 1), :],
                    "wout", writes=[f"OW{i}"])

        def ln_tile(xin, xkey, outbf, outkey, sl):
            st = stat[sl]
            s1, nm, v1, sd, rstd, mr = (st[:, i:i + 1] for i in range(6))
            kst = f"stat{sl}"
            p.emit("dve", lambda eng: eng.reduce_sum(s1, xin, axis=AX.X),
                   reads=[xkey], writes=[kst], hz=True)
            p.emit("dve", lambda eng: eng.tensor_scalar(nm, s1, -1.0 / D, None, op0=ALU.mult),
                   reads=[kst], writes=[kst], hz=True)
            p.emit("act", lambda eng: eng.activation(sq[:], xin, AF.Square, bias=nm, scale=1.0),
                   reads=[xkey, kst], writes=["sq"])
            p.emit("dve", lambda eng: eng.reduce_sum(v1, sq[:], axis=AX.X),
                   reads=["sq"], writes=[kst], hz=True)
            p.emit("dve", lambda eng: eng.tensor_scalar(v1, v1, 1.0 / D, EPS,
                                                        op0=ALU.mult, op1=ALU.add),
                   reads=[kst], writes=[kst], hz=True)
            p.emit("act", lambda eng: eng.activation(sd, v1, AF.Sqrt),
                   reads=[kst], writes=[kst], hz=True)
            p.emit("dve", lambda eng: eng.reciprocal(rstd, sd), reads=[kst], writes=[kst], hz=True)
            p.emit("dve", lambda eng: eng.tensor_tensor(mr, nm, rstd, op=ALU.mult),
                   reads=[kst], writes=[kst], hz=True)
            p.emit("dve", lambda eng: eng.tensor_scalar(outbf, xin, nm, rstd,
                                                        op0=ALU.add, op1=ALU.mult),
                   reads=[xkey, kst], writes=[outkey])

        def transpose16(src_bf, srckey, dst3, dstkey, tslice):
            for dc in range(16):
                bk = 4 + (dc % 4)
                p.emit("pe", lambda eng, dc=dc, bk=bk: eng.transpose(
                    bank_bf(bk), src_bf[:, dc * 128:(dc + 1) * 128], ident_bf[:]),
                    reads=[srckey, "ident_bf"], writes=[f"pb{bk}"])
                p.emit("dve", lambda eng, dc=dc, bk=bk: eng.tensor_copy(
                    dst3[:, dc, tslice], bank_bf(bk)),
                    reads=[f"pb{bk}"], writes=[dstkey])

        for slab in (range(4) if part in ("a", "p0") else []):
            base = slab * 2048
            for t in range(16):
                sl = t % 2
                dma("pool", xt[sl][:], xb[base + t * 128: base + (t + 1) * 128, :], "xb",
                    writes=[f"xt{sl}"])
                ln_tile(xt[sl][:], f"xt{sl}", lnt[sl][:], f"lnt{sl}", sl)
                transpose16(lnt[sl][:], f"lnt{sl}", lnx_fm, "lnxfm",
                            slice(t * 128, (t + 1) * 128))
            for f in range(8):
                for ts in range(4):
                    bk = (f * 4 + ts) % 2
                    sl = (f * 4 + ts) % 2

                    def mm_qk(eng, f=f, ts=ts, bk=bk):
                        for dc in range(16):
                            inst = eng.matmul(bank_f(bk),
                                              qkvw_sb[:, dc, f * 128:(f + 1) * 128],
                                              lnx_fm[:, dc, ts * 512:(ts + 1) * 512],
                                              start=(dc == 0), stop=(dc == 15))
                        return inst
                    p.emit("pe", mm_qk, reads=["lnxfm", "R1_0", "R1_1", "R1_2", "R1_3"],
                           writes=[f"pb{bk}"])
                    p.emit("act", lambda eng, f=f, bk=bk, sl=sl: eng.activation(
                        qkt[sl][:], bank_f(bk), AF.Identity, bias=qkvb_s[:, f:f + 1], scale=1.0),
                        reads=[f"pb{bk}", "qkvb_s"], writes=[f"qkt{sl}"])
                    dst = q_dram if f < 4 else k_dram
                    dkey = "qdram" if f < 4 else "kdram"
                    frow = (f % 4) * 128
                    dma("sp", dst[frow:frow + 128, base + ts * 512: base + (ts + 1) * 512],
                        qkt[sl][:], "qkst", reads=[f"qkt{sl}"], writes=[dkey])
            for t in range(16):
                bk = 2 + t % 2
                sl = t % 2

                def mm_v(eng, t=t, bk=bk):
                    for dc in range(16):
                        inst = eng.matmul(bank_f(bk),
                                          lnx_fm[:, dc, t * 128:(t + 1) * 128],
                                          qkvw_sb[:, dc, 1024:1536],
                                          start=(dc == 0), stop=(dc == 15))
                    return inst
                p.emit("pe", mm_v, reads=["lnxfm", "R1_0", "R1_1", "R1_2", "R1_3"],
                       writes=[f"pb{bk}"])
                p.emit("dve", lambda eng, bk=bk, sl=sl: eng.tensor_tensor(
                    vt[sl][:], bank_f(bk), vbB[:], op=ALU.add),
                    reads=[f"pb{bk}", "vbB"], writes=[f"vt{sl}"])
                dma("sp", v_dram[base + t * 128: base + (t + 1) * 128, :], vt[sl][:],
                    "vst", reads=[f"vt{sl}"], writes=["vdram"])

        if part in ("a", "p0"):
            p.barrier()

        if part == "a":
            dma("pool", diag_s[:], diagm, None, writes=["diag_s"])
            dma("pool", diagA_s[:], diagmA, None, writes=["diagA_s"])
            p.emit("dve", lambda eng: eng.memset(V_sb[:, :, 128:129], 1.0), writes=["Vsb"])

        for s in (range(NSLOT) if part == "a" else []):
            dma("sp", K_sb, k_dram[s * 128:(s + 1) * 128, :], "kqv",
                reads=["kdram"], writes=["Ksb"])
            dma("sp", Q_sb, q_dram[s * 128:(s + 1) * 128, :], "kqv",
                reads=["qdram"], writes=["Qsb"])
            dma("sp", V_sb[:, :, 0:128],
                v_dram.rearrange("(c p) d -> p c d", p=128)[:, :, s * 128:(s + 1) * 128],
                "kqv", reads=["vdram"], writes=["Vsb"])

            p.emit("act", lambda eng, s=s: eng.activation(
                alibi_loc[:], jrow_s[:], AF.Identity, bias=0.0, scale=slopes_s[0:1, s:s + 1]),
                reads=["jrow_s", "slopes_s"], writes=["aloc"])
            if s < 2:
                p.emit("dve", lambda eng, s=s: eng.tensor_scalar(
                    negirow[:], jrow_s[:], slopes_s[0:1, s:s + 1], -1.0,
                    op0=ALU.mult, op1=ALU.mult),
                    reads=["jrow_s", "slopes_s"], writes=["negi"], hz=True)
            p.emit("dve", lambda eng, s=s: eng.tensor_scalar(
                crow[:], jrow_s[0:1, 0:64], s128_s[0:1, s:s + 1], None, op0=ALU.mult),
                reads=["jrow_s", "s128_s"], writes=["crow"], hz=True)
            p.emit("dve", lambda eng, s=s: eng.tensor_scalar(
                sp_col[:], pidx_s[:], slopes_s[:, s:s + 1], None, op0=ALU.mult),
                reads=["pidx_s", "slopes_s"], writes=["sp_col"], hz=True)
            p.emit("pe", lambda eng: eng.matmul(bank_f(2, 64), ones_f[:], crow[:],
                                                start=True, stop=True),
                   reads=["ones_f", "crow"], writes=["pb2"])
            p.emit("dve", lambda eng: eng.tensor_scalar(
                alibi_p[:], bank_f(2, 64), sp_col[:], None, op0=ALU.add),
                reads=["pb2", "sp_col"], writes=["alibi_p"], hz=True)
            if s < 2:
                p.emit("dve", lambda eng, s=s: eng.tensor_scalar(
                    adjrow[:], jrow_s[0:1, 0:16], s512_s[0:1, s:s + 1], None, op0=ALU.mult),
                    reads=["jrow_s", "s512_s"], writes=["adjrow"], hz=True)
            else:
                p.emit("dve", lambda eng, s=s: eng.tensor_scalar(
                    adjrow[:], jrow_s[0:1, 0:16], s512_s[0:1, s:s + 1],
                    s511_s[0:1, s:s + 1], op0=ALU.mult, op1=ALU.add),
                    reads=["jrow_s", "s512_s", "s511_s"], writes=["adjrow"], hz=True)
            p.emit("pe", lambda eng: eng.matmul(bank_f(2, 16), ones_f[:], adjrow[:],
                                                start=True, stop=True),
                   reads=["ones_f", "adjrow"], writes=["pb2"])
            p.emit("dve", lambda eng: eng.tensor_copy(adjB[:], bank_f(2, 16)),
                   reads=["pb2"], writes=["adjB"], hz=True)

            for u in range(64):
                nch = u // 4 + 1
                for c in range(nch):
                    bk = (u + c) % 2

                    def mm_s(eng, u=u, c=c, bk=bk, s=s):
                        if s < 2:
                            eng.matmul(bank_f(bk), Q_sb[:, u * 128:(u + 1) * 128],
                                       K_sb[:, c * 512:(c + 1) * 512], start=True, stop=False)
                            return eng.matmul(bank_f(bk), ones_bf[:], alibi_loc[:],
                                              start=False, stop=True)
                        return eng.matmul(bank_f(bk), Q_sb[:, u * 128:(u + 1) * 128],
                                          K_sb[:, c * 512:(c + 1) * 512], start=True, stop=True)
                    p.emit("pe", mm_s, reads=["Qsb", "Ksb", "aloc", "ones_bf"],
                           writes=[f"pb{bk}"])
                    if s < 2 and c == nch - 1:
                        p.emit("dve", lambda eng, u=u, bk=bk: eng.tensor_tensor(
                            bank_f(bk), bank_f(bk), diagA_s[:, u % 4, :], op=ALU.add),
                            reads=["diagA_s"], writes=[f"pb{bk}"])
                    p.emit("dve", lambda eng, c=c, bk=bk: eng.reduce_max(
                        mhat2[:, c:c + 1], bank_f(bk), axis=AX.X),
                        reads=[f"pb{bk}"], writes=[f"mh{c}"], hz=True)
                p.emit("dve", lambda eng, nch=nch, c_lo=c_lo: eng.tensor_tensor(
                    mhat2[:, c_lo:nch], mhat2[:, c_lo:nch], adjB[:, c_lo:nch], op=ALU.add),
                    reads=["adjB"] + [f"mh{c}" for c in range(c_lo, nch)],
                    writes=[f"mh{c}" for c in range(c_lo, nch)], hz=True)
                p.emit("dve", lambda eng, u=u, nch=nch, c_lo=c_lo: eng.reduce_max(
                    mhat[:, u:u + 1], mhat2[:, c_lo:nch], axis=AX.X),
                    reads=[f"mh{c}" for c in range(c_lo, nch)], writes=["mhat"], hz=True)
                if s < 2:
                    p.emit("dve", lambda eng, u=u: eng.tensor_tensor(
                        mhat[:, u:u + 1], mhat[:, u:u + 1], alibi_p[:, u:u + 1],
                        op=ALU.subtract),
                        reads=["alibi_p", "mhat"], writes=["mhat"], hz=True)

            p.emit("pe", lambda eng: eng.transpose(PS[0:64, 1024:1152], mhat[:, 0:64], ident_f[:]),
                   reads=["mhat", "ident_f"], writes=["pb2"])
            p.emit("dve", lambda eng: eng.reduce_max(mcol[:], PS[0:64, 1024:1152], axis=AX.X),
                   reads=["pb2"], writes=["mcol"], hz=True)
            p.emit("pe", lambda eng: eng.transpose(PS[0:1, 1536:1600], mcol[0:64, 0:1],
                                                   ident_f[0:64, 0:64]),
                   reads=["mcol", "ident_f"], writes=["pb3"])
            p.emit("dve", lambda eng: eng.tensor_reduce(
                drow[:], PS[0:1, 1536:1600].rearrange("p (a b) -> p a b", a=16),
                axis=AX.X, op=ALU.max),
                reads=["pb3"], writes=["drow"], hz=True)
            if s < 2:
                p.emit("dve", lambda eng: eng.tensor_tensor(
                    drow[:], drow[:], adjrow[0:1, 0:16], op=ALU.add),
                    reads=["drow", "adjrow"], writes=["drow"], hz=True)
            p.emit("dve", lambda eng: eng.tensor_scalar(drow[:], drow[:], -1.0, None, op0=ALU.mult),
                   reads=["drow"], writes=["drow"], hz=True)
            p.emit("pe", lambda eng: eng.matmul(bank_f(2, 16), ones_f[:], drow[:],
                                                start=True, stop=True),
                   reads=["ones_f", "drow"], writes=["pb2"])
            p.emit("dve", lambda eng: eng.tensor_copy(negD[:], bank_f(2, 16)),
                   reads=["pb2"], writes=["negD"], hz=True)

            for t in range(16):
                nkc = 4 * t + 4
                for kc in range(nkc):
                    bk = kc % 2

                    def mm_st(eng, t=t, kc=kc, bk=bk, s=s):
                        if s < 2:
                            eng.matmul(bank_f(bk), K_sb[:, kc * 128:(kc + 1) * 128],
                                       Q_sb[:, t * 512:(t + 1) * 512], start=True, stop=False)
                            return eng.matmul(bank_f(bk), ones_bf[:], negirow[:],
                                              start=False, stop=True)
                        return eng.matmul(bank_f(bk), K_sb[:, kc * 128:(kc + 1) * 128],
                                          Q_sb[:, t * 512:(t + 1) * 512], start=True, stop=True)
                    p.emit("pe", mm_st, reads=["Ksb", "Qsb", "negi", "ones_bf"],
                           writes=[f"pb{bk}"])
                    if kc >= 4 * t:
                        p.emit("dve", lambda eng, t=t, kc=kc, bk=bk: eng.tensor_tensor(
                            bank_f(bk), bank_f(bk), diag_s[:, kc - 4 * t, :], op=ALU.add),
                            reads=["diag_s"], writes=[f"pb{bk}"])
                    psl = kc % 4
                    p.emit("act", lambda eng, bk=bk, kc=kc, psl=psl, bT=bT: eng.activation(
                        pt[psl][:], bank_f(bk), AF.Exp, bias=bT[:, kc:kc + 1], scale=1.0),
                        reads=[f"pb{bk}", f"biasT{t % 2}"], writes=[f"pt{psl}"])

                    if debug and s == 0 and t == 0 and kc == 0:
                        dma("sp", d_pt, pt[psl][:], None, reads=[f"pt{psl}"])
                        dma("sp", d_bias, bT[:, 0:1], None, reads=[f"biasT{t % 2}"])

                    def pv(eng, kc=kc, psl=psl, nkc=nkc, kc_lo=kc_lo):
                        for qq in range(4):
                            inst = eng.matmul(PS[:, (4 + qq) * 512:(4 + qq) * 512 + 129],
                                              pt[psl][:, qq * 128:(qq + 1) * 128],
                                              V_sb[:, kc, :],
                                              start=(kc == kc_lo), stop=(kc == nkc - 1))
                        return inst
                    p.emit("pe", pv, reads=[f"pt{psl}", "Vsb"],
                           writes=["pb4", "pb5", "pb6", "pb7"])
                zcols = PS[:, 2048:4096].rearrange("p (q x) -> p q x", q=4)[:, :, 128:129]
                p.emit("dve", lambda eng, zcols=zcols: eng.tensor_scalar(
                    zr4[:, :, None], zcols, 1e-37, None, op0=ALU.max),
                    reads=["pb4", "pb5", "pb6", "pb7"], writes=["zr4"], hz=True)
                p.emit("dve", lambda eng: eng.reciprocal(zr4[:], zr4[:]),
                       reads=["zr4"], writes=["zr4"], hz=True)
                for qq in range(4):
                    p.emit("dve", lambda eng, qq=qq: eng.tensor_scalar(
                        ctxbf[qq][:], PS[:, (4 + qq) * 512:(4 + qq) * 512 + 128],
                        zr4[:, qq:qq + 1], None, op0=ALU.mult),
                        reads=[f"pb{4 + qq}", "zr4"], writes=[f"ctxbf{qq}"])
                    dma("sp", ctx_dram[t * 512 + qq * 128: t * 512 + (qq + 1) * 128,
                                       s * 128:(s + 1) * 128],
                        ctxbf[qq][:], "ctxst", reads=[f"ctxbf{qq}"], writes=["ctxdram"])

        for i in (range(4) if part == "b" else []):
            dma("sp", outbrow_s[:], outbrow[0:1, i * 512:(i + 1) * 512], None,
                reads=["outbrow_done"], writes=["outbrow_s"])
            p.emit("pe", lambda eng: eng.matmul(bank_f(2), ones_f[:], outbrow_s[:],
                                                start=True, stop=True),
                   reads=["ones_f", "outbrow_s"], writes=["pb2"])
            p.emit("dve", lambda eng, i=i: eng.tensor_copy(outbB[:, i * 512:(i + 1) * 512],
                                                           bank_f(2)),
                   reads=["pb2", "outbrow_s"], writes=["outbB", "outbrow_done"], hz=True)

        for t in (range(16) if part == "b" else []):
            sl = t % 2
            for h in range(16):
                dma("sp", ctt[sl][:, h, :],
                    ctx_recv.rearrange("(r tt) d -> r tt d", r=4)[
                        h % 4, t * 128:(t + 1) * 128, (h // 4) * 128:(h // 4 + 1) * 128],
                    "ctld", reads=["ctxrecv"], writes=[f"ctt{sl}"])
            for hd in range(16):
                bk = hd % 2
                p.emit("pe", lambda eng, hd=hd, bk=bk, sl=sl: eng.transpose(
                    bank_bf(bk), ctt[sl][:, hd, :], ident_bf[:]),
                    reads=[f"ctt{sl}", "ident_bf"], writes=[f"pb{bk}"])
                p.emit("dve", lambda eng, hd=hd, bk=bk, sl=sl: eng.tensor_copy(
                    ctf[sl][:, hd, :], bank_bf(bk)),
                    reads=[f"pb{bk}"], writes=[f"ctt{sl}"])

            def mm_out(eng, sl=sl):
                for hd in range(16):
                    for n in range(4):
                        inst = eng.matmul(PS[:, (4 + n) * 512:(5 + n) * 512],
                                          ctf[sl][:, hd, :],
                                          outw_sb[:, hd, n * 512:(n + 1) * 512],
                                          start=(hd == 0), stop=(hd == 15))
                return inst
            p.emit("pe", mm_out, reads=[f"ctt{sl}", "OW0", "OW1", "OW2", "OW3"],
                   writes=["pb4", "pb5", "pb6", "pb7"])
            dma("sp", xt[sl][:], x_own[t * 128:(t + 1) * 128, :], "xo", writes=[f"xt{sl}"])
            p.emit("dve", lambda eng, sl=sl: eng.tensor_tensor(
                x1t[sl][:], PS[:, 2048:4096], outbB[:], op=ALU.add),
                reads=["pb4", "pb5", "pb6", "pb7", "outbB"], writes=[f"x1t{sl}"])
            p.emit("dve", lambda eng, sl=sl: eng.tensor_tensor(
                x1t[sl][:], x1t[sl][:], xt[sl][:], op=ALU.add),
                reads=[f"xt{sl}", f"x1t{sl}"], writes=[f"x1t{sl}"])
            dma("sp", out[t * 128:(t + 1) * 128, :], x1t[sl][:], "x1st",
                reads=[f"x1t{sl}"],
                writes=[f"y{t // 8}_{t % 8}_0", f"y{t // 8}_{t % 8}_1"])
            ln_tile(x1t[sl][:], f"x1t{sl}", lnt[sl][:], f"lnt{sl}", sl)
            transpose16(lnt[sl][:], f"lnt{sl}", lnf[sl], f"lnf{sl}", slice(0, 128))
            dma("sp", lnx2_dram[:, :, t * 128:(t + 1) * 128], lnf[sl][:, :, :], "ln2st",
                reads=[f"lnf{sl}"], writes=["ln2dram"])

        if part == "b":
            p.barrier()

        NSUB = 8
        for b2 in (range(2) if part == "b" else []):
            tb = b2 * 1024
            dma("pool", LNX2[:, :, :], lnx2_dram[:, :, tb:tb + 1024], "ln2ld",
                reads=["ln2dram"], writes=["lnx2b"])
            for tt in range(8):
                sl = tt % 2

                def mm_gate(eng, tt=tt):
                    for dc in range(16):
                        inst = eng.matmul(bank_f(6, 16), LNX2[:, dc, tt * 128:(tt + 1) * 128],
                                          gate_s[:, dc, :], start=(dc == 0), stop=(dc == 15))
                    return inst
                p.emit("pe", mm_gate, reads=["lnx2b", "gate_s"], writes=["pb6"])
                p.emit("act", lambda eng, sl=sl: eng.activation(
                    sco[sl][:], bank_f(6, 16), AF.Sigmoid, bias=0.0, scale=1.0),
                    reads=["pb6"], writes=[f"sco{sl}"])

                sc = sco[sl]
                gmax = rscr[:, 0:4]
                m1 = rscr[:, 4:5]
                mk1 = rscr[:, 5:9]
                g2 = rscr[:, 9:13]
                m2 = rscr[:, 13:14]
                gmask = rscr[:, 14:18]
                masked = rscr[:, 18:34]
                ssum = rscr[:, 35:36]
                RT = ["rt"]
                p.emit("dve", lambda eng, sc=sc: eng.tensor_reduce(
                    gmax, sc.rearrange("p (g m) -> p g m", g=4), axis=AX.X, op=ALU.max),
                    reads=[f"sco{sl}"], writes=RT, hz=True)
                p.emit("dve", lambda eng: eng.reduce_max(m1, gmax, axis=AX.X),
                       reads=RT, writes=RT, hz=True)
                p.emit("dve", lambda eng: eng.tensor_scalar(mk1, gmax, m1, None, op0=ALU.is_ge),
                       reads=RT, writes=RT, hz=True)
                p.emit("dve", lambda eng: eng.scalar_tensor_tensor(
                    g2, mk1, -1e30, gmax, op0=ALU.mult, op1=ALU.add),
                    reads=RT, writes=RT, hz=True)
                p.emit("dve", lambda eng: eng.reduce_max(m2, g2, axis=AX.X),
                       reads=RT, writes=RT, hz=True)
                p.emit("dve", lambda eng: eng.tensor_scalar(gmask, gmax, m2, None, op0=ALU.is_ge),
                       reads=RT, writes=RT, hz=True)

                def maskmul(eng, sc=sc):
                    for g4 in range(4):
                        inst = eng.tensor_scalar(masked[:, g4 * 4:(g4 + 1) * 4],
                                                 sc[:, g4 * 4:(g4 + 1) * 4],
                                                 gmask[:, g4:g4 + 1], None, op0=ALU.mult)
                    return inst
                p.emit("dve", maskmul, reads=RT + [f"sco{sl}"], writes=RT, hz=True)
                p.emit("dve", lambda eng: eng.max(out=top8b[:], in_=masked),
                       reads=RT, writes=RT + ["top8b"], hz=True)
                p.emit("dve", lambda eng: eng.tensor_scalar(
                    masked, masked, top8b[:, 3:4], None, op0=ALU.is_ge),
                    reads=RT + ["top8b"], writes=RT, hz=True)
                p.emit("dve", lambda eng, sc=sc: eng.tensor_tensor(masked, sc, masked, op=ALU.mult),
                       reads=RT + [f"sco{sl}"], writes=RT, hz=True)
                p.emit("dve", lambda eng: eng.reduce_sum(ssum, masked, axis=AX.X),
                       reads=RT, writes=RT, hz=True)
                p.emit("dve", lambda eng: eng.reciprocal(ssum, ssum),
                       reads=RT, writes=RT, hz=True)
                p.emit("dve", lambda eng: eng.tensor_scalar(masked, masked, ssum, None, op0=ALU.mult),
                       reads=RT, writes=RT, hz=True)
                p.emit("pe", lambda eng: eng.transpose(PS[0:16, 3584:3712], rscr[:, 18:34],
                                                       ident_f[:]),
                       reads=["rt", "ident_f"], writes=["pb7"])
                p.emit("dve", lambda eng, tt=tt: eng.tensor_copy(
                    combf[0:16, tt * 128:(tt + 1) * 128], PS[0:16, 3584:3712]),
                    reads=["pb7"], writes=["combf"], hz=True)
            dma("sp", comb_dram[:, tb:tb + 1024], combf[:], "combst",
                reads=["combf"], writes=["combdram"])

            for e in range(NE + 1):
                shared = (e == NE)
                nIc = NSI if shared else NI
                w1src = ws1 if shared else w1[e]
                w3src = ws3 if shared else w3[e]
                if not shared:
                    dma("pool", combrow[:], comb_dram[e:e + 1, tb:tb + 1024], "combld",
                        reads=["combdram"], writes=["combrow"])
                    for i in range(2):
                        p.emit("pe", lambda eng, i=i: eng.matmul(
                            bank_f(6), ones_f[:],
                            combrow[0:1, i * 512:(i + 1) * 512], start=True, stop=True),
                            reads=["combrow", "ones_f"], writes=["pb6"])
                        p.emit("dve", lambda eng, i=i: eng.tensor_copy(
                            combB[:, i * 512:(i + 1) * 512], bank_f(6)),
                            reads=["pb6"], writes=["combB"], hz=True)
                for ic in range(nIc):
                    wsl = ic % 2
                    dma("pool", w1b[wsl][:, :, :], w1src[:, :, ic * 128:(ic + 1) * 128],
                        "w1", writes=[f"w1b{wsl}"])
                    dma("pool", w3b[wsl][:, :, :], w3src[:, :, ic * 128:(ic + 1) * 128],
                        "w3", writes=[f"w3b{wsl}"])

                    def mm_h(eng, wb, bko, wkey):
                        for dc in range(16):
                            for ns in range(2):
                                inst = eng.matmul(bank_f(bko + ns),
                                                  wb[:, dc, :],
                                                  LNX2[:, dc, ns * 512:(ns + 1) * 512],
                                                  start=(dc == 0), stop=(dc == 15))
                        return inst
                    p.emit("pe", functools.partial(mm_h, wb=w1b[wsl], bko=0, wkey=None),
                           reads=[f"w1b{wsl}", "lnx2b"], writes=["pb0", "pb1"])
                    p.emit("pe", functools.partial(mm_h, wb=w3b[wsl], bko=2, wkey=None),
                           reads=[f"w3b{wsl}", "lnx2b"], writes=["pb2", "pb3"])
                    ssl = ic % 2
                    p.emit("act", lambda eng, ssl=ssl: eng.activation(
                        s1b[ssl][:], PS[:, 0:1024], AF.Silu, bias=0.0, scale=1.0),
                        reads=["pb0", "pb1"], writes=[f"s1b{ssl}"])
                    if shared:
                        p.emit("dve", lambda eng, ic=ic, ssl=ssl: eng.tensor_tensor(
                            g_sb[:, ic, :], PS[:, 1024:2048], s1b[ssl][:], op=ALU.mult),
                            reads=["pb2", "pb3", f"s1b{ssl}"], writes=["g_sb"])
                    else:
                        p.emit("dve", lambda eng, ssl=ssl: eng.tensor_tensor(
                            t3f[ssl][:], PS[:, 1024:2048], combB[:], op=ALU.mult),
                            reads=["pb2", "pb3", "combB"], writes=[f"t3f{ssl}"])
                        p.emit("dve", lambda eng, ic=ic, ssl=ssl: eng.tensor_tensor(
                            g_sb[:, ic, :], t3f[ssl][:], s1b[ssl][:], op=ALU.mult),
                            reads=[f"t3f{ssl}", f"s1b{ssl}"], writes=["g_sb"])
                if debug and b2 == 0 and e == 0:
                    dma("sp", g_dump, R2[:, 0:22528], None, reads=["g_sb"])
                nH = 4 if shared else 2
                wdt = 512 if shared else 1024
                w2srcv = ws2 if shared else w2[e]
                w2bv = w2b_s if shared else w2b_r
                for hf in range(nH):
                    wsl = hf % 2
                    dma("pool", w2bv[wsl][:, :, :],
                        w2srcv.rearrange("a p j -> p a j")[:, :, hf * wdt:(hf + 1) * wdt],
                        "w2", writes=[f"w2b{wsl}"])
                    for sub in range(NSUB):
                        nbk = wdt // 512
                        ybk = 4 + (sub % 2) * nbk

                        def mm_y(eng, hf=hf, sub=sub, wsl=wsl, ybk=ybk, nIc=nIc, nbk=nbk,
                                 w2bv=w2bv):
                            for ic in range(nIc):
                                for n2 in range(nbk):
                                    inst = eng.matmul(
                                        bank_f(ybk + n2),
                                        g_sb[:, ic, sub * 128:(sub + 1) * 128],
                                        w2bv[wsl][:, ic, n2 * 512:(n2 + 1) * 512],
                                        start=(ic == 0), stop=(ic == nIc - 1))
                            return inst
                        ykeys = [f"pb{ybk + n2}" for n2 in range(nbk)]
                        p.emit("pe", mm_y, reads=["g_sb", f"w2b{wsl}"], writes=ykeys)
                        ysl = sub % 2
                        p.emit("dve", lambda eng, ybk=ybk, ysl=ysl, wdt=wdt: eng.tensor_copy(
                            ytile[ysl][:, 0:wdt], PS[:, ybk * 512: ybk * 512 + wdt]),
                            reads=ykeys, writes=[f"ytile{ysl}"])
                        colo = hf * wdt
                        dma("pool",
                            out[tb + sub * 128: tb + (sub + 1) * 128, colo:colo + wdt],
                            ytile[ysl][:, 0:wdt], "yacc",
                            reads=[f"ytile{ysl}"],
                            writes=[f"y{b2}_{sub}_{colo // 1024}"], accum=True)

        if debug:
            for nm in dbg:
                dst, src = dbg[nm]
                dma("pool", dst, src, None,
                    reads=["ctxdram", "ln2dram", "qdram", "kdram", "vdram"])
            if part == "b":
                dma("pool", comb_dump, comb_dram, None, reads=["combdram"])

        p.barrier()
        p.finalize_and_emit(nc, ctx)
    return nc



def build_fused(debug=False):
    nc = bass.Bass("TRN2", target_bir_lowering=False, num_devices=8)
    p = Prog()
    NS = 8
    NTt = 8
    NU = 32

    def din(name, shape, dtype=F32):
        return nc.dram_tensor(name, list(shape), dtype, kind="ExternalInput").ap()

    identw = din("identw", [128, 128])
    onesrow = din("onesrow", [1, 128])
    xb = din("xb", [S, D])
    x_own = din("x_own", [TCORE, D])
    qkvw = din("qkvw", [128, 16, 24 * 128])
    qkvb = din("qkvb", [128, 16])
    vbrow = din("vbrow", [1, 1024])
    outw = din("outw", [128, 16, D])
    outbrow = din("outbrow", [1, D])
    gatew = din("gatew", [128, 16, 16])
    w1 = din("w1", [NE, 128, 16, I_FF])
    w3 = din("w3", [NE, 128, 16, I_FF])
    w2 = din("w2", [NE, NI, 128, D])
    ws1 = din("ws1", [128, 16, SH])
    ws3 = din("ws3", [128, 16, SH])
    ws2 = din("ws2", [NSI, 128, D])
    jrow = din("jrow", [1, 512])
    p_idx = din("p_idx", [128, 1])
    slopes = din("slopes", [128, 8])
    s128 = din("s128", [1, 8])
    s512 = din("s512", [1, 16])
    s511 = din("s511", [1, 16])
    tilepos = din("tilepos", [1, 8])
    mypos128 = din("mypos128", [1, 32])
    diagA2 = din("diagA2", [128, 8, 512])
    diagB2 = din("diagB2", [128, 8, 512])
    idxq = din("idxq", [128, 8], I32)
    idxst = din("idxst", [128, 256], I32)
    idxld = din("idxld", [128, 256], I32)

    out = nc.dram_tensor("out", [TCORE, D], F32, kind="ExternalOutput").ap()

    q_dram = nc.dram_tensor("q_dram", [2 * 8 * DH, 4096], BF16, kind="Internal").ap()
    k_dram = nc.dram_tensor("k_dram", [NS * DH, S], BF16, kind="Internal").ap()
    v_dram = nc.dram_tensor("v_dram", [S, NS * DH], BF16, kind="Internal").ap()
    ctxsh = nc.dram_tensor("ctxsh", [2 * NS * 4096, DH], BF16, kind="Internal",
                           addr_space="Shared").ap()
    lnx2_dram = nc.dram_tensor("lnx2_dram", [128, 16, TCORE], BF16, kind="Internal").ap()
    comb_dram = nc.dram_tensor("comb_dram", [16, TCORE], F32, kind="Internal").ap()

    dbg = {}
    if debug:
        for nm, t in (("d_q", q_dram), ("d_k", k_dram), ("d_v", v_dram),
                      ("d_ctx", ctxsh), ("d_ln2", lnx2_dram)):
            dbg[nm] = (nc.dram_tensor(nm, list(t.shape), BF16, kind="ExternalOutput").ap(), t)
        d_comb = nc.dram_tensor("d_comb", [16, TCORE], F32, kind="ExternalOutput").ap()
        d_diagB = nc.dram_tensor("d_diagB", [128, 8 * 512], BF16, kind="ExternalOutput").ap()
        d_mhat = nc.dram_tensor("d_mhat", [128, 32], F32, kind="ExternalOutput").ap()
        d_negD = nc.dram_tensor("d_negD", [128, 8], F32, kind="ExternalOutput").ap()
        d_alip = nc.dram_tensor("d_alip", [128, 64], F32, kind="ExternalOutput").ap()
        d_pt = nc.dram_tensor("d_pt", [128, 512], BF16, kind="ExternalOutput").ap()
        d_bias = nc.dram_tensor("d_bias", [128, 1], F32, kind="ExternalOutput").ap()
        d_qsb = nc.dram_tensor("d_qsb", [128, 4096], BF16, kind="ExternalOutput").ap()
        d_ksb = nc.dram_tensor("d_ksb", [128, 8192], BF16, kind="ExternalOutput").ap()

    with ExitStack() as ctx:
        def alloc(name, shape, dtype):
            return nc.alloc_sbuf_tensor(name, list(shape), dtype).ap()

        POOLN = 90112
        POOL = nc.alloc_sbuf_tensor("POOL", [128, POOLN], mybir.dt.uint8).ap()

        def sub(off, shape, dtype):
            es = 2 if dtype == BF16 else 4
            n = int(np.prod(shape[1:])) * es
            assert off % 4 == 0 and off + n <= POOLN, (off, n)
            ap = POOL[0:shape[0], off:off + n].bitcast(dtype)
            if len(shape) == 3:
                ap = ap.rearrange("p (a b) -> p a b", a=shape[1])
            return ap

        xt = [sub(0, [128, D], F32), sub(8192, [128, D], F32)]
        sq = sub(16384, [128, D], F32)
        lnt = [sub(24576, [128, D], BF16), sub(28672, [128, D], BF16)]
        qkt = [sub(32768 + 1024 * i, [128, 512], BF16) for i in range(4)]
        vt = [sub(36864, [128, 512], BF16), sub(37888, [128, 512], BF16)]
        wqk = [sub(38912, [128, 16, 128], BF16), sub(43008, [128, 16, 128], BF16)]
        vw = sub(47104, [128, 16, 512], BF16)
        vbB = sub(63488, [128, 1024], F32)
        vbrow_s = sub(67584, [1, 1024], F32)
        outbrow_s = sub(67584, [1, 512], F32)
        pt = [sub(38912, [128, 512], BF16), sub(39936, [128, 512], BF16),
              sub(59392, [128, 512], BF16), sub(60416, [128, 512], BF16)]
        ctt = [sub(40960, [128, 16, 128], BF16), sub(45056, [128, 16, 128], BF16)]
        lnf = [sub(49152, [128, 16, 128], BF16), sub(53248, [128, 16, 128], BF16)]
        x1t = [sub(57344, [128, D], F32), sub(65536, [128, D], F32)]
        outbB = sub(73728, [128, D], F32)
        diagA_s = sub(81920, [128, 8, 512], BF16)
        diagB_s = sub(40960, [128, 8, 512], BF16)
        ctxbf = [sub(57344 + 256 * i, [128, 128], BF16) for i in range(4)]
        negirow = sub(58368, [1, 512], BF16)
        ctf = ctt
        LNX2 = sub(0, [128, 16, 1024], BF16)
        s1b = [sub(32768, [128, 1024], BF16), sub(34816, [128, 1024], BF16)]
        t3f = [sub(36864, [128, 1024], F32), sub(40960, [128, 1024], F32)]
        ytile = [sub(45056, [128, 1024], F32), sub(49152, [128, 1024], F32)]
        combf = sub(53248, [16, 1024], F32)

        ident_bf = alloc("ident_bf", [128, 128], BF16)
        ident_f = alloc("ident_f", [128, 128], F32)
        ones_f = alloc("ones_f", [1, 128], F32)
        ones_bf = alloc("ones_bf", [1, 128], BF16)
        jrow_s = alloc("jrow_s", [1, 512], F32)
        pidx_s = alloc("pidx_s", [128, 1], F32)
        slopes_s = alloc("slopes_s", [128, 8], F32)
        s128_s = alloc("s128_s", [1, 8], F32)
        s512_s = alloc("s512_s", [1, 16], F32)
        s511_s = alloc("s511_s", [1, 16], F32)
        tilepos_s = alloc("tilepos_s", [1, 8], F32)
        mypos_s = alloc("mypos_s", [1, 32], F32)
        qkvb_s = alloc("qkvb_s", [128, 16], F32)
        gate_s = alloc("gate_s", [128, 16, 16], BF16)
        mhat2 = alloc("mhat2", [128, 16], F32)
        zr4 = alloc("zr4", [128, 4], F32)
        stat = [alloc(f"stat{i}", [128, 8], F32) for i in range(2)]
        top8b = alloc("top8b", [128, 8], F32)
        mhat = alloc("mhat", [128, 32], F32)
        mcol = alloc("mcol", [32, 1], F32)
        drow = alloc("drow", [1, 8], F32)
        alibi_loc = alloc("alibi_loc", [1, 512], BF16)
        crow = alloc("crow", [1, 64], F32)
        crowq = alloc("crowq", [1, 32], F32)
        adjrow = alloc("adjrow", [1, 16], F32)
        adjrow2 = alloc("adjrow2", [1, 8], F32)
        adjB = alloc("adjBb", [128, 16], F32)
        sp_col = alloc("sp_col", [128, 1], F32)
        alibi_p = alloc("alibi_p", [128, 64], F32)
        alibi_pq = alloc("alibi_pq", [128, 32], F32)
        negD = alloc("negD", [128, 8], F32)
        biasb = [alloc(f"biasb{i}", [128, 1], F32) for i in range(4)]
        biasT = [sub(63488, [128, 64], F32), sub(63744, [128, 64], F32)]
        sco = [alloc(f"sco{i}", [128, 16], F32) for i in range(2)]
        rscr = alloc("rscr", [128, 40], F32)
        combrow = alloc("combrow", [1, 1024], F32)
        sidxst = sub(32768, [128, 256], I32)
        sidxld = sub(33792, [128, 256], I32)
        sidxq = sub(34816, [128, 8], I32)

        R1 = alloc("R1", [128, 22528], BF16)
        R2 = alloc("R2", [128, 32768], BF16)

        PS = nc.alloc_psum_tensor("PS", [128, 4096], F32).ap()
        PSb = PS.bitcast(BF16)

        def bank_f(i, n=512):
            return PS[:, i * 512: i * 512 + n]

        def bank_bf(i, n=128):
            return PSb[:, i * 1024: i * 1024 + n]

        lnx_fm = R2.rearrange("p (dc t) -> p dc t", dc=16)
        Kslots = [R2[:, 0:8192], R2[:, 20544:28736]]
        Q_sb = R2[:, 8192:12288]
        V_sb = R2[:, 12288:12288 + 64 * 129].rearrange("p (c e) -> p c e", c=64)
        outw_sb = R2.rearrange("p (hd t) -> p hd t", hd=16)
        g_sb = R2[:, 0:22528].rearrange("p (ic t) -> p ic t", ic=22)
        w1b = [R2[:, 22528 + i * 2048: 22528 + (i + 1) * 2048].rearrange("p (dc j) -> p dc j", dc=16)
               for i in range(2)]
        w3b = [R2[:, 26624 + i * 2048: 26624 + (i + 1) * 2048].rearrange("p (dc j) -> p dc j", dc=16)
               for i in range(2)]
        combB = R2[:, 30720:32768].bitcast(F32)
        w2b_r = [R1[:, i * 11264:(i + 1) * 11264].rearrange("p (ic j) -> p ic j", ic=NI) for i in range(2)]
        w2b_s = [R1[:, i * 11264:(i + 1) * 11264].rearrange("p (ic j) -> p ic j", ic=NSI) for i in range(2)]

        _fam_ctr = [0]

        def dma(q, dst, src, fam, reads=(), writes=(), accum=False):
            if fam is None:
                _fam_ctr[0] += 1
                fam = f"u{_fam_ctr[0]}"
            if accum:
                fn = lambda eng, dst=dst, src=src: eng.dma_start(out=dst, in_=src, accum_op=ALU.add)
            else:
                fn = lambda eng, dst=dst, src=src: eng.dma_start(out=dst, in_=src)
            return p.emit(q, fn, reads=reads, writes=writes, fam=fam)

        dma("pool", ident_bf[:], identw, None, writes=["ident_bf"])
        dma("sp", ident_f[:], identw, None, writes=["ident_f"])
        dma("sp", ones_f[:], onesrow, None, writes=["ones_f"])
        dma("pool", ones_bf[:], onesrow, None, writes=["ones_bf"])
        dma("sp", jrow_s[:], jrow, None, writes=["jrow_s"])
        dma("sp", pidx_s[:], p_idx, None, writes=["pidx_s"])
        dma("sp", slopes_s[:], slopes, None, writes=["slopes_s"])
        dma("sp", s128_s[:], s128, None, writes=["s128_s"])
        dma("sp", s512_s[:], s512, None, writes=["s512_s"])
        dma("sp", s511_s[:], s511, None, writes=["s511_s"])
        dma("sp", tilepos_s[:], tilepos, None, writes=["tilepos_s"])
        dma("sp", mypos_s[:], mypos128, None, writes=["mypos_s"])
        dma("sp", qkvb_s[:], qkvb, None, writes=["qkvb_s"])
        dma("pool", gate_s[:], gatew, None, writes=["gate_s"])
        dma("sp", vbrow_s[:], vbrow, None, writes=["vbrow_s"])
        p.emit("pe", lambda eng: eng.matmul(bank_f(6), ones_f[:], vbrow_s[0:1, 0:512],
                                            start=True, stop=True),
               reads=["ones_f", "vbrow_s"], writes=["pb6"])
        p.emit("dve", lambda eng: eng.tensor_copy(vbB[:, 0:512], bank_f(6)),
               reads=["pb6"], writes=["vbB"], hz=True)
        p.emit("pe", lambda eng: eng.matmul(bank_f(6), ones_f[:], vbrow_s[0:1, 512:1024],
                                            start=True, stop=True),
               reads=["ones_f", "vbrow_s", "vbB"], writes=["pb6"])
        p.emit("dve", lambda eng: eng.tensor_copy(vbB[:, 512:1024], bank_f(6)),
               reads=["pb6"], writes=["vbB"], hz=True)

        def ln_tile(xin, xkey, outbf, outkey, sl):
            st = stat[sl]
            s1, nm, v1, sd, rstd, mr = (st[:, i:i + 1] for i in range(6))
            kst = f"stat{sl}"
            p.emit("dve", lambda eng: eng.reduce_sum(s1, xin, axis=AX.X),
                   reads=[xkey], writes=[kst], hz=True)
            p.emit("dve", lambda eng: eng.tensor_scalar(nm, s1, -1.0 / D, None, op0=ALU.mult),
                   reads=[kst], writes=[kst], hz=True)
            p.emit("act", lambda eng: eng.activation(sq[:], xin, AF.Square, bias=nm, scale=1.0),
                   reads=[xkey, kst], writes=["sq"])
            p.emit("dve", lambda eng: eng.reduce_sum(v1, sq[:], axis=AX.X),
                   reads=["sq"], writes=[kst], hz=True)
            p.emit("dve", lambda eng: eng.tensor_scalar(v1, v1, 1.0 / D, EPS,
                                                        op0=ALU.mult, op1=ALU.add),
                   reads=[kst], writes=[kst], hz=True)
            p.emit("act", lambda eng: eng.activation(sd, v1, AF.Sqrt),
                   reads=[kst], writes=[kst], hz=True)
            p.emit("dve", lambda eng: eng.reciprocal(rstd, sd), reads=[kst], writes=[kst], hz=True)
            p.emit("dve", lambda eng: eng.tensor_tensor(mr, nm, rstd, op=ALU.mult),
                   reads=[kst], writes=[kst], hz=True)
            p.emit("dve", lambda eng: eng.tensor_scalar(outbf, xin, nm, rstd,
                                                        op0=ALU.add, op1=ALU.mult),
                   reads=[xkey, kst], writes=[outkey])

        def transpose16(src_bf, srckey, dst3, dstkey, tslice):
            for dc in range(16):
                bk = 4 + (dc % 4)
                p.emit("pe", lambda eng, dc=dc, bk=bk: eng.transpose(
                    bank_bf(bk), src_bf[:, dc * 128:(dc + 1) * 128], ident_bf[:]),
                    reads=[srckey, "ident_bf"], writes=[f"pb{bk}"])
                p.emit("dve", lambda eng, dc=dc, bk=bk: eng.tensor_copy(
                    dst3[:, dc, tslice], bank_bf(bk)),
                    reads=[f"pb{bk}"], writes=[dstkey])

        for slab in range(4):
            base = slab * 2048
            for t in range(16):
                sl = t % 2
                dma("pool", xt[sl][:], xb[base + t * 128: base + (t + 1) * 128, :], "xb",
                    writes=[f"xt{sl}"])
                ln_tile(xt[sl][:], f"xt{sl}", lnt[sl][:], f"lnt{sl}", sl)
                transpose16(lnt[sl][:], f"lnt{sl}", lnx_fm, "lnxfm",
                            slice(t * 128, (t + 1) * 128))
            for f in range(16):
                wsl = f % 2
                dma("pool", wqk[wsl][:, :, :], qkvw[:, :, f * 128:(f + 1) * 128], "wqk",
                    writes=[f"wqk{wsl}"])

                for half in range(2):
                    def mm_qk(eng, f=f, wsl=wsl, half=half):
                        for dc in range(16):
                            for ts in (2 * half, 2 * half + 1):
                                inst = eng.matmul(bank_f(ts),
                                                  wqk[wsl][:, dc, :],
                                                  lnx_fm[:, dc, ts * 512:(ts + 1) * 512],
                                                  start=(dc == 0), stop=(dc == 15))
                        return inst
                    p.emit("pe", mm_qk, reads=["lnxfm", f"wqk{wsl}"],
                           writes=[f"pb{2 * half}", f"pb{2 * half + 1}"])
                for ts in range(4):
                    p.emit("act", lambda eng, f=f, ts=ts: eng.activation(
                        qkt[ts][:], bank_f(ts), AF.Identity, bias=qkvb_s[:, f:f + 1], scale=1.0),
                        reads=[f"pb{ts}", "qkvb_s"], writes=[f"qkt{ts}"])
                    g_t = slab * 4 + ts
                    if f < 8:
                        dma("sp", q_dram[(g_t % 2) * 1024 + f * 128:(g_t % 2) * 1024 + (f + 1) * 128,
                                         (g_t // 2) * 512:(g_t // 2 + 1) * 512],
                            qkt[ts][:], "qkst", reads=[f"qkt{ts}"], writes=["qdram"])
                    else:
                        dma("sp", k_dram[(f - 8) * 128:(f - 7) * 128,
                                         base + ts * 512: base + (ts + 1) * 512],
                            qkt[ts][:], "qkst", reads=[f"qkt{ts}"], writes=["kdram"])
            for vh in range(2):
                dma("pool", vw[:, :, :], qkvw[:, :, 2048 + vh * 512: 2048 + (vh + 1) * 512],
                    "vw", writes=["vw"])
                for t in range(16):
                    bk = 4 + t % 2
                    sl = t % 2

                    def mm_v2(eng, t=t, bk=bk):
                        for dc in range(16):
                            inst = eng.matmul(bank_f(bk),
                                              lnx_fm[:, dc, t * 128:(t + 1) * 128],
                                              vw[:, dc, :],
                                              start=(dc == 0), stop=(dc == 15))
                        return inst
                    p.emit("pe", mm_v2, reads=["lnxfm", "vw"], writes=[f"pb{bk}"])
                    p.emit("dve", lambda eng, sl=sl, bk=bk, vh=vh: eng.tensor_tensor(
                        vt[sl][:], bank_f(bk), vbB[:, vh * 512:(vh + 1) * 512], op=ALU.add),
                        reads=[f"pb{bk}", "vbB"], writes=[f"vt{sl}"])
                    dma("sp", v_dram[base + t * 128: base + (t + 1) * 128,
                                     vh * 512:(vh + 1) * 512], vt[sl][:],
                        "vst", reads=[f"vt{sl}"], writes=["vdram"])

        p.barrier()

        dma("pool", diagA_s[:], diagA2, None, writes=["diagA_s"])
        dma("pool", diagB_s[:], diagB2, None, writes=["diagB_s"])
        dma("sp", sidxq[:], idxq, None, writes=["sidxq"])
        dma("sp", sidxst[:], idxst, None, writes=["sidxst"])
        dma("sp", sidxld[:], idxld, None, writes=["sidxld"])
        p.emit("dve", lambda eng: eng.memset(V_sb[:, :, 128:129], 1.0), writes=["Vsb"])

        WMARG = 224.0
        dma("sp", Kslots[0], k_dram[0:128, :], "kqv", reads=["kdram"], writes=["Ksb0"])
        for s in range(NS):
            steep = (s <= 3)
            wdist = int(WMARG / float(SLOPES[2 * s + 1]))
            K_sb = Kslots[s % 2]
            kkey = f"Ksb{s % 2}"
            if s + 1 < NS:
                dma("sp", Kslots[(s + 1) % 2], k_dram[(s + 1) * 128:(s + 2) * 128, :], "kqv",
                    reads=["kdram"], writes=[f"Ksb{(s + 1) % 2}"])

            def qload(eng, s=s):
                return eng.indirect_dma_start(
                    out=Q_sb, out_offset=None,
                    in_=q_dram[0:128, :],
                    in_offset=IndirectOffsetOnAxis(sidxq[:, s:s + 1], 0))
            p.emit("pool", qload, reads=["qdram", "sidxq"], writes=["Qsb"], fam="kqv2")
            dma("sp", V_sb[:, :, 0:128],
                v_dram.rearrange("(c p) d -> p c d", p=128)[:, :, s * 128:(s + 1) * 128],
                "kqv", reads=["vdram"], writes=["Vsb"])

            p.emit("act", lambda eng, s=s: eng.activation(
                alibi_loc[:], jrow_s[:], AF.Identity, bias=0.0, scale=slopes_s[0:1, s:s + 1]),
                reads=["jrow_s", "slopes_s"], writes=["aloc"])
            if steep:
                p.emit("dve", lambda eng, s=s: eng.tensor_scalar(
                    negirow[:], jrow_s[:], slopes_s[0:1, s:s + 1], -1.0,
                    op0=ALU.mult, op1=ALU.mult),
                    reads=["jrow_s", "slopes_s"], writes=["negi"], hz=True)
            p.emit("dve", lambda eng, s=s: eng.tensor_scalar(
                crow[:], jrow_s[0:1, 0:64], s128_s[0:1, s:s + 1], None, op0=ALU.mult),
                reads=["jrow_s", "s128_s"], writes=["crow"], hz=True)
            p.emit("dve", lambda eng, s=s: eng.tensor_scalar(
                crowq[:], mypos_s[0:1, :], slopes_s[0:1, s:s + 1], None, op0=ALU.mult),
                reads=["mypos_s", "slopes_s"], writes=["crowq"], hz=True)
            p.emit("dve", lambda eng, s=s: eng.tensor_scalar(
                sp_col[:], pidx_s[:], slopes_s[:, s:s + 1], None, op0=ALU.mult),
                reads=["pidx_s", "slopes_s"], writes=["sp_col"], hz=True)
            p.emit("pe", lambda eng: eng.matmul(bank_f(2, 64), ones_f[:], crow[:],
                                                start=True, stop=True),
                   reads=["ones_f", "crow"], writes=["pb2"])
            p.emit("dve", lambda eng: eng.tensor_scalar(
                alibi_p[:], bank_f(2, 64), sp_col[:], None, op0=ALU.add),
                reads=["pb2", "sp_col"], writes=["alibi_p"], hz=True)
            p.emit("pe", lambda eng: eng.matmul(bank_f(2, 32), ones_f[:], crowq[:],
                                                start=True, stop=True),
                   reads=["ones_f", "crowq", "alibi_p"], writes=["pb2"])
            p.emit("dve", lambda eng: eng.tensor_scalar(
                alibi_pq[:], bank_f(2, 32), sp_col[:], None, op0=ALU.add),
                reads=["pb2", "sp_col"], writes=["alibi_pq"], hz=True)
            if steep:
                p.emit("dve", lambda eng, s=s: eng.tensor_scalar(
                    adjrow[:], jrow_s[0:1, 0:16], s512_s[0:1, s:s + 1], None, op0=ALU.mult),
                    reads=["jrow_s", "s512_s"], writes=["adjrow"], hz=True)
                p.emit("dve", lambda eng, s=s: eng.tensor_scalar(
                    adjrow2[:], tilepos_s[0:1, :], slopes_s[0:1, s:s + 1], None, op0=ALU.mult),
                    reads=["tilepos_s", "slopes_s"], writes=["adjrow2"], hz=True)
            else:
                p.emit("dve", lambda eng, s=s: eng.tensor_scalar(
                    adjrow[:], jrow_s[0:1, 0:16], s512_s[0:1, s:s + 1],
                    s511_s[0:1, s:s + 1], op0=ALU.mult, op1=ALU.add),
                    reads=["jrow_s", "s512_s", "s511_s"], writes=["adjrow"], hz=True)
            p.emit("pe", lambda eng: eng.matmul(bank_f(2, 16), ones_f[:], adjrow[:],
                                                start=True, stop=True),
                   reads=["ones_f", "adjrow", "alibi_pq"], writes=["pb2"])
            p.emit("dve", lambda eng: eng.tensor_copy(adjB[:], bank_f(2, 16)),
                   reads=["pb2"], writes=["adjB"], hz=True)

            for u in range(NU):
                nch = 2 * (u // 4) + 2
                pos_min = 512 * (2 * (u // 4)) + 128 * (u % 4)
                c_lo = max(0, (pos_min - wdist) // 512)
                p.emit("dve", lambda eng, c_lo=c_lo, nch=nch: eng.tensor_scalar(
                    mhat2[:, c_lo:nch], adjB[:, c_lo:nch], 0.0, -30000.0,
                    op0=ALU.mult, op1=ALU.add),
                    reads=["adjB"], writes=[f"mh{c}" for c in range(c_lo, nch)], hz=True)
                cs = list(range(c_lo, nch))
                groups = []
                i = 0
                while i < len(cs):
                    c0 = cs[i]
                    if not steep and (u + c0) % 2 == 0 and i + 1 < len(cs):
                        groups.append((c0, 2)); i += 2
                    else:
                        groups.append((c0, 1)); i += 1
                for (c0, w) in groups:
                    for c in range(c0, c0 + w):
                        bk = (u + c) % 4

                        def mm_s(eng, u=u, c=c, bk=bk, steep=steep, K_sb=K_sb):
                            if steep:
                                eng.matmul(bank_f(bk), Q_sb[:, u * 128:(u + 1) * 128],
                                           K_sb[:, c * 512:(c + 1) * 512], start=True, stop=False)
                                return eng.matmul(bank_f(bk), ones_bf[:], alibi_loc[:],
                                                  start=False, stop=True)
                            return eng.matmul(bank_f(bk), Q_sb[:, u * 128:(u + 1) * 128],
                                              K_sb[:, c * 512:(c + 1) * 512], start=True, stop=True)
                        p.emit("pe", mm_s, reads=["Qsb", kkey, "aloc", "ones_bf"],
                               writes=[f"pb{bk}"])
                        if c >= nch - 2:
                            mi = (u % 4) * 2 + (c - (nch - 2))
                            p.emit("dve", lambda eng, mi=mi, bk=bk: eng.tensor_tensor(
                                bank_f(bk), bank_f(bk), diagA_s[:, mi, :], op=ALU.add),
                                reads=["diagA_s"], writes=[f"pb{bk}"])
                    bkA = (u + c0) % 4
                    if w == 2:
                        p.emit("dve", lambda eng, c0=c0, bkA=bkA: eng.reduce_max(
                            mhat2[:, c0 + 1:c0 + 2], bank_f(bkA, 1024), axis=AX.X),
                            reads=[f"pb{bkA}", f"pb{bkA + 1}"],
                            writes=[f"mh{c0}", f"mh{c0 + 1}"], hz=True)
                    else:
                        p.emit("dve", lambda eng, c0=c0, bkA=bkA: eng.reduce_max(
                            mhat2[:, c0:c0 + 1], bank_f(bkA), axis=AX.X),
                            reads=[f"pb{bkA}"], writes=[f"mh{c0}"], hz=True)
                p.emit("dve", lambda eng, nch=nch, c_lo=c_lo: eng.tensor_tensor(
                    mhat2[:, c_lo:nch], mhat2[:, c_lo:nch], adjB[:, c_lo:nch], op=ALU.add),
                    reads=["adjB"] + [f"mh{c}" for c in range(c_lo, nch)],
                    writes=[f"mh{c}" for c in range(c_lo, nch)], hz=True)
                p.emit("dve", lambda eng, u=u, nch=nch, c_lo=c_lo: eng.reduce_max(
                    mhat[:, u:u + 1], mhat2[:, c_lo:nch], axis=AX.X),
                    reads=[f"mh{c}" for c in range(c_lo, nch)], writes=["mhat"], hz=True)
                if steep:
                    p.emit("dve", lambda eng, u=u: eng.tensor_tensor(
                        mhat[:, u:u + 1], mhat[:, u:u + 1], alibi_pq[:, u:u + 1],
                        op=ALU.subtract),
                        reads=["alibi_pq", "mhat"], writes=["mhat"], hz=True)

            p.emit("pe", lambda eng: eng.transpose(PS[0:32, 1024:1152], mhat[:, 0:32], ident_f[:]),
                   reads=["mhat", "ident_f"], writes=["pb2"])
            p.emit("dve", lambda eng: eng.reduce_max(mcol[:], PS[0:32, 1024:1152], axis=AX.X),
                   reads=["pb2"], writes=["mcol"], hz=True)
            p.emit("pe", lambda eng: eng.transpose(PS[0:1, 1536:1568], mcol[0:32, 0:1],
                                                   ident_f[0:32, 0:32]),
                   reads=["mcol", "ident_f"], writes=["pb3"])
            p.emit("dve", lambda eng: eng.tensor_reduce(
                drow[:], PS[0:1, 1536:1568].rearrange("p (a b) -> p a b", a=8),
                axis=AX.X, op=ALU.max),
                reads=["pb3"], writes=["drow"], hz=True)
            if steep:
                p.emit("dve", lambda eng: eng.tensor_tensor(
                    drow[:], drow[:], adjrow2[0:1, 0:8], op=ALU.add),
                    reads=["drow", "adjrow2"], writes=["drow"], hz=True)
            p.emit("dve", lambda eng: eng.tensor_scalar(drow[:], drow[:], -1.0, None, op0=ALU.mult),
                   reads=["drow"], writes=["drow"], hz=True)
            p.emit("pe", lambda eng: eng.matmul(bank_f(2, 8), ones_f[:], drow[:],
                                                start=True, stop=True),
                   reads=["ones_f", "drow"], writes=["pb2"])
            p.emit("dve", lambda eng: eng.tensor_copy(negD[:], bank_f(2, 8)),
                   reads=["pb2"], writes=["negD"], hz=True)

            if debug and s == 0:
                dma("sp", d_qsb, Q_sb, None, reads=["Qsb"])
                dma("sp", d_ksb, K_sb, None, reads=[kkey])
                dma("sp", d_mhat, mhat[:], None, reads=["mhat"])
                dma("sp", d_negD, negD[:], None, reads=["negD"])
                dma("sp", d_alip, alibi_p[:], None, reads=["alibi_p"])
                dma("sp", d_diagB, diag_s[:, :, :] if False else diagB_s.rearrange("p a b -> p (a b)"),
                    None, reads=["diagB_s"])
            for t in range(NTt):
                nkc = 8 * t + 8
                kc_lo = max(0, (1024 * t - wdist) // 128)
                bT = biasT[t % 2]
                p.emit("dve", lambda eng, t=t, bT=bT: eng.tensor_scalar(
                    bT[:], alibi_p[:], negD[:, t:t + 1], None, op0=ALU.add),
                    reads=["alibi_p", "negD"], writes=[f"biasT{t % 2}"], hz=True)
                for kc in range(kc_lo, nkc):
                    bk = kc % 4

                    def mm_st(eng, t=t, kc=kc, bk=bk, steep=steep, K_sb=K_sb):
                        if steep:
                            eng.matmul(bank_f(bk), K_sb[:, kc * 128:(kc + 1) * 128],
                                       Q_sb[:, t * 512:(t + 1) * 512], start=True, stop=False)
                            return eng.matmul(bank_f(bk), ones_bf[:], negirow[:],
                                              start=False, stop=True)
                        return eng.matmul(bank_f(bk), K_sb[:, kc * 128:(kc + 1) * 128],
                                          Q_sb[:, t * 512:(t + 1) * 512], start=True, stop=True)
                    p.emit("pe", mm_st, reads=[kkey, "Qsb", "negi", "ones_bf"],
                           writes=[f"pb{bk}"])
                    if kc >= 8 * t:
                        p.emit("dve", lambda eng, t=t, kc=kc, bk=bk: eng.tensor_tensor(
                            bank_f(bk), bank_f(bk), diagB_s[:, kc - 8 * t, :], op=ALU.add),
                            reads=["diagB_s"], writes=[f"pb{bk}"])
                    psl = kc % 4
                    p.emit("act", lambda eng, bk=bk, kc=kc, psl=psl, bT=bT: eng.activation(
                        pt[psl][:], bank_f(bk), AF.Exp, bias=bT[:, kc:kc + 1], scale=1.0),
                        reads=[f"pb{bk}", f"biasT{t % 2}"], writes=[f"pt{psl}"])

                    if debug and s == 0 and t == 0 and kc == 0:
                        dma("sp", d_pt, pt[psl][:], None, reads=[f"pt{psl}"])
                        dma("sp", d_bias, bT[:, 0:1], None, reads=[f"biasT{t % 2}"])

                    def pv(eng, kc=kc, psl=psl, nkc=nkc, kc_lo=kc_lo):
                        for qq in range(4):
                            inst = eng.matmul(PS[:, (4 + qq) * 512:(4 + qq) * 512 + 129],
                                              pt[psl][:, qq * 128:(qq + 1) * 128],
                                              V_sb[:, kc, :],
                                              start=(kc == kc_lo), stop=(kc == nkc - 1))
                        return inst
                    p.emit("pe", pv, reads=[f"pt{psl}", "Vsb"],
                           writes=["pb4", "pb5", "pb6", "pb7"])
                zcols = PS[:, 2048:4096].rearrange("p (q x) -> p q x", q=4)[:, :, 128:129]
                p.emit("dve", lambda eng, zcols=zcols: eng.tensor_scalar(
                    zr4[:, :, None], zcols, 1e-37, None, op0=ALU.max),
                    reads=["pb4", "pb5", "pb6", "pb7"], writes=["zr4"], hz=True)
                p.emit("dve", lambda eng: eng.reciprocal(zr4[:], zr4[:]),
                       reads=["zr4"], writes=["zr4"], hz=True)
                for qq in range(4):
                    p.emit("dve", lambda eng, qq=qq: eng.tensor_scalar(
                        ctxbf[qq][:], PS[:, (4 + qq) * 512:(4 + qq) * 512 + 128],
                        zr4[:, qq:qq + 1], None, op0=ALU.mult),
                        reads=[f"pb{4 + qq}", "zr4"], writes=[f"ctxbf{qq}"])

                    def ctxstore(eng, s=s, t=t, qq=qq):
                        col = s * 32 + t * 4 + qq
                        return eng.indirect_dma_start(
                            out=ctxsh[0:128, :],
                            out_offset=IndirectOffsetOnAxis(sidxst[:, col:col + 1], 0),
                            in_=ctxbf[qq][:], in_offset=None)
                    p.emit("pool", ctxstore, reads=[f"ctxbf{qq}", "sidxst"], writes=["ctxsh"], fam="ctxst")

        for i in range(4):
            dma("pool", outw_sb[:, 4 * i:4 * (i + 1), :], outw[:, 4 * i:4 * (i + 1), :],
                "wout", writes=["Ksb0", "Ksb1", "Qsb", "Vsb", f"OW{i}"])
        p.barrier()
        p.mark_split()

        for i in range(4):
            dma("sp", outbrow_s[:], outbrow[0:1, i * 512:(i + 1) * 512], None,
                reads=["outbrow_done"], writes=["outbrow_s"])
            p.emit("pe", lambda eng: eng.matmul(bank_f(2), ones_f[:], outbrow_s[:],
                                                start=True, stop=True),
                   reads=["ones_f", "outbrow_s"], writes=["pb2"])
            p.emit("dve", lambda eng, i=i: eng.tensor_copy(outbB[:, i * 512:(i + 1) * 512],
                                                           bank_f(2)),
                   reads=["pb2", "outbrow_s"], writes=["outbB", "outbrow_done"], hz=True)

        for t in range(16):
            sl = t % 2
            for h in range(16):
                def ctld(eng, t=t, h=h, sl=sl):
                    col = t * 16 + h
                    return eng.indirect_dma_start(
                        out=ctt[sl][:, h, :], out_offset=None,
                        in_=ctxsh[0:128, :],
                        in_offset=IndirectOffsetOnAxis(sidxld[:, col:col + 1], 0))
                p.emit("pool", ctld, reads=["ctxsh", "sidxld"], writes=[f"ctt{sl}"], fam="ctld")
            for hd in range(16):
                bk = hd % 2
                p.emit("pe", lambda eng, hd=hd, bk=bk, sl=sl: eng.transpose(
                    bank_bf(bk), ctt[sl][:, hd, :], ident_bf[:]),
                    reads=[f"ctt{sl}", "ident_bf"], writes=[f"pb{bk}"])
                p.emit("dve", lambda eng, hd=hd, bk=bk, sl=sl: eng.tensor_copy(
                    ctf[sl][:, hd, :], bank_bf(bk)),
                    reads=[f"pb{bk}"], writes=[f"ctt{sl}"])

            def mm_out(eng, sl=sl):
                for hd in range(16):
                    for n in range(4):
                        inst = eng.matmul(PS[:, (4 + n) * 512:(5 + n) * 512],
                                          ctf[sl][:, hd, :],
                                          outw_sb[:, hd, n * 512:(n + 1) * 512],
                                          start=(hd == 0), stop=(hd == 15))
                return inst
            p.emit("pe", mm_out, reads=[f"ctt{sl}", "OW0", "OW1", "OW2", "OW3"],
                   writes=["pb4", "pb5", "pb6", "pb7"])
            dma("sp", xt[sl][:], x_own[t * 128:(t + 1) * 128, :], "xo", writes=[f"xt{sl}"])
            p.emit("dve", lambda eng, sl=sl: eng.tensor_tensor(
                x1t[sl][:], PS[:, 2048:4096], outbB[:], op=ALU.add),
                reads=["pb4", "pb5", "pb6", "pb7", "outbB"], writes=[f"x1t{sl}"])
            p.emit("dve", lambda eng, sl=sl: eng.tensor_tensor(
                x1t[sl][:], x1t[sl][:], xt[sl][:], op=ALU.add),
                reads=[f"xt{sl}", f"x1t{sl}"], writes=[f"x1t{sl}"])
            dma("sp", out[t * 128:(t + 1) * 128, :], x1t[sl][:], "x1st",
                reads=[f"x1t{sl}"],
                writes=[f"y{t // 8}_{t % 8}_0", f"y{t // 8}_{t % 8}_1"])
            ln_tile(x1t[sl][:], f"x1t{sl}", lnt[sl][:], f"lnt{sl}", sl)
            transpose16(lnt[sl][:], f"lnt{sl}", lnf[sl], f"lnf{sl}", slice(0, 128))
            dma("sp", lnx2_dram[:, :, t * 128:(t + 1) * 128], lnf[sl][:, :, :], "ln2st",
                reads=[f"lnf{sl}"], writes=["ln2dram"])

        p.barrier()

        NSUB = 8
        for b2 in range(2):
            tb = b2 * 1024
            dma("pool", LNX2[:, :, :], lnx2_dram[:, :, tb:tb + 1024], "ln2ld",
                reads=["ln2dram"], writes=["lnx2b"])
            for tt in range(8):
                sl = tt % 2

                def mm_gate(eng, tt=tt):
                    for dc in range(16):
                        inst = eng.matmul(bank_f(6, 16), LNX2[:, dc, tt * 128:(tt + 1) * 128],
                                          gate_s[:, dc, :], start=(dc == 0), stop=(dc == 15))
                    return inst
                p.emit("pe", mm_gate, reads=["lnx2b", "gate_s"], writes=["pb6"])
                p.emit("act", lambda eng, sl=sl: eng.activation(
                    sco[sl][:], bank_f(6, 16), AF.Sigmoid, bias=0.0, scale=1.0),
                    reads=["pb6"], writes=[f"sco{sl}"])

                sc = sco[sl]
                gmax = rscr[:, 0:4]
                m1 = rscr[:, 4:5]
                mk1 = rscr[:, 5:9]
                g2 = rscr[:, 9:13]
                m2 = rscr[:, 13:14]
                gmask = rscr[:, 14:18]
                masked = rscr[:, 18:34]
                ssum = rscr[:, 35:36]
                RT = ["rt"]
                p.emit("dve", lambda eng, sc=sc: eng.tensor_reduce(
                    gmax, sc.rearrange("p (g m) -> p g m", g=4), axis=AX.X, op=ALU.max),
                    reads=[f"sco{sl}"], writes=RT, hz=True)
                p.emit("dve", lambda eng: eng.reduce_max(m1, gmax, axis=AX.X),
                       reads=RT, writes=RT, hz=True)
                p.emit("dve", lambda eng: eng.tensor_scalar(mk1, gmax, m1, None, op0=ALU.is_ge),
                       reads=RT, writes=RT, hz=True)
                p.emit("dve", lambda eng: eng.scalar_tensor_tensor(
                    g2, mk1, -1e30, gmax, op0=ALU.mult, op1=ALU.add),
                    reads=RT, writes=RT, hz=True)
                p.emit("dve", lambda eng: eng.reduce_max(m2, g2, axis=AX.X),
                       reads=RT, writes=RT, hz=True)
                p.emit("dve", lambda eng: eng.tensor_scalar(gmask, gmax, m2, None, op0=ALU.is_ge),
                       reads=RT, writes=RT, hz=True)

                def maskmul(eng, sc=sc):
                    for g4 in range(4):
                        inst = eng.tensor_scalar(masked[:, g4 * 4:(g4 + 1) * 4],
                                                 sc[:, g4 * 4:(g4 + 1) * 4],
                                                 gmask[:, g4:g4 + 1], None, op0=ALU.mult)
                    return inst
                p.emit("dve", maskmul, reads=RT + [f"sco{sl}"], writes=RT, hz=True)
                p.emit("dve", lambda eng: eng.max(out=top8b[:], in_=masked),
                       reads=RT, writes=RT + ["top8b"], hz=True)
                p.emit("dve", lambda eng: eng.tensor_scalar(
                    masked, masked, top8b[:, 3:4], None, op0=ALU.is_ge),
                    reads=RT + ["top8b"], writes=RT, hz=True)
                p.emit("dve", lambda eng, sc=sc: eng.tensor_tensor(masked, sc, masked, op=ALU.mult),
                       reads=RT + [f"sco{sl}"], writes=RT, hz=True)
                p.emit("dve", lambda eng: eng.reduce_sum(ssum, masked, axis=AX.X),
                       reads=RT, writes=RT, hz=True)
                p.emit("dve", lambda eng: eng.reciprocal(ssum, ssum),
                       reads=RT, writes=RT, hz=True)
                p.emit("dve", lambda eng: eng.tensor_scalar(masked, masked, ssum, None, op0=ALU.mult),
                       reads=RT, writes=RT, hz=True)
                p.emit("pe", lambda eng: eng.transpose(PS[0:16, 3584:3712], rscr[:, 18:34],
                                                       ident_f[:]),
                       reads=["rt", "ident_f"], writes=["pb7"])
                p.emit("dve", lambda eng, tt=tt: eng.tensor_copy(
                    combf[0:16, tt * 128:(tt + 1) * 128], PS[0:16, 3584:3712]),
                    reads=["pb7"], writes=["combf"], hz=True)
            dma("sp", comb_dram[:, tb:tb + 1024], combf[:], "combst",
                reads=["combf"], writes=["combdram"])

            for e in range(NE + 1):
                shared = (e == NE)
                nIc = NSI if shared else NI
                w1src = ws1 if shared else w1[e]
                w3src = ws3 if shared else w3[e]
                if not shared:
                    dma("pool", combrow[:], comb_dram[e:e + 1, tb:tb + 1024], "combld",
                        reads=["combdram"], writes=["combrow"])
                    for i in range(2):
                        p.emit("pe", lambda eng, i=i: eng.matmul(
                            bank_f(6), ones_f[:],
                            combrow[0:1, i * 512:(i + 1) * 512], start=True, stop=True),
                            reads=["combrow", "ones_f"], writes=["pb6"])
                        p.emit("dve", lambda eng, i=i: eng.tensor_copy(
                            combB[:, i * 512:(i + 1) * 512], bank_f(6)),
                            reads=["pb6"], writes=["combB"], hz=True)
                for ic in range(nIc):
                    wsl = ic % 2
                    dma("pool", w1b[wsl][:, :, :], w1src[:, :, ic * 128:(ic + 1) * 128],
                        "w1", writes=[f"w1b{wsl}"])
                    dma("pool", w3b[wsl][:, :, :], w3src[:, :, ic * 128:(ic + 1) * 128],
                        "w3", writes=[f"w3b{wsl}"])

                    def mm_h(eng, wb, bko, wkey):
                        for dc in range(16):
                            for ns in range(2):
                                inst = eng.matmul(bank_f(bko + ns),
                                                  wb[:, dc, :],
                                                  LNX2[:, dc, ns * 512:(ns + 1) * 512],
                                                  start=(dc == 0), stop=(dc == 15))
                        return inst
                    p.emit("pe", functools.partial(mm_h, wb=w1b[wsl], bko=0, wkey=None),
                           reads=[f"w1b{wsl}", "lnx2b"], writes=["pb0", "pb1"])
                    p.emit("pe", functools.partial(mm_h, wb=w3b[wsl], bko=2, wkey=None),
                           reads=[f"w3b{wsl}", "lnx2b"], writes=["pb2", "pb3"])
                    ssl = ic % 2
                    p.emit("act", lambda eng, ssl=ssl: eng.activation(
                        s1b[ssl][:], PS[:, 0:1024], AF.Silu, bias=0.0, scale=1.0),
                        reads=["pb0", "pb1"], writes=[f"s1b{ssl}"])
                    if shared:
                        p.emit("dve", lambda eng, ic=ic, ssl=ssl: eng.tensor_tensor(
                            g_sb[:, ic, :], PS[:, 1024:2048], s1b[ssl][:], op=ALU.mult),
                            reads=["pb2", "pb3", f"s1b{ssl}"], writes=["g_sb"])
                    else:
                        p.emit("dve", lambda eng, ssl=ssl: eng.tensor_tensor(
                            t3f[ssl][:], PS[:, 1024:2048], combB[:], op=ALU.mult),
                            reads=["pb2", "pb3", "combB"], writes=[f"t3f{ssl}"])
                        p.emit("dve", lambda eng, ic=ic, ssl=ssl: eng.tensor_tensor(
                            g_sb[:, ic, :], t3f[ssl][:], s1b[ssl][:], op=ALU.mult),
                            reads=[f"t3f{ssl}", f"s1b{ssl}"], writes=["g_sb"])
                nH = 4 if shared else 2
                wdt = 512 if shared else 1024
                w2srcv = ws2 if shared else w2[e]
                w2bv = w2b_s if shared else w2b_r
                for hf in range(nH):
                    wsl = hf % 2
                    dma("pool", w2bv[wsl][:, :, :],
                        w2srcv.rearrange("a p j -> p a j")[:, :, hf * wdt:(hf + 1) * wdt],
                        "w2", writes=[f"w2b{wsl}"])
                    for sb_ in range(NSUB):
                        nbk = wdt // 512
                        ybk = 4 + (sb_ % 2) * nbk

                        def mm_y(eng, hf=hf, sb_=sb_, wsl=wsl, ybk=ybk, nIc=nIc, nbk=nbk,
                                 w2bv=w2bv):
                            for ic in range(nIc):
                                for n2 in range(nbk):
                                    inst = eng.matmul(
                                        bank_f(ybk + n2),
                                        g_sb[:, ic, sb_ * 128:(sb_ + 1) * 128],
                                        w2bv[wsl][:, ic, n2 * 512:(n2 + 1) * 512],
                                        start=(ic == 0), stop=(ic == nIc - 1))
                            return inst
                        ykeys = [f"pb{ybk + n2}" for n2 in range(nbk)]
                        p.emit("pe", mm_y, reads=["g_sb", f"w2b{wsl}"], writes=ykeys)
                        ysl = sb_ % 2
                        p.emit("dve", lambda eng, ybk=ybk, ysl=ysl, wdt=wdt: eng.tensor_copy(
                            ytile[ysl][:, 0:wdt], PS[:, ybk * 512: ybk * 512 + wdt]),
                            reads=ykeys, writes=[f"ytile{ysl}"])
                        colo = hf * wdt
                        dma("pool",
                            out[tb + sb_ * 128: tb + (sb_ + 1) * 128, colo:colo + wdt],
                            ytile[ysl][:, 0:wdt], "yacc",
                            reads=[f"ytile{ysl}"],
                            writes=[f"y{b2}_{sb_}_{colo // 1024}"], accum=True)

        if debug:
            for nm in dbg:
                dst, src = dbg[nm]
                dma("pool", dst, src, None,
                    reads=["ctxsh", "ln2dram", "qdram", "kdram", "vdram"])
            dma("pool", d_comb, comb_dram, None, reads=["combdram"])

        p.barrier()
        p.finalize_and_emit(nc, ctx)
    return nc


def _images(out_w, gate_w, W1, W2, W3, Ws1, Ws2, Ws3):
    f32 = np.float32
    imgs = {}
    imgs["outw"] = np.ascontiguousarray(out_w.reshape(16, 128, D).transpose(1, 0, 2))
    gt = gate_w.T
    imgs["gatew"] = np.ascontiguousarray(gt.reshape(16, 128, 16).transpose(1, 0, 2))
    for nm, W in (("w1", W1), ("w3", W3)):
        wt = W.transpose(0, 2, 1)
        imgs[nm] = np.ascontiguousarray(wt.reshape(NE, 16, 128, I_FF).transpose(0, 2, 1, 3))
    w2t = W2.transpose(0, 2, 1)
    imgs["w2"] = np.ascontiguousarray(w2t.reshape(NE, NI, 128, D))
    for nm, W in (("ws1", Ws1), ("ws3", Ws3)):
        wt = W.T
        imgs[nm] = np.ascontiguousarray(wt.reshape(16, 128, SH).transpose(1, 0, 2))
    imgs["ws2"] = np.ascontiguousarray(Ws2.T.reshape(NSI, 128, D))
    return {k: np.asarray(v, dtype=f32) for k, v in imgs.items()}


def _consts():
    c = {}
    c["identw"] = np.eye(128, dtype=np.float32)
    c["onesrow"] = np.ones((1, 128), dtype=np.float32)
    c["jrow"] = np.arange(512, dtype=np.float32)[None, :]
    c["p_idx"] = np.arange(128, dtype=np.float32)[:, None]
    dm = np.zeros((128, 4, 512), dtype=np.float32)
    jj = np.arange(128)[:, None]
    ii = np.arange(512)[None, :]
    for kk in range(4):
        dm[:, kk, :] = np.where(128 * kk + jj > ii, -30000.0, 0.0)
    c["diagm"] = dm
    dma_ = np.zeros((128, 4, 512), dtype=np.float32)
    qq_ = np.arange(128)[:, None]
    kl_ = np.arange(512)[None, :]
    for uu in range(4):
        dma_[:, uu, :] = np.where(kl_ > 128 * uu + qq_, -30000.0, 0.0)
    c["diagmA"] = dma_
    return c


def _fused_maps(x, qkv_w, qkv_b, out_b, imgs, consts):
    f32 = np.float32
    dmA = consts["diagmA"]
    dmB = consts["diagm"]
    maps = []
    for c in range(8):
        b, P, m = c // 4, (c % 4) // 2, c % 2
        heads = [2 * s_ + m for s_ in range(8)]
        cols = []
        for part in range(3):
            for h in heads:
                cols.append(qkv_w[:, part * 2048 + h * 128: part * 2048 + (h + 1) * 128])
        qw = np.concatenate(cols, axis=1)
        qkvw_img = np.ascontiguousarray(qw.reshape(16, 128, 3072).transpose(1, 0, 2))
        bcols = []
        for part in range(2):
            for h in heads:
                bcols.append(qkv_b[part * 2048 + h * 128: part * 2048 + (h + 1) * 128])
        qkvb_img = np.ascontiguousarray(np.stack(bcols, axis=1))
        vb = np.concatenate([qkv_b[2 * 2048 + h * 128: 2 * 2048 + (h + 1) * 128]
                             for h in heads])[None, :]
        sl = np.array([SLOPES[h] for h in heads], f32)
        sl16 = np.concatenate([sl, np.zeros(8, f32)])
        tp = (512.0 * (2 * np.arange(8) + P)).astype(f32)[None, :]
        uu = np.arange(32)
        blk = 8 * (uu // 4) + 4 * P + (uu % 4)
        myp = (128.0 * blk).astype(f32)[None, :]
        dA = np.zeros((128, 8, 512), f32)
        dB = np.zeros((128, 8, 512), f32)
        for u4 in range(4):
            if P == 0:
                dA[:, u4 * 2 + 0] = dmA[:, u4]
                dA[:, u4 * 2 + 1] = -30000.0
            else:
                dA[:, u4 * 2 + 0] = 0.0
                dA[:, u4 * 2 + 1] = dmA[:, u4]
        for kcr in range(8):
            if P == 0:
                dB[:, kcr] = dmB[:, kcr] if kcr < 4 else -30000.0
            else:
                dB[:, kcr] = 0.0 if kcr < 4 else dmB[:, kcr - 4]
        xo = np.concatenate([x[b, (2 * lt + P) * 512:(2 * lt + P) * 512 + 512]
                             for lt in range(4 * m, 4 * m + 4)], axis=0)
        pp = np.arange(128, dtype=np.int64)
        iq = np.empty((128, 8), np.int32)
        for s_ in range(8):
            iq[:, s_] = P * 1024 + s_ * 128 + pp
        ist = np.empty((128, 256), np.int32)
        for s_ in range(8):
            for t_ in range(8):
                for qq_ in range(4):
                    ist[:, s_ * 32 + t_ * 4 + qq_] = (m * 32768 + s_ * 4096
                                                      + t_ * 512 + qq_ * 128 + pp)
        ild = np.empty((128, 256), np.int32)
        for t_ in range(16):
            for h_ in range(16):
                ild[:, t_ * 16 + h_] = ((h_ % 2) * 32768 + (h_ // 2) * 4096
                                        + m * 2048 + t_ * 128 + pp)
        mp = dict(imgs)
        mp.update(dict(
            identw=consts["identw"], onesrow=consts["onesrow"],
            jrow=consts["jrow"], p_idx=consts["p_idx"],
            xb=x[b], x_own=np.ascontiguousarray(xo),
            qkvw=qkvw_img, qkvb=qkvb_img, vbrow=np.ascontiguousarray(vb),
            outbrow=out_b[None, :].copy(),
            slopes=np.tile(sl[None, :], (128, 1)),
            s128=(128.0 * sl)[None, :].copy(),
            s512=(512.0 * sl16)[None, :].copy(), s511=(511.0 * sl16)[None, :].copy(),
            tilepos=tp, mypos128=myp, diagA2=dA, diagB2=dB,
            idxq=iq, idxst=ist, idxld=ild,
        ))
        maps.append(mp)
    return maps


def _fused_assemble(results):
    f32 = np.float32
    outp = np.empty((B, S, D), f32)
    for c in range(8):
        b, P, m = c // 4, (c % 4) // 2, c % 2
        sh = np.asarray(results[c]["out"], f32)
        for i, lt in enumerate(range(4 * m, 4 * m + 4)):
            g = 2 * lt + P
            outp[b, g * 512:(g + 1) * 512] = sh[i * 512:(i + 1) * 512]
    return outp


def kernel(L0_attn_x, L0_attn_qkv_w, L0_attn_qkv_b, L0_attn_out_w, L0_attn_out_b,
           L0_ffn_gate_w, L0_ffn_W1, L0_ffn_W2, L0_ffn_W3,
           L0_ffn_Ws1, L0_ffn_Ws2, L0_ffn_Ws3, _debug=False, _ncs=None):
    f32 = np.float32
    x = np.ascontiguousarray(np.asarray(L0_attn_x, f32))
    qkv_w = np.asarray(L0_attn_qkv_w, f32)
    qkv_b = np.asarray(L0_attn_qkv_b, f32)
    out_w = np.asarray(L0_attn_out_w, f32)
    out_b = np.asarray(L0_attn_out_b, f32)
    gate_w = np.asarray(L0_ffn_gate_w, f32)
    W1 = np.asarray(L0_ffn_W1, f32)
    W2 = np.asarray(L0_ffn_W2, f32)
    W3 = np.asarray(L0_ffn_W3, f32)
    Ws1 = np.asarray(L0_ffn_Ws1, f32)
    Ws2 = np.asarray(L0_ffn_Ws2, f32)
    Ws3 = np.asarray(L0_ffn_Ws3, f32)

    imgs = _images(out_w, gate_w, W1, W2, W3, Ws1, Ws2, Ws3)
    consts = _consts()
    nc = _ncs if _ncs is not None else build_fused(debug=_debug)
    maps = _fused_maps(x, qkv_w, qkv_b, out_b, imgs, consts)
    res = run_bass_kernel_spmd(nc, maps, core_ids=list(range(8)))
    outp = _fused_assemble(res.results)
    if _debug:
        return outp, res
    return outp
